# version 07
# baseline (speedup 1.0000x reference)
import numpy as np
import ml_dtypes
import concourse.bass as bass
import concourse.mybir as mybir
from concourse.bass_utils import run_bass_kernel_spmd

F32 = mybir.dt.float32
BF16 = mybir.dt.bfloat16
I32 = mybir.dt.int32
U8 = mybir.dt.uint8
AF = mybir.ActivationFunctionType
ALU = mybir.AluOpType
AX = mybir.AxisListType

KB = 1024
NCORES = 8
E = 2048
NH = 16
HD = 128
TOWN = 1024
TLOC = 1536
MLP = 16384
QK_SCALE = float(1.0 / np.sqrt(np.float32(128)))
EPS = 1e-5
EXP_SHIFT = -4.0

LUT = np.array([0.0, 0.5, 1.0, 1.5, 2.0, 3.0, 4.0, 6.0,
                0.0, -0.5, -1.0, -1.5, -2.0, -3.0, -4.0, -6.0], np.float32)


class Sched:
    ENGINES = ("tensor", "vector", "scalar", "sync", "gpsimd")

    def __init__(self, nc):
        self.nc = nc
        self.q = {k: [] for k in self.ENGINES}
        self.sems = {}

    def new_sem(self, ctx, name):
        h = ctx.enter_context(self.nc.semaphore(name))
        self.sems[name] = [h, 0]

    def cur(self, name):
        return self.sems[name][1]

    AUTO = {"vector": "dv", "scalar": "ac", "gpsimd": "gp"}

    def op(self, eng, fn, inc=None, inc_n=1):
        auto = self.AUTO.get(eng)
        if auto is not None and (inc is None or inc == auto):
            rec = self.sems[auto]
            self.wait(eng, auto, rec[1])
            rec[1] += 1
            h = rec[0]
            self.q[eng].append(lambda e, fn=fn, h=h: fn(e).then_inc(h, 1))
            return rec[1]
        if inc is None:
            self.q[eng].append(fn)
            return None
        rec = self.sems[inc]
        rec[1] += inc_n
        h = rec[0]

        def wrapped(e, fn=fn, h=h, n=inc_n):
            fn(e).then_inc(h, n)
        self.q[eng].append(wrapped)
        return rec[1]

    def wait(self, eng, sem, val):
        h = self.sems[sem][0]
        self.q[eng].append(lambda e, h=h, v=val: e.wait_ge(h, v))

    def run(self, block):
        for name in self.ENGINES:
            ops = self.q[name]
            getattr(block, name)(lambda e, ops=ops: [f(e) for f in ops])


class Arena:

    def __init__(self, nc, total_bytes):
        base = nc._sbuf_addr_for_side("left")
        aligned = (base + 31) & ~31
        nc.alloc_sbuf_tensor("arena", [128, total_bytes + (aligned - base)], U8)
        self.nc = nc
        self.base = aligned
        self.total = total_bytes

    def at(self, name, off_bytes, shape, dtype):
        assert off_bytes % 32 == 0
        h = self.nc.alloc_sbuf_tensor_at(name, shape, dtype,
                                         offset=self.base + off_bytes)
        return h


DEC_NW = 1024


def build_decode_nc():
    from contextlib import ExitStack
    nc = bass.Bass("TRN2", target_bir_lowering=False, debug=False)
    pk_g = nc.dram_tensor("pk_g", [128, MLP], I32, kind="ExternalInput").ap()
    se_g = nc.dram_tensor("se_g", [128, MLP], F32, kind="ExternalInput").ap()
    pk_u = nc.dram_tensor("pk_u", [128, MLP], I32, kind="ExternalInput").ap()
    se_u = nc.dram_tensor("se_u", [128, MLP], F32, kind="ExternalInput").ap()
    pk_d = nc.dram_tensor("pk_d", [1024, E], I32, kind="ExternalInput").ap()
    se_d = nc.dram_tensor("se_d", [1024, E], F32, kind="ExternalInput").ap()
    wg_dec = nc.dram_tensor("wg_dec", [256, MLP], BF16, kind="ExternalOutput").ap()
    wu_dec = nc.dram_tensor("wu_dec", [256, MLP], BF16, kind="ExternalOutput").ap()
    wd_dec = nc.dram_tensor("wd_dec", [2048, E], BF16, kind="ExternalOutput").ap()

    jobs = []
    for pk, se, out in ((pk_g, se_g, wg_dec), (pk_u, se_u, wu_dec),
                        (pk_d, se_d, wd_dec)):
        nchunks = pk.shape[0] // 128
        nwin = pk.shape[1] // DEC_NW
        for q in range(nchunks):
            for wi in range(nwin):
                cs = slice(wi * DEC_NW, (wi + 1) * DEC_NW)
                jobs.append((pk[q * 128:(q + 1) * 128, cs],
                             se[q * 128:(q + 1) * 128, cs],
                             out[q * 256: q * 256 + 128, cs],
                             out[q * 256 + 128: q * 256 + 256, cs]))
    n = len(jobs)
    NW = DEC_NW

    with ExitStack() as ctx:
        e = ctx.enter_context
        w_i = [e(nc.sbuf_tensor(f"w_i{p}", [128, NW], I32)) for p in range(2)]
        se_s = [e(nc.sbuf_tensor(f"se_s{p}", [128, NW], F32)) for p in range(2)]
        hs = [e(nc.sbuf_tensor(f"hs{p}", [128, NW], F32)) for p in range(2)]
        ci = [[e(nc.sbuf_tensor(f"ci{p}{pl}", [128, NW], I32)) for pl in range(2)]
              for p in range(2)]
        cb = [e(nc.sbuf_tensor(f"cb{p}", [128, NW], BF16)) for p in range(2)]
        gb = [e(nc.sbuf_tensor(f"gb{p}", [128, NW], BF16)) for p in range(2)]
        g8 = [e(nc.sbuf_tensor(f"g8{p}", [128, NW], BF16)) for p in range(2)]
        rr = [[e(nc.sbuf_tensor(f"rr{p}{pl}", [128, NW], BF16)) for pl in range(2)]
              for p in range(2)]
        ss = [[e(nc.sbuf_tensor(f"ss{p}{pl}", [128, NW], BF16)) for pl in range(2)]
              for p in range(2)]
        u1 = [[e(nc.sbuf_tensor(f"u1{p}{pl}", [128, NW], BF16)) for pl in range(2)]
              for p in range(2)]
        u2 = [[e(nc.sbuf_tensor(f"u2{p}{pl}", [128, NW], BF16)) for pl in range(2)]
              for p in range(2)]
        t1 = [[e(nc.sbuf_tensor(f"t1{p}{pl}", [128, NW], BF16)) for pl in range(2)]
              for p in range(2)]
        ob = [[e(nc.sbuf_tensor(f"ob{p}{pl}", [128, NW], BF16)) for pl in range(2)]
              for p in range(2)]
        bias4 = e(nc.sbuf_tensor("bias4", [128, 1], F32))
        bias12 = e(nc.sbuf_tensor("bias12", [128, 1], F32))

        S = Sched(nc)
        for s in ("ld0", "ld1", "st0", "st1", "dv", "ac", "gp"):
            S.new_sem(ctx, s)

        gp_bias = S.op("gpsimd", lambda g: g.memset(bias4[:], -4.0))
        gp_bias = S.op("gpsimd", lambda g: g.memset(bias12[:], -12.0))

        fin = {}
        rrv = {}
        uuv = {}
        obv = {}
        stv = {}
        ldv = {}

        for j in range(min(2, n)):
            S.op("sync", lambda s, j=j: s.dma_start(
                out=w_i[j % 2][:], in_=jobs[j][0]), inc=f"ld{j % 2}", inc_n=16)
            ldv[j] = S.op("sync", lambda s, j=j: s.dma_start(
                out=se_s[j % 2][:], in_=jobs[j][1]), inc=f"ld{j % 2}", inc_n=16)

        for i in range(n):
            p = i % 2
            S.wait("vector", f"ld{p}", ldv[i])
            S.op("vector", lambda v, p=p: v.tensor_scalar(
                ci[p][0][:], w_i[p][:], 15, None, ALU.bitwise_and))
            S.op("vector", lambda v, p=p: v.tensor_scalar(
                ci[p][1][:], w_i[p][:], 4, 15,
                ALU.logical_shift_right, ALU.bitwise_and))
            fin[i] = S.op("vector", lambda v, p=p: v.tensor_scalar(
                hs[p][:], se_s[p][:], 0.5, None, ALU.mult))
            for pl in range(2):
                if i >= 2:
                    S.wait("vector", "ac", uuv[(i - 2, pl)])
                S.op("vector", lambda v, p=p, pl=pl: v.tensor_copy(
                    cb[p][:], ci[p][pl][:]))
                S.op("vector", lambda v, p=p: v.tensor_scalar(
                    gb[p][:], cb[p][:], 7.5, None, ALU.is_ge))
                S.op("vector", lambda v, p=p: v.tensor_scalar(
                    g8[p][:], gb[p][:], 8.0, None, ALU.mult))
                S.op("vector", lambda v, p=p, pl=pl: v.tensor_scalar(
                    ss[p][pl][:], gb[p][:], -2.0, 1.0, ALU.mult, ALU.add))
                rrv[(i, pl)] = S.op("vector", lambda v, p=p, pl=pl: v.tensor_tensor(
                    rr[p][pl][:], cb[p][:], g8[p][:], ALU.subtract))
            for pl in range(2):
                S.wait("scalar", "dv", rrv[(i, pl)])
                if pl == 0 and i == 0:
                    S.wait("scalar", "gp", gp_bias)
                if i >= 2:
                    S.wait("scalar", "dv", obv[(i - 2, pl)])
                S.op("scalar", lambda s, p=p, pl=pl: s.activation(
                    u1[p][pl][:], rr[p][pl][:], AF.Relu, bias=bias4[:]))
                uuv[(i, pl)] = S.op("scalar", lambda s, p=p, pl=pl: s.activation(
                    u2[p][pl][:], rr[p][pl][:], AF.Relu, bias=bias12[:],
                    scale=2.0))
            for pl in range(2):
                S.wait("vector", "ac", uuv[(i, pl)])
                if i >= 2:
                    S.wait("vector", f"st{p}", stv[i - 2])
                S.op("vector", lambda v, p=p, pl=pl: v.tensor_tensor(
                    t1[p][pl][:], rr[p][pl][:], u1[p][pl][:], ALU.add))
                S.op("vector", lambda v, p=p, pl=pl: v.tensor_tensor(
                    cb[p][:], t1[p][pl][:], u2[p][pl][:], ALU.add))
                S.op("vector", lambda v, p=p, pl=pl: v.tensor_tensor(
                    t1[p][pl][:], cb[p][:], ss[p][pl][:], ALU.mult))
                obv[(i, pl)] = S.op("vector", lambda v, p=p, pl=pl: v.tensor_tensor(
                    ob[p][pl][:], t1[p][pl][:], hs[p][:], ALU.mult))
            S.wait("sync", "dv", obv[(i, 0)])
            S.op("sync", lambda s, i=i, p=p: s.dma_start(
                out=jobs[i][2], in_=ob[p][0][:]), inc=f"st{p}", inc_n=16)
            S.wait("sync", "dv", obv[(i, 1)])
            stv[i] = S.op("sync", lambda s, i=i, p=p: s.dma_start(
                out=jobs[i][3], in_=ob[p][1][:]), inc=f"st{p}", inc_n=16)
            if i + 2 < n:
                S.wait("sync", "dv", fin[i])
                S.op("sync", lambda s, i=i, p=p: s.dma_start(
                    out=w_i[p][:], in_=jobs[i + 2][0]), inc=f"ld{p}", inc_n=16)
                ldv[i + 2] = S.op("sync", lambda s, i=i, p=p: s.dma_start(
                    out=se_s[p][:], in_=jobs[i + 2][1]), inc=f"ld{p}", inc_n=16)
        S.wait("sync", "st0", S.cur("st0"))
        S.wait("sync", "st1", S.cur("st1"))

        block = ctx.enter_context(nc.Block())
        S.run(block)
    return nc


def build_attn_v0(dbg=False, dbg2=False):
    from contextlib import ExitStack
    nc = bass.Bass("TRN2", target_bir_lowering=False, debug=False)
    if dbg2:
        d_exp = nc.dram_tensor("d_exp", [128, 1024], BF16, kind="ExternalOutput").ap()
        d_pn = nc.dram_tensor("d_pn", [128, 1024], BF16, kind="ExternalOutput").ap()
        d_nrm = nc.dram_tensor("d_nrm", [128, 1024], BF16, kind="ExternalOutput").ap()
        d_pt = nc.dram_tensor("d_pt", [128, 4096], BF16, kind="ExternalOutput").ap()
        d_sm = nc.dram_tensor("d_sm", [128, 32], F32, kind="ExternalOutput").ap()
    if dbg:
        d_q = nc.dram_tensor("d_q", [128, NH * TOWN], BF16, kind="ExternalOutput").ap()
        d_k = nc.dram_tensor("d_k", [128, NH * TLOC], BF16, kind="ExternalOutput").ap()
        d_v = nc.dram_tensor("d_v", [128, 12 * E], BF16, kind="ExternalOutput").ap()
        d_at = nc.dram_tensor("d_at", [128, 2 * NH * 512], BF16, kind="ExternalOutput").ap()
    xt_in = nc.dram_tensor("xt_in", [E, TLOC], F32, kind="ExternalInput").ap()
    x_own = nc.dram_tensor("x_own", [TOWN, E], F32, kind="ExternalInput").ap()
    wqkvT = nc.dram_tensor("wqkvT", [E, 3 * E], F32, kind="ExternalInput").ap()
    woutT = nc.dram_tensor("woutT", [E, E], F32, kind="ExternalInput").ap()
    mask_in = nc.dram_tensor("mask_in", [128, 8 * 1024], BF16,
                             kind="ExternalInput").ap()
    x1_out = nc.dram_tensor("x1_out", [TOWN, E], F32, kind="ExternalOutput").ap()

    with ExitStack() as ctx:
        ar = Arena(nc, 204 * KB)
        kT = ar.at("kT", 0, [128, NH * TLOC], BF16)
        vv = ar.at("vv", 48 * KB, [128, 12 * E], BF16)
        qT = ar.at("qT", 96 * KB, [128, NH * TOWN], BF16)
        xt = ar.at("xt", 128 * KB, [128, 16 * TLOC], BF16)
        wq_sl = [ar.at(f"wq{p}", (176 + 4 * p) * KB, [128, 16 * 128], BF16)
                 for p in range(2)]
        wv_sl = ar.at("wv", 184 * KB, [128, 16 * 512], BF16)
        masks = ar.at("masks", 128 * KB, [128, 8 * 1024], BF16)
        probs = [ar.at(f"pb{p}", (144 + 2 * p) * KB, [128, 1024], BF16)
                 for p in range(2)]
        probsn = ar.at("pn", 148 * KB, [128, 1024], BF16)
        probsT = ar.at("probsT", 150 * KB, [128, 8 * 512], BF16)
        attnT = ar.at("attnT", 158 * KB, [128, 2 * NH * 512], BF16)
        wo = ar.at("wo", 0, [128, 16 * E], BF16)
        xo = ar.at("xo", 64 * KB, [128, E], F32)
        resid = ar.at("resid", 72 * KB, [128, E], F32)
        x1til = ar.at("x1til", 80 * KB, [128, E], F32)
        sqt = ar.at("sqt", 88 * KB, [128, E], F32)
        sm = ar.at("sm", 200 * KB, [128, 32], F32)
        ident = ar.at("ident", 201 * KB, [128, 128], BF16)
        shiftB = ar.at("shiftB", int(201.5 * KB), [128, 1], F32)
        trash = ar.at("trash", 202 * KB, [128, 1024], BF16)

        S = Sched(nc)
        for s in ("s_xt", "s_wq0", "s_wq1", "s_wv", "s_wo",
                  "sld", "pe", "dv", "ac", "gp", "st"):
            S.new_sem(ctx, s)
        ps_S0 = ctx.enter_context(nc.psum_tensor("S0", [128, 1024], F32))
        ps_S1 = ctx.enter_context(nc.psum_tensor("S1", [128, 1024], F32))
        ps_TP = [ctx.enter_context(nc.psum_tensor(f"TP{i}", [128, 1024], BF16))
                 for i in range(2)]
        ps_AV = ctx.enter_context(nc.psum_tensor("AVP", [128, 512], F32))
        ps_OP = ctx.enter_context(nc.psum_tensor("OPP", [128, 512], F32))
        surfs = [ps_S0[:, 0:512], ps_S0[:, 512:1024],
                 ps_S1[:, 0:512], ps_S1[:, 512:1024]]

        m1 = S.op("gpsimd", lambda g: g.memset(ident[:], 0.0), inc="gp")
        S.wait("gpsimd", "gp", m1)
        id_done = S.op("gpsimd", lambda g: g.affine_select(
            out=ident[:], in_=ident[:], compare_op=ALU.not_equal, fill=1.0,
            base=0, pattern=[[-1, 128]], channel_multiplier=1), inc="gp")
        c_init = S.op("gpsimd", lambda g: g.memset(shiftB[:], EXP_SHIFT),
                      inc="gp")

        xt_src = xt_in.rearrange("(c p) t -> p c t", p=128)
        xt_ld = S.op("gpsimd", lambda g: g.dma_start(
            out=xt[:].rearrange("p (c t) -> p c t", t=TLOC), in_=xt_src),
            inc="s_xt", inc_n=16)

        wsrc = wqkvT.rearrange("(c p) f -> p c f", p=128)
        group_done_dv = []
        group_idx = 0
        slab_pe_after = {}
        pe_val = None

        qk_jobs = []
        for ft in range(32):
            wins = ((512, 0), (1024, 512)) if ft < 16 else \
                   ((0, 0), (512, 512), (1024, 1024))
            qk_jobs.append((ft, wins))

        for ji, (ft, wins) in enumerate(qk_jobs):
            p = ft % 2
            if ji >= 2:
                S.wait("gpsimd", "pe", slab_pe_after[qk_jobs[ji - 2][0]])
            fsl = slice(ft * 128, (ft + 1) * 128)
            gld_slab = S.op("gpsimd", lambda g, p=p, fsl=fsl: g.dma_start(
                out=wq_sl[p][:].rearrange("p (c f) -> p c f", f=128),
                in_=wsrc[:, :, fsl]), inc=f"s_wq{p}", inc_n=16)
            for off, qoff in wins:
                surf = surfs[group_idx % 4]
                if ji == 0:
                    S.wait("tensor", "s_xt", xt_ld)
                S.wait("tensor", f"s_wq{p}", gld_slab)
                if group_idx >= 4:
                    S.wait("tensor", "dv", group_done_dv[group_idx - 4])
                for ec in range(16):
                    def mm(t, surf=surf, p=p, ec=ec, off=off):
                        return t.matmul(
                            surf, wq_sl[p][:, ec * 128:(ec + 1) * 128],
                            xt[:, ec * TLOC + off: ec * TLOC + off + 512],
                            start=(ec == 0), stop=(ec == 15))
                    pe_val = S.op("tensor", mm, inc="pe") if ec == 15 \
                        else S.op("tensor", mm)
                S.wait("vector", "pe", S.cur("pe"))
                if ft < 16:
                    dst = qT[:, ft * TOWN + qoff: ft * TOWN + qoff + 512]
                    dvv = S.op("vector", lambda v, dst=dst, surf=surf:
                               v.tensor_scalar(dst, surf, QK_SCALE, None,
                                               ALU.mult), inc="dv")
                else:
                    dst = kT[:, (ft - 16) * TLOC + off:
                             (ft - 16) * TLOC + off + 512]
                    dvv = S.op("vector", lambda v, dst=dst, surf=surf:
                               v.tensor_copy(dst, surf), inc="dv")
                group_done_dv.append(dvv)
                group_idx += 1
            slab_pe_after[ft] = S.cur("pe")

        v_pe_after = {}
        for fw in range(4):
            if fw >= 1:
                S.wait("gpsimd", "pe", v_pe_after[fw - 1])
            fsl = slice(2 * E + fw * 512, 2 * E + (fw + 1) * 512)
            gld_v = S.op("gpsimd", lambda g, fsl=fsl: g.dma_start(
                out=wv_sl[:].rearrange("p (c f) -> p c f", f=512),
                in_=wsrc[:, :, fsl]), inc="s_wv", inc_n=16)
            for tt in range(12):
                surf = surfs[group_idx % 4]
                S.wait("tensor", "s_wv", gld_v)
                if group_idx >= 4:
                    S.wait("tensor", "dv", group_done_dv[group_idx - 4])
                for ec in range(16):
                    def mmv(t, surf=surf, ec=ec, tt=tt):
                        return t.matmul(
                            surf,
                            xt[:, ec * TLOC + tt * 128: ec * TLOC + (tt + 1) * 128],
                            wv_sl[:, ec * 512:(ec + 1) * 512],
                            start=(ec == 0), stop=(ec == 15))
                    S.op("tensor", mmv, inc="pe" if ec == 15 else None)
                S.wait("vector", "pe", S.cur("pe"))
                dst = vv[:, tt * E + fw * 512: tt * E + (fw + 1) * 512]
                dvv = S.op("vector", lambda v, dst=dst, surf=surf:
                           v.tensor_copy(dst, surf), inc="dv")
                group_done_dv.append(dvv)
                group_idx += 1
            v_pe_after[fw] = S.cur("pe")

        qkv_done_dv = group_done_dv[-1]
        qkv_done_pe = S.cur("pe")

        S.wait("sync", "pe", qkv_done_pe)
        if dbg:
            S.wait("sync", "dv", qkv_done_dv)
            S.op("sync", lambda s: s.dma_start(out=d_q, in_=qT[:]),
                 inc="st", inc_n=16)
            S.op("sync", lambda s: s.dma_start(out=d_k, in_=kT[:]),
                 inc="st", inc_n=16)
            S.op("sync", lambda s: s.dma_start(out=d_v, in_=vv[:]),
                 inc="st", inc_n=16)
            S.sems["st"][1] += 0
        ld_mask = S.op("sync", lambda s: s.dma_start(out=masks[:], in_=mask_in),
                       inc="sld", inc_n=16)

        S.wait("scalar", "gp", c_init)
        S.wait("tensor", "gp", id_done)
        S.wait("vector", "sld", ld_mask)

        exp_ac, tr_pe, ptk_dv, av_pe, att_dv, sum_gp = {}, {}, {}, {}, {}, {}
        dbg_gates = {}
        last_scores_pe = None
        chs = [(c, h) for c in range(2) for h in range(16)]
        for ci_, (c, h) in enumerate(chs):
            for k in range(4):
                Sk = ps_S0 if k % 2 == 0 else ps_S1
                if ci_ == 0 and k == 0:
                    S.wait("tensor", "dv", qkv_done_dv)
                prev = (c, h, k - 2) if k >= 2 else \
                    (chs[ci_ - 1] + (k + 2,) if ci_ > 0 else None)
                if prev in exp_ac:
                    S.wait("tensor", "ac", exp_ac[prev])
                q0 = h * TOWN + c * 512 + k * 128
                k0 = h * TLOC + c * 512
                S.op("tensor", lambda t, Sk=Sk, q0=q0, k0=k0: t.matmul(
                    Sk[:, 0:512], qT[:, q0:q0 + 128], kT[:, k0:k0 + 512],
                    start=True, stop=True))
                pe_sc = S.op("tensor", lambda t, Sk=Sk, q0=q0, k0=k0: t.matmul(
                    Sk[:, 512:1024], qT[:, q0:q0 + 128],
                    kT[:, k0 + 512:k0 + 1024], start=True, stop=True), inc="pe")
                last_scores_pe = pe_sc
                S.wait("scalar", "pe", pe_sc)
                if prev in tr_pe:
                    S.wait("scalar", "pe", tr_pe[prev])
                if dbg2 and (c, h, k) == (1, 0, 2):
                    S.wait("scalar", "st", dbg_gates["nrm"])
                exp_ac[(c, h, k)] = S.op("scalar", lambda s, Sk=Sk, k=k:
                                         s.activation(probs[k % 2][:], Sk[:],
                                                      AF.Exp, bias=shiftB[:]),
                                         inc="ac")
                if dbg2 and (c, h, k) == (1, 0, 0):
                    S.wait("sync", "ac", exp_ac[(c, h, k)])
                    dbg_gates["exp"] = S.op("sync", lambda s: s.dma_start(
                        out=d_exp, in_=probs[0][:]), inc="st", inc_n=16)
                S.wait("vector", "ac", exp_ac[(c, h, k)])
                if dbg2 and (c, h, k) == (1, 0, 1):
                    S.wait("vector", "st", dbg_gates["pn"])
                pk_prev = (c, h, k - 1) if k >= 1 else \
                    (chs[ci_ - 1] + (3,) if ci_ > 0 else None)
                if pk_prev in sum_gp:
                    S.wait("vector", "ac", sum_gp[pk_prev])
                mview = masks[:, (c * 4 + k) * 1024:(c * 4 + k + 1) * 1024]
                pm = S.op("vector", lambda v, k=k, mview=mview: v.tensor_tensor(
                    probsn[:], probs[k % 2][:], mview, ALU.mult), inc="dv")
                if dbg2 and (c, h, k) == (1, 0, 0):
                    S.wait("sync", "dv", pm)
                    dbg_gates["pn"] = S.op("sync", lambda s: s.dma_start(
                        out=d_pn, in_=probsn[:]), inc="st", inc_n=16)
                S.wait("scalar", "dv", pm)
                sum_gp[(c, h, k)] = S.op("scalar", lambda s, k=k: s.activation(
                    trash[:], probsn[:], AF.Identity,
                    accum_out=sm[:, k:k + 1]), inc="ac")
                S.wait("vector", "ac", sum_gp[(c, h, k)])
                S.op("vector", lambda v, k=k: v.reciprocal(sm[:, 4 + k:5 + k],
                                                           sm[:, k:k + 1]))
                if dbg2 and (c, h, k) == (1, 0, 0):
                    S.wait("vector", "st", dbg_gates["exp"])
                nrm = S.op("vector", lambda v, k=k: v.tensor_scalar(
                    probs[k % 2][:], probsn[:], sm[:, 4 + k:5 + k], None,
                    ALU.mult), inc="dv")
                if dbg2 and (c, h, k) == (1, 0, 0):
                    S.wait("sync", "dv", nrm)
                    S.op("sync", lambda s: s.dma_start(out=d_sm, in_=sm[:]),
                         inc="st", inc_n=16)
                    dbg_gates["nrm"] = S.op("sync", lambda s: s.dma_start(
                        out=d_nrm, in_=probs[0][:]), inc="st", inc_n=16)
                S.wait("tensor", "dv", nrm)
                if prev in ptk_dv:
                    S.wait("tensor", "dv", ptk_dv[prev])
                for m in range(8):
                    def trp(t, k=k, m=m):
                        return t.transpose(
                            ps_TP[k % 2][:, m * 128:(m + 1) * 128],
                            probs[k % 2][:, m * 128:(m + 1) * 128], ident[:])
                    if m == 7:
                        tr_pe[(c, h, k)] = S.op("tensor", trp, inc="pe")
                    else:
                        S.op("tensor", trp)
                S.wait("vector", "pe", tr_pe[(c, h, k)])
                if ci_ > 0 and k == 0:
                    S.wait("vector", "pe", av_pe[chs[ci_ - 1]])
                if dbg2 and (c, h, k) == (1, 1, 0):
                    S.wait("vector", "st", dbg_gates["pt"])
                def evt(v, k=k):
                    src = ps_TP[k % 2][:].rearrange("p (m f) -> p m f", f=128)
                    dst = probsT[:].rearrange("p (m f) -> p m f", f=512)[
                        :, :, k * 128:(k + 1) * 128]
                    return v.tensor_copy(dst, src)
                ptk_dv[(c, h, k)] = S.op("vector", evt, inc="dv")
            if dbg2 and (c, h) == (1, 0):
                S.wait("sync", "dv", ptk_dv[(c, h, 3)])
                dbg_gates["pt"] = S.op("sync", lambda s: s.dma_start(
                    out=d_pt, in_=probsT[:]), inc="st", inc_n=16)
            S.wait("tensor", "dv", ptk_dv[(c, h, 3)])
            if ci_ > 0:
                S.wait("tensor", "dv", att_dv[chs[ci_ - 1]])
            for m in range(8):
                def av(t, c=c, h=h, m=m):
                    return t.matmul(
                        ps_AV[:],
                        vv[:, (4 * c + m) * E + h * 128:
                           (4 * c + m) * E + (h + 1) * 128],
                        probsT[:, m * 512:(m + 1) * 512],
                        start=(m == 0), stop=(m == 7))
                if m == 7:
                    av_pe[(c, h)] = S.op("tensor", av, inc="pe")
                else:
                    S.op("tensor", av)
            S.wait("vector", "pe", av_pe[(c, h)])
            dstA = attnT[:, (c * 16 + h) * 512:(c * 16 + h + 1) * 512]
            att_dv[(c, h)] = S.op("vector", lambda v, dstA=dstA: v.tensor_copy(
                dstA, ps_AV[:]), inc="dv")

        S.wait("gpsimd", "pe", av_pe[chs[-1]])
        wo_src = woutT.rearrange("(c p) f -> p c f", p=128)
        ld_wo = S.op("gpsimd", lambda g: g.dma_start(
            out=wo[:].rearrange("p (c f) -> p c f", f=E), in_=wo_src),
            inc="s_wo", inc_n=16)

        S.wait("sync", "dv", att_dv[chs[-1]])
        op_dv = None
        ln_sq_ac = {}
        st_val = 0
        for tt in range(8):
            c = tt // 4
            if tt >= 1:
                S.wait("sync", "ac", ln_sq_ac[tt - 1])
            xo_ld = S.op("sync", lambda s, tt=tt: s.dma_start(
                out=xo[:], in_=x_own[tt * 128:(tt + 1) * 128, :]),
                inc="sld", inc_n=16)
            for fw in range(4):
                if (tt, fw) == (0, 0):
                    S.wait("tensor", "s_wo", ld_wo)
                    S.wait("tensor", "dv", att_dv[chs[-1]])
                else:
                    S.wait("tensor", "dv", op_dv)
                for ec in range(16):
                    def mo(t, c=c, tt=tt, fw=fw, ec=ec):
                        lhs = attnT[:, (c * 16 + ec) * 512 + (tt % 4) * 128:
                                    (c * 16 + ec) * 512 + (tt % 4) * 128 + 128]
                        rhs = wo[:, ec * E + fw * 512: ec * E + fw * 512 + 512]
                        return t.matmul(ps_OP[:], lhs, rhs,
                                        start=(ec == 0), stop=(ec == 15))
                    S.op("tensor", mo, inc="pe" if ec == 15 else None)
                pe_op = S.cur("pe")
                S.wait("vector", "pe", pe_op)
                S.wait("vector", "sld", xo_ld)
                def rp(v, fw=fw):
                    return v.tensor_tensor(resid[:, fw * 512:(fw + 1) * 512],
                                           ps_OP[:],
                                           xo[:, fw * 512:(fw + 1) * 512],
                                           ALU.add)
                op_dv = S.op("vector", rp, inc="dv")
            S.op("vector", lambda v: v.reduce_sum(sm[:, 8:9], resid[:], AX.X,
                                                  negate=True))
            nm_dv = S.op("vector", lambda v: v.tensor_scalar(
                sm[:, 9:10], sm[:, 8:9], 1.0 / E, None, ALU.mult), inc="dv")
            S.wait("scalar", "dv", nm_dv)
            ln_sq_ac[tt] = S.op("scalar", lambda s: s.activation(
                sqt[:], resid[:], AF.Square, accum_out=sm[:, 10:11]), inc="ac")
            S.wait("vector", "ac", ln_sq_ac[tt])
            S.op("vector", lambda v: v.tensor_scalar(
                sm[:, 11:12], sm[:, 10:11], 1.0 / E, None, ALU.mult))
            S.op("vector", lambda v: v.tensor_tensor(
                sm[:, 12:13], sm[:, 9:10], sm[:, 9:10], ALU.mult))
            S.op("vector", lambda v: v.tensor_tensor(
                sm[:, 13:14], sm[:, 11:12], sm[:, 12:13], ALU.subtract))
            S.op("vector", lambda v: v.tensor_scalar(
                sm[:, 14:15], sm[:, 13:14], EPS, None, ALU.add))
            rv_dv = S.op("vector", lambda v: v.reciprocal(
                sm[:, 15:16], sm[:, 14:15]), inc="dv")
            S.wait("scalar", "dv", rv_dv)
            rstd_ac = S.op("scalar", lambda s: s.activation(
                sm[:, 16:17], sm[:, 15:16], AF.Sqrt), inc="ac")
            S.wait("vector", "ac", rstd_ac)
            S.wait("vector", "st", st_val)
            x1_dv = S.op("vector", lambda v: v.tensor_scalar(
                x1til[:], resid[:], sm[:, 9:10], sm[:, 16:17],
                ALU.add, ALU.mult), inc="dv")
            S.wait("sync", "dv", x1_dv)
            st_val = S.op("sync", lambda s, tt=tt: s.dma_start(
                out=x1_out[tt * 128:(tt + 1) * 128, :], in_=x1til[:]),
                inc="st", inc_n=16)
        if dbg:
            S.op("sync", lambda s: s.dma_start(out=d_at, in_=attnT[:]),
                 inc="st", inc_n=16)
        S.wait("sync", "st", S.cur("st"))

        block = ctx.enter_context(nc.Block())
        S.run(block)
    return nc


def build_ffn_v0(skip_f2=False, skip_down=False, skip_pool=False,
                 skip_gelu=False, n_groups=16):
    from contextlib import ExitStack
    nc = bass.Bass("TRN2", target_bir_lowering=False, debug=False)
    x1_in = nc.dram_tensor("x1_in", [TOWN, E], F32, kind="ExternalInput").ap()
    wg_all = nc.dram_tensor("wg_all", [E, MLP], BF16, kind="ExternalInput").ap()
    wu_all = nc.dram_tensor("wu_all", [E, MLP], BF16, kind="ExternalInput").ap()
    wd_all = nc.dram_tensor("wd_all", [MLP, E], BF16, kind="ExternalInput").ap()
    y_out = nc.dram_tensor("y_out", [TOWN, E], F32, kind="ExternalOutput").ap()
    acc_d = nc.dram_tensor("acc_d", [TOWN, E], F32).ap()

    C1 = 0.7978845608028654
    C2 = C1 * 0.044715

    with ExitStack() as ctx:
        ar = Arena(nc, 200 * KB)
        x1T = ar.at("x1T", 0, [128, 16 * TOWN], BF16)
        interT = ar.at("interT", 32 * KB, [128, 8 * TOWN], BF16)
        wg_sl = [ar.at(f"wg{p}", (48 + 16 * p) * KB, [128, 16 * 512], BF16)
                 for p in range(2)]
        wu_sl = [ar.at(f"wu{p}", (80 + 16 * p) * KB, [128, 16 * 512], BF16)
                 for p in range(2)]
        wd_sl = [ar.at(f"wd{p}", (112 + 32 * p) * KB, [128, 8 * E], BF16)
                 for p in range(2)]
        gt = [[ar.at(f"gt{p}{i}", (176 + 8 * p + i) * KB, [128, 512], BF16)
               for i in range(8)] for p in range(2)]
        accv = [ar.at(f"accv{p}", (192 + 2 * p) * KB, [128, 512], F32)
                for p in range(2)]
        xo1 = ar.at("xo1", 112 * KB, [128, E], F32)
        x1b = ar.at("x1b", 120 * KB, [128, E], BF16)
        resid = ar.at("resid3", 112 * KB, [128, E], F32)
        accr = ar.at("accr", 120 * KB, [128, E], F32)
        ytil = ar.at("ytil", 128 * KB, [128, E], F32)
        sqf = ar.at("sqf", 136 * KB, [128, E], F32)
        xo3 = ar.at("xo3", 144 * KB, [128, E], F32)
        sm = ar.at("sm3", 196 * KB, [128, 32], F32)
        ident = ar.at("ident3", 197 * KB, [128, 128], BF16)

        S = Sched(nc)
        for s in ("s_x1", "s_wd0", "s_wd1", "s_gu0", "s_gu1", "s_f3",
                  "ga0", "ga1", "pe", "dv", "ac", "gp", "st"):
            S.new_sem(ctx, s)
        ps_S0 = ctx.enter_context(nc.psum_tensor("S0", [128, 1024], F32))
        ps_S1 = ctx.enter_context(nc.psum_tensor("S1", [128, 1024], F32))
        ps_TP = [ctx.enter_context(nc.psum_tensor(f"TP{i}", [128, 1024], BF16))
                 for i in range(2)]
        ps_D = [ctx.enter_context(nc.psum_tensor(f"D{i}", [128, 512], F32))
                for i in range(2)]
        gsurf = [ps_S0[:, 0:512], ps_S0[:, 512:1024]]
        usurf = [ps_S1[:, 0:512], ps_S1[:, 512:1024]]

        m1 = S.op("gpsimd", lambda g: g.memset(ident[:], 0.0), inc="gp")
        S.wait("gpsimd", "gp", m1)
        id_gp = S.op("gpsimd", lambda g: g.affine_select(
            out=ident[:], in_=ident[:], compare_op=ALU.not_equal, fill=1.0,
            base=0, pattern=[[-1, 128]], channel_multiplier=1), inc="gp")

        S.wait("tensor", "gp", id_gp)
        f1_evac = None
        cast_dv = {}
        for tt in range(8):
            if tt >= 1:
                S.wait("sync", "dv", cast_dv[tt - 1])
            xo_ld = S.op("sync", lambda s, tt=tt: s.dma_start(
                out=xo1[:], in_=x1_in[tt * 128:(tt + 1) * 128, :]),
                inc="s_x1", inc_n=16)
            S.wait("vector", "s_x1", xo_ld)
            if tt >= 1:
                S.wait("vector", "pe", tp_pe)
            cast_dv[tt] = S.op("vector", lambda v: v.tensor_copy(x1b[:], xo1[:]),
                               inc="dv")
            S.wait("tensor", "dv", cast_dv[tt])
            if tt >= 1:
                S.wait("tensor", "dv", f1_evac)
            for hc in range(16):
                def tr1(t, hc=hc):
                    base = 256 * (hc // 2) + (hc % 2)
                    src = x1b[:, base: base + 255: 2]
                    return t.transpose(
                        ps_TP[hc // 8][:, (hc % 8) * 128:(hc % 8 + 1) * 128],
                        src, ident[:])
                S.op("tensor", tr1, inc="pe" if hc == 15 else None)
            tp_pe = S.cur("pe")
            S.wait("vector", "pe", tp_pe)
            for half in range(2):
                def ev1(v, tt=tt, half=half):
                    src = ps_TP[half][:].rearrange("p (m f) -> p m f", f=128)
                    dst = x1T[:].rearrange("p (c t) -> p c t", t=TOWN)[
                        :, half * 8:(half + 1) * 8, tt * 128:(tt + 1) * 128]
                    return v.tensor_copy(dst, src)
                f1_evac = S.op("vector", ev1, inc="dv")
        f1_done_dv = f1_evac

        wgsrc = wg_all.rearrange("(c p) n -> p c n", p=128)
        wusrc = wu_all.rearrange("(c p) n -> p c n", p=128)
        g_idx = 0
        gelu_dv = []
        gelu_free = []
        gp_p_l, ac_th_l, gp_q_l, dv_arg_l = [], [], [], []
        slab_sld = {}
        slab_pe_done = {}
        wd_sld = {}
        down_pe_G = {}
        ga_vals = []
        accv_dv = []
        inter_done = {}

        for G in range(n_groups):
            if skip_f2:
                break
            if G < 2:
                S.wait("sync", "dv", f1_done_dv)
            elif not skip_down:
                S.wait("sync", "pe", down_pe_G[G - 2])
            lds = []
            if not skip_down:
                for c_l in range(8):
                    row0 = (8 * G + c_l) * 128
                    def ldwd(s, row0=row0, G=G, c_l=c_l):
                        return s.dma_start(
                            out=wd_sl[G % 2][:, c_l * E:(c_l + 1) * E],
                            in_=wd_all[row0:row0 + 128, :])
                    lds.append(S.op("sync", ldwd, inc=f"s_wd{G % 2}", inc_n=16))
                wd_sld[G] = lds[-1]
            for sw in range(2):
                if G >= 1:
                    S.wait("sync", "pe", slab_pe_done[(G - 1, sw)])
                nsl = slice(1024 * G + 512 * sw, 1024 * G + 512 * (sw + 1))
                S.op("sync", lambda s, sw=sw, nsl=nsl: s.dma_start(
                    out=wg_sl[sw][:].rearrange("p (c n) -> p c n", n=512),
                    in_=wgsrc[:, :, nsl]), inc=f"s_gu{sw}", inc_n=16)
                slab_sld[(G, sw)] = S.op("sync", lambda s, sw=sw, nsl=nsl:
                                         s.dma_start(
                    out=wu_sl[sw][:].rearrange("p (c n) -> p c n", n=512),
                    in_=wusrc[:, :, nsl]), inc=f"s_gu{sw}", inc_n=16)
            for c_l in range(8):
                sw = c_l // 4
                woff = 256 * ((c_l % 4) // 2)
                sub = c_l % 2
                for tw in range(2):
                    gs = gsurf[g_idx % 2]
                    us = usurf[g_idx % 2]
                    S.wait("tensor", f"s_gu{sw}", slab_sld[(G, sw)])
                    if g_idx == 0:
                        S.wait("tensor", "dv", f1_done_dv)
                    if g_idx >= 2:
                        ac_g2, dv_ub = gelu_free[g_idx - 2]
                        S.wait("tensor", "ac", ac_g2)
                        S.wait("tensor", "dv", dv_ub)
                    for wgt, surf in ((wg_sl[sw], gs), (wu_sl[sw], us)):
                        for hc in range(16):
                            def mmg(t, wgt=wgt, surf=surf, hc=hc, woff=woff,
                                    sub=sub, tw=tw):
                                lhs = wgt[:, hc * 512 + woff + sub:
                                          hc * 512 + woff + sub + 255: 2]
                                rhs = x1T[:, hc * TOWN + tw * 512:
                                          hc * TOWN + tw * 512 + 512]
                                return t.matmul(surf, lhs, rhs,
                                                start=(hc == 0), stop=(hc == 15))
                            S.op("tensor", mmg,
                                 inc="pe" if hc == 15 else None)
                    pe_gu = S.cur("pe")
                    if skip_gelu:
                        S.wait("vector", "pe", pe_gu)
                        dst_i = interT[:, c_l * TOWN + tw * 512:
                                       c_l * TOWN + tw * 512 + 512]
                        dv_int = S.op("vector", lambda v, dst_i=dst_i, gs=gs:
                                      v.tensor_copy(dst_i, gs))
                        dv_ub = S.op("vector", lambda v, us=us:
                                     v.tensor_copy(gt[0][3][:], us))
                        gelu_dv.append(dv_int)
                        gelu_free.append((0, dv_ub))
                        gp_p_l.append(0); ac_th_l.append(0); gp_q_l.append(0)
                        dv_arg_l.append(dv_int)
                        slab_pe_done[(G, sw)] = pe_gu
                        g_idx += 1
                        continue
                    par = g_idx % 2
                    g2, pp, gb, ub, arg, th, gu, q8 = gt[par]
                    S.wait("vector", "pe", pe_gu)
                    dv_gb = S.op("vector", lambda v, gb=gb, gs=gs:
                                 v.tensor_copy(gb[:], gs))
                    dv_ub = S.op("vector", lambda v, ub=ub, us=us:
                                 v.tensor_copy(ub[:], us), inc="dv")
                    S.wait("scalar", "dv", dv_gb)
                    if g_idx >= 2:
                        S.wait("scalar", "gp", gp_p_l[g_idx - 2])
                        S.wait("scalar", "gp", gp_q_l[g_idx - 2])
                    ac_g2 = S.op("scalar", lambda s, g2=g2, gb=gb: s.activation(
                        g2[:], gb[:], AF.Square), inc="ac")
                    peng = "vector" if skip_pool else "gpsimd"
                    pchain = "dv" if skip_pool else "gp"
                    S.wait(peng, "ac", ac_g2)
                    if g_idx >= 2 and not skip_pool:
                        S.wait(peng, "dv", dv_arg_l[g_idx - 2])
                        S.wait(peng, "dv", gelu_dv[g_idx - 2])
                    gp_p = S.op(peng, lambda g, pp=pp, g2=g2: g.tensor_scalar(
                        pp[:], g2[:], C2, C1, ALU.mult, ALU.add), inc=pchain)
                    if not skip_pool:
                        S.wait("vector", "gp", gp_p)
                    if g_idx >= 2:
                        S.wait("vector", "ac", ac_th_l[g_idx - 2])
                    dv_arg = S.op("vector", lambda v, arg=arg, pp=pp, gb=gb:
                                  v.tensor_tensor(arg[:], pp[:], gb[:],
                                                  ALU.mult), inc="dv")
                    S.wait("scalar", "dv", dv_arg)
                    if g_idx >= 2:
                        S.wait("scalar", "gp", gp_q_l[g_idx - 2])
                    ac_th = S.op("scalar", lambda s, th=th, arg=arg: s.activation(
                        th[:], arg[:], AF.Tanh), inc="ac")
                    S.wait(peng, "ac", ac_th)
                    gp_q = S.op(peng, lambda g, q8=q8, th=th: g.tensor_scalar(
                        q8[:], th[:], 0.5, 0.5, ALU.mult, ALU.add), inc=pchain)
                    S.op("vector", lambda v, gu=gu, gb=gb, ub=ub:
                         v.tensor_tensor(gu[:], gb[:], ub[:], ALU.mult))
                    if not skip_pool:
                        S.wait("vector", "gp", gp_q)
                    dst_i = interT[:, c_l * TOWN + tw * 512:
                                   c_l * TOWN + tw * 512 + 512]
                    dv_int = S.op("vector", lambda v, dst_i=dst_i, q8=q8, gu=gu:
                                  v.tensor_tensor(dst_i, q8[:], gu[:],
                                                  ALU.mult), inc="dv")
                    gelu_dv.append(dv_int)
                    gelu_free.append((0, dv_ub))
                    gp_p_l.append(gp_p)
                    ac_th_l.append(ac_th)
                    gp_q_l.append(gp_q)
                    dv_arg_l.append(dv_arg)
                    slab_pe_done[(G, sw)] = pe_gu
                    g_idx += 1
            inter_done[G] = gelu_dv[-1]
            if skip_down:
                continue
            for tt in range(8):
                for hw in range(4):
                    di = (tt * 4 + hw) % 2
                    S.wait("tensor", "dv", inter_done[G])
                    S.wait("tensor", f"s_wd{G % 2}", wd_sld[G])
                    if (G, tt, hw) != (0, 0, 0):
                        idx = G * 32 + tt * 4 + hw
                        if idx >= 2:
                            S.wait("tensor", "dv", accv_dv[idx - 2])
                    for c_l in range(8):
                        def mmd(t, di=di, G=G, c_l=c_l, tt=tt, hw=hw):
                            lhs = interT[:, c_l * TOWN + tt * 128:
                                         c_l * TOWN + tt * 128 + 128]
                            rhs = wd_sl[G % 2][:, c_l * E + hw * 512:
                                               c_l * E + hw * 512 + 512]
                            return t.matmul(ps_D[di][:], lhs, rhs,
                                            start=(c_l == 0), stop=(c_l == 7))
                        S.op("tensor", mmd, inc="pe" if c_l == 7 else None)
                    pe_d = S.cur("pe")
                    S.wait("vector", "pe", pe_d)
                    idx = G * 32 + tt * 4 + hw
                    if idx >= 2:
                        S.wait("vector", f"ga{di}", ga_vals[idx - 2])
                    adv = S.op("vector", lambda v, di=di: v.tensor_copy(
                        accv[di][:], ps_D[di][:]), inc="dv")
                    accv_dv.append(adv)
                    S.wait("gpsimd", "dv", adv)
                    def ga_op(g, di=di, tt=tt, hw=hw, G=G):
                        dst = acc_d[tt * 128:(tt + 1) * 128,
                                    hw * 512:(hw + 1) * 512]
                        if G == 0:
                            return g.dma_start(out=dst, in_=accv[di][:])
                        return g.dma_start(out=dst, in_=accv[di][:],
                                           accum_op=ALU.add)
                    ga_vals.append(S.op("gpsimd", ga_op, inc=f"ga{di}",
                                        inc_n=16))
            down_pe_G[G] = S.cur("pe")

        S.wait("sync", "ga0", S.cur("ga0"))
        S.wait("sync", "ga1", S.cur("ga1"))
        st_val = 0
        sq_ac = {}
        for tt in range(8):
            if tt >= 1:
                S.wait("sync", "ac", sq_ac[tt - 1])
            ld_x = S.op("sync", lambda s, tt=tt: s.dma_start(
                out=xo3[:], in_=x1_in[tt * 128:(tt + 1) * 128, :]),
                inc="s_f3", inc_n=16)
            ld_a = S.op("sync", lambda s, tt=tt: s.dma_start(
                out=accr[:], in_=acc_d[tt * 128:(tt + 1) * 128, :]),
                inc="s_f3", inc_n=16)
            S.wait("vector", "s_f3", ld_a)
            S.op("vector", lambda v: v.tensor_tensor(resid[:], xo3[:], accr[:],
                                                     ALU.add))
            S.op("vector", lambda v: v.reduce_sum(sm[:, 0:1], resid[:], AX.X,
                                                  negate=True))
            nm_dv = S.op("vector", lambda v: v.tensor_scalar(
                sm[:, 1:2], sm[:, 0:1], 1.0 / E, None, ALU.mult), inc="dv")
            S.wait("scalar", "dv", nm_dv)
            sq_ac[tt] = S.op("scalar", lambda s: s.activation(
                sqf[:], resid[:], AF.Square, accum_out=sm[:, 2:3]), inc="ac")
            S.wait("vector", "ac", sq_ac[tt])
            S.op("vector", lambda v: v.tensor_scalar(
                sm[:, 3:4], sm[:, 2:3], 1.0 / E, None, ALU.mult))
            S.op("vector", lambda v: v.tensor_tensor(
                sm[:, 4:5], sm[:, 1:2], sm[:, 1:2], ALU.mult))
            S.op("vector", lambda v: v.tensor_tensor(
                sm[:, 5:6], sm[:, 3:4], sm[:, 4:5], ALU.subtract))
            S.op("vector", lambda v: v.tensor_scalar(
                sm[:, 6:7], sm[:, 5:6], EPS, None, ALU.add))
            rv_dv = S.op("vector", lambda v: v.reciprocal(
                sm[:, 7:8], sm[:, 6:7]), inc="dv")
            S.wait("scalar", "dv", rv_dv)
            rstd_ac = S.op("scalar", lambda s: s.activation(
                sm[:, 8:9], sm[:, 7:8], AF.Sqrt), inc="ac")
            S.wait("vector", "ac", rstd_ac)
            S.wait("vector", "st", st_val)
            y_dv = S.op("vector", lambda v: v.tensor_scalar(
                ytil[:], resid[:], sm[:, 1:2], sm[:, 8:9],
                ALU.add, ALU.mult), inc="dv")
            S.wait("sync", "dv", y_dv)
            st_val = S.op("sync", lambda s, tt=tt: s.dma_start(
                out=y_out[tt * 128:(tt + 1) * 128, :], in_=ytil[:]),
                inc="st", inc_n=16)
        S.wait("sync", "st", st_val)

        block = ctx.enter_context(nc.Block())
        S.run(block)
    return nc


def host_prep_inputs(L0_attn_x, L0_attn_w_qkv, L0_attn_w_out,
                     L0_ffn_gate_W_packed, L0_ffn_gate_scales,
                     L0_ffn_up_W_packed, L0_ffn_up_scales,
                     L0_ffn_down_W_packed, L0_ffn_down_scales):
    X = np.ascontiguousarray(np.asarray(L0_attn_x, np.float32).reshape(8192, E))
    XT = np.ascontiguousarray(X.T)
    XT_pad = np.zeros((E, 8192 + 512), np.float32)
    XT_pad[:, 512:] = XT
    wqkvT = np.ascontiguousarray(np.asarray(L0_attn_w_qkv, np.float32).T)
    woutT = np.ascontiguousarray(np.asarray(L0_attn_w_out, np.float32).T)

    def prep_pk_se(packed, scales, r, rows):
        pkT = np.ascontiguousarray(np.asarray(packed, np.int32).T)
        j0, j1 = r * rows, (r + 1) * rows
        sT = np.ascontiguousarray(np.asarray(scales, np.float32).T)
        se = np.repeat(sT[j0 // 16: j1 // 16], 16, axis=0).astype(np.float32)
        return np.ascontiguousarray(pkT[j0:j1]), np.ascontiguousarray(se)

    dec_maps, attn_maps = [], []
    for r in range(NCORES):
        pg, sg = prep_pk_se(L0_ffn_gate_W_packed, L0_ffn_gate_scales, r, 128)
        pu, su = prep_pk_se(L0_ffn_up_W_packed, L0_ffn_up_scales, r, 128)
        pd, sd = prep_pk_se(L0_ffn_down_W_packed, L0_ffn_down_scales, r, 1024)
        dec_maps.append(dict(pk_g=pg, se_g=sg, pk_u=pu, se_u=su, pk_d=pd, se_d=sd))

        xt_slice = np.ascontiguousarray(XT_pad[:, 1024 * r: 1024 * r + TLOC])
        x_own = np.ascontiguousarray(X[1024 * r: 1024 * (r + 1)])
        half = r % 2
        mk = np.zeros((128, 8, 1024), np.float32)
        i = np.arange(128)
        j = np.arange(1024)
        for c in range(2):
            for k in range(4):
                pq = 1024 * half + 512 * c + 128 * k + i[:, None]
                pk_pos = 1024 * half - 512 + 512 * c + j[None, :]
                ok = (pk_pos >= pq - 512) & (pk_pos <= pq) & (pk_pos >= 0)
                mk[:, 4 * c + k, :] = ok.astype(np.float32)
        mk = mk.reshape(128, 8192).astype(ml_dtypes.bfloat16)
        attn_maps.append(dict(xt_in=xt_slice, x_own=x_own, wqkvT=wqkvT,
                              woutT=woutT, mask_in=mk))
    return dec_maps, attn_maps, X


def build_mega():
    from contextlib import ExitStack
    nc = bass.Bass("TRN2", target_bir_lowering=False, debug=False)
    xt_in = nc.dram_tensor("xt_in", [E, TLOC], F32, kind="ExternalInput").ap()
    x_own = nc.dram_tensor("x_own", [TOWN, E], F32, kind="ExternalInput").ap()
    wqkvT = nc.dram_tensor("wqkvT", [E, 3 * E], F32, kind="ExternalInput").ap()
    woutT = nc.dram_tensor("woutT", [E, E], F32, kind="ExternalInput").ap()
    mask_in = nc.dram_tensor("mask_in", [128, 8 * 1024], BF16,
                             kind="ExternalInput").ap()
    pk_g = nc.dram_tensor("pk_g", [128, MLP], I32, kind="ExternalInput").ap()
    se_g = nc.dram_tensor("se_g", [128, MLP], F32, kind="ExternalInput").ap()
    pk_u = nc.dram_tensor("pk_u", [128, MLP], I32, kind="ExternalInput").ap()
    se_u = nc.dram_tensor("se_u", [128, MLP], F32, kind="ExternalInput").ap()
    pk_d = nc.dram_tensor("pk_d", [1024, E], I32, kind="ExternalInput").ap()
    se_d = nc.dram_tensor("se_d", [1024, E], F32, kind="ExternalInput").ap()
    y_out = nc.dram_tensor("y_out", [TOWN, E], F32, kind="ExternalOutput").ap()
    kT_d = nc.dram_tensor("kT_d", [128, NH * TLOC], BF16).ap()
    vv_d = nc.dram_tensor("vv_d", [128, 12 * E], BF16).ap()
    qT_d = nc.dram_tensor("qT_d", [128, NH * TOWN], BF16).ap()
    x1_d = nc.dram_tensor("x1_d", [TOWN, E], F32).ap()
    acc_d = nc.dram_tensor("acc_d", [TOWN, E], F32).ap()
    cc_in_gh = [nc.dram_tensor(f"cc_in_g{h}", [256, MLP // 2], BF16)
                for h in range(2)]
    cc_out_gh = [nc.dram_tensor(f"cc_out_g{h}", [E, MLP // 2], BF16)
                 for h in range(2)]
    cc_in_uh = [nc.dram_tensor(f"cc_in_u{h}", [256, MLP // 2], BF16)
                for h in range(2)]
    cc_out_uh = [nc.dram_tensor(f"cc_out_u{h}", [E, MLP // 2], BF16)
                 for h in range(2)]
    cc_in_dh = [nc.dram_tensor(f"cc_in_d{h}", [1024, E], BF16)
                for h in range(2)]
    cc_out_dh = [nc.dram_tensor(f"cc_out_d{h}", [MLP // 2, E], BF16)
                 for h in range(2)]

    C1 = 0.7978845608028654
    C2 = C1 * 0.044715
    NW = 1024

    with ExitStack() as ctx:
        ar = Arena(nc, 204 * KB)
        xt = ar.at("xt", 0, [128, 16 * TLOC], BF16)
        wq_sl = [ar.at(f"wq{p}", (48 + 4 * p) * KB, [128, 16 * 128], BF16)
                 for p in range(2)]
        wv_sl = ar.at("wv", 56 * KB, [128, 16 * 512], BF16)
        stg = [ar.at(f"stg{p}", (72 + 2 * p) * KB, [128, 1024], BF16)
               for p in range(2)]
        dw_i = [ar.at(f"dw{p}", (80 + 4 * p) * KB, [128, NW], I32)
                for p in range(2)]
        dse = [ar.at(f"dse{p}", (88 + 4 * p) * KB, [128, NW], F32)
               for p in range(2)]
        dhs = [ar.at(f"dhs{p}", (96 + 4 * p) * KB, [128, NW], F32)
               for p in range(2)]
        dci = [[ar.at(f"dci{p}{pl}", (104 + 4 * (2 * p + pl)) * KB,
                      [128, NW], I32) for pl in range(2)] for p in range(2)]
        dcb = [ar.at(f"dcb{p}", (120 + 2 * p) * KB, [128, NW], BF16)
               for p in range(2)]
        dgb = [ar.at(f"dgb{p}", (124 + 2 * p) * KB, [128, NW], BF16)
               for p in range(2)]
        dg8 = [ar.at(f"dg8{p}", (128 + 2 * p) * KB, [128, NW], BF16)
               for p in range(2)]
        drr = [[ar.at(f"drr{p}{pl}", (132 + 2 * (2 * p + pl)) * KB,
                      [128, NW], BF16) for pl in range(2)] for p in range(2)]
        dss = [[ar.at(f"dss{p}{pl}", (140 + 2 * (2 * p + pl)) * KB,
                      [128, NW], BF16) for pl in range(2)] for p in range(2)]
        du1 = [[ar.at(f"du1{p}{pl}", (148 + 2 * (2 * p + pl)) * KB,
                      [128, NW], BF16) for pl in range(2)] for p in range(2)]
        du2 = [[ar.at(f"du2{p}{pl}", (156 + 2 * (2 * p + pl)) * KB,
                      [128, NW], BF16) for pl in range(2)] for p in range(2)]
        dt1 = [[ar.at(f"dt1{p}{pl}", (164 + 2 * (2 * p + pl)) * KB,
                      [128, NW], BF16) for pl in range(2)] for p in range(2)]
        dob = [[ar.at(f"dob{p}{pl}", (172 + 2 * (2 * p + pl)) * KB,
                      [128, NW], BF16) for pl in range(2)] for p in range(2)]
        dbias4 = ar.at("dbias4", 180 * KB, [128, 1], F32)
        dbias12 = ar.at("dbias12", int(180.5 * KB), [128, 1], F32)
        kT = ar.at("kT", 0, [128, NH * TLOC], BF16)
        vv = ar.at("vv", 48 * KB, [128, 12 * E], BF16)
        qT = ar.at("qT", 96 * KB, [128, NH * TOWN], BF16)
        masks = ar.at("masks", 128 * KB, [128, 8 * 1024], BF16)
        probs4 = [ar.at(f"pb{p}", (144 + 2 * p) * KB, [128, 1024], BF16)
                  for p in range(8)]
        probsn = ar.at("pn", 160 * KB, [128, 1024], BF16)
        probsT = ar.at("probsT", 162 * KB, [128, 8 * 512], BF16)
        attnT = ar.at("attnT", 170 * KB, [128, 2 * NH * 512], BF16)
        wo = ar.at("wo", 0, [128, 16 * E], BF16)
        xoB = [ar.at(f"xoB{p}", (64 + 8 * p) * KB, [128, E], F32)
               for p in range(2)]
        residB = [ar.at(f"residB{p}", (80 + 8 * p) * KB, [128, E], F32)
                  for p in range(2)]
        x1til = ar.at("x1til", 96 * KB, [128, E], F32)
        sqt = ar.at("sqt", 104 * KB, [128, E], F32)
        x1T = ar.at("x1T", 0, [128, 16 * TOWN], BF16)
        interT = ar.at("interT", 32 * KB, [128, 8 * TOWN], BF16)
        wg_f = [ar.at(f"wgf{p}", (48 + 16 * p) * KB, [128, 16 * 512], BF16)
                for p in range(2)]
        wu_f = [ar.at(f"wuf{p}", (80 + 16 * p) * KB, [128, 16 * 512], BF16)
                for p in range(2)]
        wd_f = [ar.at(f"wdf{p}", (112 + 32 * p) * KB, [128, 8 * E], BF16)
                for p in range(2)]
        gt = [[ar.at(f"gt{p}{i}", (176 + 8 * p + i) * KB, [128, 512], BF16)
               for i in range(8)] for p in range(2)]
        accv = [ar.at(f"accv{p}", (192 + 2 * p) * KB, [128, 512], F32)
                for p in range(2)]
        xo1 = ar.at("xo1", 112 * KB, [128, E], F32)
        x1b = ar.at("x1b", 120 * KB, [128, E], BF16)
        resid3B = [ar.at(f"resid3B{p}", (112 + 8 * p) * KB, [128, E], F32)
                   for p in range(2)]
        accrB = [ar.at(f"accrB{p}", (128 + 8 * p) * KB, [128, E], F32)
                 for p in range(2)]
        xo3B = [ar.at(f"xo3B{p}", (144 + 8 * p) * KB, [128, E], F32)
                for p in range(2)]
        ytil = ar.at("ytil", 160 * KB, [128, E], F32)
        sqf = ar.at("sqf", 168 * KB, [128, E], F32)
        sm = ar.at("sm", 202 * KB, [128, 32], F32)
        ident = ar.at("ident", int(202.5 * KB), [128, 128], BF16)
        shiftB = ar.at("shiftB", int(203 * KB), [128, 1], F32)

        S = Sched(nc)
        for s in ("s_xt", "s_wq0", "s_wq1", "s_wv", "s_wo",
                  "s_dl0", "s_dl1", "s_cs0", "s_cs1", "s_kv0", "s_kv1",
                  "s_cc0", "s_cc1", "s_cc2", "s_cc3", "s_cc4", "s_cc5",
                  "s_rl", "s_xo0", "s_xo1", "s_x1s",
                  "s_wd0", "s_wd1", "s_gu0", "s_gu1", "s_f1", "s_f30", "s_f31",
                  "ga0", "ga1", "s_y", "s_msk",
                  "pe", "dv", "ac", "gp"):
            S.new_sem(ctx, s)
        ps_S0 = ctx.enter_context(nc.psum_tensor("S0", [128, 1024], F32))
        ps_S1 = ctx.enter_context(nc.psum_tensor("S1", [128, 1024], F32))
        ps_TP = [ctx.enter_context(nc.psum_tensor(f"TP{i}", [128, 1024], BF16))
                 for i in range(2)]
        ps_AV = ctx.enter_context(nc.psum_tensor("AVP", [128, 512], F32))
        ps_OP = ctx.enter_context(nc.psum_tensor("OPP", [128, 512], F32))
        surfs = [ps_S0[:, 0:512], ps_S0[:, 512:1024],
                 ps_S1[:, 0:512], ps_S1[:, 512:1024]]
        Sfull = [ps_S0, ps_S1]

        m1 = S.op("gpsimd", lambda g: g.memset(ident[:], 0.0))
        id_done = S.op("gpsimd", lambda g: g.affine_select(
            out=ident[:], in_=ident[:], compare_op=ALU.not_equal, fill=1.0,
            base=0, pattern=[[-1, 128]], channel_multiplier=1))
        S.op("gpsimd", lambda g: g.memset(dbias4[:], -4.0))
        c_init = S.op("gpsimd", lambda g: g.memset(dbias12[:], -12.0))
        c_shift = S.op("gpsimd", lambda g: g.memset(shiftB[:], EXP_SHIFT))

        xt_src = xt_in.rearrange("(c p) t -> p c t", p=128)
        xt_ld = S.op("gpsimd", lambda g: g.dma_start(
            out=xt[:].rearrange("p (c t) -> p c t", t=TLOC), in_=xt_src),
            inc="s_xt", inc_n=16)

        djobs = []
        for hf in range(2):
            for wi in range(hf, 16, 2):
                cs = slice(wi * NW, (wi + 1) * NW)
                dc = slice((wi // 2) * NW, (wi // 2 + 1) * NW)
                djobs.append((pk_g[:, cs], se_g[:, cs],
                              cc_in_gh[hf].ap()[0:128, dc],
                              cc_in_gh[hf].ap()[128:256, dc], None))
            djobs[-1] = djobs[-1][:4] + (3 * hf + 0,)
            for wi in range(hf, 16, 2):
                cs = slice(wi * NW, (wi + 1) * NW)
                dc = slice((wi // 2) * NW, (wi // 2 + 1) * NW)
                djobs.append((pk_u[:, cs], se_u[:, cs],
                              cc_in_uh[hf].ap()[0:128, dc],
                              cc_in_uh[hf].ap()[128:256, dc], None))
            djobs[-1] = djobs[-1][:4] + (3 * hf + 1,)
            for q in range(4 * hf, 4 * hf + 4):
                for wi in range(2):
                    cs = slice(wi * NW, (wi + 1) * NW)
                    ql = q - 4 * hf
                    djobs.append((pk_d[q * 128:(q + 1) * 128, cs],
                                  se_d[q * 128:(q + 1) * 128, cs],
                                  cc_in_dh[hf].ap()[ql * 256: ql * 256 + 128, cs],
                                  cc_in_dh[hf].ap()[ql * 256 + 128: ql * 256 + 256, cs],
                                  None))
            djobs[-1] = djobs[-1][:4] + (3 * hf + 2,)
        ndw = len(djobs)
        cc_ins = [cc_in_gh[0], cc_in_uh[0], cc_in_dh[0],
                  cc_in_gh[1], cc_in_uh[1], cc_in_dh[1]]
        cc_outs = [cc_out_gh[0], cc_out_uh[0], cc_out_dh[0],
                   cc_out_gh[1], cc_out_uh[1], cc_out_dh[1]]
        cc_done = {}

        wsrc = wqkvT.rearrange("(c p) f -> p c f", p=128)
        dfin, drrv, duuv, dobv, dstv, dldv = {}, {}, {}, {}, {}, {}

        def emit_decode_loads(i):
            p = i % 2
            if i >= 2:
                S.wait("sync", "dv", dfin[i - 2])
            S.op("sync", lambda s, i=i, p=p: s.dma_start(
                out=dw_i[p][:], in_=djobs[i][0]), inc=f"s_dl{p}", inc_n=16)
            dldv[i] = S.op("sync", lambda s, i=i, p=p: s.dma_start(
                out=dse[p][:], in_=djobs[i][1]), inc=f"s_dl{p}", inc_n=16)

        def emit_decode_window(i):
            p = i % 2
            S.wait("vector", f"s_dl{p}", dldv[i])
            S.op("vector", lambda v, p=p: v.tensor_scalar(
                dci[p][0][:], dw_i[p][:], 15, None, ALU.bitwise_and))
            S.op("vector", lambda v, p=p: v.tensor_scalar(
                dci[p][1][:], dw_i[p][:], 4, 15,
                ALU.logical_shift_right, ALU.bitwise_and))
            dfin[i] = S.op("vector", lambda v, p=p: v.tensor_scalar(
                dhs[p][:], dse[p][:], 0.5, None, ALU.mult))
            for pl in range(2):
                if i >= 2:
                    S.wait("vector", "ac", duuv[(i - 2, pl)])
                S.op("vector", lambda v, p=p, pl=pl: v.tensor_copy(
                    dcb[p][:], dci[p][pl][:]))
                S.op("vector", lambda v, p=p: v.tensor_scalar(
                    dgb[p][:], dcb[p][:], 7.5, None, ALU.is_ge))
                S.op("vector", lambda v, p=p: v.tensor_scalar(
                    dg8[p][:], dgb[p][:], 8.0, None, ALU.mult))
                S.op("vector", lambda v, p=p, pl=pl: v.tensor_scalar(
                    dss[p][pl][:], dgb[p][:], -2.0, 1.0, ALU.mult, ALU.add))
                drrv[(i, pl)] = S.op("vector", lambda v, p=p, pl=pl:
                                     v.tensor_tensor(drr[p][pl][:], dcb[p][:],
                                                     dg8[p][:], ALU.subtract))
            for pl in range(2):
                S.wait("scalar", "dv", drrv[(i, pl)])
                if pl == 0 and i == 0:
                    S.wait("scalar", "gp", c_init)
                if i >= 2:
                    S.wait("scalar", "dv", dobv[(i - 2, pl)])
                S.op("scalar", lambda s, p=p, pl=pl: s.activation(
                    du1[p][pl][:], drr[p][pl][:], AF.Relu, bias=dbias4[:]))
                duuv[(i, pl)] = S.op("scalar", lambda s, p=p, pl=pl: s.activation(
                    du2[p][pl][:], drr[p][pl][:], AF.Relu, bias=dbias12[:],
                    scale=2.0))
            for pl in range(2):
                S.wait("vector", "ac", duuv[(i, pl)])
                if i >= 2:
                    S.wait("vector", f"s_cs{p}", dstv[i - 2])
                S.op("vector", lambda v, p=p, pl=pl: v.tensor_tensor(
                    dt1[p][pl][:], drr[p][pl][:], du1[p][pl][:], ALU.add))
                S.op("vector", lambda v, p=p, pl=pl: v.tensor_tensor(
                    dcb[p][:], dt1[p][pl][:], du2[p][pl][:], ALU.add))
                S.op("vector", lambda v, p=p, pl=pl: v.tensor_tensor(
                    dt1[p][pl][:], dcb[p][:], dss[p][pl][:], ALU.mult))
                dobv[(i, pl)] = S.op("vector", lambda v, p=p, pl=pl:
                                     v.tensor_tensor(dob[p][pl][:], dt1[p][pl][:],
                                                     dhs[p][:], ALU.mult))
            S.wait("gpsimd", "dv", dobv[(i, 0)])
            S.op("gpsimd", lambda g, i=i, p=p: g.dma_start(
                out=djobs[i][2], in_=dob[p][0][:]), inc=f"s_cs{p}", inc_n=16)
            S.wait("gpsimd", "dv", dobv[(i, 1)])
            dstv[i] = S.op("gpsimd", lambda g, i=i, p=p: g.dma_start(
                out=djobs[i][3], in_=dob[p][1][:]), inc=f"s_cs{p}", inc_n=16)
            cci = djobs[i][4]
            if cci is not None:
                S.wait("gpsimd", "s_cs0", S.cur("s_cs0"))
                S.wait("gpsimd", "s_cs1", S.cur("s_cs1"))
                cc_done[cci] = S.op("gpsimd", lambda g, cci=cci:
                                    g.collective_compute(
                    "AllGather", ALU.bypass, replica_groups=[list(range(8))],
                    ins=[cc_ins[cci].ap().opt()],
                    outs=[cc_outs[cci].ap().opt()]),
                    inc=f"s_cc{cci}", inc_n=1)

        qkv_groups = []
        qk_jobs = []
        for ft in range(32):
            wins = ((512,), (1024,)) if ft < 16 else ((0,), (512,), (1024,))
            for (off,) in wins:
                qk_jobs.append((ft, off))
        v_jobs = [(fw, tt) for fw in range(4) for tt in range(12)]

        slab_emitted = {}
        slab_val = {}
        slab_pe_last = {}
        evac_ac = {}
        dstv_kv = {}

        def pair_dst_info(j):
            if j < 16:
                return qT_d[:, j * 1024:(j + 1) * 1024]
            if j < 40:
                jj = j - 16
                return kT_d[:, jj * 1024:(jj + 1) * 1024]
            jj = j - 40
            fw, tt0 = jj // 6, 2 * (jj % 6)
            i0 = tt0 * 4 + fw
            return vv_d.rearrange("p (t f) -> p t f", f=512)[
                :, i0:i0 + 5:4, :]

        n_groups = 128
        dec_emitted = 0
        for g in range(n_groups):
            pair = g // 2
            surf = surfs[g % 4]
            qpart = g < 32
            kpart = 32 <= g < 80
            if qpart or kpart:
                ft, off = qk_jobs[g]
                p = ft % 2
                if ft not in slab_emitted:
                    if ft >= 2:
                        S.wait("gpsimd", "pe", slab_pe_last[ft - 2])
                    fsl = slice(ft * 128, (ft + 1) * 128)
                    slab_val[ft] = S.op("gpsimd", lambda gg, p=p, fsl=fsl:
                                        gg.dma_start(
                        out=wq_sl[p][:].rearrange("p (c f) -> p c f", f=128),
                        in_=wsrc[:, :, fsl]), inc=f"s_wq{p}", inc_n=16)
                    slab_emitted[ft] = True
                if g == 0:
                    S.wait("tensor", "s_xt", xt_ld)
                S.wait("tensor", f"s_wq{p}", slab_val[ft])
                if pair >= 2:
                    S.wait("tensor", "dv", evac_ac[pair - 2])
                for ec in range(16):
                    def mm(t, surf=surf, p=p, ec=ec, off=off):
                        return t.matmul(
                            surf, wq_sl[p][:, ec * 128:(ec + 1) * 128],
                            xt[:, ec * TLOC + off: ec * TLOC + off + 512],
                            start=(ec == 0), stop=(ec == 15))
                    S.op("tensor", mm, inc="pe" if ec == 15 else None)
                slab_pe_last[ft] = S.cur("pe")
            else:
                fw, tt = v_jobs[g - 80]
                if tt == 0:
                    if fw >= 1:
                        S.wait("gpsimd", "pe", S.cur("pe"))
                    fsl = slice(2 * E + fw * 512, 2 * E + (fw + 1) * 512)
                    slab_val[("v", fw)] = S.op("gpsimd", lambda gg, fsl=fsl:
                                               gg.dma_start(
                        out=wv_sl[:].rearrange("p (c f) -> p c f", f=512),
                        in_=wsrc[:, :, fsl]), inc="s_wv", inc_n=16)
                S.wait("tensor", "s_wv", slab_val[("v", fw)])
                if pair >= 2:
                    S.wait("tensor", "dv", evac_ac[pair - 2])
                for ec in range(16):
                    def mmv(t, surf=surf, ec=ec, tt=tt):
                        return t.matmul(
                            surf,
                            xt[:, ec * TLOC + tt * 128: ec * TLOC + (tt + 1) * 128],
                            wv_sl[:, ec * 512:(ec + 1) * 512],
                            start=(ec == 0), stop=(ec == 15))
                    S.op("tensor", mmv, inc="pe" if ec == 15 else None)
            if g % 2 == 1:
                pe_pair = S.cur("pe")
                sp = pair % 2
                S.wait("vector", "pe", pe_pair)
                if pair >= 2:
                    S.wait("vector", f"s_kv{sp}", dstv_kv.get(pair - 2, 0))
                scl = QK_SCALE if pair < 16 else 1.0
                evac_ac[pair] = S.op("vector", lambda v, sp=sp, scl=scl:
                                     v.tensor_scalar(stg[sp][:], Sfull[sp][:],
                                                     scl, None, ALU.mult))
                S.wait("sync", "dv", evac_ac[pair])
                dst = pair_dst_info(pair)
                if pair >= 40:
                    src = stg[sp][:].rearrange("p (t f) -> p t f", f=512)
                else:
                    src = stg[sp][:]
                dstv_kv[pair] = S.op("sync", lambda s, dst=dst, src=src:
                                     s.dma_start(out=dst, in_=src),
                                     inc=f"s_kv{sp}", inc_n=16)
            want_dec = min(ndw, (g + 1) * ndw // n_groups + 2)
            while dec_emitted < want_dec:
                i = dec_emitted
                if i < 2:
                    emit_decode_loads(i)
                emit_decode_window(i)
                if i + 2 < ndw:
                    emit_decode_loads(i + 2)
                dec_emitted += 1
        while dec_emitted < ndw:
            i = dec_emitted
            if i < 2:
                emit_decode_loads(i)
            emit_decode_window(i)
            if i + 2 < ndw:
                emit_decode_loads(i + 2)
            dec_emitted += 1

        qkv_pe_last = S.cur("pe")
        dec_dv_last = S.cur("dv")
        last_evac_pair = 63

        S.wait("sync", "pe", qkv_pe_last)
        S.wait("sync", "dv", dec_dv_last)
        S.wait("sync", "s_cs0", S.cur("s_cs0"))
        S.wait("sync", "s_cs1", S.cur("s_cs1"))
        S.wait("sync", "s_kv0", S.cur("s_kv0"))
        S.wait("sync", "s_kv1", S.cur("s_kv1"))
        rl_k = S.op("sync", lambda s: s.dma_start(out=kT[:], in_=kT_d),
                    inc="s_rl", inc_n=16)
        rl_v = S.op("sync", lambda s: s.dma_start(out=vv[:], in_=vv_d),
                    inc="s_rl", inc_n=16)
        rl_q = S.op("sync", lambda s: s.dma_start(out=qT[:], in_=qT_d),
                    inc="s_rl", inc_n=16)
        rl_m = S.op("sync", lambda s: s.dma_start(out=masks[:], in_=mask_in),
                    inc="s_msk", inc_n=16)

        S.wait("scalar", "gp", c_shift)
        S.wait("tensor", "gp", id_done)
        S.wait("vector", "s_msk", rl_m)

        exp_ac, tr_pe, ptk_dv, av_pe, att_dv, sum_ac, nrm_dv = ({}, {}, {}, {},
                                                                {}, {}, {})
        last_scores_pe = None
        chs = [(c, h) for c in range(2) for h in range(16)]

        def emit_softmax(ci_):
            c, h = chs[ci_]
            for k in range(4):
                Sk = ps_S0 if k % 2 == 0 else ps_S1
                slot = (ci_ % 2) * 4 + k
                if ci_ == 0 and k == 0:
                    S.wait("tensor", "s_rl", rl_q)
                prev = (c, h, k - 2) if k >= 2 else \
                    (chs[ci_ - 1] + (k + 2,) if ci_ > 0 else None)
                if prev in exp_ac:
                    S.wait("tensor", "ac", exp_ac[prev])
                elif ci_ == 0:
                    S.wait("tensor", "dv", evac_ac[63 - (1 - k % 2)])
                q0 = h * TOWN + c * 512 + k * 128
                k0 = h * TLOC + c * 512
                S.op("tensor", lambda t, Sk=Sk, q0=q0, k0=k0: t.matmul(
                    Sk[:, 0:512], qT[:, q0:q0 + 128], kT[:, k0:k0 + 512],
                    start=True, stop=True))
                pe_sc = S.op("tensor", lambda t, Sk=Sk, q0=q0, k0=k0: t.matmul(
                    Sk[:, 512:1024], qT[:, q0:q0 + 128],
                    kT[:, k0 + 512:k0 + 1024], start=True, stop=True), inc="pe")
                S.wait("scalar", "pe", pe_sc)
                pslot_prev = (chs[ci_ - 2] + (k,)) if ci_ >= 2 else None
                if pslot_prev in tr_pe:
                    S.wait("scalar", "pe", tr_pe[pslot_prev])
                exp_ac[(c, h, k)] = S.op("scalar", lambda s, Sk=Sk, slot=slot:
                                         s.activation(probs4[slot][:], Sk[:],
                                                      AF.Exp, bias=shiftB[:]))
                S.wait("vector", "ac", exp_ac[(c, h, k)])
                pk_prev = (c, h, k - 1) if k >= 1 else \
                    (chs[ci_ - 1] + (3,) if ci_ > 0 else None)
                if pk_prev in sum_ac:
                    S.wait("vector", "ac", sum_ac[pk_prev])
                mview = masks[:, (c * 4 + k) * 1024:(c * 4 + k + 1) * 1024]
                pm = S.op("vector", lambda v, slot=slot, mview=mview:
                          v.tensor_tensor(probsn[:], probs4[slot][:], mview,
                                          ALU.mult))
                S.wait("scalar", "dv", pm)
                sum_ac[(c, h, k)] = S.op("scalar", lambda s, k=k, slot=slot:
                                         s.activation(
                    probs4[slot][:], probsn[:], AF.Identity,
                    accum_out=sm[:, k:k + 1]))
                S.wait("vector", "ac", sum_ac[(c, h, k)])
                S.op("vector", lambda v, k=k: v.reciprocal(sm[:, 4 + k:5 + k],
                                                           sm[:, k:k + 1]))
                nrm_dv[(c, h, k)] = S.op("vector", lambda v, k=k, slot=slot:
                                         v.tensor_scalar(
                    probs4[slot][:], probsn[:], sm[:, 4 + k:5 + k], None,
                    ALU.mult))

        def emit_trs_av(ci_):
            c, h = chs[ci_]
            for k in range(4):
                slot = (ci_ % 2) * 4 + k
                S.wait("tensor", "dv", nrm_dv[(c, h, k)])
                prev = (c, h, k - 2) if k >= 2 else \
                    (chs[ci_ - 1] + (k + 2,) if ci_ > 0 else None)
                if prev in ptk_dv:
                    S.wait("tensor", "dv", ptk_dv[prev])
                for m in range(8):
                    def trp(t, k=k, m=m, slot=slot):
                        return t.transpose(
                            ps_TP[k % 2][:, m * 128:(m + 1) * 128],
                            probs4[slot][:, m * 128:(m + 1) * 128], ident[:])
                    if m == 7:
                        tr_pe[(c, h, k)] = S.op("tensor", trp, inc="pe")
                    else:
                        S.op("tensor", trp)
                S.wait("vector", "pe", tr_pe[(c, h, k)])
                if ci_ > 0 and k == 0:
                    S.wait("vector", "pe", av_pe[chs[ci_ - 1]])
                def evt(v, k=k):
                    src = ps_TP[k % 2][:].rearrange("p (m f) -> p m f", f=128)
                    dst = probsT[:].rearrange("p (m f) -> p m f", f=512)[
                        :, :, k * 128:(k + 1) * 128]
                    return v.tensor_copy(dst, src)
                ptk_dv[(c, h, k)] = S.op("vector", evt)
            S.wait("tensor", "dv", ptk_dv[(c, h, 3)])
            if ci_ > 0:
                S.wait("tensor", "dv", att_dv[chs[ci_ - 1]])
            if ci_ == 0:
                S.wait("tensor", "s_rl", rl_q)
            for m in range(8):
                def av(t, c=c, h=h, m=m):
                    return t.matmul(
                        ps_AV[:],
                        vv[:, (4 * c + m) * E + h * 128:
                           (4 * c + m) * E + (h + 1) * 128],
                        probsT[:, m * 512:(m + 1) * 512],
                        start=(m == 0), stop=(m == 7))
                if m == 7:
                    av_pe[(c, h)] = S.op("tensor", av, inc="pe")
                else:
                    S.op("tensor", av)
            S.wait("vector", "pe", av_pe[(c, h)])
            dstA = attnT[:, (c * 16 + h) * 512:(c * 16 + h + 1) * 512]
            att_dv[(c, h)] = S.op("vector", lambda v, dstA=dstA: v.tensor_copy(
                dstA, ps_AV[:]))

        for ci_ in range(33):
            if ci_ < 32:
                emit_softmax(ci_)
            if ci_ >= 1:
                emit_trs_av(ci_ - 1)
        last_scores_pe = S.cur("pe")

        S.wait("gpsimd", "pe", av_pe[chs[-1]])
        wo_src = woutT.rearrange("(c p) f -> p c f", p=128)
        ld_wo = S.op("gpsimd", lambda g: g.dma_start(
            out=wo[:].rearrange("p (c f) -> p c f", f=E), in_=wo_src),
            inc="s_wo", inc_n=16)

        S.wait("sync", "dv", att_dv[chs[-1]])
        op_dv = None
        xo_ld = {}
        resid_dv = {}
        x1s_val = {}
        st_val = 0
        sq_of = {}

        for step in range(9):
            tt = step
            if tt < 8:
                c = tt // 4
                bp = tt % 2
                if tt >= 2:
                    S.wait("sync", "dv", resid_dv[tt - 2])
                xo_ld[tt] = S.op("sync", lambda s, tt=tt, bp=bp: s.dma_start(
                    out=xoB[bp][:], in_=x_own[tt * 128:(tt + 1) * 128, :]),
                    inc=f"s_xo{bp}", inc_n=16)
                for fw in range(4):
                    if (tt, fw) == (0, 0):
                        S.wait("tensor", "s_wo", ld_wo)
                        S.wait("tensor", "dv", att_dv[chs[-1]])
                    else:
                        S.wait("tensor", "dv", op_dv)
                    for ec in range(16):
                        def mo(t, c=c, tt=tt, fw=fw, ec=ec):
                            lhs = attnT[:, (c * 16 + ec) * 512 + (tt % 4) * 128:
                                        (c * 16 + ec) * 512 + (tt % 4) * 128 + 128]
                            rhs = wo[:, ec * E + fw * 512: ec * E + fw * 512 + 512]
                            return t.matmul(ps_OP[:], lhs, rhs,
                                            start=(ec == 0), stop=(ec == 15))
                        S.op("tensor", mo, inc="pe" if ec == 15 else None)
                    pe_op = S.cur("pe")
                    S.wait("vector", "pe", pe_op)
                    S.wait("vector", f"s_xo{bp}", xo_ld[tt])
                    def rp(v, fw=fw, bp=bp):
                        return v.tensor_tensor(
                            residB[bp][:, fw * 512:(fw + 1) * 512], ps_OP[:],
                            xoB[bp][:, fw * 512:(fw + 1) * 512], ALU.add)
                    op_dv = S.op("vector", rp)
                resid_dv[tt] = op_dv
            if step >= 1:
                lt = step - 1
                bp = lt % 2
                S.op("vector", lambda v, bp=bp: v.reduce_sum(
                    sm[:, 8:9], residB[bp][:], AX.X, negate=True))
                nm_dv = S.op("vector", lambda v: v.tensor_scalar(
                    sm[:, 9:10], sm[:, 8:9], 1.0 / E, None, ALU.mult))
                S.wait("scalar", "dv", nm_dv)
                sq_of[lt] = S.op("scalar", lambda s, bp=bp: s.activation(
                    sqt[:], residB[bp][:], AF.Square, accum_out=sm[:, 10:11]))
                S.wait("vector", "ac", sq_of[lt])
                S.op("vector", lambda v: v.tensor_scalar(
                    sm[:, 11:12], sm[:, 10:11], 1.0 / E, None, ALU.mult))
                S.op("vector", lambda v: v.tensor_tensor(
                    sm[:, 12:13], sm[:, 9:10], sm[:, 9:10], ALU.mult))
                S.op("vector", lambda v: v.tensor_tensor(
                    sm[:, 13:14], sm[:, 11:12], sm[:, 12:13], ALU.subtract))
                S.op("vector", lambda v: v.tensor_scalar(
                    sm[:, 14:15], sm[:, 13:14], EPS, None, ALU.add))
                rv_dv = S.op("vector", lambda v: v.reciprocal(
                    sm[:, 15:16], sm[:, 14:15]))
                S.wait("scalar", "dv", rv_dv)
                rstd_ac = S.op("scalar", lambda s: s.activation(
                    sm[:, 16:17], sm[:, 15:16], AF.Sqrt))
                S.wait("vector", "ac", rstd_ac)
                S.wait("vector", "s_x1s", st_val)
                x1_dv = S.op("vector", lambda v, bp=bp: v.tensor_scalar(
                    x1til[:], residB[bp][:], sm[:, 9:10], sm[:, 16:17],
                    ALU.add, ALU.mult))
                S.wait("sync", "dv", x1_dv)
                st_val = S.op("sync", lambda s, lt=lt: s.dma_start(
                    out=x1_d[lt * 128:(lt + 1) * 128, :], in_=x1til[:]),
                    inc="s_x1s", inc_n=16)
                x1s_val[lt] = st_val
        x1_dv_last = S.cur("dv")

        wgsrc_h = [cc_out_gh[h].ap().rearrange("(c p) n -> p c n", p=128)
                   for h in range(2)]
        wusrc_h = [cc_out_uh[h].ap().rearrange("(c p) n -> p c n", p=128)
                   for h in range(2)]
        wd_ap_h = [cc_out_dh[h].ap() for h in range(2)]

        f1_evac = None
        cast_dv = {}
        tp_pe = 0
        for tt in range(8):
            if tt >= 1:
                S.wait("sync", "dv", cast_dv[tt - 1])
            S.wait("sync", "s_x1s", x1s_val[tt])
            xo_ld = S.op("sync", lambda s, tt=tt: s.dma_start(
                out=xo1[:], in_=x1_d[tt * 128:(tt + 1) * 128, :]),
                inc="s_f1", inc_n=16)
            S.wait("vector", "s_f1", xo_ld)
            if tt >= 1:
                S.wait("vector", "pe", tp_pe)
            cast_dv[tt] = S.op("vector", lambda v: v.tensor_copy(x1b[:], xo1[:]))
            S.wait("tensor", "dv", cast_dv[tt])
            if tt >= 1:
                S.wait("tensor", "dv", f1_evac)
            for hc in range(16):
                def tr1(t, hc=hc):
                    base = 256 * (hc // 2) + (hc % 2)
                    src = x1b[:, base: base + 255: 2]
                    return t.transpose(
                        ps_TP[hc // 8][:, (hc % 8) * 128:(hc % 8 + 1) * 128],
                        src, ident[:])
                S.op("tensor", tr1, inc="pe" if hc == 15 else None)
            tp_pe = S.cur("pe")
            S.wait("vector", "pe", tp_pe)
            for half in range(2):
                def ev1(v, tt=tt, half=half):
                    src = ps_TP[half][:].rearrange("p (m f) -> p m f", f=128)
                    dst = x1T[:].rearrange("p (c t) -> p c t", t=TOWN)[
                        :, half * 8:(half + 1) * 8, tt * 128:(tt + 1) * 128]
                    return v.tensor_copy(dst, src)
                f1_evac = S.op("vector", ev1)
        f1_done_dv = f1_evac

        g_idx = 0
        gelu_dv, gelu_free = [], []
        gp_p_l, ac_th_l, gp_q_l, dv_arg_l = [], [], [], []
        slab_sld, slab_pe_done = {}, {}
        wd_sld, down_pe_G = {}, {}
        ga_vals, accv_dv = [], []
        inter_done = {}
        first_wg = True
        Gseq = [0, 2, 4, 6, 8, 10, 12, 14, 1, 3, 5, 7, 9, 11, 13, 15]
        for gi, G in enumerate(Gseq):
            hf = G % 2
            wdbuf = gi % 2
            if gi < 2:
                S.wait("sync", "dv", f1_done_dv)
            else:
                S.wait("sync", "pe", down_pe_G[Gseq[gi - 2]])
            S.wait("sync", f"s_cc{3 * hf + 2}", cc_done[3 * hf + 2])
            for c_l in range(8):
                row0 = 1024 * (G // 2) + 256 * (c_l // 2) + 128 * (c_l % 2)
                def ldwd(s, row0=row0, hf=hf, wdbuf=wdbuf, c_l=c_l):
                    return s.dma_start(
                        out=wd_f[wdbuf][:, c_l * E:(c_l + 1) * E],
                        in_=wd_ap_h[hf][row0:row0 + 128, :])
                wd_sld[G] = S.op("sync", ldwd, inc=f"s_wd{wdbuf}", inc_n=16)
            for sw in range(2):
                if gi >= 1:
                    S.wait("sync", "pe", slab_pe_done[(Gseq[gi - 1], sw)])
                if first_wg:
                    S.wait("sync", "dv", x1_dv_last)
                    S.wait("sync", "s_x1s", S.cur("s_x1s"))
                    first_wg = False
                S.wait("sync", f"s_cc{3 * hf}", cc_done[3 * hf])
                S.wait("sync", f"s_cc{3 * hf + 1}", cc_done[3 * hf + 1])
                nsl = slice(1024 * (G // 2) + 512 * sw,
                            1024 * (G // 2) + 512 * (sw + 1))
                S.op("sync", lambda s, sw=sw, nsl=nsl, hf=hf: s.dma_start(
                    out=wg_f[sw][:].rearrange("p (c n) -> p c n", n=512),
                    in_=wgsrc_h[hf][:, :, nsl]), inc=f"s_gu{sw}", inc_n=16)
                slab_sld[(G, sw)] = S.op("sync", lambda s, sw=sw, nsl=nsl,
                                         hf=hf: s.dma_start(
                    out=wu_f[sw][:].rearrange("p (c n) -> p c n", n=512),
                    in_=wusrc_h[hf][:, :, nsl]), inc=f"s_gu{sw}", inc_n=16)
            for c_l in range(8):
                sw = c_l // 4
                woff = 256 * ((c_l % 4) // 2)
                sub = c_l % 2
                for tw in range(2):
                    gs = surfs[g_idx % 2]
                    us = surfs[2 + g_idx % 2]
                    S.wait("tensor", f"s_gu{sw}", slab_sld[(G, sw)])
                    if g_idx == 0:
                        S.wait("tensor", "dv", f1_done_dv)
                    if g_idx >= 2:
                        _, dv_ub2 = gelu_free[g_idx - 2]
                        S.wait("tensor", "dv", dv_ub2)
                    for wgt, surf in ((wg_f[sw], gs), (wu_f[sw], us)):
                        for hc in range(16):
                            def mmg(t, wgt=wgt, surf=surf, hc=hc, woff=woff,
                                    sub=sub, tw=tw):
                                lhs = wgt[:, hc * 512 + woff + sub:
                                          hc * 512 + woff + sub + 255: 2]
                                rhs = x1T[:, hc * TOWN + tw * 512:
                                          hc * TOWN + tw * 512 + 512]
                                return t.matmul(surf, lhs, rhs,
                                                start=(hc == 0), stop=(hc == 15))
                            S.op("tensor", mmg,
                                 inc="pe" if hc == 15 else None)
                    pe_gu = S.cur("pe")
                    par = g_idx % 2
                    g2, pp, gb, ub, arg, th, gu, q8 = gt[par]
                    S.wait("vector", "pe", pe_gu)
                    dv_gb = S.op("vector", lambda v, gb=gb, gs=gs:
                                 v.tensor_copy(gb[:], gs))
                    dv_ub = S.op("vector", lambda v, ub=ub, us=us:
                                 v.tensor_copy(ub[:], us))
                    S.wait("scalar", "dv", dv_gb)
                    if g_idx >= 2:
                        S.wait("scalar", "gp", gp_p_l[g_idx - 2])
                        S.wait("scalar", "gp", gp_q_l[g_idx - 2])
                    ac_g2 = S.op("scalar", lambda s, g2=g2, gb=gb: s.activation(
                        g2[:], gb[:], AF.Square))
                    S.wait("gpsimd", "ac", ac_g2)
                    if g_idx >= 2:
                        S.wait("gpsimd", "dv", dv_arg_l[g_idx - 2])
                        S.wait("gpsimd", "dv", gelu_dv[g_idx - 2])
                    gp_p = S.op("gpsimd", lambda g, pp=pp, g2=g2: g.tensor_scalar(
                        pp[:], g2[:], C2, C1, ALU.mult, ALU.add))
                    S.wait("vector", "gp", gp_p)
                    if g_idx >= 2:
                        S.wait("vector", "ac", ac_th_l[g_idx - 2])
                    dv_arg = S.op("vector", lambda v, arg=arg, pp=pp, gb=gb:
                                  v.tensor_tensor(arg[:], pp[:], gb[:],
                                                  ALU.mult))
                    S.wait("scalar", "dv", dv_arg)
                    if g_idx >= 2:
                        S.wait("scalar", "gp", gp_q_l[g_idx - 2])
                    ac_th = S.op("scalar", lambda s, th=th, arg=arg: s.activation(
                        th[:], arg[:], AF.Tanh))
                    S.wait("gpsimd", "ac", ac_th)
                    gp_q = S.op("gpsimd", lambda g, q8=q8, th=th: g.tensor_scalar(
                        q8[:], th[:], 0.5, 0.5, ALU.mult, ALU.add))
                    S.op("vector", lambda v, gu=gu, gb=gb, ub=ub:
                         v.tensor_tensor(gu[:], gb[:], ub[:], ALU.mult))
                    S.wait("vector", "gp", gp_q)
                    dst_i = interT[:, c_l * TOWN + tw * 512:
                                   c_l * TOWN + tw * 512 + 512]
                    dv_int = S.op("vector", lambda v, dst_i=dst_i, q8=q8, gu=gu:
                                  v.tensor_tensor(dst_i, q8[:], gu[:],
                                                  ALU.mult))
                    gelu_dv.append(dv_int)
                    gelu_free.append((0, dv_ub))
                    gp_p_l.append(gp_p)
                    ac_th_l.append(ac_th)
                    gp_q_l.append(gp_q)
                    dv_arg_l.append(dv_arg)
                    slab_pe_done[(G, sw)] = pe_gu
                    g_idx += 1
            inter_done[G] = gelu_dv[-1]
            for tt in range(8):
                for hw in range(4):
                    di = (tt * 4 + hw) % 2
                    ps_D = ps_AV if di == 0 else ps_OP
                    S.wait("tensor", "dv", inter_done[G])
                    S.wait("tensor", f"s_wd{wdbuf}", wd_sld[G])
                    idx = gi * 32 + tt * 4 + hw
                    if idx >= 2:
                        S.wait("tensor", "dv", accv_dv[idx - 2])
                    for c_l in range(8):
                        def mmd(t, ps_D=ps_D, wdbuf=wdbuf, c_l=c_l, tt=tt, hw=hw):
                            lhs = interT[:, c_l * TOWN + tt * 128:
                                         c_l * TOWN + tt * 128 + 128]
                            rhs = wd_f[wdbuf][:, c_l * E + hw * 512:
                                              c_l * E + hw * 512 + 512]
                            return t.matmul(ps_D[:], lhs, rhs,
                                            start=(c_l == 0), stop=(c_l == 7))
                        S.op("tensor", mmd, inc="pe" if c_l == 7 else None)
                    pe_d = S.cur("pe")
                    S.wait("vector", "pe", pe_d)
                    if idx >= 2:
                        S.wait("vector", f"ga{di}", ga_vals[idx - 2])
                    adv = S.op("vector", lambda v, di=di, ps_D=ps_D:
                               v.tensor_copy(accv[di][:], ps_D[:]))
                    accv_dv.append(adv)
                    S.wait("gpsimd", "dv", adv)
                    def ga_op(g, di=di, tt=tt, hw=hw, gi=gi):
                        dst = acc_d[tt * 128:(tt + 1) * 128,
                                    hw * 512:(hw + 1) * 512]
                        if gi == 0:
                            return g.dma_start(out=dst, in_=accv[di][:])
                        return g.dma_start(out=dst, in_=accv[di][:],
                                           accum_op=ALU.add)
                    ga_vals.append(S.op("gpsimd", ga_op, inc=f"ga{di}",
                                        inc_n=16))
            down_pe_G[G] = S.cur("pe")

        S.wait("sync", "ga0", S.cur("ga0"))
        S.wait("sync", "ga1", S.cur("ga1"))
        y_val = 0
        r3_dv = {}
        for step in range(9):
            tt = step
            if tt < 8:
                bp = tt % 2
                if tt >= 2:
                    S.wait("sync", "dv", r3_dv[tt - 2])
                S.op("sync", lambda s, tt=tt, bp=bp: s.dma_start(
                    out=xo3B[bp][:], in_=x1_d[tt * 128:(tt + 1) * 128, :]),
                    inc=f"s_f3{bp}", inc_n=16)
                ld_a = S.op("sync", lambda s, tt=tt, bp=bp: s.dma_start(
                    out=accrB[bp][:], in_=acc_d[tt * 128:(tt + 1) * 128, :]),
                    inc=f"s_f3{bp}", inc_n=16)
                S.wait("vector", f"s_f3{bp}", ld_a)
                r3_dv[tt] = S.op("vector", lambda v, bp=bp: v.tensor_tensor(
                    resid3B[bp][:], xo3B[bp][:], accrB[bp][:], ALU.add))
            if step >= 1:
                lt = step - 1
                bp = lt % 2
                S.op("vector", lambda v, bp=bp: v.reduce_sum(
                    sm[:, 0:1], resid3B[bp][:], AX.X, negate=True))
                nm3 = S.op("vector", lambda v: v.tensor_scalar(
                    sm[:, 1:2], sm[:, 0:1], 1.0 / E, None, ALU.mult))
                S.wait("scalar", "dv", nm3)
                sq3 = S.op("scalar", lambda s, bp=bp: s.activation(
                    sqf[:], resid3B[bp][:], AF.Square, accum_out=sm[:, 2:3]))
                S.wait("vector", "ac", sq3)
                S.op("vector", lambda v: v.tensor_scalar(
                    sm[:, 3:4], sm[:, 2:3], 1.0 / E, None, ALU.mult))
                S.op("vector", lambda v: v.tensor_tensor(
                    sm[:, 4:5], sm[:, 1:2], sm[:, 1:2], ALU.mult))
                S.op("vector", lambda v: v.tensor_tensor(
                    sm[:, 5:6], sm[:, 3:4], sm[:, 4:5], ALU.subtract))
                S.op("vector", lambda v: v.tensor_scalar(
                    sm[:, 6:7], sm[:, 5:6], EPS, None, ALU.add))
                rv3 = S.op("vector", lambda v: v.reciprocal(sm[:, 7:8],
                                                            sm[:, 6:7]))
                S.wait("scalar", "dv", rv3)
                rstd3 = S.op("scalar", lambda s: s.activation(
                    sm[:, 8:9], sm[:, 7:8], AF.Sqrt))
                S.wait("vector", "ac", rstd3)
                S.wait("vector", "s_y", y_val)
                y_dv = S.op("vector", lambda v, bp=bp: v.tensor_scalar(
                    ytil[:], resid3B[bp][:], sm[:, 1:2], sm[:, 8:9],
                    ALU.add, ALU.mult))
                S.wait("sync", "dv", y_dv)
                y_val = S.op("sync", lambda s, lt=lt: s.dma_start(
                    out=y_out[lt * 128:(lt + 1) * 128, :], in_=ytil[:]),
                    inc="s_y", inc_n=16)
        S.wait("sync", "s_y", y_val)

        block = ctx.enter_context(nc.Block())
        S.run(block)
    return nc


_BUILD_CACHE = {}


def _cached(name, builder):
    if name not in _BUILD_CACHE:
        _BUILD_CACHE[name] = builder()
    return _BUILD_CACHE[name]


USE_MEGA = True


def kernel(**inputs):
    dec_maps, attn_maps, X = host_prep_inputs(**inputs)

    if USE_MEGA:
        nc = _cached("mega", build_mega)
        maps = [dict(**dec_maps[r], **attn_maps[r]) for r in range(NCORES)]
        res = run_bass_kernel_spmd(nc, maps, core_ids=list(range(NCORES)))
        out = np.concatenate([np.asarray(res.results[r]["y_out"])
                              for r in range(NCORES)], axis=0)
        return out.reshape(4, 2048, E).astype(np.float32)

    nc_dec = _cached("dec", build_decode_nc)
    res_dec = run_bass_kernel_spmd(nc_dec, dec_maps, core_ids=list(range(NCORES)))
    wg_all = np.concatenate([np.asarray(res_dec.results[r]["wg_dec"])
                             for r in range(NCORES)], axis=0)
    wu_all = np.concatenate([np.asarray(res_dec.results[r]["wu_dec"])
                             for r in range(NCORES)], axis=0)
    wd_all = np.concatenate([np.asarray(res_dec.results[r]["wd_dec"])
                             for r in range(NCORES)], axis=0)

    nc_attn = _cached("attn", build_attn_v0)
    res_attn = run_bass_kernel_spmd(nc_attn, attn_maps, core_ids=list(range(NCORES)))
    x1_parts = [np.asarray(res_attn.results[r]["x1_out"]) for r in range(NCORES)]

    nc_ffn = _cached("ffn", build_ffn_v0)
    ffn_maps = [dict(x1_in=x1_parts[r], wg_all=wg_all, wu_all=wu_all,
                     wd_all=wd_all) for r in range(NCORES)]
    res_ffn = run_bass_kernel_spmd(nc_ffn, ffn_maps, core_ids=list(range(NCORES)))
    out = np.concatenate([np.asarray(res_ffn.results[r]["y_out"])
                          for r in range(NCORES)], axis=0)
    return out.reshape(4, 2048, E).astype(np.float32)


# version 08
# speedup vs baseline: 1.0799x; 1.0799x over previous
import numpy as np
import ml_dtypes
import concourse.bass as bass
import concourse.mybir as mybir
from concourse.bass_utils import run_bass_kernel_spmd

F32 = mybir.dt.float32
BF16 = mybir.dt.bfloat16
I32 = mybir.dt.int32
U8 = mybir.dt.uint8
AF = mybir.ActivationFunctionType
ALU = mybir.AluOpType
AX = mybir.AxisListType

KB = 1024
NCORES = 8
E = 2048
NH = 16
HD = 128
TOWN = 1024
TLOC = 1536
MLP = 16384
QK_SCALE = float(1.0 / np.sqrt(np.float32(128)))
EPS = 1e-5
EXP_SHIFT = -4.0

LUT = np.array([0.0, 0.5, 1.0, 1.5, 2.0, 3.0, 4.0, 6.0,
                0.0, -0.5, -1.0, -1.5, -2.0, -3.0, -4.0, -6.0], np.float32)


class Sched:
    ENGINES = ("tensor", "vector", "scalar", "sync", "gpsimd")

    def __init__(self, nc):
        self.nc = nc
        self.q = {k: [] for k in self.ENGINES}
        self.sems = {}

    def new_sem(self, ctx, name):
        h = ctx.enter_context(self.nc.semaphore(name))
        self.sems[name] = [h, 0]

    def cur(self, name):
        return self.sems[name][1]

    AUTO = {"vector": "dv", "scalar": "ac", "gpsimd": "gp"}

    def op(self, eng, fn, inc=None, inc_n=1):
        auto = self.AUTO.get(eng)
        if auto is not None and (inc is None or inc == auto):
            rec = self.sems[auto]
            self.wait(eng, auto, rec[1])
            rec[1] += 1
            h = rec[0]
            self.q[eng].append(lambda e, fn=fn, h=h: fn(e).then_inc(h, 1))
            return rec[1]
        if inc is None:
            self.q[eng].append(fn)
            return None
        rec = self.sems[inc]
        rec[1] += inc_n
        h = rec[0]

        def wrapped(e, fn=fn, h=h, n=inc_n):
            fn(e).then_inc(h, n)
        self.q[eng].append(wrapped)
        return rec[1]

    def wait(self, eng, sem, val):
        h = self.sems[sem][0]
        self.q[eng].append(lambda e, h=h, v=val: e.wait_ge(h, v))

    def run(self, block):
        for name in self.ENGINES:
            ops = self.q[name]
            getattr(block, name)(lambda e, ops=ops: [f(e) for f in ops])


class Arena:

    def __init__(self, nc, total_bytes):
        base = nc._sbuf_addr_for_side("left")
        aligned = (base + 31) & ~31
        nc.alloc_sbuf_tensor("arena", [128, total_bytes + (aligned - base)], U8)
        self.nc = nc
        self.base = aligned
        self.total = total_bytes

    def at(self, name, off_bytes, shape, dtype):
        assert off_bytes % 32 == 0
        h = self.nc.alloc_sbuf_tensor_at(name, shape, dtype,
                                         offset=self.base + off_bytes)
        return h


DEC_NW = 1024


def build_decode_nc():
    from contextlib import ExitStack
    nc = bass.Bass("TRN2", target_bir_lowering=False, debug=False)
    pk_g = nc.dram_tensor("pk_g", [128, MLP], I32, kind="ExternalInput").ap()
    se_g = nc.dram_tensor("se_g", [128, MLP], F32, kind="ExternalInput").ap()
    pk_u = nc.dram_tensor("pk_u", [128, MLP], I32, kind="ExternalInput").ap()
    se_u = nc.dram_tensor("se_u", [128, MLP], F32, kind="ExternalInput").ap()
    pk_d = nc.dram_tensor("pk_d", [1024, E], I32, kind="ExternalInput").ap()
    se_d = nc.dram_tensor("se_d", [1024, E], F32, kind="ExternalInput").ap()
    wg_dec = nc.dram_tensor("wg_dec", [256, MLP], BF16, kind="ExternalOutput").ap()
    wu_dec = nc.dram_tensor("wu_dec", [256, MLP], BF16, kind="ExternalOutput").ap()
    wd_dec = nc.dram_tensor("wd_dec", [2048, E], BF16, kind="ExternalOutput").ap()

    jobs = []
    for pk, se, out in ((pk_g, se_g, wg_dec), (pk_u, se_u, wu_dec),
                        (pk_d, se_d, wd_dec)):
        nchunks = pk.shape[0] // 128
        nwin = pk.shape[1] // DEC_NW
        for q in range(nchunks):
            for wi in range(nwin):
                cs = slice(wi * DEC_NW, (wi + 1) * DEC_NW)
                jobs.append((pk[q * 128:(q + 1) * 128, cs],
                             se[q * 128:(q + 1) * 128, cs],
                             out[q * 256: q * 256 + 128, cs],
                             out[q * 256 + 128: q * 256 + 256, cs]))
    n = len(jobs)
    NW = DEC_NW

    with ExitStack() as ctx:
        e = ctx.enter_context
        w_i = [e(nc.sbuf_tensor(f"w_i{p}", [128, NW], I32)) for p in range(2)]
        se_s = [e(nc.sbuf_tensor(f"se_s{p}", [128, NW], F32)) for p in range(2)]
        hs = [e(nc.sbuf_tensor(f"hs{p}", [128, NW], F32)) for p in range(2)]
        ci = [[e(nc.sbuf_tensor(f"ci{p}{pl}", [128, NW], I32)) for pl in range(2)]
              for p in range(2)]
        cb = [e(nc.sbuf_tensor(f"cb{p}", [128, NW], BF16)) for p in range(2)]
        gb = [e(nc.sbuf_tensor(f"gb{p}", [128, NW], BF16)) for p in range(2)]
        g8 = [e(nc.sbuf_tensor(f"g8{p}", [128, NW], BF16)) for p in range(2)]
        rr = [[e(nc.sbuf_tensor(f"rr{p}{pl}", [128, NW], BF16)) for pl in range(2)]
              for p in range(2)]
        ss = [[e(nc.sbuf_tensor(f"ss{p}{pl}", [128, NW], BF16)) for pl in range(2)]
              for p in range(2)]
        u1 = [[e(nc.sbuf_tensor(f"u1{p}{pl}", [128, NW], BF16)) for pl in range(2)]
              for p in range(2)]
        u2 = [[e(nc.sbuf_tensor(f"u2{p}{pl}", [128, NW], BF16)) for pl in range(2)]
              for p in range(2)]
        t1 = [[e(nc.sbuf_tensor(f"t1{p}{pl}", [128, NW], BF16)) for pl in range(2)]
              for p in range(2)]
        ob = [[e(nc.sbuf_tensor(f"ob{p}{pl}", [128, NW], BF16)) for pl in range(2)]
              for p in range(2)]
        bias4 = e(nc.sbuf_tensor("bias4", [128, 1], F32))
        bias12 = e(nc.sbuf_tensor("bias12", [128, 1], F32))

        S = Sched(nc)
        for s in ("ld0", "ld1", "st0", "st1", "dv", "ac", "gp"):
            S.new_sem(ctx, s)

        gp_bias = S.op("gpsimd", lambda g: g.memset(bias4[:], -4.0))
        gp_bias = S.op("gpsimd", lambda g: g.memset(bias12[:], -12.0))

        fin = {}
        rrv = {}
        uuv = {}
        obv = {}
        stv = {}
        ldv = {}

        for j in range(min(2, n)):
            S.op("sync", lambda s, j=j: s.dma_start(
                out=w_i[j % 2][:], in_=jobs[j][0]), inc=f"ld{j % 2}", inc_n=16)
            ldv[j] = S.op("sync", lambda s, j=j: s.dma_start(
                out=se_s[j % 2][:], in_=jobs[j][1]), inc=f"ld{j % 2}", inc_n=16)

        for i in range(n):
            p = i % 2
            S.wait("vector", f"ld{p}", ldv[i])
            S.op("vector", lambda v, p=p: v.tensor_scalar(
                ci[p][0][:], w_i[p][:], 15, None, ALU.bitwise_and))
            S.op("vector", lambda v, p=p: v.tensor_scalar(
                ci[p][1][:], w_i[p][:], 4, 15,
                ALU.logical_shift_right, ALU.bitwise_and))
            fin[i] = S.op("vector", lambda v, p=p: v.tensor_scalar(
                hs[p][:], se_s[p][:], 0.5, None, ALU.mult))
            for pl in range(2):
                if i >= 2:
                    S.wait("vector", "ac", uuv[(i - 2, pl)])
                S.op("vector", lambda v, p=p, pl=pl: v.tensor_copy(
                    cb[p][:], ci[p][pl][:]))
                S.op("vector", lambda v, p=p: v.tensor_scalar(
                    gb[p][:], cb[p][:], 7.5, None, ALU.is_ge))
                S.op("vector", lambda v, p=p: v.tensor_scalar(
                    g8[p][:], gb[p][:], 8.0, None, ALU.mult))
                S.op("vector", lambda v, p=p, pl=pl: v.tensor_scalar(
                    ss[p][pl][:], gb[p][:], -2.0, 1.0, ALU.mult, ALU.add))
                rrv[(i, pl)] = S.op("vector", lambda v, p=p, pl=pl: v.tensor_tensor(
                    rr[p][pl][:], cb[p][:], g8[p][:], ALU.subtract))
            for pl in range(2):
                S.wait("scalar", "dv", rrv[(i, pl)])
                if pl == 0 and i == 0:
                    S.wait("scalar", "gp", gp_bias)
                if i >= 2:
                    S.wait("scalar", "dv", obv[(i - 2, pl)])
                S.op("scalar", lambda s, p=p, pl=pl: s.activation(
                    u1[p][pl][:], rr[p][pl][:], AF.Relu, bias=bias4[:]))
                uuv[(i, pl)] = S.op("scalar", lambda s, p=p, pl=pl: s.activation(
                    u2[p][pl][:], rr[p][pl][:], AF.Relu, bias=bias12[:],
                    scale=2.0))
            for pl in range(2):
                S.wait("vector", "ac", uuv[(i, pl)])
                if i >= 2:
                    S.wait("vector", f"st{p}", stv[i - 2])
                S.op("vector", lambda v, p=p, pl=pl: v.tensor_tensor(
                    t1[p][pl][:], rr[p][pl][:], u1[p][pl][:], ALU.add))
                S.op("vector", lambda v, p=p, pl=pl: v.tensor_tensor(
                    cb[p][:], t1[p][pl][:], u2[p][pl][:], ALU.add))
                S.op("vector", lambda v, p=p, pl=pl: v.tensor_tensor(
                    t1[p][pl][:], cb[p][:], ss[p][pl][:], ALU.mult))
                obv[(i, pl)] = S.op("vector", lambda v, p=p, pl=pl: v.tensor_tensor(
                    ob[p][pl][:], t1[p][pl][:], hs[p][:], ALU.mult))
            S.wait("sync", "dv", obv[(i, 0)])
            S.op("sync", lambda s, i=i, p=p: s.dma_start(
                out=jobs[i][2], in_=ob[p][0][:]), inc=f"st{p}", inc_n=16)
            S.wait("sync", "dv", obv[(i, 1)])
            stv[i] = S.op("sync", lambda s, i=i, p=p: s.dma_start(
                out=jobs[i][3], in_=ob[p][1][:]), inc=f"st{p}", inc_n=16)
            if i + 2 < n:
                S.wait("sync", "dv", fin[i])
                S.op("sync", lambda s, i=i, p=p: s.dma_start(
                    out=w_i[p][:], in_=jobs[i + 2][0]), inc=f"ld{p}", inc_n=16)
                ldv[i + 2] = S.op("sync", lambda s, i=i, p=p: s.dma_start(
                    out=se_s[p][:], in_=jobs[i + 2][1]), inc=f"ld{p}", inc_n=16)
        S.wait("sync", "st0", S.cur("st0"))
        S.wait("sync", "st1", S.cur("st1"))

        block = ctx.enter_context(nc.Block())
        S.run(block)
    return nc


def build_attn_v0(dbg=False, dbg2=False):
    from contextlib import ExitStack
    nc = bass.Bass("TRN2", target_bir_lowering=False, debug=False)
    if dbg2:
        d_exp = nc.dram_tensor("d_exp", [128, 1024], BF16, kind="ExternalOutput").ap()
        d_pn = nc.dram_tensor("d_pn", [128, 1024], BF16, kind="ExternalOutput").ap()
        d_nrm = nc.dram_tensor("d_nrm", [128, 1024], BF16, kind="ExternalOutput").ap()
        d_pt = nc.dram_tensor("d_pt", [128, 4096], BF16, kind="ExternalOutput").ap()
        d_sm = nc.dram_tensor("d_sm", [128, 32], F32, kind="ExternalOutput").ap()
    if dbg:
        d_q = nc.dram_tensor("d_q", [128, NH * TOWN], BF16, kind="ExternalOutput").ap()
        d_k = nc.dram_tensor("d_k", [128, NH * TLOC], BF16, kind="ExternalOutput").ap()
        d_v = nc.dram_tensor("d_v", [128, 12 * E], BF16, kind="ExternalOutput").ap()
        d_at = nc.dram_tensor("d_at", [128, 2 * NH * 512], BF16, kind="ExternalOutput").ap()
    xt_in = nc.dram_tensor("xt_in", [E, TLOC], F32, kind="ExternalInput").ap()
    x_own = nc.dram_tensor("x_own", [TOWN, E], F32, kind="ExternalInput").ap()
    wqkvT = nc.dram_tensor("wqkvT", [E, 3 * E], F32, kind="ExternalInput").ap()
    woutT = nc.dram_tensor("woutT", [E, E], F32, kind="ExternalInput").ap()
    mask_in = nc.dram_tensor("mask_in", [128, 8 * 1024], BF16,
                             kind="ExternalInput").ap()
    x1_out = nc.dram_tensor("x1_out", [TOWN, E], F32, kind="ExternalOutput").ap()

    with ExitStack() as ctx:
        ar = Arena(nc, 204 * KB)
        kT = ar.at("kT", 0, [128, NH * TLOC], BF16)
        vv = ar.at("vv", 48 * KB, [128, 12 * E], BF16)
        qT = ar.at("qT", 96 * KB, [128, NH * TOWN], BF16)
        xt = ar.at("xt", 128 * KB, [128, 16 * TLOC], BF16)
        wq_sl = [ar.at(f"wq{p}", (176 + 4 * p) * KB, [128, 16 * 128], BF16)
                 for p in range(2)]
        wv_sl = ar.at("wv", 184 * KB, [128, 16 * 512], BF16)
        masks = ar.at("masks", 128 * KB, [128, 8 * 1024], BF16)
        probs = [ar.at(f"pb{p}", (144 + 2 * p) * KB, [128, 1024], BF16)
                 for p in range(2)]
        probsn = ar.at("pn", 148 * KB, [128, 1024], BF16)
        probsT = ar.at("probsT", 150 * KB, [128, 8 * 512], BF16)
        attnT = ar.at("attnT", 158 * KB, [128, 2 * NH * 512], BF16)
        wo = ar.at("wo", 0, [128, 16 * E], BF16)
        xo = ar.at("xo", 64 * KB, [128, E], F32)
        resid = ar.at("resid", 72 * KB, [128, E], F32)
        x1til = ar.at("x1til", 80 * KB, [128, E], F32)
        sqt = ar.at("sqt", 88 * KB, [128, E], F32)
        sm = ar.at("sm", 200 * KB, [128, 32], F32)
        ident = ar.at("ident", 201 * KB, [128, 128], BF16)
        shiftB = ar.at("shiftB", int(201.5 * KB), [128, 1], F32)
        trash = ar.at("trash", 202 * KB, [128, 1024], BF16)

        S = Sched(nc)
        for s in ("s_xt", "s_wq0", "s_wq1", "s_wv", "s_wo",
                  "sld", "pe", "dv", "ac", "gp", "st"):
            S.new_sem(ctx, s)
        ps_S0 = ctx.enter_context(nc.psum_tensor("S0", [128, 1024], F32))
        ps_S1 = ctx.enter_context(nc.psum_tensor("S1", [128, 1024], F32))
        ps_TP = [ctx.enter_context(nc.psum_tensor(f"TP{i}", [128, 1024], BF16))
                 for i in range(2)]
        ps_AV = ctx.enter_context(nc.psum_tensor("AVP", [128, 512], F32))
        ps_OP = ctx.enter_context(nc.psum_tensor("OPP", [128, 512], F32))
        surfs = [ps_S0[:, 0:512], ps_S0[:, 512:1024],
                 ps_S1[:, 0:512], ps_S1[:, 512:1024]]

        m1 = S.op("gpsimd", lambda g: g.memset(ident[:], 0.0), inc="gp")
        S.wait("gpsimd", "gp", m1)
        id_done = S.op("gpsimd", lambda g: g.affine_select(
            out=ident[:], in_=ident[:], compare_op=ALU.not_equal, fill=1.0,
            base=0, pattern=[[-1, 128]], channel_multiplier=1), inc="gp")
        c_init = S.op("gpsimd", lambda g: g.memset(shiftB[:], EXP_SHIFT),
                      inc="gp")

        xt_src = xt_in.rearrange("(c p) t -> p c t", p=128)
        xt_ld = S.op("gpsimd", lambda g: g.dma_start(
            out=xt[:].rearrange("p (c t) -> p c t", t=TLOC), in_=xt_src),
            inc="s_xt", inc_n=16)

        wsrc = wqkvT.rearrange("(c p) f -> p c f", p=128)
        group_done_dv = []
        group_idx = 0
        slab_pe_after = {}
        pe_val = None

        qk_jobs = []
        for ft in range(32):
            wins = ((512, 0), (1024, 512)) if ft < 16 else \
                   ((0, 0), (512, 512), (1024, 1024))
            qk_jobs.append((ft, wins))

        for ji, (ft, wins) in enumerate(qk_jobs):
            p = ft % 2
            if ji >= 2:
                S.wait("gpsimd", "pe", slab_pe_after[qk_jobs[ji - 2][0]])
            fsl = slice(ft * 128, (ft + 1) * 128)
            gld_slab = S.op("gpsimd", lambda g, p=p, fsl=fsl: g.dma_start(
                out=wq_sl[p][:].rearrange("p (c f) -> p c f", f=128),
                in_=wsrc[:, :, fsl]), inc=f"s_wq{p}", inc_n=16)
            for off, qoff in wins:
                surf = surfs[group_idx % 4]
                if ji == 0:
                    S.wait("tensor", "s_xt", xt_ld)
                S.wait("tensor", f"s_wq{p}", gld_slab)
                if group_idx >= 4:
                    S.wait("tensor", "dv", group_done_dv[group_idx - 4])
                for ec in range(16):
                    def mm(t, surf=surf, p=p, ec=ec, off=off):
                        return t.matmul(
                            surf, wq_sl[p][:, ec * 128:(ec + 1) * 128],
                            xt[:, ec * TLOC + off: ec * TLOC + off + 512],
                            start=(ec == 0), stop=(ec == 15))
                    pe_val = S.op("tensor", mm, inc="pe") if ec == 15 \
                        else S.op("tensor", mm)
                S.wait("vector", "pe", S.cur("pe"))
                if ft < 16:
                    dst = qT[:, ft * TOWN + qoff: ft * TOWN + qoff + 512]
                    dvv = S.op("vector", lambda v, dst=dst, surf=surf:
                               v.tensor_scalar(dst, surf, QK_SCALE, None,
                                               ALU.mult), inc="dv")
                else:
                    dst = kT[:, (ft - 16) * TLOC + off:
                             (ft - 16) * TLOC + off + 512]
                    dvv = S.op("vector", lambda v, dst=dst, surf=surf:
                               v.tensor_copy(dst, surf), inc="dv")
                group_done_dv.append(dvv)
                group_idx += 1
            slab_pe_after[ft] = S.cur("pe")

        v_pe_after = {}
        for fw in range(4):
            if fw >= 1:
                S.wait("gpsimd", "pe", v_pe_after[fw - 1])
            fsl = slice(2 * E + fw * 512, 2 * E + (fw + 1) * 512)
            gld_v = S.op("gpsimd", lambda g, fsl=fsl: g.dma_start(
                out=wv_sl[:].rearrange("p (c f) -> p c f", f=512),
                in_=wsrc[:, :, fsl]), inc="s_wv", inc_n=16)
            for tt in range(12):
                surf = surfs[group_idx % 4]
                S.wait("tensor", "s_wv", gld_v)
                if group_idx >= 4:
                    S.wait("tensor", "dv", group_done_dv[group_idx - 4])
                for ec in range(16):
                    def mmv(t, surf=surf, ec=ec, tt=tt):
                        return t.matmul(
                            surf,
                            xt[:, ec * TLOC + tt * 128: ec * TLOC + (tt + 1) * 128],
                            wv_sl[:, ec * 512:(ec + 1) * 512],
                            start=(ec == 0), stop=(ec == 15))
                    S.op("tensor", mmv, inc="pe" if ec == 15 else None)
                S.wait("vector", "pe", S.cur("pe"))
                dst = vv[:, tt * E + fw * 512: tt * E + (fw + 1) * 512]
                dvv = S.op("vector", lambda v, dst=dst, surf=surf:
                           v.tensor_copy(dst, surf), inc="dv")
                group_done_dv.append(dvv)
                group_idx += 1
            v_pe_after[fw] = S.cur("pe")

        qkv_done_dv = group_done_dv[-1]
        qkv_done_pe = S.cur("pe")

        S.wait("sync", "pe", qkv_done_pe)
        if dbg:
            S.wait("sync", "dv", qkv_done_dv)
            S.op("sync", lambda s: s.dma_start(out=d_q, in_=qT[:]),
                 inc="st", inc_n=16)
            S.op("sync", lambda s: s.dma_start(out=d_k, in_=kT[:]),
                 inc="st", inc_n=16)
            S.op("sync", lambda s: s.dma_start(out=d_v, in_=vv[:]),
                 inc="st", inc_n=16)
            S.sems["st"][1] += 0
        ld_mask = S.op("sync", lambda s: s.dma_start(out=masks[:], in_=mask_in),
                       inc="sld", inc_n=16)

        S.wait("scalar", "gp", c_init)
        S.wait("tensor", "gp", id_done)
        S.wait("vector", "sld", ld_mask)

        exp_ac, tr_pe, ptk_dv, av_pe, att_dv, sum_gp = {}, {}, {}, {}, {}, {}
        dbg_gates = {}
        last_scores_pe = None
        chs = [(c, h) for c in range(2) for h in range(16)]
        for ci_, (c, h) in enumerate(chs):
            for k in range(4):
                Sk = ps_S0 if k % 2 == 0 else ps_S1
                if ci_ == 0 and k == 0:
                    S.wait("tensor", "dv", qkv_done_dv)
                prev = (c, h, k - 2) if k >= 2 else \
                    (chs[ci_ - 1] + (k + 2,) if ci_ > 0 else None)
                if prev in exp_ac:
                    S.wait("tensor", "ac", exp_ac[prev])
                q0 = h * TOWN + c * 512 + k * 128
                k0 = h * TLOC + c * 512
                S.op("tensor", lambda t, Sk=Sk, q0=q0, k0=k0: t.matmul(
                    Sk[:, 0:512], qT[:, q0:q0 + 128], kT[:, k0:k0 + 512],
                    start=True, stop=True))
                pe_sc = S.op("tensor", lambda t, Sk=Sk, q0=q0, k0=k0: t.matmul(
                    Sk[:, 512:1024], qT[:, q0:q0 + 128],
                    kT[:, k0 + 512:k0 + 1024], start=True, stop=True), inc="pe")
                last_scores_pe = pe_sc
                S.wait("scalar", "pe", pe_sc)
                if prev in tr_pe:
                    S.wait("scalar", "pe", tr_pe[prev])
                if dbg2 and (c, h, k) == (1, 0, 2):
                    S.wait("scalar", "st", dbg_gates["nrm"])
                exp_ac[(c, h, k)] = S.op("scalar", lambda s, Sk=Sk, k=k:
                                         s.activation(probs[k % 2][:], Sk[:],
                                                      AF.Exp, bias=shiftB[:]),
                                         inc="ac")
                if dbg2 and (c, h, k) == (1, 0, 0):
                    S.wait("sync", "ac", exp_ac[(c, h, k)])
                    dbg_gates["exp"] = S.op("sync", lambda s: s.dma_start(
                        out=d_exp, in_=probs[0][:]), inc="st", inc_n=16)
                S.wait("vector", "ac", exp_ac[(c, h, k)])
                if dbg2 and (c, h, k) == (1, 0, 1):
                    S.wait("vector", "st", dbg_gates["pn"])
                pk_prev = (c, h, k - 1) if k >= 1 else \
                    (chs[ci_ - 1] + (3,) if ci_ > 0 else None)
                if pk_prev in sum_gp:
                    S.wait("vector", "ac", sum_gp[pk_prev])
                mview = masks[:, (c * 4 + k) * 1024:(c * 4 + k + 1) * 1024]
                pm = S.op("vector", lambda v, k=k, mview=mview: v.tensor_tensor(
                    probsn[:], probs[k % 2][:], mview, ALU.mult), inc="dv")
                if dbg2 and (c, h, k) == (1, 0, 0):
                    S.wait("sync", "dv", pm)
                    dbg_gates["pn"] = S.op("sync", lambda s: s.dma_start(
                        out=d_pn, in_=probsn[:]), inc="st", inc_n=16)
                S.wait("scalar", "dv", pm)
                sum_gp[(c, h, k)] = S.op("scalar", lambda s, k=k: s.activation(
                    trash[:], probsn[:], AF.Identity,
                    accum_out=sm[:, k:k + 1]), inc="ac")
                S.wait("vector", "ac", sum_gp[(c, h, k)])
                S.op("vector", lambda v, k=k: v.reciprocal(sm[:, 4 + k:5 + k],
                                                           sm[:, k:k + 1]))
                if dbg2 and (c, h, k) == (1, 0, 0):
                    S.wait("vector", "st", dbg_gates["exp"])
                nrm = S.op("vector", lambda v, k=k: v.tensor_scalar(
                    probs[k % 2][:], probsn[:], sm[:, 4 + k:5 + k], None,
                    ALU.mult), inc="dv")
                if dbg2 and (c, h, k) == (1, 0, 0):
                    S.wait("sync", "dv", nrm)
                    S.op("sync", lambda s: s.dma_start(out=d_sm, in_=sm[:]),
                         inc="st", inc_n=16)
                    dbg_gates["nrm"] = S.op("sync", lambda s: s.dma_start(
                        out=d_nrm, in_=probs[0][:]), inc="st", inc_n=16)
                S.wait("tensor", "dv", nrm)
                if prev in ptk_dv:
                    S.wait("tensor", "dv", ptk_dv[prev])
                for m in range(8):
                    def trp(t, k=k, m=m):
                        return t.transpose(
                            ps_TP[k % 2][:, m * 128:(m + 1) * 128],
                            probs[k % 2][:, m * 128:(m + 1) * 128], ident[:])
                    if m == 7:
                        tr_pe[(c, h, k)] = S.op("tensor", trp, inc="pe")
                    else:
                        S.op("tensor", trp)
                S.wait("vector", "pe", tr_pe[(c, h, k)])
                if ci_ > 0 and k == 0:
                    S.wait("vector", "pe", av_pe[chs[ci_ - 1]])
                if dbg2 and (c, h, k) == (1, 1, 0):
                    S.wait("vector", "st", dbg_gates["pt"])
                def evt(v, k=k):
                    src = ps_TP[k % 2][:].rearrange("p (m f) -> p m f", f=128)
                    dst = probsT[:].rearrange("p (m f) -> p m f", f=512)[
                        :, :, k * 128:(k + 1) * 128]
                    return v.tensor_copy(dst, src)
                ptk_dv[(c, h, k)] = S.op("vector", evt, inc="dv")
            if dbg2 and (c, h) == (1, 0):
                S.wait("sync", "dv", ptk_dv[(c, h, 3)])
                dbg_gates["pt"] = S.op("sync", lambda s: s.dma_start(
                    out=d_pt, in_=probsT[:]), inc="st", inc_n=16)
            S.wait("tensor", "dv", ptk_dv[(c, h, 3)])
            if ci_ > 0:
                S.wait("tensor", "dv", att_dv[chs[ci_ - 1]])
            for m in range(8):
                def av(t, c=c, h=h, m=m):
                    return t.matmul(
                        ps_AV[:],
                        vv[:, (4 * c + m) * E + h * 128:
                           (4 * c + m) * E + (h + 1) * 128],
                        probsT[:, m * 512:(m + 1) * 512],
                        start=(m == 0), stop=(m == 7))
                if m == 7:
                    av_pe[(c, h)] = S.op("tensor", av, inc="pe")
                else:
                    S.op("tensor", av)
            S.wait("vector", "pe", av_pe[(c, h)])
            dstA = attnT[:, (c * 16 + h) * 512:(c * 16 + h + 1) * 512]
            att_dv[(c, h)] = S.op("vector", lambda v, dstA=dstA: v.tensor_copy(
                dstA, ps_AV[:]), inc="dv")

        S.wait("gpsimd", "pe", av_pe[chs[-1]])
        wo_src = woutT.rearrange("(c p) f -> p c f", p=128)
        ld_wo = S.op("gpsimd", lambda g: g.dma_start(
            out=wo[:].rearrange("p (c f) -> p c f", f=E), in_=wo_src),
            inc="s_wo", inc_n=16)

        S.wait("sync", "dv", att_dv[chs[-1]])
        op_dv = None
        ln_sq_ac = {}
        st_val = 0
        for tt in range(8):
            c = tt // 4
            if tt >= 1:
                S.wait("sync", "ac", ln_sq_ac[tt - 1])
            xo_ld = S.op("sync", lambda s, tt=tt: s.dma_start(
                out=xo[:], in_=x_own[tt * 128:(tt + 1) * 128, :]),
                inc="sld", inc_n=16)
            for fw in range(4):
                if (tt, fw) == (0, 0):
                    S.wait("tensor", "s_wo", ld_wo)
                    S.wait("tensor", "dv", att_dv[chs[-1]])
                else:
                    S.wait("tensor", "dv", op_dv)
                for ec in range(16):
                    def mo(t, c=c, tt=tt, fw=fw, ec=ec):
                        lhs = attnT[:, (c * 16 + ec) * 512 + (tt % 4) * 128:
                                    (c * 16 + ec) * 512 + (tt % 4) * 128 + 128]
                        rhs = wo[:, ec * E + fw * 512: ec * E + fw * 512 + 512]
                        return t.matmul(ps_OP[:], lhs, rhs,
                                        start=(ec == 0), stop=(ec == 15))
                    S.op("tensor", mo, inc="pe" if ec == 15 else None)
                pe_op = S.cur("pe")
                S.wait("vector", "pe", pe_op)
                S.wait("vector", "sld", xo_ld)
                def rp(v, fw=fw):
                    return v.tensor_tensor(resid[:, fw * 512:(fw + 1) * 512],
                                           ps_OP[:],
                                           xo[:, fw * 512:(fw + 1) * 512],
                                           ALU.add)
                op_dv = S.op("vector", rp, inc="dv")
            S.op("vector", lambda v: v.reduce_sum(sm[:, 8:9], resid[:], AX.X,
                                                  negate=True))
            nm_dv = S.op("vector", lambda v: v.tensor_scalar(
                sm[:, 9:10], sm[:, 8:9], 1.0 / E, None, ALU.mult), inc="dv")
            S.wait("scalar", "dv", nm_dv)
            ln_sq_ac[tt] = S.op("scalar", lambda s: s.activation(
                sqt[:], resid[:], AF.Square, accum_out=sm[:, 10:11]), inc="ac")
            S.wait("vector", "ac", ln_sq_ac[tt])
            S.op("vector", lambda v: v.tensor_scalar(
                sm[:, 11:12], sm[:, 10:11], 1.0 / E, None, ALU.mult))
            S.op("vector", lambda v: v.tensor_tensor(
                sm[:, 12:13], sm[:, 9:10], sm[:, 9:10], ALU.mult))
            S.op("vector", lambda v: v.tensor_tensor(
                sm[:, 13:14], sm[:, 11:12], sm[:, 12:13], ALU.subtract))
            S.op("vector", lambda v: v.tensor_scalar(
                sm[:, 14:15], sm[:, 13:14], EPS, None, ALU.add))
            rv_dv = S.op("vector", lambda v: v.reciprocal(
                sm[:, 15:16], sm[:, 14:15]), inc="dv")
            S.wait("scalar", "dv", rv_dv)
            rstd_ac = S.op("scalar", lambda s: s.activation(
                sm[:, 16:17], sm[:, 15:16], AF.Sqrt), inc="ac")
            S.wait("vector", "ac", rstd_ac)
            S.wait("vector", "st", st_val)
            x1_dv = S.op("vector", lambda v: v.tensor_scalar(
                x1til[:], resid[:], sm[:, 9:10], sm[:, 16:17],
                ALU.add, ALU.mult), inc="dv")
            S.wait("sync", "dv", x1_dv)
            st_val = S.op("sync", lambda s, tt=tt: s.dma_start(
                out=x1_out[tt * 128:(tt + 1) * 128, :], in_=x1til[:]),
                inc="st", inc_n=16)
        if dbg:
            S.op("sync", lambda s: s.dma_start(out=d_at, in_=attnT[:]),
                 inc="st", inc_n=16)
        S.wait("sync", "st", S.cur("st"))

        block = ctx.enter_context(nc.Block())
        S.run(block)
    return nc


def build_ffn_v0(skip_f2=False, skip_down=False, skip_pool=False,
                 skip_gelu=False, n_groups=16):
    from contextlib import ExitStack
    nc = bass.Bass("TRN2", target_bir_lowering=False, debug=False)
    x1_in = nc.dram_tensor("x1_in", [TOWN, E], F32, kind="ExternalInput").ap()
    wg_all = nc.dram_tensor("wg_all", [E, MLP], BF16, kind="ExternalInput").ap()
    wu_all = nc.dram_tensor("wu_all", [E, MLP], BF16, kind="ExternalInput").ap()
    wd_all = nc.dram_tensor("wd_all", [MLP, E], BF16, kind="ExternalInput").ap()
    y_out = nc.dram_tensor("y_out", [TOWN, E], F32, kind="ExternalOutput").ap()
    acc_d = nc.dram_tensor("acc_d", [TOWN, E], F32).ap()

    C1 = 0.7978845608028654
    C2 = C1 * 0.044715

    with ExitStack() as ctx:
        ar = Arena(nc, 200 * KB)
        x1T = ar.at("x1T", 0, [128, 16 * TOWN], BF16)
        interT = ar.at("interT", 32 * KB, [128, 8 * TOWN], BF16)
        wg_sl = [ar.at(f"wg{p}", (48 + 16 * p) * KB, [128, 16 * 512], BF16)
                 for p in range(2)]
        wu_sl = [ar.at(f"wu{p}", (80 + 16 * p) * KB, [128, 16 * 512], BF16)
                 for p in range(2)]
        wd_sl = [ar.at(f"wd{p}", (112 + 32 * p) * KB, [128, 8 * E], BF16)
                 for p in range(2)]
        gt = [[ar.at(f"gt{p}{i}", (176 + 8 * p + i) * KB, [128, 512], BF16)
               for i in range(8)] for p in range(2)]
        accv = [ar.at(f"accv{p}", (192 + 2 * p) * KB, [128, 512], F32)
                for p in range(2)]
        xo1 = ar.at("xo1", 112 * KB, [128, E], F32)
        x1b = ar.at("x1b", 120 * KB, [128, E], BF16)
        resid = ar.at("resid3", 112 * KB, [128, E], F32)
        accr = ar.at("accr", 120 * KB, [128, E], F32)
        ytil = ar.at("ytil", 128 * KB, [128, E], F32)
        sqf = ar.at("sqf", 136 * KB, [128, E], F32)
        xo3 = ar.at("xo3", 144 * KB, [128, E], F32)
        sm = ar.at("sm3", 196 * KB, [128, 32], F32)
        ident = ar.at("ident3", 197 * KB, [128, 128], BF16)

        S = Sched(nc)
        for s in ("s_x1", "s_wd0", "s_wd1", "s_gu0", "s_gu1", "s_f3",
                  "ga0", "ga1", "pe", "dv", "ac", "gp", "st"):
            S.new_sem(ctx, s)
        ps_S0 = ctx.enter_context(nc.psum_tensor("S0", [128, 1024], F32))
        ps_S1 = ctx.enter_context(nc.psum_tensor("S1", [128, 1024], F32))
        ps_TP = [ctx.enter_context(nc.psum_tensor(f"TP{i}", [128, 1024], BF16))
                 for i in range(2)]
        ps_D = [ctx.enter_context(nc.psum_tensor(f"D{i}", [128, 512], F32))
                for i in range(2)]
        gsurf = [ps_S0[:, 0:512], ps_S0[:, 512:1024]]
        usurf = [ps_S1[:, 0:512], ps_S1[:, 512:1024]]

        m1 = S.op("gpsimd", lambda g: g.memset(ident[:], 0.0), inc="gp")
        S.wait("gpsimd", "gp", m1)
        id_gp = S.op("gpsimd", lambda g: g.affine_select(
            out=ident[:], in_=ident[:], compare_op=ALU.not_equal, fill=1.0,
            base=0, pattern=[[-1, 128]], channel_multiplier=1), inc="gp")

        S.wait("tensor", "gp", id_gp)
        f1_evac = None
        cast_dv = {}
        for tt in range(8):
            if tt >= 1:
                S.wait("sync", "dv", cast_dv[tt - 1])
            xo_ld = S.op("sync", lambda s, tt=tt: s.dma_start(
                out=xo1[:], in_=x1_in[tt * 128:(tt + 1) * 128, :]),
                inc="s_x1", inc_n=16)
            S.wait("vector", "s_x1", xo_ld)
            if tt >= 1:
                S.wait("vector", "pe", tp_pe)
            cast_dv[tt] = S.op("vector", lambda v: v.tensor_copy(x1b[:], xo1[:]),
                               inc="dv")
            S.wait("tensor", "dv", cast_dv[tt])
            if tt >= 1:
                S.wait("tensor", "dv", f1_evac)
            for hc in range(16):
                def tr1(t, hc=hc):
                    base = 256 * (hc // 2) + (hc % 2)
                    src = x1b[:, base: base + 255: 2]
                    return t.transpose(
                        ps_TP[hc // 8][:, (hc % 8) * 128:(hc % 8 + 1) * 128],
                        src, ident[:])
                S.op("tensor", tr1, inc="pe" if hc == 15 else None)
            tp_pe = S.cur("pe")
            S.wait("vector", "pe", tp_pe)
            for half in range(2):
                def ev1(v, tt=tt, half=half):
                    src = ps_TP[half][:].rearrange("p (m f) -> p m f", f=128)
                    dst = x1T[:].rearrange("p (c t) -> p c t", t=TOWN)[
                        :, half * 8:(half + 1) * 8, tt * 128:(tt + 1) * 128]
                    return v.tensor_copy(dst, src)
                f1_evac = S.op("vector", ev1, inc="dv")
        f1_done_dv = f1_evac

        wgsrc = wg_all.rearrange("(c p) n -> p c n", p=128)
        wusrc = wu_all.rearrange("(c p) n -> p c n", p=128)
        g_idx = 0
        gelu_dv = []
        gelu_free = []
        gp_p_l, ac_th_l, gp_q_l, dv_arg_l = [], [], [], []
        slab_sld = {}
        slab_pe_done = {}
        wd_sld = {}
        down_pe_G = {}
        ga_vals = []
        accv_dv = []
        inter_done = {}

        for G in range(n_groups):
            if skip_f2:
                break
            if G < 2:
                S.wait("sync", "dv", f1_done_dv)
            elif not skip_down:
                S.wait("sync", "pe", down_pe_G[G - 2])
            lds = []
            if not skip_down:
                for c_l in range(8):
                    row0 = (8 * G + c_l) * 128
                    def ldwd(s, row0=row0, G=G, c_l=c_l):
                        return s.dma_start(
                            out=wd_sl[G % 2][:, c_l * E:(c_l + 1) * E],
                            in_=wd_all[row0:row0 + 128, :])
                    lds.append(S.op("sync", ldwd, inc=f"s_wd{G % 2}", inc_n=16))
                wd_sld[G] = lds[-1]
            for sw in range(2):
                if G >= 1:
                    S.wait("sync", "pe", slab_pe_done[(G - 1, sw)])
                nsl = slice(1024 * G + 512 * sw, 1024 * G + 512 * (sw + 1))
                S.op("sync", lambda s, sw=sw, nsl=nsl: s.dma_start(
                    out=wg_sl[sw][:].rearrange("p (c n) -> p c n", n=512),
                    in_=wgsrc[:, :, nsl]), inc=f"s_gu{sw}", inc_n=16)
                slab_sld[(G, sw)] = S.op("sync", lambda s, sw=sw, nsl=nsl:
                                         s.dma_start(
                    out=wu_sl[sw][:].rearrange("p (c n) -> p c n", n=512),
                    in_=wusrc[:, :, nsl]), inc=f"s_gu{sw}", inc_n=16)
            for c_l in range(8):
                sw = c_l // 4
                woff = 256 * ((c_l % 4) // 2)
                sub = c_l % 2
                for tw in range(2):
                    gs = gsurf[g_idx % 2]
                    us = usurf[g_idx % 2]
                    S.wait("tensor", f"s_gu{sw}", slab_sld[(G, sw)])
                    if g_idx == 0:
                        S.wait("tensor", "dv", f1_done_dv)
                    if g_idx >= 2:
                        ac_g2, dv_ub = gelu_free[g_idx - 2]
                        S.wait("tensor", "ac", ac_g2)
                        S.wait("tensor", "dv", dv_ub)
                    for wgt, surf in ((wg_sl[sw], gs), (wu_sl[sw], us)):
                        for hc in range(16):
                            def mmg(t, wgt=wgt, surf=surf, hc=hc, woff=woff,
                                    sub=sub, tw=tw):
                                lhs = wgt[:, hc * 512 + woff + sub:
                                          hc * 512 + woff + sub + 255: 2]
                                rhs = x1T[:, hc * TOWN + tw * 512:
                                          hc * TOWN + tw * 512 + 512]
                                return t.matmul(surf, lhs, rhs,
                                                start=(hc == 0), stop=(hc == 15))
                            S.op("tensor", mmg,
                                 inc="pe" if hc == 15 else None)
                    pe_gu = S.cur("pe")
                    if skip_gelu:
                        S.wait("vector", "pe", pe_gu)
                        dst_i = interT[:, c_l * TOWN + tw * 512:
                                       c_l * TOWN + tw * 512 + 512]
                        dv_int = S.op("vector", lambda v, dst_i=dst_i, gs=gs:
                                      v.tensor_copy(dst_i, gs))
                        dv_ub = S.op("vector", lambda v, us=us:
                                     v.tensor_copy(gt[0][3][:], us))
                        gelu_dv.append(dv_int)
                        gelu_free.append((0, dv_ub))
                        gp_p_l.append(0); ac_th_l.append(0); gp_q_l.append(0)
                        dv_arg_l.append(dv_int)
                        slab_pe_done[(G, sw)] = pe_gu
                        g_idx += 1
                        continue
                    par = g_idx % 2
                    g2, pp, gb, ub, arg, th, gu, q8 = gt[par]
                    S.wait("vector", "pe", pe_gu)
                    dv_gb = S.op("vector", lambda v, gb=gb, gs=gs:
                                 v.tensor_copy(gb[:], gs))
                    dv_ub = S.op("vector", lambda v, ub=ub, us=us:
                                 v.tensor_copy(ub[:], us), inc="dv")
                    S.wait("scalar", "dv", dv_gb)
                    if g_idx >= 2:
                        S.wait("scalar", "gp", gp_p_l[g_idx - 2])
                        S.wait("scalar", "gp", gp_q_l[g_idx - 2])
                    ac_g2 = S.op("scalar", lambda s, g2=g2, gb=gb: s.activation(
                        g2[:], gb[:], AF.Square), inc="ac")
                    peng = "vector" if skip_pool else "gpsimd"
                    pchain = "dv" if skip_pool else "gp"
                    S.wait(peng, "ac", ac_g2)
                    if g_idx >= 2 and not skip_pool:
                        S.wait(peng, "dv", dv_arg_l[g_idx - 2])
                        S.wait(peng, "dv", gelu_dv[g_idx - 2])
                    gp_p = S.op(peng, lambda g, pp=pp, g2=g2: g.tensor_scalar(
                        pp[:], g2[:], C2, C1, ALU.mult, ALU.add), inc=pchain)
                    if not skip_pool:
                        S.wait("vector", "gp", gp_p)
                    if g_idx >= 2:
                        S.wait("vector", "ac", ac_th_l[g_idx - 2])
                    dv_arg = S.op("vector", lambda v, arg=arg, pp=pp, gb=gb:
                                  v.tensor_tensor(arg[:], pp[:], gb[:],
                                                  ALU.mult), inc="dv")
                    S.wait("scalar", "dv", dv_arg)
                    if g_idx >= 2:
                        S.wait("scalar", "gp", gp_q_l[g_idx - 2])
                    ac_th = S.op("scalar", lambda s, th=th, arg=arg: s.activation(
                        th[:], arg[:], AF.Tanh), inc="ac")
                    S.wait(peng, "ac", ac_th)
                    gp_q = S.op(peng, lambda g, q8=q8, th=th: g.tensor_scalar(
                        q8[:], th[:], 0.5, 0.5, ALU.mult, ALU.add), inc=pchain)
                    S.op("vector", lambda v, gu=gu, gb=gb, ub=ub:
                         v.tensor_tensor(gu[:], gb[:], ub[:], ALU.mult))
                    if not skip_pool:
                        S.wait("vector", "gp", gp_q)
                    dst_i = interT[:, c_l * TOWN + tw * 512:
                                   c_l * TOWN + tw * 512 + 512]
                    dv_int = S.op("vector", lambda v, dst_i=dst_i, q8=q8, gu=gu:
                                  v.tensor_tensor(dst_i, q8[:], gu[:],
                                                  ALU.mult), inc="dv")
                    gelu_dv.append(dv_int)
                    gelu_free.append((0, dv_ub))
                    gp_p_l.append(gp_p)
                    ac_th_l.append(ac_th)
                    gp_q_l.append(gp_q)
                    dv_arg_l.append(dv_arg)
                    slab_pe_done[(G, sw)] = pe_gu
                    g_idx += 1
            inter_done[G] = gelu_dv[-1]
            if skip_down:
                continue
            for tt in range(8):
                for hw in range(4):
                    di = (tt * 4 + hw) % 2
                    S.wait("tensor", "dv", inter_done[G])
                    S.wait("tensor", f"s_wd{G % 2}", wd_sld[G])
                    if (G, tt, hw) != (0, 0, 0):
                        idx = G * 32 + tt * 4 + hw
                        if idx >= 2:
                            S.wait("tensor", "dv", accv_dv[idx - 2])
                    for c_l in range(8):
                        def mmd(t, di=di, G=G, c_l=c_l, tt=tt, hw=hw):
                            lhs = interT[:, c_l * TOWN + tt * 128:
                                         c_l * TOWN + tt * 128 + 128]
                            rhs = wd_sl[G % 2][:, c_l * E + hw * 512:
                                               c_l * E + hw * 512 + 512]
                            return t.matmul(ps_D[di][:], lhs, rhs,
                                            start=(c_l == 0), stop=(c_l == 7))
                        S.op("tensor", mmd, inc="pe" if c_l == 7 else None)
                    pe_d = S.cur("pe")
                    S.wait("vector", "pe", pe_d)
                    idx = G * 32 + tt * 4 + hw
                    if idx >= 2:
                        S.wait("vector", f"ga{di}", ga_vals[idx - 2])
                    adv = S.op("vector", lambda v, di=di: v.tensor_copy(
                        accv[di][:], ps_D[di][:]), inc="dv")
                    accv_dv.append(adv)
                    S.wait("gpsimd", "dv", adv)
                    def ga_op(g, di=di, tt=tt, hw=hw, G=G):
                        dst = acc_d[tt * 128:(tt + 1) * 128,
                                    hw * 512:(hw + 1) * 512]
                        if G == 0:
                            return g.dma_start(out=dst, in_=accv[di][:])
                        return g.dma_start(out=dst, in_=accv[di][:],
                                           accum_op=ALU.add)
                    ga_vals.append(S.op("gpsimd", ga_op, inc=f"ga{di}",
                                        inc_n=16))
            down_pe_G[G] = S.cur("pe")

        S.wait("sync", "ga0", S.cur("ga0"))
        S.wait("sync", "ga1", S.cur("ga1"))
        st_val = 0
        sq_ac = {}
        for tt in range(8):
            if tt >= 1:
                S.wait("sync", "ac", sq_ac[tt - 1])
            ld_x = S.op("sync", lambda s, tt=tt: s.dma_start(
                out=xo3[:], in_=x1_in[tt * 128:(tt + 1) * 128, :]),
                inc="s_f3", inc_n=16)
            ld_a = S.op("sync", lambda s, tt=tt: s.dma_start(
                out=accr[:], in_=acc_d[tt * 128:(tt + 1) * 128, :]),
                inc="s_f3", inc_n=16)
            S.wait("vector", "s_f3", ld_a)
            S.op("vector", lambda v: v.tensor_tensor(resid[:], xo3[:], accr[:],
                                                     ALU.add))
            S.op("vector", lambda v: v.reduce_sum(sm[:, 0:1], resid[:], AX.X,
                                                  negate=True))
            nm_dv = S.op("vector", lambda v: v.tensor_scalar(
                sm[:, 1:2], sm[:, 0:1], 1.0 / E, None, ALU.mult), inc="dv")
            S.wait("scalar", "dv", nm_dv)
            sq_ac[tt] = S.op("scalar", lambda s: s.activation(
                sqf[:], resid[:], AF.Square, accum_out=sm[:, 2:3]), inc="ac")
            S.wait("vector", "ac", sq_ac[tt])
            S.op("vector", lambda v: v.tensor_scalar(
                sm[:, 3:4], sm[:, 2:3], 1.0 / E, None, ALU.mult))
            S.op("vector", lambda v: v.tensor_tensor(
                sm[:, 4:5], sm[:, 1:2], sm[:, 1:2], ALU.mult))
            S.op("vector", lambda v: v.tensor_tensor(
                sm[:, 5:6], sm[:, 3:4], sm[:, 4:5], ALU.subtract))
            S.op("vector", lambda v: v.tensor_scalar(
                sm[:, 6:7], sm[:, 5:6], EPS, None, ALU.add))
            rv_dv = S.op("vector", lambda v: v.reciprocal(
                sm[:, 7:8], sm[:, 6:7]), inc="dv")
            S.wait("scalar", "dv", rv_dv)
            rstd_ac = S.op("scalar", lambda s: s.activation(
                sm[:, 8:9], sm[:, 7:8], AF.Sqrt), inc="ac")
            S.wait("vector", "ac", rstd_ac)
            S.wait("vector", "st", st_val)
            y_dv = S.op("vector", lambda v: v.tensor_scalar(
                ytil[:], resid[:], sm[:, 1:2], sm[:, 8:9],
                ALU.add, ALU.mult), inc="dv")
            S.wait("sync", "dv", y_dv)
            st_val = S.op("sync", lambda s, tt=tt: s.dma_start(
                out=y_out[tt * 128:(tt + 1) * 128, :], in_=ytil[:]),
                inc="st", inc_n=16)
        S.wait("sync", "st", st_val)

        block = ctx.enter_context(nc.Block())
        S.run(block)
    return nc


def host_prep_inputs(L0_attn_x, L0_attn_w_qkv, L0_attn_w_out,
                     L0_ffn_gate_W_packed, L0_ffn_gate_scales,
                     L0_ffn_up_W_packed, L0_ffn_up_scales,
                     L0_ffn_down_W_packed, L0_ffn_down_scales):
    X = np.ascontiguousarray(np.asarray(L0_attn_x, np.float32).reshape(8192, E))
    XT = np.ascontiguousarray(X.T)
    XT_pad = np.zeros((E, 8192 + 512), np.float32)
    XT_pad[:, 512:] = XT
    wqkvT = np.ascontiguousarray(np.asarray(L0_attn_w_qkv, np.float32).T)
    woutT = np.ascontiguousarray(np.asarray(L0_attn_w_out, np.float32).T)

    def prep_pk_se(packed, scales, r, rows):
        pkT = np.ascontiguousarray(np.asarray(packed, np.int32).T)
        j0, j1 = r * rows, (r + 1) * rows
        sT = np.ascontiguousarray(np.asarray(scales, np.float32).T)
        se = np.repeat(sT[j0 // 16: j1 // 16], 16, axis=0).astype(np.float32)
        return np.ascontiguousarray(pkT[j0:j1]), np.ascontiguousarray(se)

    dec_maps, attn_maps = [], []
    for r in range(NCORES):
        pg, sg = prep_pk_se(L0_ffn_gate_W_packed, L0_ffn_gate_scales, r, 128)
        pu, su = prep_pk_se(L0_ffn_up_W_packed, L0_ffn_up_scales, r, 128)
        pd, sd = prep_pk_se(L0_ffn_down_W_packed, L0_ffn_down_scales, r, 1024)
        dec_maps.append(dict(pk_g=pg, se_g=sg, pk_u=pu, se_u=su, pk_d=pd, se_d=sd))

        xt_slice = np.ascontiguousarray(XT_pad[:, 1024 * r: 1024 * r + TLOC])
        x_own = np.ascontiguousarray(X[1024 * r: 1024 * (r + 1)])
        half = r % 2
        mk = np.zeros((128, 8, 1024), np.float32)
        i = np.arange(128)
        j = np.arange(1024)
        for c in range(2):
            for k in range(4):
                pq = 1024 * half + 512 * c + 128 * k + i[:, None]
                pk_pos = 1024 * half - 512 + 512 * c + j[None, :]
                ok = (pk_pos >= pq - 512) & (pk_pos <= pq) & (pk_pos >= 0)
                mk[:, 4 * c + k, :] = ok.astype(np.float32)
        mk = mk.reshape(128, 8192).astype(ml_dtypes.bfloat16)
        attn_maps.append(dict(xt_in=xt_slice, x_own=x_own, wqkvT=wqkvT,
                              woutT=woutT, mask_in=mk))
    return dec_maps, attn_maps, X


def build_mega():
    from contextlib import ExitStack
    nc = bass.Bass("TRN2", target_bir_lowering=False, debug=False)
    xt_in = nc.dram_tensor("xt_in", [E, TLOC], F32, kind="ExternalInput").ap()
    x_own = nc.dram_tensor("x_own", [TOWN, E], F32, kind="ExternalInput").ap()
    wqkvT = nc.dram_tensor("wqkvT", [E, 3 * E], F32, kind="ExternalInput").ap()
    woutT = nc.dram_tensor("woutT", [E, E], F32, kind="ExternalInput").ap()
    mask_in = nc.dram_tensor("mask_in", [128, 8 * 1024], BF16,
                             kind="ExternalInput").ap()
    pk_g = nc.dram_tensor("pk_g", [128, MLP], I32, kind="ExternalInput").ap()
    se_g = nc.dram_tensor("se_g", [128, MLP], F32, kind="ExternalInput").ap()
    pk_u = nc.dram_tensor("pk_u", [128, MLP], I32, kind="ExternalInput").ap()
    se_u = nc.dram_tensor("se_u", [128, MLP], F32, kind="ExternalInput").ap()
    pk_d = nc.dram_tensor("pk_d", [1024, E], I32, kind="ExternalInput").ap()
    se_d = nc.dram_tensor("se_d", [1024, E], F32, kind="ExternalInput").ap()
    y_out = nc.dram_tensor("y_out", [TOWN, E], F32, kind="ExternalOutput").ap()
    kT_d = nc.dram_tensor("kT_d", [128, NH * TLOC], BF16).ap()
    vv_d = nc.dram_tensor("vv_d", [128, 12 * E], BF16).ap()
    qT_d = nc.dram_tensor("qT_d", [128, NH * TOWN], BF16).ap()
    x1_d = nc.dram_tensor("x1_d", [TOWN, E], F32).ap()
    acc_d = nc.dram_tensor("acc_d", [TOWN, E], F32).ap()
    cc_in_gh = [nc.dram_tensor(f"cc_in_g{h}", [256, MLP // 2], BF16)
                for h in range(2)]
    cc_out_gh = [nc.dram_tensor(f"cc_out_g{h}", [E, MLP // 2], BF16)
                 for h in range(2)]
    cc_in_uh = [nc.dram_tensor(f"cc_in_u{h}", [256, MLP // 2], BF16)
                for h in range(2)]
    cc_out_uh = [nc.dram_tensor(f"cc_out_u{h}", [E, MLP // 2], BF16)
                 for h in range(2)]
    cc_in_dh = [nc.dram_tensor(f"cc_in_d{h}", [1024, E], BF16)
                for h in range(2)]
    cc_out_dh = [nc.dram_tensor(f"cc_out_d{h}", [MLP // 2, E], BF16)
                 for h in range(2)]

    C1 = 0.7978845608028654
    C2 = C1 * 0.044715
    NW = 1024

    with ExitStack() as ctx:
        ar = Arena(nc, 204 * KB)
        xt = ar.at("xt", 0, [128, 16 * TLOC], BF16)
        wq_sl = [ar.at(f"wq{p}", (48 + 4 * p) * KB, [128, 16 * 128], BF16)
                 for p in range(2)]
        wv_sl = ar.at("wv", 56 * KB, [128, 16 * 512], BF16)
        stg = [ar.at(f"stg{p}", (72 + 2 * p) * KB, [128, 1024], BF16)
               for p in range(2)]
        dw_i = [ar.at(f"dw{p}", (80 + 4 * p) * KB, [128, NW], I32)
                for p in range(2)]
        dse = [ar.at(f"dse{p}", (88 + 4 * p) * KB, [128, NW], F32)
               for p in range(2)]
        dhs = [ar.at(f"dhs{p}", (96 + 4 * p) * KB, [128, NW], F32)
               for p in range(2)]
        dci = [[ar.at(f"dci{p}{pl}", (104 + 4 * (2 * p + pl)) * KB,
                      [128, NW], I32) for pl in range(2)] for p in range(2)]
        dcb = [ar.at(f"dcb{p}", (120 + 2 * p) * KB, [128, NW], BF16)
               for p in range(2)]
        dgb = [ar.at(f"dgb{p}", (124 + 2 * p) * KB, [128, NW], BF16)
               for p in range(2)]
        dg8 = [ar.at(f"dg8{p}", (128 + 2 * p) * KB, [128, NW], BF16)
               for p in range(2)]
        drr = [[ar.at(f"drr{p}{pl}", (132 + 2 * (2 * p + pl)) * KB,
                      [128, NW], BF16) for pl in range(2)] for p in range(2)]
        dss = [[ar.at(f"dss{p}{pl}", (140 + 2 * (2 * p + pl)) * KB,
                      [128, NW], BF16) for pl in range(2)] for p in range(2)]
        du1 = [[ar.at(f"du1{p}{pl}", (148 + 2 * (2 * p + pl)) * KB,
                      [128, NW], BF16) for pl in range(2)] for p in range(2)]
        du2 = [[ar.at(f"du2{p}{pl}", (156 + 2 * (2 * p + pl)) * KB,
                      [128, NW], BF16) for pl in range(2)] for p in range(2)]
        dt1 = [[ar.at(f"dt1{p}{pl}", (164 + 2 * (2 * p + pl)) * KB,
                      [128, NW], BF16) for pl in range(2)] for p in range(2)]
        dob = [[ar.at(f"dob{p}{pl}", (172 + 2 * (2 * p + pl)) * KB,
                      [128, NW], BF16) for pl in range(2)] for p in range(2)]
        dbias4 = ar.at("dbias4", 180 * KB, [128, 1], F32)
        dbias12 = ar.at("dbias12", int(180.5 * KB), [128, 1], F32)
        kT = ar.at("kT", 0, [128, NH * TLOC], BF16)
        vv = ar.at("vv", 48 * KB, [128, 12 * E], BF16)
        qT = ar.at("qT", 96 * KB, [128, NH * TOWN], BF16)
        masks = ar.at("masks", 128 * KB, [128, 8 * 1024], BF16)
        probs4 = [ar.at(f"pb{p}", (144 + 2 * p) * KB, [128, 1024], BF16)
                  for p in range(8)]
        probsn = ar.at("pn", 160 * KB, [128, 1024], BF16)
        probsT = ar.at("probsT", 162 * KB, [128, 8 * 512], BF16)
        attnT = ar.at("attnT", 170 * KB, [128, 2 * NH * 512], BF16)
        wo = ar.at("wo", 0, [128, 16 * E], BF16)
        xoB = [ar.at(f"xoB{p}", (64 + 8 * p) * KB, [128, E], F32)
               for p in range(2)]
        residB = [ar.at(f"residB{p}", (80 + 8 * p) * KB, [128, E], F32)
                  for p in range(2)]
        x1til = ar.at("x1til", 96 * KB, [128, E], F32)
        sqt = ar.at("sqt", 104 * KB, [128, E], F32)
        x1T = ar.at("x1T", 0, [128, 16 * TOWN], BF16)
        interT = ar.at("interT", 32 * KB, [128, 8 * TOWN], BF16)
        wg_f = [ar.at(f"wgf{p}", (48 + 16 * p) * KB, [128, 16 * 512], BF16)
                for p in range(2)]
        wu_f = [ar.at(f"wuf{p}", (80 + 16 * p) * KB, [128, 16 * 512], BF16)
                for p in range(2)]
        wd_f = [ar.at(f"wdf{p}", (112 + 32 * p) * KB, [128, 8 * E], BF16)
                for p in range(2)]
        gt = [[ar.at(f"gt{p}{i}", (176 + 8 * p + i) * KB, [128, 512], BF16)
               for i in range(8)] for p in range(2)]
        accv = [ar.at(f"accv{p}", (192 + 2 * p) * KB, [128, 512], F32)
                for p in range(2)]
        xo1 = ar.at("xo1", 112 * KB, [128, E], F32)
        x1b = ar.at("x1b", 120 * KB, [128, E], BF16)
        resid3B = [ar.at(f"resid3B{p}", (112 + 8 * p) * KB, [128, E], F32)
                   for p in range(2)]
        accrB = [ar.at(f"accrB{p}", (128 + 8 * p) * KB, [128, E], F32)
                 for p in range(2)]
        xo3B = [ar.at(f"xo3B{p}", (144 + 8 * p) * KB, [128, E], F32)
                for p in range(2)]
        ytil = ar.at("ytil", 160 * KB, [128, E], F32)
        sqf = ar.at("sqf", 168 * KB, [128, E], F32)
        sm = ar.at("sm", 202 * KB, [128, 32], F32)
        ident = ar.at("ident", int(202.5 * KB), [128, 128], BF16)
        shiftB = ar.at("shiftB", int(203 * KB), [128, 1], F32)

        S = Sched(nc)
        for s in ("s_xt", "s_wq0", "s_wq1", "s_wv", "s_wo",
                  "s_dl0", "s_dl1", "s_cs0", "s_cs1", "s_kv0", "s_kv1",
                  "s_cc0", "s_cc1", "s_cc2", "s_cc3", "s_cc4", "s_cc5",
                  "s_rl", "s_xo0", "s_xo1", "s_x1s",
                  "s_wd0", "s_wd1", "s_gu0", "s_gu1", "s_f1", "s_f30", "s_f31",
                  "ga0", "ga1", "s_y", "s_msk",
                  "pe", "dv", "ac", "gp"):
            S.new_sem(ctx, s)
        ps_S0 = ctx.enter_context(nc.psum_tensor("S0", [128, 1024], F32))
        ps_S1 = ctx.enter_context(nc.psum_tensor("S1", [128, 1024], F32))
        ps_TP = [ctx.enter_context(nc.psum_tensor(f"TP{i}", [128, 1024], BF16))
                 for i in range(2)]
        ps_AV = ctx.enter_context(nc.psum_tensor("AVP", [128, 512], F32))
        ps_OP = ctx.enter_context(nc.psum_tensor("OPP", [128, 512], F32))
        surfs = [ps_S0[:, 0:512], ps_S0[:, 512:1024],
                 ps_S1[:, 0:512], ps_S1[:, 512:1024]]
        Sfull = [ps_S0, ps_S1]

        m1 = S.op("gpsimd", lambda g: g.memset(ident[:], 0.0))
        id_done = S.op("gpsimd", lambda g: g.affine_select(
            out=ident[:], in_=ident[:], compare_op=ALU.not_equal, fill=1.0,
            base=0, pattern=[[-1, 128]], channel_multiplier=1))
        S.op("gpsimd", lambda g: g.memset(dbias4[:], -4.0))
        c_init = S.op("gpsimd", lambda g: g.memset(dbias12[:], -12.0))
        c_shift = S.op("gpsimd", lambda g: g.memset(shiftB[:], EXP_SHIFT))

        xt_src = xt_in.rearrange("(c p) t -> p c t", p=128)
        xt_ld = S.op("gpsimd", lambda g: g.dma_start(
            out=xt[:].rearrange("p (c t) -> p c t", t=TLOC), in_=xt_src),
            inc="s_xt", inc_n=16)

        djobs = []
        for hf in range(2):
            for wi in range(hf, 16, 2):
                cs = slice(wi * NW, (wi + 1) * NW)
                dc = slice((wi // 2) * NW, (wi // 2 + 1) * NW)
                djobs.append((pk_g[:, cs], se_g[:, cs],
                              cc_in_gh[hf].ap()[0:128, dc],
                              cc_in_gh[hf].ap()[128:256, dc], None))
            djobs[-1] = djobs[-1][:4] + (3 * hf + 0,)
            for wi in range(hf, 16, 2):
                cs = slice(wi * NW, (wi + 1) * NW)
                dc = slice((wi // 2) * NW, (wi // 2 + 1) * NW)
                djobs.append((pk_u[:, cs], se_u[:, cs],
                              cc_in_uh[hf].ap()[0:128, dc],
                              cc_in_uh[hf].ap()[128:256, dc], None))
            djobs[-1] = djobs[-1][:4] + (3 * hf + 1,)
            for q in range(4 * hf, 4 * hf + 4):
                for wi in range(2):
                    cs = slice(wi * NW, (wi + 1) * NW)
                    ql = q - 4 * hf
                    djobs.append((pk_d[q * 128:(q + 1) * 128, cs],
                                  se_d[q * 128:(q + 1) * 128, cs],
                                  cc_in_dh[hf].ap()[ql * 256: ql * 256 + 128, cs],
                                  cc_in_dh[hf].ap()[ql * 256 + 128: ql * 256 + 256, cs],
                                  None))
            djobs[-1] = djobs[-1][:4] + (3 * hf + 2,)
        ndw = len(djobs)
        cc_ins = [cc_in_gh[0], cc_in_uh[0], cc_in_dh[0],
                  cc_in_gh[1], cc_in_uh[1], cc_in_dh[1]]
        cc_outs = [cc_out_gh[0], cc_out_uh[0], cc_out_dh[0],
                   cc_out_gh[1], cc_out_uh[1], cc_out_dh[1]]
        cc_done = {}
        cc_pending = []

        def flush_cc():
            while cc_pending:
                cci, v0, v1 = cc_pending.pop(0)
                S.wait("gpsimd", "s_cs0", v0)
                S.wait("gpsimd", "s_cs1", v1)
                cc_done[cci] = S.op("gpsimd", lambda g, cci=cci:
                                    g.collective_compute(
                    "AllGather", ALU.bypass, replica_groups=[list(range(8))],
                    ins=[cc_ins[cci].ap().opt()],
                    outs=[cc_outs[cci].ap().opt()]),
                    inc=f"s_cc{cci}", inc_n=1)

        wsrc = wqkvT.rearrange("(c p) f -> p c f", p=128)
        dfin, drrv, duuv, dobv, dstv, dldv = {}, {}, {}, {}, {}, {}

        def emit_decode_loads(i):
            p = i % 2
            if i >= 2:
                S.wait("sync", "dv", dfin[i - 2])
            S.op("sync", lambda s, i=i, p=p: s.dma_start(
                out=dw_i[p][:], in_=djobs[i][0]), inc=f"s_dl{p}", inc_n=16)
            dldv[i] = S.op("sync", lambda s, i=i, p=p: s.dma_start(
                out=dse[p][:], in_=djobs[i][1]), inc=f"s_dl{p}", inc_n=16)

        def emit_decode_window(i):
            p = i % 2
            S.wait("vector", f"s_dl{p}", dldv[i])
            S.op("vector", lambda v, p=p: v.tensor_scalar(
                dci[p][0][:], dw_i[p][:], 15, None, ALU.bitwise_and))
            S.op("vector", lambda v, p=p: v.tensor_scalar(
                dci[p][1][:], dw_i[p][:], 4, 15,
                ALU.logical_shift_right, ALU.bitwise_and))
            dfin[i] = S.op("vector", lambda v, p=p: v.tensor_scalar(
                dhs[p][:], dse[p][:], 0.5, None, ALU.mult))
            for pl in range(2):
                if i >= 2:
                    S.wait("vector", "ac", duuv[(i - 2, pl)])
                S.op("vector", lambda v, p=p, pl=pl: v.tensor_copy(
                    dcb[p][:], dci[p][pl][:]))
                S.op("vector", lambda v, p=p: v.tensor_scalar(
                    dgb[p][:], dcb[p][:], 7.5, None, ALU.is_ge))
                S.op("vector", lambda v, p=p: v.tensor_scalar(
                    dg8[p][:], dgb[p][:], 8.0, None, ALU.mult))
                S.op("vector", lambda v, p=p, pl=pl: v.tensor_scalar(
                    dss[p][pl][:], dgb[p][:], -2.0, 1.0, ALU.mult, ALU.add))
                drrv[(i, pl)] = S.op("vector", lambda v, p=p, pl=pl:
                                     v.tensor_tensor(drr[p][pl][:], dcb[p][:],
                                                     dg8[p][:], ALU.subtract))
            for pl in range(2):
                S.wait("scalar", "dv", drrv[(i, pl)])
                if pl == 0 and i == 0:
                    S.wait("scalar", "gp", c_init)
                if i >= 2:
                    S.wait("scalar", "dv", dobv[(i - 2, pl)])
                S.op("scalar", lambda s, p=p, pl=pl: s.activation(
                    du1[p][pl][:], drr[p][pl][:], AF.Relu, bias=dbias4[:]))
                duuv[(i, pl)] = S.op("scalar", lambda s, p=p, pl=pl: s.activation(
                    du2[p][pl][:], drr[p][pl][:], AF.Relu, bias=dbias12[:],
                    scale=2.0))
            for pl in range(2):
                S.wait("vector", "ac", duuv[(i, pl)])
                if i >= 2:
                    S.wait("vector", f"s_cs{p}", dstv[i - 2])
                S.op("vector", lambda v, p=p, pl=pl: v.tensor_tensor(
                    dt1[p][pl][:], drr[p][pl][:], du1[p][pl][:], ALU.add))
                S.op("vector", lambda v, p=p, pl=pl: v.tensor_tensor(
                    dcb[p][:], dt1[p][pl][:], du2[p][pl][:], ALU.add))
                S.op("vector", lambda v, p=p, pl=pl: v.tensor_tensor(
                    dt1[p][pl][:], dcb[p][:], dss[p][pl][:], ALU.mult))
                dobv[(i, pl)] = S.op("vector", lambda v, p=p, pl=pl:
                                     v.tensor_tensor(dob[p][pl][:], dt1[p][pl][:],
                                                     dhs[p][:], ALU.mult))
            S.wait("sync", "dv", dobv[(i, 0)])
            S.op("sync", lambda s, i=i, p=p: s.dma_start(
                out=djobs[i][2], in_=dob[p][0][:]), inc=f"s_cs{p}", inc_n=16)
            S.wait("sync", "dv", dobv[(i, 1)])
            dstv[i] = S.op("sync", lambda s, i=i, p=p: s.dma_start(
                out=djobs[i][3], in_=dob[p][1][:]), inc=f"s_cs{p}", inc_n=16)
            cci = djobs[i][4]
            if cci is not None:
                cc_pending.append((cci, S.cur("s_cs0"), S.cur("s_cs1")))

        qkv_groups = []
        qk_jobs = []
        for ft in range(32):
            wins = ((512,), (1024,)) if ft < 16 else ((0,), (512,), (1024,))
            for (off,) in wins:
                qk_jobs.append((ft, off))
        v_jobs = [(fw, tt) for fw in range(4) for tt in range(12)]

        slab_emitted = {}
        slab_val = {}
        slab_pe_last = {}
        evac_ac = {}
        dstv_kv = {}

        def pair_dst_info(j):
            if j < 16:
                return qT_d[:, j * 1024:(j + 1) * 1024]
            if j < 40:
                jj = j - 16
                return kT_d[:, jj * 1024:(jj + 1) * 1024]
            jj = j - 40
            fw, tt0 = jj // 6, 2 * (jj % 6)
            i0 = tt0 * 4 + fw
            return vv_d.rearrange("p (t f) -> p t f", f=512)[
                :, i0:i0 + 5:4, :]

        n_groups = 128
        dec_emitted = 0
        for g in range(n_groups):
            pair = g // 2
            surf = surfs[g % 4]
            qpart = g < 32
            kpart = 32 <= g < 80
            if qpart or kpart:
                ft, off = qk_jobs[g]
                p = ft % 2
                if ft not in slab_emitted:
                    if ft >= 2:
                        S.wait("gpsimd", "pe", slab_pe_last[ft - 2])
                    fsl = slice(ft * 128, (ft + 1) * 128)
                    slab_val[ft] = S.op("gpsimd", lambda gg, p=p, fsl=fsl:
                                        gg.dma_start(
                        out=wq_sl[p][:].rearrange("p (c f) -> p c f", f=128),
                        in_=wsrc[:, :, fsl]), inc=f"s_wq{p}", inc_n=16)
                    slab_emitted[ft] = True
                if g == 0:
                    S.wait("tensor", "s_xt", xt_ld)
                S.wait("tensor", f"s_wq{p}", slab_val[ft])
                if pair >= 2:
                    S.wait("tensor", "dv", evac_ac[pair - 2])
                for ec in range(16):
                    def mm(t, surf=surf, p=p, ec=ec, off=off):
                        return t.matmul(
                            surf, wq_sl[p][:, ec * 128:(ec + 1) * 128],
                            xt[:, ec * TLOC + off: ec * TLOC + off + 512],
                            start=(ec == 0), stop=(ec == 15))
                    S.op("tensor", mm, inc="pe" if ec == 15 else None)
                slab_pe_last[ft] = S.cur("pe")
            else:
                fw, tt = v_jobs[g - 80]
                if tt == 0:
                    if fw >= 1:
                        S.wait("gpsimd", "pe", S.cur("pe"))
                    fsl = slice(2 * E + fw * 512, 2 * E + (fw + 1) * 512)
                    slab_val[("v", fw)] = S.op("gpsimd", lambda gg, fsl=fsl:
                                               gg.dma_start(
                        out=wv_sl[:].rearrange("p (c f) -> p c f", f=512),
                        in_=wsrc[:, :, fsl]), inc="s_wv", inc_n=16)
                S.wait("tensor", "s_wv", slab_val[("v", fw)])
                if pair >= 2:
                    S.wait("tensor", "dv", evac_ac[pair - 2])
                for ec in range(16):
                    def mmv(t, surf=surf, ec=ec, tt=tt):
                        return t.matmul(
                            surf,
                            xt[:, ec * TLOC + tt * 128: ec * TLOC + (tt + 1) * 128],
                            wv_sl[:, ec * 512:(ec + 1) * 512],
                            start=(ec == 0), stop=(ec == 15))
                    S.op("tensor", mmv, inc="pe" if ec == 15 else None)
            if g % 2 == 1:
                pe_pair = S.cur("pe")
                sp = pair % 2
                S.wait("vector", "pe", pe_pair)
                if pair >= 2:
                    S.wait("vector", f"s_kv{sp}", dstv_kv.get(pair - 2, 0))
                scl = QK_SCALE if pair < 16 else 1.0
                evac_ac[pair] = S.op("vector", lambda v, sp=sp, scl=scl:
                                     v.tensor_scalar(stg[sp][:], Sfull[sp][:],
                                                     scl, None, ALU.mult))
                S.wait("sync", "dv", evac_ac[pair])
                dst = pair_dst_info(pair)
                if pair >= 40:
                    src = stg[sp][:].rearrange("p (t f) -> p t f", f=512)
                else:
                    src = stg[sp][:]
                dstv_kv[pair] = S.op("sync", lambda s, dst=dst, src=src:
                                     s.dma_start(out=dst, in_=src),
                                     inc=f"s_kv{sp}", inc_n=16)
            want_dec = min(ndw, (g + 1) * ndw // n_groups + 2)
            while dec_emitted < want_dec:
                i = dec_emitted
                if i < 2:
                    emit_decode_loads(i)
                emit_decode_window(i)
                if i + 2 < ndw:
                    emit_decode_loads(i + 2)
                dec_emitted += 1
            if g in (63, 120):
                flush_cc()
        while dec_emitted < ndw:
            i = dec_emitted
            if i < 2:
                emit_decode_loads(i)
            emit_decode_window(i)
            if i + 2 < ndw:
                emit_decode_loads(i + 2)
            dec_emitted += 1

        flush_cc()
        qkv_pe_last = S.cur("pe")
        dec_dv_last = S.cur("dv")
        last_evac_pair = 63

        S.wait("sync", "pe", qkv_pe_last)
        S.wait("sync", "dv", dec_dv_last)
        S.wait("sync", "s_cs0", S.cur("s_cs0"))
        S.wait("sync", "s_cs1", S.cur("s_cs1"))
        S.wait("sync", "s_kv0", S.cur("s_kv0"))
        S.wait("sync", "s_kv1", S.cur("s_kv1"))
        rl_k = S.op("sync", lambda s: s.dma_start(out=kT[:], in_=kT_d),
                    inc="s_rl", inc_n=16)
        rl_v = S.op("sync", lambda s: s.dma_start(out=vv[:], in_=vv_d),
                    inc="s_rl", inc_n=16)
        rl_q = S.op("sync", lambda s: s.dma_start(out=qT[:], in_=qT_d),
                    inc="s_rl", inc_n=16)
        rl_m = S.op("sync", lambda s: s.dma_start(out=masks[:], in_=mask_in),
                    inc="s_msk", inc_n=16)

        S.wait("scalar", "gp", c_shift)
        S.wait("tensor", "gp", id_done)
        S.wait("vector", "s_msk", rl_m)

        exp_ac, tr_pe, ptk_dv, av_pe, att_dv, sum_ac, nrm_dv = ({}, {}, {}, {},
                                                                {}, {}, {})
        last_scores_pe = None
        chs = [(c, h) for c in range(2) for h in range(16)]

        def emit_softmax(ci_):
            c, h = chs[ci_]
            for k in range(4):
                Sk = ps_S0 if k % 2 == 0 else ps_S1
                slot = (ci_ % 2) * 4 + k
                if ci_ == 0 and k == 0:
                    S.wait("tensor", "s_rl", rl_q)
                prev = (c, h, k - 2) if k >= 2 else \
                    (chs[ci_ - 1] + (k + 2,) if ci_ > 0 else None)
                if prev in exp_ac:
                    S.wait("tensor", "ac", exp_ac[prev])
                elif ci_ == 0:
                    S.wait("tensor", "dv", evac_ac[63 - (1 - k % 2)])
                q0 = h * TOWN + c * 512 + k * 128
                k0 = h * TLOC + c * 512
                S.op("tensor", lambda t, Sk=Sk, q0=q0, k0=k0: t.matmul(
                    Sk[:, 0:512], qT[:, q0:q0 + 128], kT[:, k0:k0 + 512],
                    start=True, stop=True))
                pe_sc = S.op("tensor", lambda t, Sk=Sk, q0=q0, k0=k0: t.matmul(
                    Sk[:, 512:1024], qT[:, q0:q0 + 128],
                    kT[:, k0 + 512:k0 + 1024], start=True, stop=True), inc="pe")
                S.wait("scalar", "pe", pe_sc)
                pslot_prev = (chs[ci_ - 2] + (k,)) if ci_ >= 2 else None
                if pslot_prev in tr_pe:
                    S.wait("scalar", "pe", tr_pe[pslot_prev])
                exp_ac[(c, h, k)] = S.op("scalar", lambda s, Sk=Sk, slot=slot:
                                         s.activation(probs4[slot][:], Sk[:],
                                                      AF.Exp, bias=shiftB[:]))
                S.wait("vector", "ac", exp_ac[(c, h, k)])
                pk_prev = (c, h, k - 1) if k >= 1 else \
                    (chs[ci_ - 1] + (3,) if ci_ > 0 else None)
                if pk_prev in sum_ac:
                    S.wait("vector", "ac", sum_ac[pk_prev])
                mview = masks[:, (c * 4 + k) * 1024:(c * 4 + k + 1) * 1024]
                pm = S.op("vector", lambda v, slot=slot, mview=mview:
                          v.tensor_tensor(probsn[:], probs4[slot][:], mview,
                                          ALU.mult))
                S.wait("scalar", "dv", pm)
                sum_ac[(c, h, k)] = S.op("scalar", lambda s, k=k, slot=slot:
                                         s.activation(
                    probs4[slot][:], probsn[:], AF.Identity,
                    accum_out=sm[:, k:k + 1]))
                S.wait("vector", "ac", sum_ac[(c, h, k)])
                S.op("vector", lambda v, k=k: v.reciprocal(sm[:, 4 + k:5 + k],
                                                           sm[:, k:k + 1]))
                nrm_dv[(c, h, k)] = S.op("vector", lambda v, k=k, slot=slot:
                                         v.tensor_scalar(
                    probs4[slot][:], probsn[:], sm[:, 4 + k:5 + k], None,
                    ALU.mult))

        def emit_trs_av(ci_):
            c, h = chs[ci_]
            for k in range(4):
                slot = (ci_ % 2) * 4 + k
                S.wait("tensor", "dv", nrm_dv[(c, h, k)])
                prev = (c, h, k - 2) if k >= 2 else \
                    (chs[ci_ - 1] + (k + 2,) if ci_ > 0 else None)
                if prev in ptk_dv:
                    S.wait("tensor", "dv", ptk_dv[prev])
                for m in range(8):
                    def trp(t, k=k, m=m, slot=slot):
                        return t.transpose(
                            ps_TP[k % 2][:, m * 128:(m + 1) * 128],
                            probs4[slot][:, m * 128:(m + 1) * 128], ident[:])
                    if m == 7:
                        tr_pe[(c, h, k)] = S.op("tensor", trp, inc="pe")
                    else:
                        S.op("tensor", trp)
                S.wait("vector", "pe", tr_pe[(c, h, k)])
                if ci_ > 0 and k == 0:
                    S.wait("vector", "pe", av_pe[chs[ci_ - 1]])
                def evt(v, k=k):
                    src = ps_TP[k % 2][:].rearrange("p (m f) -> p m f", f=128)
                    dst = probsT[:].rearrange("p (m f) -> p m f", f=512)[
                        :, :, k * 128:(k + 1) * 128]
                    return v.tensor_copy(dst, src)
                ptk_dv[(c, h, k)] = S.op("vector", evt)
            S.wait("tensor", "dv", ptk_dv[(c, h, 3)])
            if ci_ > 0:
                S.wait("tensor", "dv", att_dv[chs[ci_ - 1]])
            if ci_ == 0:
                S.wait("tensor", "s_rl", rl_q)
            for m in range(8):
                def av(t, c=c, h=h, m=m):
                    return t.matmul(
                        ps_AV[:],
                        vv[:, (4 * c + m) * E + h * 128:
                           (4 * c + m) * E + (h + 1) * 128],
                        probsT[:, m * 512:(m + 1) * 512],
                        start=(m == 0), stop=(m == 7))
                if m == 7:
                    av_pe[(c, h)] = S.op("tensor", av, inc="pe")
                else:
                    S.op("tensor", av)
            S.wait("vector", "pe", av_pe[(c, h)])
            dstA = attnT[:, (c * 16 + h) * 512:(c * 16 + h + 1) * 512]
            att_dv[(c, h)] = S.op("vector", lambda v, dstA=dstA: v.tensor_copy(
                dstA, ps_AV[:]))

        for ci_ in range(33):
            if ci_ < 32:
                emit_softmax(ci_)
            if ci_ >= 1:
                emit_trs_av(ci_ - 1)
        last_scores_pe = S.cur("pe")

        S.wait("gpsimd", "pe", av_pe[chs[-1]])
        wo_src = woutT.rearrange("(c p) f -> p c f", p=128)
        ld_wo = S.op("gpsimd", lambda g: g.dma_start(
            out=wo[:].rearrange("p (c f) -> p c f", f=E), in_=wo_src),
            inc="s_wo", inc_n=16)

        S.wait("sync", "dv", att_dv[chs[-1]])
        op_dv = None
        xo_ld = {}
        resid_dv = {}
        x1s_val = {}
        st_val = 0
        sq_of = {}

        for step in range(9):
            tt = step
            if tt < 8:
                c = tt // 4
                bp = tt % 2
                if tt >= 2:
                    S.wait("sync", "dv", resid_dv[tt - 2])
                xo_ld[tt] = S.op("sync", lambda s, tt=tt, bp=bp: s.dma_start(
                    out=xoB[bp][:], in_=x_own[tt * 128:(tt + 1) * 128, :]),
                    inc=f"s_xo{bp}", inc_n=16)
                for fw in range(4):
                    if (tt, fw) == (0, 0):
                        S.wait("tensor", "s_wo", ld_wo)
                        S.wait("tensor", "dv", att_dv[chs[-1]])
                    else:
                        S.wait("tensor", "dv", op_dv)
                    for ec in range(16):
                        def mo(t, c=c, tt=tt, fw=fw, ec=ec):
                            lhs = attnT[:, (c * 16 + ec) * 512 + (tt % 4) * 128:
                                        (c * 16 + ec) * 512 + (tt % 4) * 128 + 128]
                            rhs = wo[:, ec * E + fw * 512: ec * E + fw * 512 + 512]
                            return t.matmul(ps_OP[:], lhs, rhs,
                                            start=(ec == 0), stop=(ec == 15))
                        S.op("tensor", mo, inc="pe" if ec == 15 else None)
                    pe_op = S.cur("pe")
                    S.wait("vector", "pe", pe_op)
                    S.wait("vector", f"s_xo{bp}", xo_ld[tt])
                    def rp(v, fw=fw, bp=bp):
                        return v.tensor_tensor(
                            residB[bp][:, fw * 512:(fw + 1) * 512], ps_OP[:],
                            xoB[bp][:, fw * 512:(fw + 1) * 512], ALU.add)
                    op_dv = S.op("vector", rp)
                resid_dv[tt] = op_dv
            if step >= 1:
                lt = step - 1
                bp = lt % 2
                S.op("vector", lambda v, bp=bp: v.reduce_sum(
                    sm[:, 8:9], residB[bp][:], AX.X, negate=True))
                nm_dv = S.op("vector", lambda v: v.tensor_scalar(
                    sm[:, 9:10], sm[:, 8:9], 1.0 / E, None, ALU.mult))
                S.wait("scalar", "dv", nm_dv)
                sq_of[lt] = S.op("scalar", lambda s, bp=bp: s.activation(
                    sqt[:], residB[bp][:], AF.Square, accum_out=sm[:, 10:11]))
                S.wait("vector", "ac", sq_of[lt])
                S.op("vector", lambda v: v.tensor_scalar(
                    sm[:, 11:12], sm[:, 10:11], 1.0 / E, None, ALU.mult))
                S.op("vector", lambda v: v.tensor_tensor(
                    sm[:, 12:13], sm[:, 9:10], sm[:, 9:10], ALU.mult))
                S.op("vector", lambda v: v.tensor_tensor(
                    sm[:, 13:14], sm[:, 11:12], sm[:, 12:13], ALU.subtract))
                S.op("vector", lambda v: v.tensor_scalar(
                    sm[:, 14:15], sm[:, 13:14], EPS, None, ALU.add))
                rv_dv = S.op("vector", lambda v: v.reciprocal(
                    sm[:, 15:16], sm[:, 14:15]))
                S.wait("scalar", "dv", rv_dv)
                rstd_ac = S.op("scalar", lambda s: s.activation(
                    sm[:, 16:17], sm[:, 15:16], AF.Sqrt))
                S.wait("vector", "ac", rstd_ac)
                S.wait("vector", "s_x1s", st_val)
                x1_dv = S.op("vector", lambda v, bp=bp: v.tensor_scalar(
                    x1til[:], residB[bp][:], sm[:, 9:10], sm[:, 16:17],
                    ALU.add, ALU.mult))
                S.wait("sync", "dv", x1_dv)
                st_val = S.op("sync", lambda s, lt=lt: s.dma_start(
                    out=x1_d[lt * 128:(lt + 1) * 128, :], in_=x1til[:]),
                    inc="s_x1s", inc_n=16)
                x1s_val[lt] = st_val
        x1_dv_last = S.cur("dv")

        wgsrc_h = [cc_out_gh[h].ap().rearrange("(c p) n -> p c n", p=128)
                   for h in range(2)]
        wusrc_h = [cc_out_uh[h].ap().rearrange("(c p) n -> p c n", p=128)
                   for h in range(2)]
        wd_ap_h = [cc_out_dh[h].ap() for h in range(2)]

        f1_evac = None
        cast_dv = {}
        tp_pe = 0
        for tt in range(8):
            if tt >= 1:
                S.wait("sync", "dv", cast_dv[tt - 1])
            S.wait("sync", "s_x1s", x1s_val[tt])
            xo_ld = S.op("sync", lambda s, tt=tt: s.dma_start(
                out=xo1[:], in_=x1_d[tt * 128:(tt + 1) * 128, :]),
                inc="s_f1", inc_n=16)
            S.wait("vector", "s_f1", xo_ld)
            if tt >= 1:
                S.wait("vector", "pe", tp_pe)
            cast_dv[tt] = S.op("vector", lambda v: v.tensor_copy(x1b[:], xo1[:]))
            S.wait("tensor", "dv", cast_dv[tt])
            if tt >= 1:
                S.wait("tensor", "dv", f1_evac)
            for hc in range(16):
                def tr1(t, hc=hc):
                    base = 256 * (hc // 2) + (hc % 2)
                    src = x1b[:, base: base + 255: 2]
                    return t.transpose(
                        ps_TP[hc // 8][:, (hc % 8) * 128:(hc % 8 + 1) * 128],
                        src, ident[:])
                S.op("tensor", tr1, inc="pe" if hc == 15 else None)
            tp_pe = S.cur("pe")
            S.wait("vector", "pe", tp_pe)
            for half in range(2):
                def ev1(v, tt=tt, half=half):
                    src = ps_TP[half][:].rearrange("p (m f) -> p m f", f=128)
                    dst = x1T[:].rearrange("p (c t) -> p c t", t=TOWN)[
                        :, half * 8:(half + 1) * 8, tt * 128:(tt + 1) * 128]
                    return v.tensor_copy(dst, src)
                f1_evac = S.op("vector", ev1)
        f1_done_dv = f1_evac

        g_idx = 0
        gelu_dv, gelu_free = [], []
        gp_p_l, ac_th_l, gp_q_l, dv_arg_l = [], [], [], []
        slab_sld, slab_pe_done = {}, {}
        wd_sld, down_pe_G = {}, {}
        ga_vals, accv_dv = [], []
        inter_done = {}
        first_wg = True
        Gseq = [0, 2, 4, 6, 8, 10, 12, 14, 1, 3, 5, 7, 9, 11, 13, 15]
        for gi, G in enumerate(Gseq):
            hf = G % 2
            wdbuf = gi % 2
            if gi < 2:
                S.wait("sync", "dv", f1_done_dv)
            else:
                S.wait("sync", "pe", down_pe_G[Gseq[gi - 2]])
            S.wait("sync", f"s_cc{3 * hf + 2}", cc_done[3 * hf + 2])
            for c_l in range(8):
                row0 = 1024 * (G // 2) + 256 * (c_l // 2) + 128 * (c_l % 2)
                def ldwd(s, row0=row0, hf=hf, wdbuf=wdbuf, c_l=c_l):
                    return s.dma_start(
                        out=wd_f[wdbuf][:, c_l * E:(c_l + 1) * E],
                        in_=wd_ap_h[hf][row0:row0 + 128, :])
                wd_sld[G] = S.op("sync", ldwd, inc=f"s_wd{wdbuf}", inc_n=16)
            for sw in range(2):
                if gi >= 1:
                    S.wait("sync", "pe", slab_pe_done[(Gseq[gi - 1], sw)])
                if first_wg:
                    S.wait("sync", "dv", x1_dv_last)
                    S.wait("sync", "s_x1s", S.cur("s_x1s"))
                    first_wg = False
                S.wait("sync", f"s_cc{3 * hf}", cc_done[3 * hf])
                S.wait("sync", f"s_cc{3 * hf + 1}", cc_done[3 * hf + 1])
                nsl = slice(1024 * (G // 2) + 512 * sw,
                            1024 * (G // 2) + 512 * (sw + 1))
                S.op("sync", lambda s, sw=sw, nsl=nsl, hf=hf: s.dma_start(
                    out=wg_f[sw][:].rearrange("p (c n) -> p c n", n=512),
                    in_=wgsrc_h[hf][:, :, nsl]), inc=f"s_gu{sw}", inc_n=16)
                slab_sld[(G, sw)] = S.op("sync", lambda s, sw=sw, nsl=nsl,
                                         hf=hf: s.dma_start(
                    out=wu_f[sw][:].rearrange("p (c n) -> p c n", n=512),
                    in_=wusrc_h[hf][:, :, nsl]), inc=f"s_gu{sw}", inc_n=16)
            for c_l in range(8):
                sw = c_l // 4
                woff = 256 * ((c_l % 4) // 2)
                sub = c_l % 2
                for tw in range(2):
                    gs = surfs[g_idx % 2]
                    us = surfs[2 + g_idx % 2]
                    S.wait("tensor", f"s_gu{sw}", slab_sld[(G, sw)])
                    if g_idx == 0:
                        S.wait("tensor", "dv", f1_done_dv)
                    if g_idx >= 2:
                        _, dv_ub2 = gelu_free[g_idx - 2]
                        S.wait("tensor", "dv", dv_ub2)
                    for wgt, surf in ((wg_f[sw], gs), (wu_f[sw], us)):
                        for hc in range(16):
                            def mmg(t, wgt=wgt, surf=surf, hc=hc, woff=woff,
                                    sub=sub, tw=tw):
                                lhs = wgt[:, hc * 512 + woff + sub:
                                          hc * 512 + woff + sub + 255: 2]
                                rhs = x1T[:, hc * TOWN + tw * 512:
                                          hc * TOWN + tw * 512 + 512]
                                return t.matmul(surf, lhs, rhs,
                                                start=(hc == 0), stop=(hc == 15))
                            S.op("tensor", mmg,
                                 inc="pe" if hc == 15 else None)
                    pe_gu = S.cur("pe")
                    par = g_idx % 2
                    g2, pp, gb, ub, arg, th, gu, q8 = gt[par]
                    S.wait("vector", "pe", pe_gu)
                    dv_gb = S.op("vector", lambda v, gb=gb, gs=gs:
                                 v.tensor_copy(gb[:], gs))
                    dv_ub = S.op("vector", lambda v, ub=ub, us=us:
                                 v.tensor_copy(ub[:], us))
                    S.wait("scalar", "dv", dv_gb)
                    if g_idx >= 2:
                        S.wait("scalar", "gp", gp_p_l[g_idx - 2])
                        S.wait("scalar", "gp", gp_q_l[g_idx - 2])
                    ac_g2 = S.op("scalar", lambda s, g2=g2, gb=gb: s.activation(
                        g2[:], gb[:], AF.Square))
                    S.wait("gpsimd", "ac", ac_g2)
                    if g_idx >= 2:
                        S.wait("gpsimd", "dv", dv_arg_l[g_idx - 2])
                        S.wait("gpsimd", "dv", gelu_dv[g_idx - 2])
                    gp_p = S.op("gpsimd", lambda g, pp=pp, g2=g2: g.tensor_scalar(
                        pp[:], g2[:], C2, C1, ALU.mult, ALU.add))
                    S.wait("vector", "gp", gp_p)
                    if g_idx >= 2:
                        S.wait("vector", "ac", ac_th_l[g_idx - 2])
                    dv_arg = S.op("vector", lambda v, arg=arg, pp=pp, gb=gb:
                                  v.tensor_tensor(arg[:], pp[:], gb[:],
                                                  ALU.mult))
                    S.wait("scalar", "dv", dv_arg)
                    if g_idx >= 2:
                        S.wait("scalar", "gp", gp_q_l[g_idx - 2])
                    ac_th = S.op("scalar", lambda s, th=th, arg=arg: s.activation(
                        th[:], arg[:], AF.Tanh))
                    S.wait("gpsimd", "ac", ac_th)
                    gp_q = S.op("gpsimd", lambda g, q8=q8, th=th: g.tensor_scalar(
                        q8[:], th[:], 0.5, 0.5, ALU.mult, ALU.add))
                    S.op("vector", lambda v, gu=gu, gb=gb, ub=ub:
                         v.tensor_tensor(gu[:], gb[:], ub[:], ALU.mult))
                    S.wait("vector", "gp", gp_q)
                    dst_i = interT[:, c_l * TOWN + tw * 512:
                                   c_l * TOWN + tw * 512 + 512]
                    dv_int = S.op("vector", lambda v, dst_i=dst_i, q8=q8, gu=gu:
                                  v.tensor_tensor(dst_i, q8[:], gu[:],
                                                  ALU.mult))
                    gelu_dv.append(dv_int)
                    gelu_free.append((0, dv_ub))
                    gp_p_l.append(gp_p)
                    ac_th_l.append(ac_th)
                    gp_q_l.append(gp_q)
                    dv_arg_l.append(dv_arg)
                    slab_pe_done[(G, sw)] = pe_gu
                    g_idx += 1
            inter_done[G] = gelu_dv[-1]
            for tt in range(8):
                for hw in range(4):
                    di = (tt * 4 + hw) % 2
                    ps_D = ps_AV if di == 0 else ps_OP
                    S.wait("tensor", "dv", inter_done[G])
                    S.wait("tensor", f"s_wd{wdbuf}", wd_sld[G])
                    idx = gi * 32 + tt * 4 + hw
                    if idx >= 2:
                        S.wait("tensor", "dv", accv_dv[idx - 2])
                    for c_l in range(8):
                        def mmd(t, ps_D=ps_D, wdbuf=wdbuf, c_l=c_l, tt=tt, hw=hw):
                            lhs = interT[:, c_l * TOWN + tt * 128:
                                         c_l * TOWN + tt * 128 + 128]
                            rhs = wd_f[wdbuf][:, c_l * E + hw * 512:
                                              c_l * E + hw * 512 + 512]
                            return t.matmul(ps_D[:], lhs, rhs,
                                            start=(c_l == 0), stop=(c_l == 7))
                        S.op("tensor", mmd, inc="pe" if c_l == 7 else None)
                    pe_d = S.cur("pe")
                    S.wait("vector", "pe", pe_d)
                    if idx >= 2:
                        S.wait("vector", f"ga{di}", ga_vals[idx - 2])
                    adv = S.op("vector", lambda v, di=di, ps_D=ps_D:
                               v.tensor_copy(accv[di][:], ps_D[:]))
                    accv_dv.append(adv)
                    S.wait("gpsimd", "dv", adv)
                    def ga_op(g, di=di, tt=tt, hw=hw, gi=gi):
                        dst = acc_d[tt * 128:(tt + 1) * 128,
                                    hw * 512:(hw + 1) * 512]
                        if gi == 0:
                            return g.dma_start(out=dst, in_=accv[di][:])
                        return g.dma_start(out=dst, in_=accv[di][:],
                                           accum_op=ALU.add)
                    ga_vals.append(S.op("gpsimd", ga_op, inc=f"ga{di}",
                                        inc_n=16))
            down_pe_G[G] = S.cur("pe")

        S.wait("sync", "ga0", S.cur("ga0"))
        S.wait("sync", "ga1", S.cur("ga1"))
        y_val = 0
        r3_dv = {}
        for step in range(9):
            tt = step
            if tt < 8:
                bp = tt % 2
                if tt >= 2:
                    S.wait("sync", "dv", r3_dv[tt - 2])
                S.op("sync", lambda s, tt=tt, bp=bp: s.dma_start(
                    out=xo3B[bp][:], in_=x1_d[tt * 128:(tt + 1) * 128, :]),
                    inc=f"s_f3{bp}", inc_n=16)
                ld_a = S.op("sync", lambda s, tt=tt, bp=bp: s.dma_start(
                    out=accrB[bp][:], in_=acc_d[tt * 128:(tt + 1) * 128, :]),
                    inc=f"s_f3{bp}", inc_n=16)
                S.wait("vector", f"s_f3{bp}", ld_a)
                r3_dv[tt] = S.op("vector", lambda v, bp=bp: v.tensor_tensor(
                    resid3B[bp][:], xo3B[bp][:], accrB[bp][:], ALU.add))
            if step >= 1:
                lt = step - 1
                bp = lt % 2
                S.op("vector", lambda v, bp=bp: v.reduce_sum(
                    sm[:, 0:1], resid3B[bp][:], AX.X, negate=True))
                nm3 = S.op("vector", lambda v: v.tensor_scalar(
                    sm[:, 1:2], sm[:, 0:1], 1.0 / E, None, ALU.mult))
                S.wait("scalar", "dv", nm3)
                sq3 = S.op("scalar", lambda s, bp=bp: s.activation(
                    sqf[:], resid3B[bp][:], AF.Square, accum_out=sm[:, 2:3]))
                S.wait("vector", "ac", sq3)
                S.op("vector", lambda v: v.tensor_scalar(
                    sm[:, 3:4], sm[:, 2:3], 1.0 / E, None, ALU.mult))
                S.op("vector", lambda v: v.tensor_tensor(
                    sm[:, 4:5], sm[:, 1:2], sm[:, 1:2], ALU.mult))
                S.op("vector", lambda v: v.tensor_tensor(
                    sm[:, 5:6], sm[:, 3:4], sm[:, 4:5], ALU.subtract))
                S.op("vector", lambda v: v.tensor_scalar(
                    sm[:, 6:7], sm[:, 5:6], EPS, None, ALU.add))
                rv3 = S.op("vector", lambda v: v.reciprocal(sm[:, 7:8],
                                                            sm[:, 6:7]))
                S.wait("scalar", "dv", rv3)
                rstd3 = S.op("scalar", lambda s: s.activation(
                    sm[:, 8:9], sm[:, 7:8], AF.Sqrt))
                S.wait("vector", "ac", rstd3)
                S.wait("vector", "s_y", y_val)
                y_dv = S.op("vector", lambda v, bp=bp: v.tensor_scalar(
                    ytil[:], resid3B[bp][:], sm[:, 1:2], sm[:, 8:9],
                    ALU.add, ALU.mult))
                S.wait("sync", "dv", y_dv)
                y_val = S.op("sync", lambda s, lt=lt: s.dma_start(
                    out=y_out[lt * 128:(lt + 1) * 128, :], in_=ytil[:]),
                    inc="s_y", inc_n=16)
        S.wait("sync", "s_y", y_val)

        block = ctx.enter_context(nc.Block())
        S.run(block)
    return nc


_BUILD_CACHE = {}


def _cached(name, builder):
    if name not in _BUILD_CACHE:
        _BUILD_CACHE[name] = builder()
    return _BUILD_CACHE[name]


USE_MEGA = True


def kernel(**inputs):
    dec_maps, attn_maps, X = host_prep_inputs(**inputs)

    if USE_MEGA:
        nc = _cached("mega", build_mega)
        maps = [dict(**dec_maps[r], **attn_maps[r]) for r in range(NCORES)]
        res = run_bass_kernel_spmd(nc, maps, core_ids=list(range(NCORES)))
        out = np.concatenate([np.asarray(res.results[r]["y_out"])
                              for r in range(NCORES)], axis=0)
        return out.reshape(4, 2048, E).astype(np.float32)

    nc_dec = _cached("dec", build_decode_nc)
    res_dec = run_bass_kernel_spmd(nc_dec, dec_maps, core_ids=list(range(NCORES)))
    wg_all = np.concatenate([np.asarray(res_dec.results[r]["wg_dec"])
                             for r in range(NCORES)], axis=0)
    wu_all = np.concatenate([np.asarray(res_dec.results[r]["wu_dec"])
                             for r in range(NCORES)], axis=0)
    wd_all = np.concatenate([np.asarray(res_dec.results[r]["wd_dec"])
                             for r in range(NCORES)], axis=0)

    nc_attn = _cached("attn", build_attn_v0)
    res_attn = run_bass_kernel_spmd(nc_attn, attn_maps, core_ids=list(range(NCORES)))
    x1_parts = [np.asarray(res_attn.results[r]["x1_out"]) for r in range(NCORES)]

    nc_ffn = _cached("ffn", build_ffn_v0)
    ffn_maps = [dict(x1_in=x1_parts[r], wg_all=wg_all, wu_all=wu_all,
                     wd_all=wd_all) for r in range(NCORES)]
    res_ffn = run_bass_kernel_spmd(nc_ffn, ffn_maps, core_ids=list(range(NCORES)))
    out = np.concatenate([np.asarray(res_ffn.results[r]["y_out"])
                          for r in range(NCORES)], axis=0)
    return out.reshape(4, 2048, E).astype(np.float32)


# version 12
# speedup vs baseline: 1.1443x; 1.0597x over previous
import numpy as np
import ml_dtypes
import concourse.bass as bass
import concourse.mybir as mybir
from concourse.bass_utils import run_bass_kernel_spmd

F32 = mybir.dt.float32
BF16 = mybir.dt.bfloat16
I32 = mybir.dt.int32
U8 = mybir.dt.uint8
AF = mybir.ActivationFunctionType
ALU = mybir.AluOpType
AX = mybir.AxisListType

KB = 1024
NCORES = 8
E = 2048
NH = 16
HD = 128
TOWN = 1024
TLOC = 1536
MLP = 16384
QK_SCALE = float(1.0 / np.sqrt(np.float32(128)))
EPS = 1e-5
EXP_SHIFT = -4.0

LUT = np.array([0.0, 0.5, 1.0, 1.5, 2.0, 3.0, 4.0, 6.0,
                0.0, -0.5, -1.0, -1.5, -2.0, -3.0, -4.0, -6.0], np.float32)


class Sched:
    ENGINES = ("tensor", "vector", "scalar", "sync", "gpsimd")

    def __init__(self, nc):
        self.nc = nc
        self.q = {k: [] for k in self.ENGINES}
        self.sems = {}

    def new_sem(self, ctx, name):
        h = ctx.enter_context(self.nc.semaphore(name))
        self.sems[name] = [h, 0]

    def cur(self, name):
        return self.sems[name][1]

    AUTO = {"vector": "dv", "scalar": "ac", "gpsimd": "gp"}

    def op(self, eng, fn, inc=None, inc_n=1):
        auto = self.AUTO.get(eng)
        if auto is not None and (inc is None or inc == auto):
            rec = self.sems[auto]
            self.wait(eng, auto, rec[1])
            rec[1] += 1
            h = rec[0]
            self.q[eng].append(lambda e, fn=fn, h=h: fn(e).then_inc(h, 1))
            return rec[1]
        if inc is None:
            self.q[eng].append(fn)
            return None
        rec = self.sems[inc]
        rec[1] += inc_n
        h = rec[0]

        def wrapped(e, fn=fn, h=h, n=inc_n):
            fn(e).then_inc(h, n)
        self.q[eng].append(wrapped)
        return rec[1]

    def wait(self, eng, sem, val):
        h = self.sems[sem][0]
        self.q[eng].append(lambda e, h=h, v=val: e.wait_ge(h, v))

    def run(self, block):
        for name in self.ENGINES:
            ops = self.q[name]
            getattr(block, name)(lambda e, ops=ops: [f(e) for f in ops])


class Arena:

    def __init__(self, nc, total_bytes):
        base = nc._sbuf_addr_for_side("left")
        aligned = (base + 31) & ~31
        nc.alloc_sbuf_tensor("arena", [128, total_bytes + (aligned - base)], U8)
        self.nc = nc
        self.base = aligned
        self.total = total_bytes

    def at(self, name, off_bytes, shape, dtype):
        assert off_bytes % 32 == 0
        h = self.nc.alloc_sbuf_tensor_at(name, shape, dtype,
                                         offset=self.base + off_bytes)
        return h


DEC_NW = 1024


def build_decode_nc():
    from contextlib import ExitStack
    nc = bass.Bass("TRN2", target_bir_lowering=False, debug=False)
    pk_g = nc.dram_tensor("pk_g", [128, MLP], I32, kind="ExternalInput").ap()
    se_g = nc.dram_tensor("se_g", [128, MLP], F32, kind="ExternalInput").ap()
    pk_u = nc.dram_tensor("pk_u", [128, MLP], I32, kind="ExternalInput").ap()
    se_u = nc.dram_tensor("se_u", [128, MLP], F32, kind="ExternalInput").ap()
    pk_d = nc.dram_tensor("pk_d", [1024, E], I32, kind="ExternalInput").ap()
    se_d = nc.dram_tensor("se_d", [1024, E], F32, kind="ExternalInput").ap()
    wg_dec = nc.dram_tensor("wg_dec", [256, MLP], BF16, kind="ExternalOutput").ap()
    wu_dec = nc.dram_tensor("wu_dec", [256, MLP], BF16, kind="ExternalOutput").ap()
    wd_dec = nc.dram_tensor("wd_dec", [2048, E], BF16, kind="ExternalOutput").ap()

    jobs = []
    for pk, se, out in ((pk_g, se_g, wg_dec), (pk_u, se_u, wu_dec),
                        (pk_d, se_d, wd_dec)):
        nchunks = pk.shape[0] // 128
        nwin = pk.shape[1] // DEC_NW
        for q in range(nchunks):
            for wi in range(nwin):
                cs = slice(wi * DEC_NW, (wi + 1) * DEC_NW)
                jobs.append((pk[q * 128:(q + 1) * 128, cs],
                             se[q * 128:(q + 1) * 128, cs],
                             out[q * 256: q * 256 + 128, cs],
                             out[q * 256 + 128: q * 256 + 256, cs]))
    n = len(jobs)
    NW = DEC_NW

    with ExitStack() as ctx:
        e = ctx.enter_context
        w_i = [e(nc.sbuf_tensor(f"w_i{p}", [128, NW], I32)) for p in range(2)]
        se_s = [e(nc.sbuf_tensor(f"se_s{p}", [128, NW], F32)) for p in range(2)]
        hs = [e(nc.sbuf_tensor(f"hs{p}", [128, NW], F32)) for p in range(2)]
        ci = [[e(nc.sbuf_tensor(f"ci{p}{pl}", [128, NW], I32)) for pl in range(2)]
              for p in range(2)]
        cb = [e(nc.sbuf_tensor(f"cb{p}", [128, NW], BF16)) for p in range(2)]
        gb = [e(nc.sbuf_tensor(f"gb{p}", [128, NW], BF16)) for p in range(2)]
        g8 = [e(nc.sbuf_tensor(f"g8{p}", [128, NW], BF16)) for p in range(2)]
        rr = [[e(nc.sbuf_tensor(f"rr{p}{pl}", [128, NW], BF16)) for pl in range(2)]
              for p in range(2)]
        ss = [[e(nc.sbuf_tensor(f"ss{p}{pl}", [128, NW], BF16)) for pl in range(2)]
              for p in range(2)]
        u1 = [[e(nc.sbuf_tensor(f"u1{p}{pl}", [128, NW], BF16)) for pl in range(2)]
              for p in range(2)]
        u2 = [[e(nc.sbuf_tensor(f"u2{p}{pl}", [128, NW], BF16)) for pl in range(2)]
              for p in range(2)]
        t1 = [[e(nc.sbuf_tensor(f"t1{p}{pl}", [128, NW], BF16)) for pl in range(2)]
              for p in range(2)]
        ob = [[e(nc.sbuf_tensor(f"ob{p}{pl}", [128, NW], BF16)) for pl in range(2)]
              for p in range(2)]
        bias4 = e(nc.sbuf_tensor("bias4", [128, 1], F32))
        bias12 = e(nc.sbuf_tensor("bias12", [128, 1], F32))

        S = Sched(nc)
        for s in ("ld0", "ld1", "st0", "st1", "dv", "ac", "gp"):
            S.new_sem(ctx, s)

        gp_bias = S.op("gpsimd", lambda g: g.memset(bias4[:], -4.0))
        gp_bias = S.op("gpsimd", lambda g: g.memset(bias12[:], -12.0))

        fin = {}
        rrv = {}
        uuv = {}
        obv = {}
        stv = {}
        ldv = {}

        for j in range(min(2, n)):
            S.op("sync", lambda s, j=j: s.dma_start(
                out=w_i[j % 2][:], in_=jobs[j][0]), inc=f"ld{j % 2}", inc_n=16)
            ldv[j] = S.op("sync", lambda s, j=j: s.dma_start(
                out=se_s[j % 2][:], in_=jobs[j][1]), inc=f"ld{j % 2}", inc_n=16)

        for i in range(n):
            p = i % 2
            S.wait("vector", f"ld{p}", ldv[i])
            S.op("vector", lambda v, p=p: v.tensor_scalar(
                ci[p][0][:], w_i[p][:], 15, None, ALU.bitwise_and))
            S.op("vector", lambda v, p=p: v.tensor_scalar(
                ci[p][1][:], w_i[p][:], 4, 15,
                ALU.logical_shift_right, ALU.bitwise_and))
            fin[i] = S.op("vector", lambda v, p=p: v.tensor_scalar(
                hs[p][:], se_s[p][:], 0.5, None, ALU.mult))
            for pl in range(2):
                if i >= 2:
                    S.wait("vector", "ac", uuv[(i - 2, pl)])
                S.op("vector", lambda v, p=p, pl=pl: v.tensor_copy(
                    cb[p][:], ci[p][pl][:]))
                S.op("vector", lambda v, p=p: v.tensor_scalar(
                    gb[p][:], cb[p][:], 7.5, None, ALU.is_ge))
                S.op("vector", lambda v, p=p: v.tensor_scalar(
                    g8[p][:], gb[p][:], 8.0, None, ALU.mult))
                S.op("vector", lambda v, p=p, pl=pl: v.tensor_scalar(
                    ss[p][pl][:], gb[p][:], -2.0, 1.0, ALU.mult, ALU.add))
                rrv[(i, pl)] = S.op("vector", lambda v, p=p, pl=pl: v.tensor_tensor(
                    rr[p][pl][:], cb[p][:], g8[p][:], ALU.subtract))
            for pl in range(2):
                S.wait("scalar", "dv", rrv[(i, pl)])
                if pl == 0 and i == 0:
                    S.wait("scalar", "gp", gp_bias)
                if i >= 2:
                    S.wait("scalar", "dv", obv[(i - 2, pl)])
                S.op("scalar", lambda s, p=p, pl=pl: s.activation(
                    u1[p][pl][:], rr[p][pl][:], AF.Relu, bias=bias4[:]))
                uuv[(i, pl)] = S.op("scalar", lambda s, p=p, pl=pl: s.activation(
                    u2[p][pl][:], rr[p][pl][:], AF.Relu, bias=bias12[:],
                    scale=2.0))
            for pl in range(2):
                S.wait("vector", "ac", uuv[(i, pl)])
                if i >= 2:
                    S.wait("vector", f"st{p}", stv[i - 2])
                S.op("vector", lambda v, p=p, pl=pl: v.tensor_tensor(
                    t1[p][pl][:], rr[p][pl][:], u1[p][pl][:], ALU.add))
                S.op("vector", lambda v, p=p, pl=pl: v.tensor_tensor(
                    cb[p][:], t1[p][pl][:], u2[p][pl][:], ALU.add))
                S.op("vector", lambda v, p=p, pl=pl: v.tensor_tensor(
                    t1[p][pl][:], cb[p][:], ss[p][pl][:], ALU.mult))
                obv[(i, pl)] = S.op("vector", lambda v, p=p, pl=pl: v.tensor_tensor(
                    ob[p][pl][:], t1[p][pl][:], hs[p][:], ALU.mult))
            S.wait("sync", "dv", obv[(i, 0)])
            S.op("sync", lambda s, i=i, p=p: s.dma_start(
                out=jobs[i][2], in_=ob[p][0][:]), inc=f"st{p}", inc_n=16)
            S.wait("sync", "dv", obv[(i, 1)])
            stv[i] = S.op("sync", lambda s, i=i, p=p: s.dma_start(
                out=jobs[i][3], in_=ob[p][1][:]), inc=f"st{p}", inc_n=16)
            if i + 2 < n:
                S.wait("sync", "dv", fin[i])
                S.op("sync", lambda s, i=i, p=p: s.dma_start(
                    out=w_i[p][:], in_=jobs[i + 2][0]), inc=f"ld{p}", inc_n=16)
                ldv[i + 2] = S.op("sync", lambda s, i=i, p=p: s.dma_start(
                    out=se_s[p][:], in_=jobs[i + 2][1]), inc=f"ld{p}", inc_n=16)
        S.wait("sync", "st0", S.cur("st0"))
        S.wait("sync", "st1", S.cur("st1"))

        block = ctx.enter_context(nc.Block())
        S.run(block)
    return nc


def build_attn_v0(dbg=False, dbg2=False):
    from contextlib import ExitStack
    nc = bass.Bass("TRN2", target_bir_lowering=False, debug=False)
    if dbg2:
        d_exp = nc.dram_tensor("d_exp", [128, 1024], BF16, kind="ExternalOutput").ap()
        d_pn = nc.dram_tensor("d_pn", [128, 1024], BF16, kind="ExternalOutput").ap()
        d_nrm = nc.dram_tensor("d_nrm", [128, 1024], BF16, kind="ExternalOutput").ap()
        d_pt = nc.dram_tensor("d_pt", [128, 4096], BF16, kind="ExternalOutput").ap()
        d_sm = nc.dram_tensor("d_sm", [128, 32], F32, kind="ExternalOutput").ap()
    if dbg:
        d_q = nc.dram_tensor("d_q", [128, NH * TOWN], BF16, kind="ExternalOutput").ap()
        d_k = nc.dram_tensor("d_k", [128, NH * TLOC], BF16, kind="ExternalOutput").ap()
        d_v = nc.dram_tensor("d_v", [128, 12 * E], BF16, kind="ExternalOutput").ap()
        d_at = nc.dram_tensor("d_at", [128, 2 * NH * 512], BF16, kind="ExternalOutput").ap()
    xt_in = nc.dram_tensor("xt_in", [E, TLOC], F32, kind="ExternalInput").ap()
    x_own = nc.dram_tensor("x_own", [TOWN, E], F32, kind="ExternalInput").ap()
    wqkvT = nc.dram_tensor("wqkvT", [E, 3 * E], F32, kind="ExternalInput").ap()
    woutT = nc.dram_tensor("woutT", [E, E], F32, kind="ExternalInput").ap()
    mask_in = nc.dram_tensor("mask_in", [128, 8 * 1024], BF16,
                             kind="ExternalInput").ap()
    x1_out = nc.dram_tensor("x1_out", [TOWN, E], F32, kind="ExternalOutput").ap()

    with ExitStack() as ctx:
        ar = Arena(nc, 204 * KB)
        kT = ar.at("kT", 0, [128, NH * TLOC], BF16)
        vv = ar.at("vv", 48 * KB, [128, 12 * E], BF16)
        qT = ar.at("qT", 96 * KB, [128, NH * TOWN], BF16)
        xt = ar.at("xt", 128 * KB, [128, 16 * TLOC], BF16)
        wq_sl = [ar.at(f"wq{p}", (176 + 4 * p) * KB, [128, 16 * 128], BF16)
                 for p in range(2)]
        wv_sl = ar.at("wv", 184 * KB, [128, 16 * 512], BF16)
        masks = ar.at("masks", 128 * KB, [128, 8 * 1024], BF16)
        probs = [ar.at(f"pb{p}", (144 + 2 * p) * KB, [128, 1024], BF16)
                 for p in range(2)]
        probsn = ar.at("pn", 148 * KB, [128, 1024], BF16)
        probsT = ar.at("probsT", 150 * KB, [128, 8 * 512], BF16)
        attnT = ar.at("attnT", 158 * KB, [128, 2 * NH * 512], BF16)
        wo = ar.at("wo", 0, [128, 16 * E], BF16)
        xo = ar.at("xo", 64 * KB, [128, E], F32)
        resid = ar.at("resid", 72 * KB, [128, E], F32)
        x1til = ar.at("x1til", 80 * KB, [128, E], F32)
        sqt = ar.at("sqt", 88 * KB, [128, E], F32)
        sm = ar.at("sm", 200 * KB, [128, 32], F32)
        ident = ar.at("ident", 201 * KB, [128, 128], BF16)
        shiftB = ar.at("shiftB", int(201.5 * KB), [128, 1], F32)
        trash = ar.at("trash", 202 * KB, [128, 1024], BF16)

        S = Sched(nc)
        for s in ("s_xt", "s_wq0", "s_wq1", "s_wv", "s_wo",
                  "sld", "pe", "dv", "ac", "gp", "st"):
            S.new_sem(ctx, s)
        ps_S0 = ctx.enter_context(nc.psum_tensor("S0", [128, 1024], F32))
        ps_S1 = ctx.enter_context(nc.psum_tensor("S1", [128, 1024], F32))
        ps_TP = [ctx.enter_context(nc.psum_tensor(f"TP{i}", [128, 1024], BF16))
                 for i in range(2)]
        ps_AV = ctx.enter_context(nc.psum_tensor("AVP", [128, 512], F32))
        ps_OP = ctx.enter_context(nc.psum_tensor("OPP", [128, 512], F32))
        surfs = [ps_S0[:, 0:512], ps_S0[:, 512:1024],
                 ps_S1[:, 0:512], ps_S1[:, 512:1024]]

        m1 = S.op("gpsimd", lambda g: g.memset(ident[:], 0.0), inc="gp")
        S.wait("gpsimd", "gp", m1)
        id_done = S.op("gpsimd", lambda g: g.affine_select(
            out=ident[:], in_=ident[:], compare_op=ALU.not_equal, fill=1.0,
            base=0, pattern=[[-1, 128]], channel_multiplier=1), inc="gp")
        c_init = S.op("gpsimd", lambda g: g.memset(shiftB[:], EXP_SHIFT),
                      inc="gp")

        xt_src = xt_in.rearrange("(c p) t -> p c t", p=128)
        xt_ld = S.op("gpsimd", lambda g: g.dma_start(
            out=xt[:].rearrange("p (c t) -> p c t", t=TLOC), in_=xt_src),
            inc="s_xt", inc_n=16)

        wsrc = wqkvT.rearrange("(c p) f -> p c f", p=128)
        group_done_dv = []
        group_idx = 0
        slab_pe_after = {}
        pe_val = None

        qk_jobs = []
        for ft in range(32):
            wins = ((512, 0), (1024, 512)) if ft < 16 else \
                   ((0, 0), (512, 512), (1024, 1024))
            qk_jobs.append((ft, wins))

        for ji, (ft, wins) in enumerate(qk_jobs):
            p = ft % 2
            if ji >= 2:
                S.wait("gpsimd", "pe", slab_pe_after[qk_jobs[ji - 2][0]])
            fsl = slice(ft * 128, (ft + 1) * 128)
            gld_slab = S.op("gpsimd", lambda g, p=p, fsl=fsl: g.dma_start(
                out=wq_sl[p][:].rearrange("p (c f) -> p c f", f=128),
                in_=wsrc[:, :, fsl]), inc=f"s_wq{p}", inc_n=16)
            for off, qoff in wins:
                surf = surfs[group_idx % 4]
                if ji == 0:
                    S.wait("tensor", "s_xt", xt_ld)
                S.wait("tensor", f"s_wq{p}", gld_slab)
                if group_idx >= 4:
                    S.wait("tensor", "dv", group_done_dv[group_idx - 4])
                for ec in range(16):
                    def mm(t, surf=surf, p=p, ec=ec, off=off):
                        return t.matmul(
                            surf, wq_sl[p][:, ec * 128:(ec + 1) * 128],
                            xt[:, ec * TLOC + off: ec * TLOC + off + 512],
                            start=(ec == 0), stop=(ec == 15))
                    pe_val = S.op("tensor", mm, inc="pe") if ec == 15 \
                        else S.op("tensor", mm)
                S.wait("vector", "pe", S.cur("pe"))
                if ft < 16:
                    dst = qT[:, ft * TOWN + qoff: ft * TOWN + qoff + 512]
                    dvv = S.op("vector", lambda v, dst=dst, surf=surf:
                               v.tensor_scalar(dst, surf, QK_SCALE, None,
                                               ALU.mult), inc="dv")
                else:
                    dst = kT[:, (ft - 16) * TLOC + off:
                             (ft - 16) * TLOC + off + 512]
                    dvv = S.op("vector", lambda v, dst=dst, surf=surf:
                               v.tensor_copy(dst, surf), inc="dv")
                group_done_dv.append(dvv)
                group_idx += 1
            slab_pe_after[ft] = S.cur("pe")

        v_pe_after = {}
        for fw in range(4):
            if fw >= 1:
                S.wait("gpsimd", "pe", v_pe_after[fw - 1])
            fsl = slice(2 * E + fw * 512, 2 * E + (fw + 1) * 512)
            gld_v = S.op("gpsimd", lambda g, fsl=fsl: g.dma_start(
                out=wv_sl[:].rearrange("p (c f) -> p c f", f=512),
                in_=wsrc[:, :, fsl]), inc="s_wv", inc_n=16)
            for tt in range(12):
                surf = surfs[group_idx % 4]
                S.wait("tensor", "s_wv", gld_v)
                if group_idx >= 4:
                    S.wait("tensor", "dv", group_done_dv[group_idx - 4])
                for ec in range(16):
                    def mmv(t, surf=surf, ec=ec, tt=tt):
                        return t.matmul(
                            surf,
                            xt[:, ec * TLOC + tt * 128: ec * TLOC + (tt + 1) * 128],
                            wv_sl[:, ec * 512:(ec + 1) * 512],
                            start=(ec == 0), stop=(ec == 15))
                    S.op("tensor", mmv, inc="pe" if ec == 15 else None)
                S.wait("vector", "pe", S.cur("pe"))
                dst = vv[:, tt * E + fw * 512: tt * E + (fw + 1) * 512]
                dvv = S.op("vector", lambda v, dst=dst, surf=surf:
                           v.tensor_copy(dst, surf), inc="dv")
                group_done_dv.append(dvv)
                group_idx += 1
            v_pe_after[fw] = S.cur("pe")

        qkv_done_dv = group_done_dv[-1]
        qkv_done_pe = S.cur("pe")

        S.wait("sync", "pe", qkv_done_pe)
        if dbg:
            S.wait("sync", "dv", qkv_done_dv)
            S.op("sync", lambda s: s.dma_start(out=d_q, in_=qT[:]),
                 inc="st", inc_n=16)
            S.op("sync", lambda s: s.dma_start(out=d_k, in_=kT[:]),
                 inc="st", inc_n=16)
            S.op("sync", lambda s: s.dma_start(out=d_v, in_=vv[:]),
                 inc="st", inc_n=16)
            S.sems["st"][1] += 0
        ld_mask = S.op("sync", lambda s: s.dma_start(out=masks[:], in_=mask_in),
                       inc="sld", inc_n=16)

        S.wait("scalar", "gp", c_init)
        S.wait("tensor", "gp", id_done)
        S.wait("vector", "sld", ld_mask)

        exp_ac, tr_pe, ptk_dv, av_pe, att_dv, sum_gp = {}, {}, {}, {}, {}, {}
        dbg_gates = {}
        last_scores_pe = None
        chs = [(c, h) for h in range(15, -1, -1) for c in range(2)]
        for ci_, (c, h) in enumerate(chs):
            for k in range(4):
                Sk = ps_S0 if k % 2 == 0 else ps_S1
                if ci_ == 0 and k == 0:
                    S.wait("tensor", "dv", qkv_done_dv)
                prev = (c, h, k - 2) if k >= 2 else \
                    (chs[ci_ - 1] + (k + 2,) if ci_ > 0 else None)
                if prev in exp_ac:
                    S.wait("tensor", "ac", exp_ac[prev])
                q0 = h * TOWN + c * 512 + k * 128
                k0 = h * TLOC + c * 512
                S.op("tensor", lambda t, Sk=Sk, q0=q0, k0=k0: t.matmul(
                    Sk[:, 0:512], qT[:, q0:q0 + 128], kT[:, k0:k0 + 512],
                    start=True, stop=True))
                pe_sc = S.op("tensor", lambda t, Sk=Sk, q0=q0, k0=k0: t.matmul(
                    Sk[:, 512:1024], qT[:, q0:q0 + 128],
                    kT[:, k0 + 512:k0 + 1024], start=True, stop=True), inc="pe")
                last_scores_pe = pe_sc
                S.wait("scalar", "pe", pe_sc)
                if prev in tr_pe:
                    S.wait("scalar", "pe", tr_pe[prev])
                if dbg2 and (c, h, k) == (1, 0, 2):
                    S.wait("scalar", "st", dbg_gates["nrm"])
                exp_ac[(c, h, k)] = S.op("scalar", lambda s, Sk=Sk, k=k:
                                         s.activation(probs[k % 2][:], Sk[:],
                                                      AF.Exp, bias=shiftB[:]),
                                         inc="ac")
                if dbg2 and (c, h, k) == (1, 0, 0):
                    S.wait("sync", "ac", exp_ac[(c, h, k)])
                    dbg_gates["exp"] = S.op("sync", lambda s: s.dma_start(
                        out=d_exp, in_=probs[0][:]), inc="st", inc_n=16)
                S.wait("vector", "ac", exp_ac[(c, h, k)])
                if dbg2 and (c, h, k) == (1, 0, 1):
                    S.wait("vector", "st", dbg_gates["pn"])
                pk_prev = (c, h, k - 1) if k >= 1 else \
                    (chs[ci_ - 1] + (3,) if ci_ > 0 else None)
                if pk_prev in sum_gp:
                    S.wait("vector", "ac", sum_gp[pk_prev])
                mview = masks[:, (c * 4 + k) * 1024:(c * 4 + k + 1) * 1024]
                pm = S.op("vector", lambda v, k=k, mview=mview: v.tensor_tensor(
                    probsn[:], probs[k % 2][:], mview, ALU.mult), inc="dv")
                if dbg2 and (c, h, k) == (1, 0, 0):
                    S.wait("sync", "dv", pm)
                    dbg_gates["pn"] = S.op("sync", lambda s: s.dma_start(
                        out=d_pn, in_=probsn[:]), inc="st", inc_n=16)
                S.wait("scalar", "dv", pm)
                sum_gp[(c, h, k)] = S.op("scalar", lambda s, k=k: s.activation(
                    trash[:], probsn[:], AF.Identity,
                    accum_out=sm[:, k:k + 1]), inc="ac")
                S.wait("vector", "ac", sum_gp[(c, h, k)])
                S.op("vector", lambda v, k=k: v.reciprocal(sm[:, 4 + k:5 + k],
                                                           sm[:, k:k + 1]))
                if dbg2 and (c, h, k) == (1, 0, 0):
                    S.wait("vector", "st", dbg_gates["exp"])
                nrm = S.op("vector", lambda v, k=k: v.tensor_scalar(
                    probs[k % 2][:], probsn[:], sm[:, 4 + k:5 + k], None,
                    ALU.mult), inc="dv")
                if dbg2 and (c, h, k) == (1, 0, 0):
                    S.wait("sync", "dv", nrm)
                    S.op("sync", lambda s: s.dma_start(out=d_sm, in_=sm[:]),
                         inc="st", inc_n=16)
                    dbg_gates["nrm"] = S.op("sync", lambda s: s.dma_start(
                        out=d_nrm, in_=probs[0][:]), inc="st", inc_n=16)
                S.wait("tensor", "dv", nrm)
                if prev in ptk_dv:
                    S.wait("tensor", "dv", ptk_dv[prev])
                for m in range(8):
                    def trp(t, k=k, m=m):
                        return t.transpose(
                            ps_TP[k % 2][:, m * 128:(m + 1) * 128],
                            probs[k % 2][:, m * 128:(m + 1) * 128], ident[:])
                    if m == 7:
                        tr_pe[(c, h, k)] = S.op("tensor", trp, inc="pe")
                    else:
                        S.op("tensor", trp)
                S.wait("vector", "pe", tr_pe[(c, h, k)])
                if ci_ > 0 and k == 0:
                    S.wait("vector", "pe", av_pe[chs[ci_ - 1]])
                if dbg2 and (c, h, k) == (1, 1, 0):
                    S.wait("vector", "st", dbg_gates["pt"])
                def evt(v, k=k):
                    src = ps_TP[k % 2][:].rearrange("p (m f) -> p m f", f=128)
                    dst = probsT[:].rearrange("p (m f) -> p m f", f=512)[
                        :, :, k * 128:(k + 1) * 128]
                    return v.tensor_copy(dst, src)
                ptk_dv[(c, h, k)] = S.op("vector", evt, inc="dv")
            if dbg2 and (c, h) == (1, 0):
                S.wait("sync", "dv", ptk_dv[(c, h, 3)])
                dbg_gates["pt"] = S.op("sync", lambda s: s.dma_start(
                    out=d_pt, in_=probsT[:]), inc="st", inc_n=16)
            S.wait("tensor", "dv", ptk_dv[(c, h, 3)])
            if ci_ > 0:
                S.wait("tensor", "dv", att_dv[chs[ci_ - 1]])
            for m in range(8):
                def av(t, c=c, h=h, m=m):
                    return t.matmul(
                        ps_AV[:],
                        vv[:, (4 * c + m) * E + h * 128:
                           (4 * c + m) * E + (h + 1) * 128],
                        probsT[:, m * 512:(m + 1) * 512],
                        start=(m == 0), stop=(m == 7))
                if m == 7:
                    av_pe[(c, h)] = S.op("tensor", av, inc="pe")
                else:
                    S.op("tensor", av)
            S.wait("vector", "pe", av_pe[(c, h)])
            dstA = attnT[:, (c * 16 + h) * 512:(c * 16 + h + 1) * 512]
            att_dv[(c, h)] = S.op("vector", lambda v, dstA=dstA: v.tensor_copy(
                dstA, ps_AV[:]), inc="dv")

        S.wait("gpsimd", "pe", av_pe[chs[-1]])
        wo_src = woutT.rearrange("(c p) f -> p c f", p=128)
        ld_wo = S.op("gpsimd", lambda g: g.dma_start(
            out=wo[:].rearrange("p (c f) -> p c f", f=E), in_=wo_src),
            inc="s_wo", inc_n=16)

        S.wait("sync", "dv", att_dv[chs[-1]])
        op_dv = None
        ln_sq_ac = {}
        st_val = 0
        for tt in range(8):
            c = tt // 4
            if tt >= 1:
                S.wait("sync", "ac", ln_sq_ac[tt - 1])
            xo_ld = S.op("sync", lambda s, tt=tt: s.dma_start(
                out=xo[:], in_=x_own[tt * 128:(tt + 1) * 128, :]),
                inc="sld", inc_n=16)
            for fw in range(4):
                if (tt, fw) == (0, 0):
                    S.wait("tensor", "s_wo", ld_wo)
                    S.wait("tensor", "dv", att_dv[chs[-1]])
                else:
                    S.wait("tensor", "dv", op_dv)
                for ec in range(16):
                    def mo(t, c=c, tt=tt, fw=fw, ec=ec):
                        lhs = attnT[:, (c * 16 + ec) * 512 + (tt % 4) * 128:
                                    (c * 16 + ec) * 512 + (tt % 4) * 128 + 128]
                        rhs = wo[:, ec * E + fw * 512: ec * E + fw * 512 + 512]
                        return t.matmul(ps_OP[:], lhs, rhs,
                                        start=(ec == 0), stop=(ec == 15))
                    S.op("tensor", mo, inc="pe" if ec == 15 else None)
                pe_op = S.cur("pe")
                S.wait("vector", "pe", pe_op)
                S.wait("vector", "sld", xo_ld)
                def rp(v, fw=fw):
                    return v.tensor_tensor(resid[:, fw * 512:(fw + 1) * 512],
                                           ps_OP[:],
                                           xo[:, fw * 512:(fw + 1) * 512],
                                           ALU.add)
                op_dv = S.op("vector", rp, inc="dv")
            S.op("vector", lambda v: v.reduce_sum(sm[:, 8:9], resid[:], AX.X,
                                                  negate=True))
            nm_dv = S.op("vector", lambda v: v.tensor_scalar(
                sm[:, 9:10], sm[:, 8:9], 1.0 / E, None, ALU.mult), inc="dv")
            S.wait("scalar", "dv", nm_dv)
            ln_sq_ac[tt] = S.op("scalar", lambda s: s.activation(
                sqt[:], resid[:], AF.Square, accum_out=sm[:, 10:11]), inc="ac")
            S.wait("vector", "ac", ln_sq_ac[tt])
            S.op("vector", lambda v: v.tensor_scalar(
                sm[:, 11:12], sm[:, 10:11], 1.0 / E, None, ALU.mult))
            S.op("vector", lambda v: v.tensor_tensor(
                sm[:, 12:13], sm[:, 9:10], sm[:, 9:10], ALU.mult))
            S.op("vector", lambda v: v.tensor_tensor(
                sm[:, 13:14], sm[:, 11:12], sm[:, 12:13], ALU.subtract))
            S.op("vector", lambda v: v.tensor_scalar(
                sm[:, 14:15], sm[:, 13:14], EPS, None, ALU.add))
            rv_dv = S.op("vector", lambda v: v.reciprocal(
                sm[:, 15:16], sm[:, 14:15]), inc="dv")
            S.wait("scalar", "dv", rv_dv)
            rstd_ac = S.op("scalar", lambda s: s.activation(
                sm[:, 16:17], sm[:, 15:16], AF.Sqrt), inc="ac")
            S.wait("vector", "ac", rstd_ac)
            S.wait("vector", "st", st_val)
            x1_dv = S.op("vector", lambda v: v.tensor_scalar(
                x1til[:], resid[:], sm[:, 9:10], sm[:, 16:17],
                ALU.add, ALU.mult), inc="dv")
            S.wait("sync", "dv", x1_dv)
            st_val = S.op("sync", lambda s, tt=tt: s.dma_start(
                out=x1_out[tt * 128:(tt + 1) * 128, :], in_=x1til[:]),
                inc="st", inc_n=16)
        if dbg:
            S.op("sync", lambda s: s.dma_start(out=d_at, in_=attnT[:]),
                 inc="st", inc_n=16)
        S.wait("sync", "st", S.cur("st"))

        block = ctx.enter_context(nc.Block())
        S.run(block)
    return nc


def build_ffn_v0(skip_f2=False, skip_down=False, skip_pool=False,
                 skip_gelu=False, n_groups=16):
    from contextlib import ExitStack
    nc = bass.Bass("TRN2", target_bir_lowering=False, debug=False)
    x1_in = nc.dram_tensor("x1_in", [TOWN, E], F32, kind="ExternalInput").ap()
    wg_all = nc.dram_tensor("wg_all", [E, MLP], BF16, kind="ExternalInput").ap()
    wu_all = nc.dram_tensor("wu_all", [E, MLP], BF16, kind="ExternalInput").ap()
    wd_all = nc.dram_tensor("wd_all", [MLP, E], BF16, kind="ExternalInput").ap()
    y_out = nc.dram_tensor("y_out", [TOWN, E], F32, kind="ExternalOutput").ap()
    acc_d = nc.dram_tensor("acc_d", [TOWN, E], F32).ap()

    C1 = 0.7978845608028654
    C2 = C1 * 0.044715

    with ExitStack() as ctx:
        ar = Arena(nc, 200 * KB)
        x1T = ar.at("x1T", 0, [128, 16 * TOWN], BF16)
        interT = ar.at("interT", 32 * KB, [128, 8 * TOWN], BF16)
        wg_sl = [ar.at(f"wg{p}", (48 + 16 * p) * KB, [128, 16 * 512], BF16)
                 for p in range(2)]
        wu_sl = [ar.at(f"wu{p}", (80 + 16 * p) * KB, [128, 16 * 512], BF16)
                 for p in range(2)]
        wd_sl = [ar.at(f"wd{p}", (112 + 32 * p) * KB, [128, 8 * E], BF16)
                 for p in range(2)]
        gt = [[ar.at(f"gt{p}{i}", (176 + 8 * p + i) * KB, [128, 512], BF16)
               for i in range(8)] for p in range(2)]
        accv = [ar.at(f"accv{p}", (192 + 2 * p) * KB, [128, 512], F32)
                for p in range(2)]
        xo1 = ar.at("xo1", 112 * KB, [128, E], F32)
        x1b = ar.at("x1b", 120 * KB, [128, E], BF16)
        resid = ar.at("resid3", 112 * KB, [128, E], F32)
        accr = ar.at("accr", 120 * KB, [128, E], F32)
        ytil = ar.at("ytil", 128 * KB, [128, E], F32)
        sqf = ar.at("sqf", 136 * KB, [128, E], F32)
        xo3 = ar.at("xo3", 144 * KB, [128, E], F32)
        sm = ar.at("sm3", 196 * KB, [128, 32], F32)
        ident = ar.at("ident3", 197 * KB, [128, 128], BF16)

        S = Sched(nc)
        for s in ("s_x1", "s_wd0", "s_wd1", "s_gu0", "s_gu1", "s_f3",
                  "ga0", "ga1", "pe", "dv", "ac", "gp", "st"):
            S.new_sem(ctx, s)
        ps_S0 = ctx.enter_context(nc.psum_tensor("S0", [128, 1024], F32))
        ps_S1 = ctx.enter_context(nc.psum_tensor("S1", [128, 1024], F32))
        ps_TP = [ctx.enter_context(nc.psum_tensor(f"TP{i}", [128, 1024], BF16))
                 for i in range(2)]
        ps_D = [ctx.enter_context(nc.psum_tensor(f"D{i}", [128, 512], F32))
                for i in range(2)]
        gsurf = [ps_S0[:, 0:512], ps_S0[:, 512:1024]]
        usurf = [ps_S1[:, 0:512], ps_S1[:, 512:1024]]

        m1 = S.op("gpsimd", lambda g: g.memset(ident[:], 0.0), inc="gp")
        S.wait("gpsimd", "gp", m1)
        id_gp = S.op("gpsimd", lambda g: g.affine_select(
            out=ident[:], in_=ident[:], compare_op=ALU.not_equal, fill=1.0,
            base=0, pattern=[[-1, 128]], channel_multiplier=1), inc="gp")

        S.wait("tensor", "gp", id_gp)
        f1_evac = None
        cast_dv = {}
        for tt in range(8):
            if tt >= 1:
                S.wait("sync", "dv", cast_dv[tt - 1])
            xo_ld = S.op("sync", lambda s, tt=tt: s.dma_start(
                out=xo1[:], in_=x1_in[tt * 128:(tt + 1) * 128, :]),
                inc="s_x1", inc_n=16)
            S.wait("vector", "s_x1", xo_ld)
            if tt >= 1:
                S.wait("vector", "pe", tp_pe)
            cast_dv[tt] = S.op("vector", lambda v: v.tensor_copy(x1b[:], xo1[:]),
                               inc="dv")
            S.wait("tensor", "dv", cast_dv[tt])
            if tt >= 1:
                S.wait("tensor", "dv", f1_evac)
            for hc in range(16):
                def tr1(t, hc=hc):
                    base = 256 * (hc // 2) + (hc % 2)
                    src = x1b[:, base: base + 255: 2]
                    return t.transpose(
                        ps_TP[hc // 8][:, (hc % 8) * 128:(hc % 8 + 1) * 128],
                        src, ident[:])
                S.op("tensor", tr1, inc="pe" if hc == 15 else None)
            tp_pe = S.cur("pe")
            S.wait("vector", "pe", tp_pe)
            for half in range(2):
                def ev1(v, tt=tt, half=half):
                    src = ps_TP[half][:].rearrange("p (m f) -> p m f", f=128)
                    dst = x1T[:].rearrange("p (c t) -> p c t", t=TOWN)[
                        :, half * 8:(half + 1) * 8, tt * 128:(tt + 1) * 128]
                    return v.tensor_copy(dst, src)
                f1_evac = S.op("vector", ev1, inc="dv")
        f1_done_dv = f1_evac

        wgsrc = wg_all.rearrange("(c p) n -> p c n", p=128)
        wusrc = wu_all.rearrange("(c p) n -> p c n", p=128)
        g_idx = 0
        gelu_dv = []
        gelu_free = []
        ac_th_l = []
        slab_sld = {}
        slab_pe_done = {}
        wd_sld = {}
        down_pe_G = {}
        ga_vals = []
        accv_dv = []
        inter_done = {}

        for G in range(n_groups):
            if skip_f2:
                break
            if G < 2:
                S.wait("sync", "dv", f1_done_dv)
            elif not skip_down:
                S.wait("sync", "pe", down_pe_G[G - 2])
            lds = []
            if not skip_down:
                for c_l in range(8):
                    row0 = (8 * G + c_l) * 128
                    def ldwd(s, row0=row0, G=G, c_l=c_l):
                        return s.dma_start(
                            out=wd_sl[G % 2][:, c_l * E:(c_l + 1) * E],
                            in_=wd_all[row0:row0 + 128, :])
                    lds.append(S.op("sync", ldwd, inc=f"s_wd{G % 2}", inc_n=16))
                wd_sld[G] = lds[-1]
            for sw in range(2):
                if G >= 1:
                    S.wait("sync", "pe", slab_pe_done[(G - 1, sw)])
                nsl = slice(1024 * G + 512 * sw, 1024 * G + 512 * (sw + 1))
                S.op("sync", lambda s, sw=sw, nsl=nsl: s.dma_start(
                    out=wg_sl[sw][:].rearrange("p (c n) -> p c n", n=512),
                    in_=wgsrc[:, :, nsl]), inc=f"s_gu{sw}", inc_n=16)
                slab_sld[(G, sw)] = S.op("sync", lambda s, sw=sw, nsl=nsl:
                                         s.dma_start(
                    out=wu_sl[sw][:].rearrange("p (c n) -> p c n", n=512),
                    in_=wusrc[:, :, nsl]), inc=f"s_gu{sw}", inc_n=16)
            for c_l in range(8):
                sw = c_l // 4
                woff = 256 * ((c_l % 4) // 2)
                sub = c_l % 2
                for tw in range(2):
                    gs = gsurf[g_idx % 2]
                    us = usurf[g_idx % 2]
                    S.wait("tensor", f"s_gu{sw}", slab_sld[(G, sw)])
                    if g_idx == 0:
                        S.wait("tensor", "dv", f1_done_dv)
                    if g_idx >= 2:
                        ac_g2, dv_ub = gelu_free[g_idx - 2]
                        S.wait("tensor", "ac", ac_g2)
                        S.wait("tensor", "dv", dv_ub)
                    for wgt, surf in ((wg_sl[sw], gs), (wu_sl[sw], us)):
                        for hc in range(16):
                            def mmg(t, wgt=wgt, surf=surf, hc=hc, woff=woff,
                                    sub=sub, tw=tw):
                                lhs = wgt[:, hc * 512 + woff + sub:
                                          hc * 512 + woff + sub + 255: 2]
                                rhs = x1T[:, hc * TOWN + tw * 512:
                                          hc * TOWN + tw * 512 + 512]
                                return t.matmul(surf, lhs, rhs,
                                                start=(hc == 0), stop=(hc == 15))
                            S.op("tensor", mmg,
                                 inc="pe" if hc == 15 else None)
                    pe_gu = S.cur("pe")
                    if skip_gelu:
                        S.wait("vector", "pe", pe_gu)
                        dst_i = interT[:, c_l * TOWN + tw * 512:
                                       c_l * TOWN + tw * 512 + 512]
                        dv_int = S.op("vector", lambda v, dst_i=dst_i, gs=gs:
                                      v.tensor_copy(dst_i, gs))
                        dv_ub = S.op("vector", lambda v, us=us:
                                     v.tensor_copy(gt[0][3][:], us))
                        gelu_dv.append(dv_int)
                        gelu_free.append((0, dv_ub))
                        gp_p_l.append(0); ac_th_l.append(0); gp_q_l.append(0)
                        dv_arg_l.append(dv_int)
                        slab_pe_done[(G, sw)] = pe_gu
                        g_idx += 1
                        continue
                    par = g_idx % 2
                    g2, pp, gb, ub, arg, th, gu, q8 = gt[par]
                    S.wait("vector", "pe", pe_gu)
                    dv_gb = S.op("vector", lambda v, gb=gb, gs=gs:
                                 v.tensor_copy(gb[:], gs))
                    dv_ub = S.op("vector", lambda v, ub=ub, us=us:
                                 v.tensor_copy(ub[:], us), inc="dv")
                    S.wait("scalar", "dv", dv_gb)
                    if g_idx >= 2:
                        S.wait("scalar", "gp", gp_p_l[g_idx - 2])
                        S.wait("scalar", "gp", gp_q_l[g_idx - 2])
                    ac_g2 = S.op("scalar", lambda s, g2=g2, gb=gb: s.activation(
                        g2[:], gb[:], AF.Square), inc="ac")
                    peng = "vector" if skip_pool else "gpsimd"
                    pchain = "dv" if skip_pool else "gp"
                    S.wait(peng, "ac", ac_g2)
                    if g_idx >= 2 and not skip_pool:
                        S.wait(peng, "dv", dv_arg_l[g_idx - 2])
                        S.wait(peng, "dv", gelu_dv[g_idx - 2])
                    gp_p = S.op(peng, lambda g, pp=pp, g2=g2: g.tensor_scalar(
                        pp[:], g2[:], C2, C1, ALU.mult, ALU.add), inc=pchain)
                    if not skip_pool:
                        S.wait("vector", "gp", gp_p)
                    if g_idx >= 2:
                        S.wait("vector", "ac", ac_th_l[g_idx - 2])
                    dv_arg = S.op("vector", lambda v, arg=arg, pp=pp, gb=gb:
                                  v.tensor_tensor(arg[:], pp[:], gb[:],
                                                  ALU.mult), inc="dv")
                    S.wait("scalar", "dv", dv_arg)
                    if g_idx >= 2:
                        S.wait("scalar", "gp", gp_q_l[g_idx - 2])
                    ac_th = S.op("scalar", lambda s, th=th, arg=arg: s.activation(
                        th[:], arg[:], AF.Tanh), inc="ac")
                    S.wait(peng, "ac", ac_th)
                    gp_q = S.op(peng, lambda g, q8=q8, th=th: g.tensor_scalar(
                        q8[:], th[:], 0.5, 0.5, ALU.mult, ALU.add), inc=pchain)
                    S.op("vector", lambda v, gu=gu, gb=gb, ub=ub:
                         v.tensor_tensor(gu[:], gb[:], ub[:], ALU.mult))
                    if not skip_pool:
                        S.wait("vector", "gp", gp_q)
                    dst_i = interT[:, c_l * TOWN + tw * 512:
                                   c_l * TOWN + tw * 512 + 512]
                    dv_int = S.op("vector", lambda v, dst_i=dst_i, q8=q8, gu=gu:
                                  v.tensor_tensor(dst_i, q8[:], gu[:],
                                                  ALU.mult), inc="dv")
                    gelu_dv.append(dv_int)
                    gelu_free.append((0, dv_ub))
                    gp_p_l.append(gp_p)
                    ac_th_l.append(ac_th)
                    gp_q_l.append(gp_q)
                    dv_arg_l.append(dv_arg)
                    slab_pe_done[(G, sw)] = pe_gu
                    g_idx += 1
            inter_done[G] = gelu_dv[-1]
            if skip_down:
                continue
            for tt in range(8):
                for hw in range(4):
                    di = (tt * 4 + hw) % 2
                    S.wait("tensor", "dv", inter_done[G])
                    S.wait("tensor", f"s_wd{G % 2}", wd_sld[G])
                    if (G, tt, hw) != (0, 0, 0):
                        idx = G * 32 + tt * 4 + hw
                        if idx >= 2:
                            S.wait("tensor", "dv", accv_dv[idx - 2])
                    for c_l in range(8):
                        def mmd(t, di=di, G=G, c_l=c_l, tt=tt, hw=hw):
                            lhs = interT[:, c_l * TOWN + tt * 128:
                                         c_l * TOWN + tt * 128 + 128]
                            rhs = wd_sl[G % 2][:, c_l * E + hw * 512:
                                               c_l * E + hw * 512 + 512]
                            return t.matmul(ps_D[di][:], lhs, rhs,
                                            start=(c_l == 0), stop=(c_l == 7))
                        S.op("tensor", mmd, inc="pe" if c_l == 7 else None)
                    pe_d = S.cur("pe")
                    S.wait("vector", "pe", pe_d)
                    idx = G * 32 + tt * 4 + hw
                    if idx >= 2:
                        S.wait("vector", f"ga{di}", ga_vals[idx - 2])
                    adv = S.op("vector", lambda v, di=di: v.tensor_copy(
                        accv[di][:], ps_D[di][:]), inc="dv")
                    accv_dv.append(adv)
                    S.wait("gpsimd", "dv", adv)
                    def ga_op(g, di=di, tt=tt, hw=hw, G=G):
                        dst = acc_d[tt * 128:(tt + 1) * 128,
                                    hw * 512:(hw + 1) * 512]
                        if G == 0:
                            return g.dma_start(out=dst, in_=accv[di][:])
                        return g.dma_start(out=dst, in_=accv[di][:],
                                           accum_op=ALU.add)
                    ga_vals.append(S.op("gpsimd", ga_op, inc=f"ga{di}",
                                        inc_n=16))
            down_pe_G[G] = S.cur("pe")

        S.wait("sync", "ga0", S.cur("ga0"))
        S.wait("sync", "ga1", S.cur("ga1"))
        st_val = 0
        sq_ac = {}
        for tt in range(8):
            if tt >= 1:
                S.wait("sync", "ac", sq_ac[tt - 1])
            ld_x = S.op("sync", lambda s, tt=tt: s.dma_start(
                out=xo3[:], in_=x1_in[tt * 128:(tt + 1) * 128, :]),
                inc="s_f3", inc_n=16)
            ld_a = S.op("sync", lambda s, tt=tt: s.dma_start(
                out=accr[:], in_=acc_d[tt * 128:(tt + 1) * 128, :]),
                inc="s_f3", inc_n=16)
            S.wait("vector", "s_f3", ld_a)
            S.op("vector", lambda v: v.tensor_tensor(resid[:], xo3[:], accr[:],
                                                     ALU.add))
            S.op("vector", lambda v: v.reduce_sum(sm[:, 0:1], resid[:], AX.X,
                                                  negate=True))
            nm_dv = S.op("vector", lambda v: v.tensor_scalar(
                sm[:, 1:2], sm[:, 0:1], 1.0 / E, None, ALU.mult), inc="dv")
            S.wait("scalar", "dv", nm_dv)
            sq_ac[tt] = S.op("scalar", lambda s: s.activation(
                sqf[:], resid[:], AF.Square, accum_out=sm[:, 2:3]), inc="ac")
            S.wait("vector", "ac", sq_ac[tt])
            S.op("vector", lambda v: v.tensor_scalar(
                sm[:, 3:4], sm[:, 2:3], 1.0 / E, None, ALU.mult))
            S.op("vector", lambda v: v.tensor_tensor(
                sm[:, 4:5], sm[:, 1:2], sm[:, 1:2], ALU.mult))
            S.op("vector", lambda v: v.tensor_tensor(
                sm[:, 5:6], sm[:, 3:4], sm[:, 4:5], ALU.subtract))
            S.op("vector", lambda v: v.tensor_scalar(
                sm[:, 6:7], sm[:, 5:6], EPS, None, ALU.add))
            rv_dv = S.op("vector", lambda v: v.reciprocal(
                sm[:, 7:8], sm[:, 6:7]), inc="dv")
            S.wait("scalar", "dv", rv_dv)
            rstd_ac = S.op("scalar", lambda s: s.activation(
                sm[:, 8:9], sm[:, 7:8], AF.Sqrt), inc="ac")
            S.wait("vector", "ac", rstd_ac)
            S.wait("vector", "st", st_val)
            y_dv = S.op("vector", lambda v: v.tensor_scalar(
                ytil[:], resid[:], sm[:, 1:2], sm[:, 8:9],
                ALU.add, ALU.mult), inc="dv")
            S.wait("sync", "dv", y_dv)
            st_val = S.op("sync", lambda s, tt=tt: s.dma_start(
                out=y_out[tt * 128:(tt + 1) * 128, :], in_=ytil[:]),
                inc="st", inc_n=16)
        S.wait("sync", "st", st_val)

        block = ctx.enter_context(nc.Block())
        S.run(block)
    return nc


def host_prep_inputs(L0_attn_x, L0_attn_w_qkv, L0_attn_w_out,
                     L0_ffn_gate_W_packed, L0_ffn_gate_scales,
                     L0_ffn_up_W_packed, L0_ffn_up_scales,
                     L0_ffn_down_W_packed, L0_ffn_down_scales):
    X = np.ascontiguousarray(np.asarray(L0_attn_x, np.float32).reshape(8192, E))
    XT = np.ascontiguousarray(X.T)
    XT_pad = np.zeros((E, 8192 + 512), np.float32)
    XT_pad[:, 512:] = XT
    wqkvT = np.ascontiguousarray(np.asarray(L0_attn_w_qkv, np.float32).T)
    woutT = np.ascontiguousarray(np.asarray(L0_attn_w_out, np.float32).T)

    def prep_pk_se(packed, scales, r, rows):
        pkT = np.ascontiguousarray(np.asarray(packed, np.int32).T)
        j0, j1 = r * rows, (r + 1) * rows
        sT = np.ascontiguousarray(np.asarray(scales, np.float32).T)
        se = np.repeat(sT[j0 // 16: j1 // 16], 16, axis=0).astype(np.float32)
        return np.ascontiguousarray(pkT[j0:j1]), np.ascontiguousarray(se)

    dec_maps, attn_maps = [], []
    for r in range(NCORES):
        pg, sg = prep_pk_se(L0_ffn_gate_W_packed, L0_ffn_gate_scales, r, 128)
        pu, su = prep_pk_se(L0_ffn_up_W_packed, L0_ffn_up_scales, r, 128)
        pd, sd = prep_pk_se(L0_ffn_down_W_packed, L0_ffn_down_scales, r, 1024)
        dec_maps.append(dict(pk_g=pg, se_g=sg, pk_u=pu, se_u=su, pk_d=pd, se_d=sd))

        xt_slice = np.ascontiguousarray(XT_pad[:, 1024 * r: 1024 * r + TLOC])
        x_own = np.ascontiguousarray(X[1024 * r: 1024 * (r + 1)])
        half = r % 2
        i = np.arange(128)
        j = np.arange(1024)
        mk0 = np.zeros((128, 4, 1024), np.float32)
        for k in range(4):
            pq = 1024 * half + 128 * k + i[:, None]
            pk_pos = 1024 * half - 512 + j[None, :]
            ok = (pk_pos >= pq - 512) & (pk_pos <= pq) & (pk_pos >= 0)
            mk0[:, k, :] = ok.astype(np.float32)
        mk0 = mk0.reshape(128, 4096).astype(ml_dtypes.bfloat16)
        u = np.arange(1408)
        d = u[None, :] - 384 - i[:, None]
        mband = ((d >= 0) & (d <= 512)).astype(ml_dtypes.bfloat16)
        attn_maps.append(dict(xt_in=xt_slice, x_own=x_own, wqkvT=wqkvT,
                              woutT=woutT, mk0_in=mk0, mband_in=mband))
    return dec_maps, attn_maps, X


DBG_TAPS = False
WO_PREFETCH = True
WO_CI = 32


def build_mega():
    from contextlib import ExitStack
    nc = bass.Bass("TRN2", target_bir_lowering=False, debug=False)
    xt_in = nc.dram_tensor("xt_in", [E, TLOC], F32, kind="ExternalInput").ap()
    x_own = nc.dram_tensor("x_own", [TOWN, E], F32, kind="ExternalInput").ap()
    wqkvT = nc.dram_tensor("wqkvT", [E, 3 * E], F32, kind="ExternalInput").ap()
    woutT = nc.dram_tensor("woutT", [E, E], F32, kind="ExternalInput").ap()
    mk0_in = nc.dram_tensor("mk0_in", [128, 4 * 1024], BF16,
                            kind="ExternalInput").ap()
    mband_in = nc.dram_tensor("mband_in", [128, 1408], BF16,
                              kind="ExternalInput").ap()
    pk_g = nc.dram_tensor("pk_g", [128, MLP], I32, kind="ExternalInput").ap()
    se_g = nc.dram_tensor("se_g", [128, MLP], F32, kind="ExternalInput").ap()
    pk_u = nc.dram_tensor("pk_u", [128, MLP], I32, kind="ExternalInput").ap()
    se_u = nc.dram_tensor("se_u", [128, MLP], F32, kind="ExternalInput").ap()
    pk_d = nc.dram_tensor("pk_d", [1024, E], I32, kind="ExternalInput").ap()
    se_d = nc.dram_tensor("se_d", [1024, E], F32, kind="ExternalInput").ap()
    y_out = nc.dram_tensor("y_out", [TOWN, E], F32, kind="ExternalOutput").ap()
    kT_d = nc.dram_tensor("kT_d", [128, NH * TLOC], BF16).ap()
    vv_d = nc.dram_tensor("vv_d", [128, 12 * E], BF16).ap()
    qT_d = nc.dram_tensor("qT_d", [128, NH * TOWN], BF16).ap()
    dbg_scr = nc.dram_tensor("dbg_scr", [128, 16, 2048], F32).ap()
    kindX = "ExternalOutput" if DBG_TAPS else "Internal"
    dbg_q = nc.dram_tensor("dbg_q", [128, NH * TOWN], BF16, kind=kindX).ap()
    kindQ = kindX
    qT_dbgd = nc.dram_tensor("qT_dbg", [128, NH * TOWN], BF16,
                             kind=kindQ).ap()
    x1_d = nc.dram_tensor("x1_d", [TOWN, E], F32, kind=kindX).ap()
    acc_d = nc.dram_tensor("acc_d", [TOWN, E], F32, kind=kindX).ap()
    cc_in_gh = [nc.dram_tensor(f"cc_in_g{h}", [256, MLP // 2], BF16)
                for h in range(2)]
    cc_out_gh = [nc.dram_tensor(f"cc_out_g{h}", [E, MLP // 2], BF16)
                 for h in range(2)]
    cc_in_uh = [nc.dram_tensor(f"cc_in_u{h}", [256, MLP // 2], BF16)
                for h in range(2)]
    cc_out_uh = [nc.dram_tensor(f"cc_out_u{h}", [E, MLP // 2], BF16)
                 for h in range(2)]
    cc_in_dh = [nc.dram_tensor(f"cc_in_d{h}", [1024, E], BF16)
                for h in range(2)]
    cc_out_dh = [nc.dram_tensor(f"cc_out_d{h}", [MLP // 2, E], BF16)
                 for h in range(2)]

    C1 = 0.7978845608028654
    C2 = C1 * 0.044715
    NW = 1024

    with ExitStack() as ctx:
        ar = Arena(nc, 204 * KB)
        xt = ar.at("xt", 0, [128, 16 * TLOC], BF16)
        wq_sl = [ar.at(f"wq{p}", (48 + 4 * p) * KB, [128, 16 * 128], BF16)
                 for p in range(2)]
        wv_sl = ar.at("wv", 56 * KB, [128, 16 * 512], BF16)
        stg = [ar.at(f"stg{p}", (72 + 2 * p) * KB, [128, 1024], BF16)
               for p in range(2)]
        dw_i = [ar.at(f"dw{p}", (80 + 4 * p) * KB, [128, NW], I32)
                for p in range(2)]
        dse = [ar.at(f"dse{p}", (88 + 4 * p) * KB, [128, NW], F32)
               for p in range(2)]
        dhs = [ar.at(f"dhs{p}", (96 + 4 * p) * KB, [128, NW], F32)
               for p in range(2)]
        dci = [[ar.at(f"dci{p}{pl}", (104 + 4 * (2 * p + pl)) * KB,
                      [128, NW], I32) for pl in range(2)] for p in range(2)]
        dcb = [ar.at(f"dcb{p}", (120 + 2 * p) * KB, [128, NW], BF16)
               for p in range(2)]
        dgb = [ar.at(f"dgb{p}", (124 + 2 * p) * KB, [128, NW], BF16)
               for p in range(2)]
        dg8 = [ar.at(f"dg8{p}", (128 + 2 * p) * KB, [128, NW], BF16)
               for p in range(2)]
        drr = [[ar.at(f"drr{p}{pl}", (132 + 2 * (2 * p + pl)) * KB,
                      [128, NW], BF16) for pl in range(2)] for p in range(2)]
        dss = [[ar.at(f"dss{p}{pl}", (140 + 2 * (2 * p + pl)) * KB,
                      [128, NW], BF16) for pl in range(2)] for p in range(2)]
        du1 = [[ar.at(f"du1{p}{pl}", (148 + 2 * (2 * p + pl)) * KB,
                      [128, NW], BF16) for pl in range(2)] for p in range(2)]
        du2 = [[ar.at(f"du2{p}{pl}", (156 + 2 * (2 * p + pl)) * KB,
                      [128, NW], BF16) for pl in range(2)] for p in range(2)]
        dt1 = [[ar.at(f"dt1{p}{pl}", (164 + 2 * (2 * p + pl)) * KB,
                      [128, NW], BF16) for pl in range(2)] for p in range(2)]
        dob = [[ar.at(f"dob{p}{pl}", (172 + 2 * (2 * p + pl)) * KB,
                      [128, NW], BF16) for pl in range(2)] for p in range(2)]
        dbias4 = ar.at("dbias4", 180 * KB, [128, 1], F32)
        dbias12 = ar.at("dbias12", int(180.5 * KB), [128, 1], F32)
        kT = ar.at("kT", 0, [128, NH * TLOC], BF16)
        vv = ar.at("vv", 48 * KB, [128, 12 * E], BF16)
        qT = ar.at("qT", 96 * KB, [128, NH * TOWN], BF16)
        mk0 = ar.at("mk0", 128 * KB, [128, 4 * 1024], BF16)
        mband = ar.at("mband", 136 * KB, [128, 1408], BF16)
        probs4 = [ar.at(f"pb{p}", (140 + 2 * p) * KB, [128, 1024], BF16)
                  for p in range(8)]
        probsn2 = [ar.at(f"pn{p}", (156 + 2 * p) * KB, [128, 1024], BF16)
                   for p in range(2)]
        probsT = ar.at("probsT", 160 * KB, [128, 8 * 512], BF16)
        attnT = ar.at("attnT", 168 * KB, [128, 2 * NH * 512], BF16)
        wo = ar.at("wo", 0, [128, 16 * E], BF16)
        xoB = [ar.at(f"xoB{p}", (64 + 8 * p) * KB, [128, E], F32)
               for p in range(2)]
        residB = [ar.at(f"residB{p}", (80 + 8 * p) * KB, [128, E], F32)
                  for p in range(2)]
        x1til = ar.at("x1til", 96 * KB, [128, E], F32)
        sqt = ar.at("sqt", 104 * KB, [128, E], F32)
        wof32 = [ar.at(f"wof{p}", (112 + 16 * p) * KB, [128, 8 * 512], F32)
                 for p in range(2)]
        x1T = ar.at("x1T", 0, [128, 16 * TOWN], BF16)
        interT = ar.at("interT", 32 * KB, [128, 8 * TOWN], BF16)
        wg_f = [ar.at(f"wgf{p}", (48 + 16 * p) * KB, [128, 16 * 512], BF16)
                for p in range(2)]
        wu_f = [ar.at(f"wuf{p}", (80 + 16 * p) * KB, [128, 16 * 512], BF16)
                for p in range(2)]
        wd_f = [ar.at(f"wdf{p}", (112 + 32 * p) * KB, [128, 8 * E], BF16)
                for p in range(2)]
        gt = [[ar.at(f"gt{p}{i}", (176 + 8 * p + i) * KB, [128, 512], BF16)
               for i in range(8)] for p in range(2)]
        accv = [ar.at(f"accv{p}", (192 + 4 * p) * KB, [128, 1024], F32)
                for p in range(2)]
        xo1 = ar.at("xo1", 112 * KB, [128, E], F32)
        x1b = ar.at("x1b", 120 * KB, [128, E], BF16)
        resid3B = [ar.at(f"resid3B{p}", (112 + 8 * p) * KB, [128, E], F32)
                   for p in range(2)]
        accrB = [ar.at(f"accrB{p}", (128 + 8 * p) * KB, [128, E], F32)
                 for p in range(2)]
        xo3B = [ar.at(f"xo3B{p}", (144 + 8 * p) * KB, [128, E], F32)
                for p in range(2)]
        ytil = ar.at("ytil", 160 * KB, [128, E], F32)
        sqf = ar.at("sqf", 168 * KB, [128, E], F32)
        sm = ar.at("sm", 202 * KB, [128, 32], F32)
        ident = ar.at("ident", int(202.5 * KB), [128, 128], BF16)
        shiftB = ar.at("shiftB", int(203 * KB), [128, 1], F32)

        S = Sched(nc)
        for s in ("s_xt", "s_wq0", "s_wq1", "s_wv", "s_wo",
                  "s_dl0", "s_dl1", "s_cs0", "s_cs1", "s_kv0", "s_kv1",
                  "s_cc0", "s_cc1", "s_cc2", "s_cc3", "s_cc4", "s_cc5",
                  "s_rl", "s_xo0", "s_xo1", "s_x1s",
                  "s_wd0", "s_wd1", "s_gu0", "s_gu1", "s_f1", "s_f30", "s_f31",
                  "ga0", "ga1", "s_y", "s_msk", "s_wo1",
                  "pe", "dv", "ac", "gp"):
            S.new_sem(ctx, s)
        ps_S0 = ctx.enter_context(nc.psum_tensor("S0", [128, 1024], F32))
        ps_S1 = ctx.enter_context(nc.psum_tensor("S1", [128, 1024], F32))
        ps_TP = [ctx.enter_context(nc.psum_tensor(f"TP{i}", [128, 1024], BF16))
                 for i in range(2)]
        ps_AV = ctx.enter_context(nc.psum_tensor("AVP", [128, 512], F32))
        ps_OP = ctx.enter_context(nc.psum_tensor("OPP", [128, 512], F32))
        surfs = [ps_S0[:, 0:512], ps_S0[:, 512:1024],
                 ps_S1[:, 0:512], ps_S1[:, 512:1024]]
        Sfull = [ps_S0, ps_S1]

        m1 = S.op("gpsimd", lambda g: g.memset(ident[:], 0.0))
        id_done = S.op("gpsimd", lambda g: g.affine_select(
            out=ident[:], in_=ident[:], compare_op=ALU.not_equal, fill=1.0,
            base=0, pattern=[[-1, 128]], channel_multiplier=1))
        S.op("gpsimd", lambda g: g.memset(dbias4[:], -4.0))
        c_init = S.op("gpsimd", lambda g: g.memset(dbias12[:], -12.0))
        c_shift = S.op("gpsimd", lambda g: g.memset(shiftB[:], EXP_SHIFT))

        xt_src = xt_in.rearrange("(c p) t -> p c t", p=128)
        xt_ld = S.op("gpsimd", lambda g: g.dma_start(
            out=xt[:].rearrange("p (c t) -> p c t", t=TLOC), in_=xt_src),
            inc="s_xt", inc_n=16)

        djobs = []
        for hf in range(2):
            for wi in range(hf, 16, 2):
                cs = slice(wi * NW, (wi + 1) * NW)
                dc = slice((wi // 2) * NW, (wi // 2 + 1) * NW)
                djobs.append((pk_g[:, cs], se_g[:, cs],
                              cc_in_gh[hf].ap()[0:128, dc],
                              cc_in_gh[hf].ap()[128:256, dc], None))
            djobs[-1] = djobs[-1][:4] + (3 * hf + 0,)
            for wi in range(hf, 16, 2):
                cs = slice(wi * NW, (wi + 1) * NW)
                dc = slice((wi // 2) * NW, (wi // 2 + 1) * NW)
                djobs.append((pk_u[:, cs], se_u[:, cs],
                              cc_in_uh[hf].ap()[0:128, dc],
                              cc_in_uh[hf].ap()[128:256, dc], None))
            djobs[-1] = djobs[-1][:4] + (3 * hf + 1,)
            for q in range(4 * hf, 4 * hf + 4):
                for wi in range(2):
                    cs = slice(wi * NW, (wi + 1) * NW)
                    ql = q - 4 * hf
                    djobs.append((pk_d[q * 128:(q + 1) * 128, cs],
                                  se_d[q * 128:(q + 1) * 128, cs],
                                  cc_in_dh[hf].ap()[ql * 256: ql * 256 + 128, cs],
                                  cc_in_dh[hf].ap()[ql * 256 + 128: ql * 256 + 256, cs],
                                  None))
            djobs[-1] = djobs[-1][:4] + (3 * hf + 2,)
        ndw = len(djobs)
        cc_ins = [cc_in_gh[0], cc_in_uh[0], cc_in_dh[0],
                  cc_in_gh[1], cc_in_uh[1], cc_in_dh[1]]
        cc_outs = [cc_out_gh[0], cc_out_uh[0], cc_out_dh[0],
                   cc_out_gh[1], cc_out_uh[1], cc_out_dh[1]]
        cc_done = {}
        cc_pending = []

        def flush_cc(lim=6):
            while cc_pending and cc_pending[0][0] < lim:
                cci, v0, v1 = cc_pending.pop(0)
                S.wait("gpsimd", "s_cs0", v0)
                S.wait("gpsimd", "s_cs1", v1)
                cc_done[cci] = S.op("gpsimd", lambda g, cci=cci:
                                    g.collective_compute(
                    "AllGather", ALU.bypass, replica_groups=[list(range(8))],
                    ins=[cc_ins[cci].ap().opt()],
                    outs=[cc_outs[cci].ap().opt()]),
                    inc=f"s_cc{cci}", inc_n=1)

        wsrc = wqkvT.rearrange("(c p) f -> p c f", p=128)
        dfin, drrv, duuv, dobv, dstv, dldv = {}, {}, {}, {}, {}, {}

        def emit_decode_loads(i):
            p = i % 2
            if i >= 2:
                S.wait("sync", "dv", dfin[i - 2])
            S.op("sync", lambda s, i=i, p=p: s.dma_start(
                out=dw_i[p][:], in_=djobs[i][0]), inc=f"s_dl{p}", inc_n=16)
            dldv[i] = S.op("sync", lambda s, i=i, p=p: s.dma_start(
                out=dse[p][:], in_=djobs[i][1]), inc=f"s_dl{p}", inc_n=16)

        def emit_decode_window(i):
            p = i % 2
            S.wait("vector", f"s_dl{p}", dldv[i])
            S.op("vector", lambda v, p=p: v.tensor_scalar(
                dci[p][0][:], dw_i[p][:], 15, None, ALU.bitwise_and))
            S.op("vector", lambda v, p=p: v.tensor_scalar(
                dci[p][1][:], dw_i[p][:], 4, 15,
                ALU.logical_shift_right, ALU.bitwise_and))
            dfin[i] = S.op("vector", lambda v, p=p: v.tensor_scalar(
                dhs[p][:], dse[p][:], 0.5, None, ALU.mult))
            for pl in range(2):
                if i >= 2:
                    S.wait("vector", "ac", duuv[(i - 2, pl)])
                S.op("vector", lambda v, p=p, pl=pl: v.tensor_copy(
                    dcb[p][:], dci[p][pl][:]))
                S.op("vector", lambda v, p=p: v.tensor_scalar(
                    dgb[p][:], dcb[p][:], 7.5, None, ALU.is_ge))
                S.op("vector", lambda v, p=p: v.tensor_scalar(
                    dg8[p][:], dgb[p][:], 8.0, None, ALU.mult))
                S.op("vector", lambda v, p=p, pl=pl: v.tensor_scalar(
                    dss[p][pl][:], dgb[p][:], -2.0, 1.0, ALU.mult, ALU.add))
                drrv[(i, pl)] = S.op("vector", lambda v, p=p, pl=pl:
                                     v.tensor_tensor(drr[p][pl][:], dcb[p][:],
                                                     dg8[p][:], ALU.subtract))
            for pl in range(2):
                S.wait("scalar", "dv", drrv[(i, pl)])
                if pl == 0 and i == 0:
                    S.wait("scalar", "gp", c_init)
                if i >= 2:
                    S.wait("scalar", "dv", dobv[(i - 2, pl)])
                S.op("scalar", lambda s, p=p, pl=pl: s.activation(
                    du1[p][pl][:], drr[p][pl][:], AF.Relu, bias=dbias4[:]))
                duuv[(i, pl)] = S.op("scalar", lambda s, p=p, pl=pl: s.activation(
                    du2[p][pl][:], drr[p][pl][:], AF.Relu, bias=dbias12[:],
                    scale=2.0))
            for pl in range(2):
                S.wait("vector", "ac", duuv[(i, pl)])
                if i >= 2:
                    S.wait("vector", f"s_cs{p}", dstv[i - 2])
                S.op("vector", lambda v, p=p, pl=pl: v.tensor_tensor(
                    dt1[p][pl][:], drr[p][pl][:], du1[p][pl][:], ALU.add))
                S.op("vector", lambda v, p=p, pl=pl: v.tensor_tensor(
                    dcb[p][:], dt1[p][pl][:], du2[p][pl][:], ALU.add))
                S.op("vector", lambda v, p=p, pl=pl: v.tensor_tensor(
                    dt1[p][pl][:], dcb[p][:], dss[p][pl][:], ALU.mult))
                dobv[(i, pl)] = S.op("vector", lambda v, p=p, pl=pl:
                                     v.tensor_tensor(dob[p][pl][:], dt1[p][pl][:],
                                                     dhs[p][:], ALU.mult))
            S.wait("sync", "dv", dobv[(i, 0)])
            S.op("sync", lambda s, i=i, p=p: s.dma_start(
                out=djobs[i][2], in_=dob[p][0][:]), inc=f"s_cs{p}", inc_n=16)
            S.wait("sync", "dv", dobv[(i, 1)])
            dstv[i] = S.op("sync", lambda s, i=i, p=p: s.dma_start(
                out=djobs[i][3], in_=dob[p][1][:]), inc=f"s_cs{p}", inc_n=16)
            cci = djobs[i][4]
            if cci is not None:
                cc_pending.append((cci, S.cur("s_cs0"), S.cur("s_cs1")))

        qkv_groups = []
        qk_jobs = []
        for ft in range(32):
            wins = ((512,), (1024,)) if ft < 16 else ((0,), (512,), (1024,))
            for (off,) in wins:
                qk_jobs.append((ft, off))
        v_jobs = [(fw, tt) for fw in range(4) for tt in range(12)]

        slab_emitted = {}
        slab_val = {}
        slab_pe_last = {}
        evac_ac = {}
        dstv_kv = {}

        def pair_dst_info(j):
            if j < 16:
                return qT_d[:, j * 1024:(j + 1) * 1024]
            if j < 40:
                jj = j - 16
                return kT_d[:, jj * 1024:(jj + 1) * 1024]
            jj = j - 40
            fw, tt0 = jj // 6, 2 * (jj % 6)
            i0 = tt0 * 4 + fw
            return vv_d.rearrange("p (t f) -> p t f", f=512)[
                :, i0:i0 + 5:4, :]

        n_groups = 128
        dec_emitted = 0
        for g in range(n_groups):
            pair = g // 2
            surf = surfs[g % 4]
            qpart = g < 32
            kpart = 32 <= g < 80
            if qpart or kpart:
                ft, off = qk_jobs[g]
                p = ft % 2
                if ft not in slab_emitted:
                    if ft >= 2:
                        S.wait("gpsimd", "pe", slab_pe_last[ft - 2])
                    fsl = slice(ft * 128, (ft + 1) * 128)
                    slab_val[ft] = S.op("gpsimd", lambda gg, p=p, fsl=fsl:
                                        gg.dma_start(
                        out=wq_sl[p][:].rearrange("p (c f) -> p c f", f=128),
                        in_=wsrc[:, :, fsl]), inc=f"s_wq{p}", inc_n=16)
                    slab_emitted[ft] = True
                if g == 0:
                    S.wait("tensor", "s_xt", xt_ld)
                S.wait("tensor", f"s_wq{p}", slab_val[ft])
                if pair >= 2:
                    S.wait("tensor", "dv", evac_ac[pair - 2])
                for ec in range(16):
                    def mm(t, surf=surf, p=p, ec=ec, off=off):
                        return t.matmul(
                            surf, wq_sl[p][:, ec * 128:(ec + 1) * 128],
                            xt[:, ec * TLOC + off: ec * TLOC + off + 512],
                            start=(ec == 0), stop=(ec == 15))
                    S.op("tensor", mm, inc="pe" if ec == 15 else None)
                slab_pe_last[ft] = S.cur("pe")
            else:
                fw, tt = v_jobs[g - 80]
                if tt == 0:
                    if fw >= 1:
                        S.wait("gpsimd", "pe", S.cur("pe"))
                    fsl = slice(2 * E + fw * 512, 2 * E + (fw + 1) * 512)
                    slab_val[("v", fw)] = S.op("gpsimd", lambda gg, fsl=fsl:
                                               gg.dma_start(
                        out=wv_sl[:].rearrange("p (c f) -> p c f", f=512),
                        in_=wsrc[:, :, fsl]), inc="s_wv", inc_n=16)
                S.wait("tensor", "s_wv", slab_val[("v", fw)])
                if pair >= 2:
                    S.wait("tensor", "dv", evac_ac[pair - 2])
                for ec in range(16):
                    def mmv(t, surf=surf, ec=ec, tt=tt):
                        return t.matmul(
                            surf,
                            xt[:, ec * TLOC + tt * 128: ec * TLOC + (tt + 1) * 128],
                            wv_sl[:, ec * 512:(ec + 1) * 512],
                            start=(ec == 0), stop=(ec == 15))
                    S.op("tensor", mmv, inc="pe" if ec == 15 else None)
            if g % 2 == 1:
                pe_pair = S.cur("pe")
                sp = pair % 2
                S.wait("vector", "pe", pe_pair)
                if pair >= 2:
                    S.wait("vector", f"s_kv{sp}", dstv_kv.get(pair - 2, 0))
                scl = QK_SCALE if pair < 16 else 1.0
                evac_ac[pair] = S.op("vector", lambda v, sp=sp, scl=scl:
                                     v.tensor_scalar(stg[sp][:], Sfull[sp][:],
                                                     scl, None, ALU.mult))
                S.wait("sync", "dv", evac_ac[pair])
                dst = pair_dst_info(pair)
                if pair >= 40:
                    src = stg[sp][:].rearrange("p (t f) -> p t f", f=512)
                else:
                    src = stg[sp][:]
                dstv_kv[pair] = S.op("sync", lambda s, dst=dst, src=src:
                                     s.dma_start(out=dst, in_=src),
                                     inc=f"s_kv{sp}", inc_n=16)
            want_dec = min(ndw, (g + 1) * ndw // n_groups + 2)
            while dec_emitted < want_dec:
                i = dec_emitted
                if i < 2:
                    emit_decode_loads(i)
                emit_decode_window(i)
                if i + 2 < ndw:
                    emit_decode_loads(i + 2)
                dec_emitted += 1
            if g in (63, 120):
                flush_cc(3)
        while dec_emitted < ndw:
            i = dec_emitted
            if i < 2:
                emit_decode_loads(i)
            emit_decode_window(i)
            if i + 2 < ndw:
                emit_decode_loads(i + 2)
            dec_emitted += 1

        flush_cc(3)
        qkv_pe_last = S.cur("pe")
        dec_dv_last = S.cur("dv")
        last_evac_pair = 63

        S.wait("sync", "pe", qkv_pe_last)
        S.wait("sync", "dv", dec_dv_last)
        S.wait("sync", "s_cs0", S.cur("s_cs0"))
        S.wait("sync", "s_cs1", S.cur("s_cs1"))
        S.wait("sync", "s_kv0", S.cur("s_kv0"))
        S.wait("sync", "s_kv1", S.cur("s_kv1"))
        rl_k = S.op("sync", lambda s: s.dma_start(out=kT[:], in_=kT_d),
                    inc="s_rl", inc_n=16)
        S.wait("scalar", "pe", qkv_pe_last)
        S.wait("scalar", "dv", dec_dv_last)
        S.wait("scalar", "s_kv0", S.cur("s_kv0"))
        S.wait("scalar", "s_kv1", S.cur("s_kv1"))
        rl_v = S.op("scalar", lambda s: s.dma_start(out=vv[:], in_=vv_d),
                    inc="s_rl", inc_n=16)
        S.op("scalar", lambda s: s.dma_start(out=qT[:, 0:8 * TOWN],
                                             in_=qT_d[:, 0:8 * TOWN]),
             inc="s_rl", inc_n=16)
        rl_q = S.op("sync", lambda s: s.dma_start(out=qT[:, 8 * TOWN:],
                                                  in_=qT_d[:, 8 * TOWN:]),
                    inc="s_rl", inc_n=16)
        S.op("sync", lambda s: s.dma_start(out=mk0[:], in_=mk0_in),
             inc="s_msk", inc_n=16)
        rl_m = S.op("sync", lambda s: s.dma_start(out=mband[:], in_=mband_in),
                    inc="s_msk", inc_n=16)

        S.wait("scalar", "gp", c_shift)
        S.wait("tensor", "gp", id_done)
        S.wait("vector", "s_msk", rl_m)

        exp_ac, tr_pe, ptk_dv, av_pe, att_ac, nrm_dv = ({}, {}, {}, {},
                                                         {}, {})
        sum_tok = {}
        recip_dv = {}
        last_scores_pe = None
        chs = [(c, h) for c in range(2) for h in range(16)]

        def emit_softmax(ci_):
            c, h = chs[ci_]
            for k in range(4):
                Sk = ps_S0 if k % 2 == 0 else ps_S1
                slot = (ci_ % 2) * 4 + k
                pn = probsn2[k % 2]
                if ci_ == 0 and k == 0:
                    S.wait("tensor", "s_rl", rl_q)
                prev = (c, h, k - 2) if k >= 2 else \
                    (chs[ci_ - 1] + (k + 2,) if ci_ > 0 else None)
                if prev in exp_ac:
                    S.wait("tensor", "ac", exp_ac[prev])
                elif ci_ == 0:
                    S.wait("tensor", "dv", evac_ac[63 - (1 - k % 2)])
                q0 = h * TOWN + c * 512 + k * 128
                k0 = h * TLOC + c * 512
                S.op("tensor", lambda t, Sk=Sk, q0=q0, k0=k0: t.matmul(
                    Sk[:, 0:512], qT[:, q0:q0 + 128], kT[:, k0:k0 + 512],
                    start=True, stop=True))
                pe_sc = S.op("tensor", lambda t, Sk=Sk, q0=q0, k0=k0: t.matmul(
                    Sk[:, 512:1024], qT[:, q0:q0 + 128],
                    kT[:, k0 + 512:k0 + 1024], start=True, stop=True), inc="pe")
                S.wait("scalar", "pe", pe_sc)
                pslot_prev = (chs[ci_ - 2] + (k,)) if ci_ >= 2 else None
                if pslot_prev in tr_pe:
                    S.wait("scalar", "pe", tr_pe[pslot_prev])
                exp_ac[(c, h, k)] = S.op("scalar", lambda s, Sk=Sk, slot=slot:
                                         s.activation(probs4[slot][:], Sk[:],
                                                      AF.Exp, bias=shiftB[:]))
                S.wait("vector", "ac", exp_ac[(c, h, k)])
                pk_prev = (c, h, k - 2) if k >= 2 else \
                    (chs[ci_ - 1] + (k + 2,) if ci_ > 0 else None)
                if k % 2 == 0 and pk_prev in sum_tok:
                    S.wait("vector", "ac", sum_tok[pk_prev][1])
                pm = S.op("vector", lambda v, pn=pn, slot=slot,
                          mview=(mk0[:, k * 1024:(k + 1) * 1024] if c == 0 else
                                 mband[:, 384 - 128 * k: 1408 - 128 * k]):
                          v.tensor_tensor(pn[:], probs4[slot][:], mview,
                                          ALU.mult))
                rprev = (chs[ci_ - 1] + (k,)) if ci_ > 0 else None
                if k % 2 == 0:
                    S.wait("scalar", "dv", pm)
                    if rprev in recip_dv:
                        S.wait("scalar", "dv", recip_dv[rprev])
                    sum_tok[(c, h, k)] = ("scalar", S.op(
                        "scalar", lambda s, pn=pn, k=k, slot=slot: s.activation(
                            probs4[slot][:], pn[:], AF.Identity,
                            accum_out=sm[:, k:k + 1])))
                    S.wait("vector", "ac", sum_tok[(c, h, k)][1])
                else:
                    sum_tok[(c, h, k)] = ("vector", S.op(
                        "vector", lambda v, pn=pn, k=k: v.reduce_sum(
                            sm[:, k:k + 1], pn[:], AX.X)))
                recip_dv[(c, h, k)] = S.op(
                    "vector", lambda v, k=k: v.reciprocal(sm[:, 4 + k:5 + k],
                                                          sm[:, k:k + 1]))
                nrm_dv[(c, h, k)] = S.op("vector", lambda v, pn=pn, k=k,
                                         slot=slot: v.tensor_scalar(
                    probs4[slot][:], pn[:], sm[:, 4 + k:5 + k], None,
                    ALU.mult))

        def emit_trs_av(ci_):
            c, h = chs[ci_]
            for k in range(4):
                slot = (ci_ % 2) * 4 + k
                S.wait("tensor", "dv", nrm_dv[(c, h, k)])
                prev = (c, h, k - 2) if k >= 2 else \
                    (chs[ci_ - 1] + (k + 2,) if ci_ > 0 else None)
                if prev in ptk_dv:
                    S.wait("tensor", "dv", ptk_dv[prev])
                for m in range(8):
                    def trp(t, k=k, m=m, slot=slot):
                        return t.transpose(
                            ps_TP[k % 2][:, m * 128:(m + 1) * 128],
                            probs4[slot][:, m * 128:(m + 1) * 128], ident[:])
                    if m == 7:
                        tr_pe[(c, h, k)] = S.op("tensor", trp, inc="pe")
                    else:
                        S.op("tensor", trp)
                S.wait("vector", "pe", tr_pe[(c, h, k)])
                if ci_ > 0 and k == 0:
                    S.wait("vector", "pe", av_pe[chs[ci_ - 1]])
                def evt(v, k=k):
                    src = ps_TP[k % 2][:].rearrange("p (m f) -> p m f", f=128)
                    dst = probsT[:].rearrange("p (m f) -> p m f", f=512)[
                        :, :, k * 128:(k + 1) * 128]
                    return v.tensor_copy(dst, src)
                ptk_dv[(c, h, k)] = S.op("vector", evt)
            S.wait("tensor", "dv", ptk_dv[(c, h, 3)])
            if ci_ > 0:
                S.wait("tensor", "ac", att_ac[chs[ci_ - 1]])
            if ci_ == 0:
                S.wait("tensor", "s_rl", rl_q)
            for m in range(8):
                def av(t, c=c, h=h, m=m):
                    return t.matmul(
                        ps_AV[:],
                        vv[:, (4 * c + m) * E + h * 128:
                           (4 * c + m) * E + (h + 1) * 128],
                        probsT[:, m * 512:(m + 1) * 512],
                        start=(m == 0), stop=(m == 7))
                if m == 7:
                    av_pe[(c, h)] = S.op("tensor", av, inc="pe")
                else:
                    S.op("tensor", av)
            S.wait("scalar", "pe", av_pe[(c, h)])
            dstA = attnT[:, (c * 16 + h) * 512:(c * 16 + h + 1) * 512]
            att_ac[(c, h)] = S.op("scalar", lambda s, dstA=dstA: s.activation(
                dstA, ps_AV[:], AF.Identity))

        wsrc_f32 = woutT.rearrange("(c p) f -> p c f", p=128)
        ld_wo_r = {}
        for ci_ in range(33):
            if ci_ == WO_CI and WO_PREFETCH:
                pe16 = S.cur("pe")
                S.wait("sync", "pe", pe16)
                if WO_PREFETCH == 2:
                    ld_wo_r[0] = S.op("sync", lambda s: s.dma_start(
                        out=dbg_scr[:, 0:8, 0:512],
                        in_=wsrc_f32[:, 0:8, 0:512]), inc="s_wo", inc_n=16)
                elif WO_PREFETCH == 3:
                    S.op("sync", lambda s: s.dma_start(
                        out=wof32[0][:].rearrange("p (c f) -> p c f", f=512),
                        in_=wsrc_f32[:, 0:8, 0:512]), inc="s_wo", inc_n=16)
                else:
                    ld_wo_r[0] = S.op("sync", lambda s: s.dma_start(
                        out=wof32[0][:].rearrange("p (c f) -> p c f", f=512),
                        in_=wsrc_f32[:, 0:8, 0:512]), inc="s_wo", inc_n=16)
            if ci_ < 32:
                emit_softmax(ci_)
            if ci_ >= 1:
                emit_trs_av(ci_ - 1)
        last_scores_pe = S.cur("pe")

        pe_b1_end = av_pe[chs[-1]]
        dv_b1_end = S.cur("dv")
        ac_b1_end = S.cur("ac")
        cast_wo = {}

        def emit_ld_wo(r):
            ring = "scalar" if r % 2 else "sync"
            fw, hh = r // 2, r % 2
            if r <= 2:
                S.wait(ring, "pe", pe_b1_end)
                S.wait(ring, "dv", dv_b1_end)
                S.wait(ring, "ac", ac_b1_end)
                if r - 2 in cast_wo:
                    S.wait(ring, "dv", cast_wo[r - 2])
            else:
                S.wait(ring, "dv", cast_wo[r - 2])
            ld_wo_r[r] = S.op(ring, lambda e, fw=fw, hh=hh, r=r: e.dma_start(
                out=wof32[r % 2][:].rearrange("p (c f) -> p c f", f=512),
                in_=wsrc_f32[:, 8 * hh:8 * hh + 8, fw * 512:(fw + 1) * 512]),
                inc="s_wo" if r % 2 == 0 else "s_wo1", inc_n=16)

        def emit_cast_wo(r):
            fw, hh = r // 2, r % 2
            if r == 0:
                S.wait("vector", "pe", pe_b1_end)
            S.wait("vector", "s_wo" if r % 2 == 0 else "s_wo1",
                   ld_wo_r[r])
            cast_wo[r] = S.op("vector", lambda v, fw=fw, hh=hh, r=r:
                              v.tensor_copy(
                wo[:, fw * 8192 + hh * 4096: fw * 8192 + (hh + 1) * 4096],
                wof32[r % 2][:]))

        if WO_PREFETCH != 1:
            rr0_ = 0
            S.wait("sync", "pe", pe_b1_end)
            S.wait("sync", "dv", dv_b1_end)
            S.wait("sync", "ac", ac_b1_end)
            ld_wo_r[0] = S.op("sync", lambda s: s.dma_start(
                out=wof32[0][:].rearrange("p (c f) -> p c f", f=512),
                in_=wsrc_f32[:, 0:8, 0:512]), inc="s_wo", inc_n=16)
        if DBG_TAPS:
            S.wait("sync", "pe", pe_b1_end)
            S.op("sync", lambda s: s.dma_start(out=dbg_q, in_=qT[:]),
                 inc="s_y", inc_n=16)
            S.op("sync", lambda s: s.dma_start(out=qT_dbgd, in_=qT_d),
                 inc="s_y", inc_n=16)
        emit_ld_wo(1)
        emit_cast_wo(0)
        emit_ld_wo(2)
        emit_cast_wo(1)
        emit_ld_wo(3)
        S.wait("sync", "ac", att_ac[chs[-1]])
        op_dv = None
        xo_ld = {}
        resid_dv = {}
        x1s_val = {}
        st_val = 0
        sq_of = {}

        for step in range(9):
            tt = step
            if tt < 8:
                c = tt // 4
                bp = tt % 2
                if tt >= 2:
                    S.wait("sync", "dv", resid_dv[tt - 2])
                xo_ld[tt] = S.op("sync", lambda s, tt=tt, bp=bp: s.dma_start(
                    out=xoB[bp][:], in_=x_own[tt * 128:(tt + 1) * 128, :]),
                    inc=f"s_xo{bp}", inc_n=16)
                for fw in range(4):
                    if (tt, fw) == (0, 0):
                        S.wait("tensor", "ac", att_ac[chs[-1]])
                    else:
                        S.wait("tensor", "dv", op_dv)
                    if tt == 0:
                        S.wait("tensor", "dv", cast_wo[2 * fw + 1])
                    for ec in range(16):
                        def mo(t, c=c, tt=tt, fw=fw, ec=ec):
                            lhs = attnT[:, (c * 16 + ec) * 512 + (tt % 4) * 128:
                                        (c * 16 + ec) * 512 + (tt % 4) * 128 + 128]
                            rhs = wo[:, (fw * 16 + ec) * 512:
                                      (fw * 16 + ec) * 512 + 512]
                            return t.matmul(ps_OP[:], lhs, rhs,
                                            start=(ec == 0), stop=(ec == 15))
                        S.op("tensor", mo, inc="pe" if ec == 15 else None)
                    pe_op = S.cur("pe")
                    S.wait("vector", "pe", pe_op)
                    S.wait("vector", f"s_xo{bp}", xo_ld[tt])
                    def rp(v, fw=fw, bp=bp):
                        return v.tensor_tensor(
                            residB[bp][:, fw * 512:(fw + 1) * 512], ps_OP[:],
                            xoB[bp][:, fw * 512:(fw + 1) * 512], ALU.add)
                    op_dv = S.op("vector", rp)
                    if tt == 0 and fw <= 2:
                        emit_cast_wo(2 * fw + 2)
                        emit_cast_wo(2 * fw + 3)
                        for rr_ in (2 * fw + 4, 2 * fw + 5):
                            if rr_ <= 7:
                                emit_ld_wo(rr_)
                resid_dv[tt] = op_dv
            if step >= 1:
                lt = step - 1
                bp = lt % 2
                S.op("vector", lambda v, bp=bp: v.reduce_sum(
                    sm[:, 8:9], residB[bp][:], AX.X, negate=True))
                nm_dv = S.op("vector", lambda v: v.tensor_scalar(
                    sm[:, 9:10], sm[:, 8:9], 1.0 / E, None, ALU.mult))
                S.wait("scalar", "dv", nm_dv)
                sq_of[lt] = S.op("scalar", lambda s, bp=bp: s.activation(
                    sqt[:], residB[bp][:], AF.Square, accum_out=sm[:, 10:11]))
                S.wait("vector", "ac", sq_of[lt])
                S.op("vector", lambda v: v.tensor_scalar(
                    sm[:, 11:12], sm[:, 10:11], 1.0 / E, None, ALU.mult))
                S.op("vector", lambda v: v.tensor_tensor(
                    sm[:, 12:13], sm[:, 9:10], sm[:, 9:10], ALU.mult))
                S.op("vector", lambda v: v.tensor_tensor(
                    sm[:, 13:14], sm[:, 11:12], sm[:, 12:13], ALU.subtract))
                S.op("vector", lambda v: v.tensor_scalar(
                    sm[:, 14:15], sm[:, 13:14], EPS, None, ALU.add))
                rv_dv = S.op("vector", lambda v: v.reciprocal(
                    sm[:, 15:16], sm[:, 14:15]))
                S.wait("scalar", "dv", rv_dv)
                rstd_ac = S.op("scalar", lambda s: s.activation(
                    sm[:, 16:17], sm[:, 15:16], AF.Sqrt))
                S.wait("vector", "ac", rstd_ac)
                S.wait("vector", "s_x1s", st_val)
                x1_dv = S.op("vector", lambda v, bp=bp: v.tensor_scalar(
                    x1til[:], residB[bp][:], sm[:, 9:10], sm[:, 16:17],
                    ALU.add, ALU.mult))
                S.wait("sync", "dv", x1_dv)
                st_val = S.op("sync", lambda s, lt=lt: s.dma_start(
                    out=x1_d[lt * 128:(lt + 1) * 128, :], in_=x1til[:]),
                    inc="s_x1s", inc_n=16)
                x1s_val[lt] = st_val
        x1_dv_last = S.cur("dv")

        S.wait("gpsimd", "s_x1s", S.cur("s_x1s"))
        flush_cc()

        wgsrc_h = [cc_out_gh[h].ap().rearrange("(c p) n -> p c n", p=128)
                   for h in range(2)]
        wusrc_h = [cc_out_uh[h].ap().rearrange("(c p) n -> p c n", p=128)
                   for h in range(2)]
        wd_ap_h = [cc_out_dh[h].ap() for h in range(2)]

        f1_evac = None
        cast_dv = {}
        tp_pe = 0
        for tt in range(8):
            if tt >= 1:
                S.wait("sync", "dv", cast_dv[tt - 1])
            S.wait("sync", "s_x1s", x1s_val[tt])
            xo_ld = S.op("sync", lambda s, tt=tt: s.dma_start(
                out=xo1[:], in_=x1_d[tt * 128:(tt + 1) * 128, :]),
                inc="s_f1", inc_n=16)
            S.wait("vector", "s_f1", xo_ld)
            if tt >= 1:
                S.wait("vector", "pe", tp_pe)
            cast_dv[tt] = S.op("vector", lambda v: v.tensor_copy(x1b[:], xo1[:]))
            S.wait("tensor", "dv", cast_dv[tt])
            if tt >= 1:
                S.wait("tensor", "dv", f1_evac)
            for hc in range(16):
                def tr1(t, hc=hc):
                    base = 256 * (hc // 2) + (hc % 2)
                    src = x1b[:, base: base + 255: 2]
                    return t.transpose(
                        ps_TP[hc // 8][:, (hc % 8) * 128:(hc % 8 + 1) * 128],
                        src, ident[:])
                S.op("tensor", tr1, inc="pe" if hc == 15 else None)
            tp_pe = S.cur("pe")
            S.wait("vector", "pe", tp_pe)
            for half in range(2):
                def ev1(v, tt=tt, half=half):
                    src = ps_TP[half][:].rearrange("p (m f) -> p m f", f=128)
                    dst = x1T[:].rearrange("p (c t) -> p c t", t=TOWN)[
                        :, half * 8:(half + 1) * 8, tt * 128:(tt + 1) * 128]
                    return v.tensor_copy(dst, src)
                f1_evac = S.op("vector", ev1)
        f1_done_dv = f1_evac

        g_idx = 0
        gelu_dv, gelu_free = [], []
        gp_p_l, ac_th_l, gp_q_l, dv_arg_l = [], [], [], []
        slab_sld, slab_pe_done = {}, {}
        wd_sld, down_pe_G = {}, {}
        ga_vals, accv_dv = [], []
        inter_done = {}
        first_wg = True
        Gseq = [0, 2, 4, 6, 8, 10, 12, 14, 1, 3, 5, 7, 9, 11, 13, 15]
        for gi, G in enumerate(Gseq):
            hf = G % 2
            wdbuf = gi % 2
            if gi < 2:
                S.wait("sync", "dv", f1_done_dv)
            else:
                S.wait("sync", "pe", down_pe_G[Gseq[gi - 2]])
            S.wait("sync", f"s_cc{3 * hf + 2}", cc_done[3 * hf + 2])
            for c_l in range(8):
                row0 = 1024 * (G // 2) + 256 * (c_l // 2) + 128 * (c_l % 2)
                def ldwd(s, row0=row0, hf=hf, wdbuf=wdbuf, c_l=c_l):
                    return s.dma_start(
                        out=wd_f[wdbuf][:, c_l * E:(c_l + 1) * E],
                        in_=wd_ap_h[hf][row0:row0 + 128, :])
                wd_sld[G] = S.op("sync", ldwd, inc=f"s_wd{wdbuf}", inc_n=16)
            for sw in range(2):
                if gi >= 1:
                    S.wait("sync", "pe", slab_pe_done[(Gseq[gi - 1], sw)])
                if first_wg:
                    S.wait("sync", "dv", x1_dv_last)
                    S.wait("sync", "s_x1s", S.cur("s_x1s"))
                    first_wg = False
                S.wait("sync", f"s_cc{3 * hf}", cc_done[3 * hf])
                S.wait("sync", f"s_cc{3 * hf + 1}", cc_done[3 * hf + 1])
                nsl = slice(1024 * (G // 2) + 512 * sw,
                            1024 * (G // 2) + 512 * (sw + 1))
                S.op("sync", lambda s, sw=sw, nsl=nsl, hf=hf: s.dma_start(
                    out=wg_f[sw][:].rearrange("p (c n) -> p c n", n=512),
                    in_=wgsrc_h[hf][:, :, nsl]), inc=f"s_gu{sw}", inc_n=16)
                slab_sld[(G, sw)] = S.op("sync", lambda s, sw=sw, nsl=nsl,
                                         hf=hf: s.dma_start(
                    out=wu_f[sw][:].rearrange("p (c n) -> p c n", n=512),
                    in_=wusrc_h[hf][:, :, nsl]), inc=f"s_gu{sw}", inc_n=16)
            for c_l in range(8):
                sw = c_l // 4
                woff = 256 * ((c_l % 4) // 2)
                sub = c_l % 2
                for tw in range(2):
                    gs = surfs[g_idx % 2]
                    us = surfs[2 + g_idx % 2]
                    S.wait("tensor", f"s_gu{sw}", slab_sld[(G, sw)])
                    if g_idx == 0:
                        S.wait("tensor", "dv", f1_done_dv)
                    if g_idx >= 2:
                        _, dv_ub2 = gelu_free[g_idx - 2]
                        S.wait("tensor", "dv", dv_ub2)
                    for wgt, surf in ((wg_f[sw], gs), (wu_f[sw], us)):
                        for hc in range(16):
                            def mmg(t, wgt=wgt, surf=surf, hc=hc, woff=woff,
                                    sub=sub, tw=tw):
                                lhs = wgt[:, hc * 512 + woff + sub:
                                          hc * 512 + woff + sub + 255: 2]
                                rhs = x1T[:, hc * TOWN + tw * 512:
                                          hc * TOWN + tw * 512 + 512]
                                return t.matmul(surf, lhs, rhs,
                                                start=(hc == 0), stop=(hc == 15))
                            S.op("tensor", mmg,
                                 inc="pe" if hc == 15 else None)
                    pe_gu = S.cur("pe")
                    par = g_idx % 2
                    g2, pp, gb, ub, arg, th, gu, q8 = gt[par]
                    S.wait("vector", "pe", pe_gu)
                    dv_gb = S.op("vector", lambda v, gb=gb, gs=gs:
                                 v.tensor_copy(gb[:], gs))
                    dv_ub = S.op("vector", lambda v, ub=ub, us=us:
                                 v.tensor_copy(ub[:], us))
                    S.wait("scalar", "dv", dv_gb)
                    ac_g2 = S.op("scalar", lambda s, g2=g2, gb=gb: s.activation(
                        g2[:], gb[:], AF.Square))
                    S.wait("vector", "ac", ac_g2)
                    S.op("vector", lambda v, pp=pp, g2=g2: v.tensor_scalar(
                        pp[:], g2[:], C2, C1, ALU.mult, ALU.add))
                    if g_idx >= 2:
                        S.wait("vector", "ac", ac_th_l[g_idx - 2])
                    dv_arg = S.op("vector", lambda v, arg=arg, pp=pp, gb=gb:
                                  v.tensor_tensor(arg[:], pp[:], gb[:],
                                                  ALU.mult))
                    gu_dv = S.op("vector", lambda v, gu=gu, gb=gb, ub=ub:
                                 v.tensor_tensor(gu[:], gb[:], ub[:], ALU.mult))
                    S.wait("scalar", "dv", dv_arg)
                    ac_th = S.op("scalar", lambda s, th=th, arg=arg: s.activation(
                        th[:], arg[:], AF.Tanh))
                    S.wait("vector", "ac", ac_th)
                    S.op("vector", lambda v, q8=q8, th=th: v.tensor_scalar(
                        q8[:], th[:], 0.5, 0.5, ALU.mult, ALU.add))
                    dst_i = interT[:, c_l * TOWN + tw * 512:
                                   c_l * TOWN + tw * 512 + 512]
                    dv_int = S.op("vector", lambda v, dst_i=dst_i, q8=q8, gu=gu:
                                  v.tensor_tensor(dst_i, q8[:], gu[:],
                                                  ALU.mult))
                    gelu_dv.append(dv_int)
                    gelu_free.append((0, dv_ub))
                    ac_th_l.append(ac_th)
                    slab_pe_done[(G, sw)] = pe_gu
                    g_idx += 1
            inter_done[G] = gelu_dv[-1]
            for tt in range(8):
                for hw in range(4):
                    psi = (tt * 4 + hw) % 2
                    ps_D = ps_AV if psi == 0 else ps_OP
                    S.wait("tensor", "dv", inter_done[G])
                    S.wait("tensor", f"s_wd{wdbuf}", wd_sld[G])
                    idx = gi * 32 + tt * 4 + hw
                    if idx >= 2:
                        S.wait("tensor", "dv", accv_dv[idx - 2])
                    for c_l in range(8):
                        def mmd(t, ps_D=ps_D, wdbuf=wdbuf, c_l=c_l, tt=tt, hw=hw):
                            lhs = interT[:, c_l * TOWN + tt * 128:
                                         c_l * TOWN + tt * 128 + 128]
                            rhs = wd_f[wdbuf][:, c_l * E + hw * 512:
                                              c_l * E + hw * 512 + 512]
                            return t.matmul(ps_D[:], lhs, rhs,
                                            start=(c_l == 0), stop=(c_l == 7))
                        S.op("tensor", mmd, inc="pe" if c_l == 7 else None)
                    pe_d = S.cur("pe")
                    S.wait("vector", "pe", pe_d)
                    pidx = gi * 16 + tt * 2 + hw // 2
                    di = pidx % 2
                    if hw % 2 == 0 and pidx >= 2:
                        S.wait("vector", f"ga{di}", ga_vals[pidx - 2])
                    adv = S.op("vector", lambda v, di=di, hw=hw, ps_D=ps_D:
                               v.tensor_copy(
                        accv[di][:, (hw % 2) * 512:(hw % 2 + 1) * 512],
                        ps_D[:]))
                    accv_dv.append(adv)
                    if hw % 2 == 1:
                        S.wait("gpsimd", "dv", adv)
                        def ga_op(g, di=di, tt=tt, hw=hw, gi=gi):
                            dst = acc_d[tt * 128:(tt + 1) * 128,
                                        (hw // 2) * 1024:(hw // 2 + 1) * 1024]
                            if gi == 0:
                                return g.dma_start(out=dst, in_=accv[di][:])
                            return g.dma_start(out=dst, in_=accv[di][:],
                                               accum_op=ALU.add)
                        ga_vals.append(S.op("gpsimd", ga_op, inc=f"ga{di}",
                                            inc_n=16))
            down_pe_G[G] = S.cur("pe")

        S.wait("sync", "ga0", S.cur("ga0"))
        S.wait("sync", "ga1", S.cur("ga1"))
        y_val = 0
        r3_dv = {}
        for step in range(9):
            tt = step
            if tt < 8:
                bp = tt % 2
                if tt >= 2:
                    S.wait("sync", "dv", r3_dv[tt - 2])
                S.op("sync", lambda s, tt=tt, bp=bp: s.dma_start(
                    out=xo3B[bp][:], in_=x1_d[tt * 128:(tt + 1) * 128, :]),
                    inc=f"s_f3{bp}", inc_n=16)
                ld_a = S.op("sync", lambda s, tt=tt, bp=bp: s.dma_start(
                    out=accrB[bp][:], in_=acc_d[tt * 128:(tt + 1) * 128, :]),
                    inc=f"s_f3{bp}", inc_n=16)
                S.wait("vector", f"s_f3{bp}", ld_a)
                r3_dv[tt] = S.op("vector", lambda v, bp=bp: v.tensor_tensor(
                    resid3B[bp][:], xo3B[bp][:], accrB[bp][:], ALU.add))
            if step >= 1:
                lt = step - 1
                bp = lt % 2
                S.op("vector", lambda v, bp=bp: v.reduce_sum(
                    sm[:, 0:1], resid3B[bp][:], AX.X, negate=True))
                nm3 = S.op("vector", lambda v: v.tensor_scalar(
                    sm[:, 1:2], sm[:, 0:1], 1.0 / E, None, ALU.mult))
                S.wait("scalar", "dv", nm3)
                sq3 = S.op("scalar", lambda s, bp=bp: s.activation(
                    sqf[:], resid3B[bp][:], AF.Square, accum_out=sm[:, 2:3]))
                S.wait("vector", "ac", sq3)
                S.op("vector", lambda v: v.tensor_scalar(
                    sm[:, 3:4], sm[:, 2:3], 1.0 / E, None, ALU.mult))
                S.op("vector", lambda v: v.tensor_tensor(
                    sm[:, 4:5], sm[:, 1:2], sm[:, 1:2], ALU.mult))
                S.op("vector", lambda v: v.tensor_tensor(
                    sm[:, 5:6], sm[:, 3:4], sm[:, 4:5], ALU.subtract))
                S.op("vector", lambda v: v.tensor_scalar(
                    sm[:, 6:7], sm[:, 5:6], EPS, None, ALU.add))
                rv3 = S.op("vector", lambda v: v.reciprocal(sm[:, 7:8],
                                                            sm[:, 6:7]))
                S.wait("scalar", "dv", rv3)
                rstd3 = S.op("scalar", lambda s: s.activation(
                    sm[:, 8:9], sm[:, 7:8], AF.Sqrt))
                S.wait("vector", "ac", rstd3)
                S.wait("vector", "s_y", y_val)
                y_dv = S.op("vector", lambda v, bp=bp: v.tensor_scalar(
                    ytil[:], resid3B[bp][:], sm[:, 1:2], sm[:, 8:9],
                    ALU.add, ALU.mult))
                S.wait("sync", "dv", y_dv)
                y_val = S.op("sync", lambda s, lt=lt: s.dma_start(
                    out=y_out[lt * 128:(lt + 1) * 128, :], in_=ytil[:]),
                    inc="s_y", inc_n=16)
        S.wait("sync", "s_y", y_val)

        block = ctx.enter_context(nc.Block())
        S.run(block)
    return nc


_BUILD_CACHE = {}


def _cached(name, builder):
    if name not in _BUILD_CACHE:
        _BUILD_CACHE[name] = builder()
    return _BUILD_CACHE[name]


USE_MEGA = True


def kernel(**inputs):
    dec_maps, attn_maps, X = host_prep_inputs(**inputs)

    if USE_MEGA:
        nc = _cached("mega", build_mega)
        maps = [dict(**dec_maps[r], **attn_maps[r]) for r in range(NCORES)]
        res = run_bass_kernel_spmd(nc, maps, core_ids=list(range(NCORES)))
        out = np.concatenate([np.asarray(res.results[r]["y_out"])
                              for r in range(NCORES)], axis=0)
        return out.reshape(4, 2048, E).astype(np.float32)

    nc_dec = _cached("dec", build_decode_nc)
    res_dec = run_bass_kernel_spmd(nc_dec, dec_maps, core_ids=list(range(NCORES)))
    wg_all = np.concatenate([np.asarray(res_dec.results[r]["wg_dec"])
                             for r in range(NCORES)], axis=0)
    wu_all = np.concatenate([np.asarray(res_dec.results[r]["wu_dec"])
                             for r in range(NCORES)], axis=0)
    wd_all = np.concatenate([np.asarray(res_dec.results[r]["wd_dec"])
                             for r in range(NCORES)], axis=0)

    nc_attn = _cached("attn", build_attn_v0)
    res_attn = run_bass_kernel_spmd(nc_attn, attn_maps, core_ids=list(range(NCORES)))
    x1_parts = [np.asarray(res_attn.results[r]["x1_out"]) for r in range(NCORES)]

    nc_ffn = _cached("ffn", build_ffn_v0)
    ffn_maps = [dict(x1_in=x1_parts[r], wg_all=wg_all, wu_all=wu_all,
                     wd_all=wd_all) for r in range(NCORES)]
    res_ffn = run_bass_kernel_spmd(nc_ffn, ffn_maps, core_ids=list(range(NCORES)))
    out = np.concatenate([np.asarray(res_ffn.results[r]["y_out"])
                          for r in range(NCORES)], axis=0)
    return out.reshape(4, 2048, E).astype(np.float32)


# version 13
# speedup vs baseline: 1.2350x; 1.0793x over previous
import numpy as np
import ml_dtypes
import concourse.bass as bass
import concourse.mybir as mybir
from concourse.bass_utils import run_bass_kernel_spmd

F32 = mybir.dt.float32
BF16 = mybir.dt.bfloat16
I32 = mybir.dt.int32
U8 = mybir.dt.uint8
AF = mybir.ActivationFunctionType
ALU = mybir.AluOpType
AX = mybir.AxisListType

KB = 1024
NCORES = 8
E = 2048
NH = 16
HD = 128
TOWN = 1024
TLOC = 1536
MLP = 16384
QK_SCALE = float(1.0 / np.sqrt(np.float32(128)))
EPS = 1e-5
EXP_SHIFT = -4.0

LUT = np.array([0.0, 0.5, 1.0, 1.5, 2.0, 3.0, 4.0, 6.0,
                0.0, -0.5, -1.0, -1.5, -2.0, -3.0, -4.0, -6.0], np.float32)


class Sched:
    ENGINES = ("tensor", "vector", "scalar", "sync", "gpsimd")

    def __init__(self, nc):
        self.nc = nc
        self.q = {k: [] for k in self.ENGINES}
        self.sems = {}

    def new_sem(self, ctx, name):
        h = ctx.enter_context(self.nc.semaphore(name))
        self.sems[name] = [h, 0]

    def cur(self, name):
        return self.sems[name][1]

    AUTO = {"vector": "dv", "scalar": "ac", "gpsimd": "gp"}

    def op(self, eng, fn, inc=None, inc_n=1):
        auto = self.AUTO.get(eng)
        if auto is not None and (inc is None or inc == auto):
            rec = self.sems[auto]
            self.wait(eng, auto, rec[1])
            rec[1] += 1
            h = rec[0]
            self.q[eng].append(lambda e, fn=fn, h=h: fn(e).then_inc(h, 1))
            return rec[1]
        if inc is None:
            self.q[eng].append(fn)
            return None
        rec = self.sems[inc]
        rec[1] += inc_n
        h = rec[0]

        def wrapped(e, fn=fn, h=h, n=inc_n):
            fn(e).then_inc(h, n)
        self.q[eng].append(wrapped)
        return rec[1]

    def wait(self, eng, sem, val):
        h = self.sems[sem][0]
        self.q[eng].append(lambda e, h=h, v=val: e.wait_ge(h, v))

    def run(self, block):
        for name in self.ENGINES:
            ops = self.q[name]
            getattr(block, name)(lambda e, ops=ops: [f(e) for f in ops])


class Arena:

    def __init__(self, nc, total_bytes):
        base = nc._sbuf_addr_for_side("left")
        aligned = (base + 31) & ~31
        nc.alloc_sbuf_tensor("arena", [128, total_bytes + (aligned - base)], U8)
        self.nc = nc
        self.base = aligned
        self.total = total_bytes

    def at(self, name, off_bytes, shape, dtype):
        assert off_bytes % 32 == 0
        h = self.nc.alloc_sbuf_tensor_at(name, shape, dtype,
                                         offset=self.base + off_bytes)
        return h


DEC_NW = 1024


def build_decode_nc():
    from contextlib import ExitStack
    nc = bass.Bass("TRN2", target_bir_lowering=False, debug=False)
    pk_g = nc.dram_tensor("pk_g", [128, MLP], I32, kind="ExternalInput").ap()
    se_g = nc.dram_tensor("se_g", [128, MLP], F32, kind="ExternalInput").ap()
    pk_u = nc.dram_tensor("pk_u", [128, MLP], I32, kind="ExternalInput").ap()
    se_u = nc.dram_tensor("se_u", [128, MLP], F32, kind="ExternalInput").ap()
    pk_d = nc.dram_tensor("pk_d", [1024, E], I32, kind="ExternalInput").ap()
    se_d = nc.dram_tensor("se_d", [1024, E], F32, kind="ExternalInput").ap()
    wg_dec = nc.dram_tensor("wg_dec", [256, MLP], BF16, kind="ExternalOutput").ap()
    wu_dec = nc.dram_tensor("wu_dec", [256, MLP], BF16, kind="ExternalOutput").ap()
    wd_dec = nc.dram_tensor("wd_dec", [2048, E], BF16, kind="ExternalOutput").ap()

    jobs = []
    for pk, se, out in ((pk_g, se_g, wg_dec), (pk_u, se_u, wu_dec),
                        (pk_d, se_d, wd_dec)):
        nchunks = pk.shape[0] // 128
        nwin = pk.shape[1] // DEC_NW
        for q in range(nchunks):
            for wi in range(nwin):
                cs = slice(wi * DEC_NW, (wi + 1) * DEC_NW)
                jobs.append((pk[q * 128:(q + 1) * 128, cs],
                             se[q * 128:(q + 1) * 128, cs],
                             out[q * 256: q * 256 + 128, cs],
                             out[q * 256 + 128: q * 256 + 256, cs]))
    n = len(jobs)
    NW = DEC_NW

    with ExitStack() as ctx:
        e = ctx.enter_context
        w_i = [e(nc.sbuf_tensor(f"w_i{p}", [128, NW], I32)) for p in range(2)]
        se_s = [e(nc.sbuf_tensor(f"se_s{p}", [128, NW], F32)) for p in range(2)]
        hs = [e(nc.sbuf_tensor(f"hs{p}", [128, NW], F32)) for p in range(2)]
        ci = [[e(nc.sbuf_tensor(f"ci{p}{pl}", [128, NW], I32)) for pl in range(2)]
              for p in range(2)]
        cb = [e(nc.sbuf_tensor(f"cb{p}", [128, NW], BF16)) for p in range(2)]
        gb = [e(nc.sbuf_tensor(f"gb{p}", [128, NW], BF16)) for p in range(2)]
        g8 = [e(nc.sbuf_tensor(f"g8{p}", [128, NW], BF16)) for p in range(2)]
        rr = [[e(nc.sbuf_tensor(f"rr{p}{pl}", [128, NW], BF16)) for pl in range(2)]
              for p in range(2)]
        ss = [[e(nc.sbuf_tensor(f"ss{p}{pl}", [128, NW], BF16)) for pl in range(2)]
              for p in range(2)]
        u1 = [[e(nc.sbuf_tensor(f"u1{p}{pl}", [128, NW], BF16)) for pl in range(2)]
              for p in range(2)]
        u2 = [[e(nc.sbuf_tensor(f"u2{p}{pl}", [128, NW], BF16)) for pl in range(2)]
              for p in range(2)]
        t1 = [[e(nc.sbuf_tensor(f"t1{p}{pl}", [128, NW], BF16)) for pl in range(2)]
              for p in range(2)]
        ob = [[e(nc.sbuf_tensor(f"ob{p}{pl}", [128, NW], BF16)) for pl in range(2)]
              for p in range(2)]
        bias4 = e(nc.sbuf_tensor("bias4", [128, 1], F32))
        bias12 = e(nc.sbuf_tensor("bias12", [128, 1], F32))

        S = Sched(nc)
        for s in ("ld0", "ld1", "st0", "st1", "dv", "ac", "gp"):
            S.new_sem(ctx, s)

        gp_bias = S.op("gpsimd", lambda g: g.memset(bias4[:], -4.0))
        gp_bias = S.op("gpsimd", lambda g: g.memset(bias12[:], -12.0))

        fin = {}
        rrv = {}
        uuv = {}
        obv = {}
        stv = {}
        ldv = {}

        for j in range(min(2, n)):
            S.op("sync", lambda s, j=j: s.dma_start(
                out=w_i[j % 2][:], in_=jobs[j][0]), inc=f"ld{j % 2}", inc_n=16)
            ldv[j] = S.op("sync", lambda s, j=j: s.dma_start(
                out=se_s[j % 2][:], in_=jobs[j][1]), inc=f"ld{j % 2}", inc_n=16)

        for i in range(n):
            p = i % 2
            S.wait("vector", f"ld{p}", ldv[i])
            S.op("vector", lambda v, p=p: v.tensor_scalar(
                ci[p][0][:], w_i[p][:], 15, None, ALU.bitwise_and))
            S.op("vector", lambda v, p=p: v.tensor_scalar(
                ci[p][1][:], w_i[p][:], 4, 15,
                ALU.logical_shift_right, ALU.bitwise_and))
            fin[i] = S.op("vector", lambda v, p=p: v.tensor_scalar(
                hs[p][:], se_s[p][:], 0.5, None, ALU.mult))
            for pl in range(2):
                if i >= 2:
                    S.wait("vector", "ac", uuv[(i - 2, pl)])
                S.op("vector", lambda v, p=p, pl=pl: v.tensor_copy(
                    cb[p][:], ci[p][pl][:]))
                S.op("vector", lambda v, p=p: v.tensor_scalar(
                    gb[p][:], cb[p][:], 7.5, None, ALU.is_ge))
                S.op("vector", lambda v, p=p: v.tensor_scalar(
                    g8[p][:], gb[p][:], 8.0, None, ALU.mult))
                S.op("vector", lambda v, p=p, pl=pl: v.tensor_scalar(
                    ss[p][pl][:], gb[p][:], -2.0, 1.0, ALU.mult, ALU.add))
                rrv[(i, pl)] = S.op("vector", lambda v, p=p, pl=pl: v.tensor_tensor(
                    rr[p][pl][:], cb[p][:], g8[p][:], ALU.subtract))
            for pl in range(2):
                S.wait("scalar", "dv", rrv[(i, pl)])
                if pl == 0 and i == 0:
                    S.wait("scalar", "gp", gp_bias)
                if i >= 2:
                    S.wait("scalar", "dv", obv[(i - 2, pl)])
                S.op("scalar", lambda s, p=p, pl=pl: s.activation(
                    u1[p][pl][:], rr[p][pl][:], AF.Relu, bias=bias4[:]))
                uuv[(i, pl)] = S.op("scalar", lambda s, p=p, pl=pl: s.activation(
                    u2[p][pl][:], rr[p][pl][:], AF.Relu, bias=bias12[:],
                    scale=2.0))
            for pl in range(2):
                S.wait("vector", "ac", uuv[(i, pl)])
                if i >= 2:
                    S.wait("vector", f"st{p}", stv[i - 2])
                S.op("vector", lambda v, p=p, pl=pl: v.tensor_tensor(
                    t1[p][pl][:], rr[p][pl][:], u1[p][pl][:], ALU.add))
                S.op("vector", lambda v, p=p, pl=pl: v.tensor_tensor(
                    cb[p][:], t1[p][pl][:], u2[p][pl][:], ALU.add))
                S.op("vector", lambda v, p=p, pl=pl: v.tensor_tensor(
                    t1[p][pl][:], cb[p][:], ss[p][pl][:], ALU.mult))
                obv[(i, pl)] = S.op("vector", lambda v, p=p, pl=pl: v.tensor_tensor(
                    ob[p][pl][:], t1[p][pl][:], hs[p][:], ALU.mult))
            S.wait("sync", "dv", obv[(i, 0)])
            S.op("sync", lambda s, i=i, p=p: s.dma_start(
                out=jobs[i][2], in_=ob[p][0][:]), inc=f"st{p}", inc_n=16)
            S.wait("sync", "dv", obv[(i, 1)])
            stv[i] = S.op("sync", lambda s, i=i, p=p: s.dma_start(
                out=jobs[i][3], in_=ob[p][1][:]), inc=f"st{p}", inc_n=16)
            if i + 2 < n:
                S.wait("sync", "dv", fin[i])
                S.op("sync", lambda s, i=i, p=p: s.dma_start(
                    out=w_i[p][:], in_=jobs[i + 2][0]), inc=f"ld{p}", inc_n=16)
                ldv[i + 2] = S.op("sync", lambda s, i=i, p=p: s.dma_start(
                    out=se_s[p][:], in_=jobs[i + 2][1]), inc=f"ld{p}", inc_n=16)
        S.wait("sync", "st0", S.cur("st0"))
        S.wait("sync", "st1", S.cur("st1"))

        block = ctx.enter_context(nc.Block())
        S.run(block)
    return nc


def build_attn_v0(dbg=False, dbg2=False):
    from contextlib import ExitStack
    nc = bass.Bass("TRN2", target_bir_lowering=False, debug=False)
    if dbg2:
        d_exp = nc.dram_tensor("d_exp", [128, 1024], BF16, kind="ExternalOutput").ap()
        d_pn = nc.dram_tensor("d_pn", [128, 1024], BF16, kind="ExternalOutput").ap()
        d_nrm = nc.dram_tensor("d_nrm", [128, 1024], BF16, kind="ExternalOutput").ap()
        d_pt = nc.dram_tensor("d_pt", [128, 4096], BF16, kind="ExternalOutput").ap()
        d_sm = nc.dram_tensor("d_sm", [128, 32], F32, kind="ExternalOutput").ap()
    if dbg:
        d_q = nc.dram_tensor("d_q", [128, NH * TOWN], BF16, kind="ExternalOutput").ap()
        d_k = nc.dram_tensor("d_k", [128, NH * TLOC], BF16, kind="ExternalOutput").ap()
        d_v = nc.dram_tensor("d_v", [128, 12 * E], BF16, kind="ExternalOutput").ap()
        d_at = nc.dram_tensor("d_at", [128, 2 * NH * 512], BF16, kind="ExternalOutput").ap()
    xt_in = nc.dram_tensor("xt_in", [E, TLOC], F32, kind="ExternalInput").ap()
    x_own = nc.dram_tensor("x_own", [TOWN, E], F32, kind="ExternalInput").ap()
    wqkvT = nc.dram_tensor("wqkvT", [E, 3 * E], F32, kind="ExternalInput").ap()
    woutT = nc.dram_tensor("woutT", [E, E], F32, kind="ExternalInput").ap()
    mask_in = nc.dram_tensor("mask_in", [128, 8 * 1024], BF16,
                             kind="ExternalInput").ap()
    x1_out = nc.dram_tensor("x1_out", [TOWN, E], F32, kind="ExternalOutput").ap()

    with ExitStack() as ctx:
        ar = Arena(nc, 204 * KB)
        kT = ar.at("kT", 0, [128, NH * TLOC], BF16)
        vv = ar.at("vv", 48 * KB, [128, 12 * E], BF16)
        qT = ar.at("qT", 96 * KB, [128, NH * TOWN], BF16)
        xt = ar.at("xt", 128 * KB, [128, 16 * TLOC], BF16)
        wq_sl = [ar.at(f"wq{p}", (176 + 4 * p) * KB, [128, 16 * 128], BF16)
                 for p in range(2)]
        wv_sl = ar.at("wv", 184 * KB, [128, 16 * 512], BF16)
        masks = ar.at("masks", 128 * KB, [128, 8 * 1024], BF16)
        probs = [ar.at(f"pb{p}", (144 + 2 * p) * KB, [128, 1024], BF16)
                 for p in range(2)]
        probsn = ar.at("pn", 148 * KB, [128, 1024], BF16)
        probsT = ar.at("probsT", 150 * KB, [128, 8 * 512], BF16)
        attnT = ar.at("attnT", 158 * KB, [128, 2 * NH * 512], BF16)
        wo = ar.at("wo", 0, [128, 16 * E], BF16)
        xo = ar.at("xo", 64 * KB, [128, E], F32)
        resid = ar.at("resid", 72 * KB, [128, E], F32)
        x1til = ar.at("x1til", 80 * KB, [128, E], F32)
        sqt = ar.at("sqt", 88 * KB, [128, E], F32)
        sm = ar.at("sm", 200 * KB, [128, 32], F32)
        ident = ar.at("ident", 201 * KB, [128, 128], BF16)
        shiftB = ar.at("shiftB", int(201.5 * KB), [128, 1], F32)
        trash = ar.at("trash", 202 * KB, [128, 1024], BF16)

        S = Sched(nc)
        for s in ("s_xt", "s_wq0", "s_wq1", "s_wv", "s_wo",
                  "sld", "pe", "dv", "ac", "gp", "st"):
            S.new_sem(ctx, s)
        ps_S0 = ctx.enter_context(nc.psum_tensor("S0", [128, 1024], F32))
        ps_S1 = ctx.enter_context(nc.psum_tensor("S1", [128, 1024], F32))
        ps_TP = [ctx.enter_context(nc.psum_tensor(f"TP{i}", [128, 1024], BF16))
                 for i in range(2)]
        ps_AV = ctx.enter_context(nc.psum_tensor("AVP", [128, 512], F32))
        ps_OP = ctx.enter_context(nc.psum_tensor("OPP", [128, 512], F32))
        surfs = [ps_S0[:, 0:512], ps_S0[:, 512:1024],
                 ps_S1[:, 0:512], ps_S1[:, 512:1024]]

        m1 = S.op("gpsimd", lambda g: g.memset(ident[:], 0.0), inc="gp")
        S.wait("gpsimd", "gp", m1)
        id_done = S.op("gpsimd", lambda g: g.affine_select(
            out=ident[:], in_=ident[:], compare_op=ALU.not_equal, fill=1.0,
            base=0, pattern=[[-1, 128]], channel_multiplier=1), inc="gp")
        c_init = S.op("gpsimd", lambda g: g.memset(shiftB[:], EXP_SHIFT),
                      inc="gp")

        xt_src = xt_in.rearrange("(c p) t -> p c t", p=128)
        xt_ld = S.op("gpsimd", lambda g: g.dma_start(
            out=xt[:].rearrange("p (c t) -> p c t", t=TLOC), in_=xt_src),
            inc="s_xt", inc_n=16)

        wsrc = wqkvT.rearrange("(c p) f -> p c f", p=128)
        group_done_dv = []
        group_idx = 0
        slab_pe_after = {}
        pe_val = None

        qk_jobs = []
        for ft in range(32):
            wins = ((512, 0), (1024, 512)) if ft < 16 else \
                   ((0, 0), (512, 512), (1024, 1024))
            qk_jobs.append((ft, wins))

        for ji, (ft, wins) in enumerate(qk_jobs):
            p = ft % 2
            if ji >= 2:
                S.wait("gpsimd", "pe", slab_pe_after[qk_jobs[ji - 2][0]])
            fsl = slice(ft * 128, (ft + 1) * 128)
            gld_slab = S.op("gpsimd", lambda g, p=p, fsl=fsl: g.dma_start(
                out=wq_sl[p][:].rearrange("p (c f) -> p c f", f=128),
                in_=wsrc[:, :, fsl]), inc=f"s_wq{p}", inc_n=16)
            for off, qoff in wins:
                surf = surfs[group_idx % 4]
                if ji == 0:
                    S.wait("tensor", "s_xt", xt_ld)
                S.wait("tensor", f"s_wq{p}", gld_slab)
                if group_idx >= 4:
                    S.wait("tensor", "dv", group_done_dv[group_idx - 4])
                for ec in range(16):
                    def mm(t, surf=surf, p=p, ec=ec, off=off):
                        return t.matmul(
                            surf, wq_sl[p][:, ec * 128:(ec + 1) * 128],
                            xt[:, ec * TLOC + off: ec * TLOC + off + 512],
                            start=(ec == 0), stop=(ec == 15))
                    pe_val = S.op("tensor", mm, inc="pe") if ec == 15 \
                        else S.op("tensor", mm)
                S.wait("vector", "pe", S.cur("pe"))
                if ft < 16:
                    dst = qT[:, ft * TOWN + qoff: ft * TOWN + qoff + 512]
                    dvv = S.op("vector", lambda v, dst=dst, surf=surf:
                               v.tensor_scalar(dst, surf, QK_SCALE, None,
                                               ALU.mult), inc="dv")
                else:
                    dst = kT[:, (ft - 16) * TLOC + off:
                             (ft - 16) * TLOC + off + 512]
                    dvv = S.op("vector", lambda v, dst=dst, surf=surf:
                               v.tensor_copy(dst, surf), inc="dv")
                group_done_dv.append(dvv)
                group_idx += 1
            slab_pe_after[ft] = S.cur("pe")

        v_pe_after = {}
        for fw in range(4):
            if fw >= 1:
                S.wait("gpsimd", "pe", v_pe_after[fw - 1])
            fsl = slice(2 * E + fw * 512, 2 * E + (fw + 1) * 512)
            gld_v = S.op("gpsimd", lambda g, fsl=fsl: g.dma_start(
                out=wv_sl[:].rearrange("p (c f) -> p c f", f=512),
                in_=wsrc[:, :, fsl]), inc="s_wv", inc_n=16)
            for tt in range(12):
                surf = surfs[group_idx % 4]
                S.wait("tensor", "s_wv", gld_v)
                if group_idx >= 4:
                    S.wait("tensor", "dv", group_done_dv[group_idx - 4])
                for ec in range(16):
                    def mmv(t, surf=surf, ec=ec, tt=tt):
                        return t.matmul(
                            surf,
                            xt[:, ec * TLOC + tt * 128: ec * TLOC + (tt + 1) * 128],
                            wv_sl[:, ec * 512:(ec + 1) * 512],
                            start=(ec == 0), stop=(ec == 15))
                    S.op("tensor", mmv, inc="pe" if ec == 15 else None)
                S.wait("vector", "pe", S.cur("pe"))
                dst = vv[:, tt * E + fw * 512: tt * E + (fw + 1) * 512]
                dvv = S.op("vector", lambda v, dst=dst, surf=surf:
                           v.tensor_copy(dst, surf), inc="dv")
                group_done_dv.append(dvv)
                group_idx += 1
            v_pe_after[fw] = S.cur("pe")

        qkv_done_dv = group_done_dv[-1]
        qkv_done_pe = S.cur("pe")

        S.wait("sync", "pe", qkv_done_pe)
        if dbg:
            S.wait("sync", "dv", qkv_done_dv)
            S.op("sync", lambda s: s.dma_start(out=d_q, in_=qT[:]),
                 inc="st", inc_n=16)
            S.op("sync", lambda s: s.dma_start(out=d_k, in_=kT[:]),
                 inc="st", inc_n=16)
            S.op("sync", lambda s: s.dma_start(out=d_v, in_=vv[:]),
                 inc="st", inc_n=16)
            S.sems["st"][1] += 0
        ld_mask = S.op("sync", lambda s: s.dma_start(out=masks[:], in_=mask_in),
                       inc="sld", inc_n=16)

        S.wait("scalar", "gp", c_init)
        S.wait("tensor", "gp", id_done)
        S.wait("vector", "sld", ld_mask)

        exp_ac, tr_pe, ptk_dv, av_pe, att_dv, sum_gp = {}, {}, {}, {}, {}, {}
        dbg_gates = {}
        last_scores_pe = None
        chs = [(c, h) for h in range(15, -1, -1) for c in range(2)]
        for ci_, (c, h) in enumerate(chs):
            for k in range(4):
                Sk = ps_S0 if k % 2 == 0 else ps_S1
                if ci_ == 0 and k == 0:
                    S.wait("tensor", "dv", qkv_done_dv)
                prev = (c, h, k - 2) if k >= 2 else \
                    (chs[ci_ - 1] + (k + 2,) if ci_ > 0 else None)
                if prev in exp_ac:
                    S.wait("tensor", "ac", exp_ac[prev])
                q0 = h * TOWN + c * 512 + k * 128
                k0 = h * TLOC + c * 512
                S.op("tensor", lambda t, Sk=Sk, q0=q0, k0=k0: t.matmul(
                    Sk[:, 0:512], qT[:, q0:q0 + 128], kT[:, k0:k0 + 512],
                    start=True, stop=True))
                pe_sc = S.op("tensor", lambda t, Sk=Sk, q0=q0, k0=k0: t.matmul(
                    Sk[:, 512:1024], qT[:, q0:q0 + 128],
                    kT[:, k0 + 512:k0 + 1024], start=True, stop=True), inc="pe")
                last_scores_pe = pe_sc
                S.wait("scalar", "pe", pe_sc)
                if prev in tr_pe:
                    S.wait("scalar", "pe", tr_pe[prev])
                if dbg2 and (c, h, k) == (1, 0, 2):
                    S.wait("scalar", "st", dbg_gates["nrm"])
                exp_ac[(c, h, k)] = S.op("scalar", lambda s, Sk=Sk, k=k:
                                         s.activation(probs[k % 2][:], Sk[:],
                                                      AF.Exp, bias=shiftB[:]),
                                         inc="ac")
                if dbg2 and (c, h, k) == (1, 0, 0):
                    S.wait("sync", "ac", exp_ac[(c, h, k)])
                    dbg_gates["exp"] = S.op("sync", lambda s: s.dma_start(
                        out=d_exp, in_=probs[0][:]), inc="st", inc_n=16)
                S.wait("vector", "ac", exp_ac[(c, h, k)])
                if dbg2 and (c, h, k) == (1, 0, 1):
                    S.wait("vector", "st", dbg_gates["pn"])
                pk_prev = (c, h, k - 1) if k >= 1 else \
                    (chs[ci_ - 1] + (3,) if ci_ > 0 else None)
                if pk_prev in sum_gp:
                    S.wait("vector", "ac", sum_gp[pk_prev])
                mview = masks[:, (c * 4 + k) * 1024:(c * 4 + k + 1) * 1024]
                pm = S.op("vector", lambda v, k=k, mview=mview: v.tensor_tensor(
                    probsn[:], probs[k % 2][:], mview, ALU.mult), inc="dv")
                if dbg2 and (c, h, k) == (1, 0, 0):
                    S.wait("sync", "dv", pm)
                    dbg_gates["pn"] = S.op("sync", lambda s: s.dma_start(
                        out=d_pn, in_=probsn[:]), inc="st", inc_n=16)
                S.wait("scalar", "dv", pm)
                sum_gp[(c, h, k)] = S.op("scalar", lambda s, k=k: s.activation(
                    trash[:], probsn[:], AF.Identity,
                    accum_out=sm[:, k:k + 1]), inc="ac")
                S.wait("vector", "ac", sum_gp[(c, h, k)])
                S.op("vector", lambda v, k=k: v.reciprocal(sm[:, 4 + k:5 + k],
                                                           sm[:, k:k + 1]))
                if dbg2 and (c, h, k) == (1, 0, 0):
                    S.wait("vector", "st", dbg_gates["exp"])
                nrm = S.op("vector", lambda v, k=k: v.tensor_scalar(
                    probs[k % 2][:], probsn[:], sm[:, 4 + k:5 + k], None,
                    ALU.mult), inc="dv")
                if dbg2 and (c, h, k) == (1, 0, 0):
                    S.wait("sync", "dv", nrm)
                    S.op("sync", lambda s: s.dma_start(out=d_sm, in_=sm[:]),
                         inc="st", inc_n=16)
                    dbg_gates["nrm"] = S.op("sync", lambda s: s.dma_start(
                        out=d_nrm, in_=probs[0][:]), inc="st", inc_n=16)
                S.wait("tensor", "dv", nrm)
                if prev in ptk_dv:
                    S.wait("tensor", "dv", ptk_dv[prev])
                for m in range(8):
                    def trp(t, k=k, m=m):
                        return t.transpose(
                            ps_TP[k % 2][:, m * 128:(m + 1) * 128],
                            probs[k % 2][:, m * 128:(m + 1) * 128], ident[:])
                    if m == 7:
                        tr_pe[(c, h, k)] = S.op("tensor", trp, inc="pe")
                    else:
                        S.op("tensor", trp)
                S.wait("vector", "pe", tr_pe[(c, h, k)])
                if ci_ > 0 and k == 0:
                    S.wait("vector", "pe", av_pe[chs[ci_ - 1]])
                if dbg2 and (c, h, k) == (1, 1, 0):
                    S.wait("vector", "st", dbg_gates["pt"])
                def evt(v, k=k):
                    src = ps_TP[k % 2][:].rearrange("p (m f) -> p m f", f=128)
                    dst = probsT[:].rearrange("p (m f) -> p m f", f=512)[
                        :, :, k * 128:(k + 1) * 128]
                    return v.tensor_copy(dst, src)
                ptk_dv[(c, h, k)] = S.op("vector", evt, inc="dv")
            if dbg2 and (c, h) == (1, 0):
                S.wait("sync", "dv", ptk_dv[(c, h, 3)])
                dbg_gates["pt"] = S.op("sync", lambda s: s.dma_start(
                    out=d_pt, in_=probsT[:]), inc="st", inc_n=16)
            S.wait("tensor", "dv", ptk_dv[(c, h, 3)])
            if ci_ > 0:
                S.wait("tensor", "dv", att_dv[chs[ci_ - 1]])
            for m in range(8):
                def av(t, c=c, h=h, m=m):
                    return t.matmul(
                        ps_AV[:],
                        vv[:, (4 * c + m) * E + h * 128:
                           (4 * c + m) * E + (h + 1) * 128],
                        probsT[:, m * 512:(m + 1) * 512],
                        start=(m == 0), stop=(m == 7))
                if m == 7:
                    av_pe[(c, h)] = S.op("tensor", av, inc="pe")
                else:
                    S.op("tensor", av)
            S.wait("vector", "pe", av_pe[(c, h)])
            dstA = attnT[:, (c * 16 + h) * 512:(c * 16 + h + 1) * 512]
            att_dv[(c, h)] = S.op("vector", lambda v, dstA=dstA: v.tensor_copy(
                dstA, ps_AV[:]), inc="dv")

        S.wait("gpsimd", "pe", av_pe[chs[-1]])
        wo_src = woutT.rearrange("(c p) f -> p c f", p=128)
        ld_wo = S.op("gpsimd", lambda g: g.dma_start(
            out=wo[:].rearrange("p (c f) -> p c f", f=E), in_=wo_src),
            inc="s_wo", inc_n=16)

        S.wait("sync", "dv", att_dv[chs[-1]])
        op_dv = None
        ln_sq_ac = {}
        st_val = 0
        for tt in range(8):
            c = tt // 4
            if tt >= 1:
                S.wait("sync", "ac", ln_sq_ac[tt - 1])
            xo_ld = S.op("sync", lambda s, tt=tt: s.dma_start(
                out=xo[:], in_=x_own[tt * 128:(tt + 1) * 128, :]),
                inc="sld", inc_n=16)
            for fw in range(4):
                if (tt, fw) == (0, 0):
                    S.wait("tensor", "s_wo", ld_wo)
                    S.wait("tensor", "dv", att_dv[chs[-1]])
                else:
                    S.wait("tensor", "dv", op_dv)
                for ec in range(16):
                    def mo(t, c=c, tt=tt, fw=fw, ec=ec):
                        lhs = attnT[:, (c * 16 + ec) * 512 + (tt % 4) * 128:
                                    (c * 16 + ec) * 512 + (tt % 4) * 128 + 128]
                        rhs = wo[:, ec * E + fw * 512: ec * E + fw * 512 + 512]
                        return t.matmul(ps_OP[:], lhs, rhs,
                                        start=(ec == 0), stop=(ec == 15))
                    S.op("tensor", mo, inc="pe" if ec == 15 else None)
                pe_op = S.cur("pe")
                S.wait("vector", "pe", pe_op)
                S.wait("vector", "sld", xo_ld)
                def rp(v, fw=fw):
                    return v.tensor_tensor(resid[:, fw * 512:(fw + 1) * 512],
                                           ps_OP[:],
                                           xo[:, fw * 512:(fw + 1) * 512],
                                           ALU.add)
                op_dv = S.op("vector", rp, inc="dv")
            S.op("vector", lambda v: v.reduce_sum(sm[:, 8:9], resid[:], AX.X,
                                                  negate=True))
            nm_dv = S.op("vector", lambda v: v.tensor_scalar(
                sm[:, 9:10], sm[:, 8:9], 1.0 / E, None, ALU.mult), inc="dv")
            S.wait("scalar", "dv", nm_dv)
            ln_sq_ac[tt] = S.op("scalar", lambda s: s.activation(
                sqt[:], resid[:], AF.Square, accum_out=sm[:, 10:11]), inc="ac")
            S.wait("vector", "ac", ln_sq_ac[tt])
            S.op("vector", lambda v: v.tensor_scalar(
                sm[:, 11:12], sm[:, 10:11], 1.0 / E, None, ALU.mult))
            S.op("vector", lambda v: v.tensor_tensor(
                sm[:, 12:13], sm[:, 9:10], sm[:, 9:10], ALU.mult))
            S.op("vector", lambda v: v.tensor_tensor(
                sm[:, 13:14], sm[:, 11:12], sm[:, 12:13], ALU.subtract))
            S.op("vector", lambda v: v.tensor_scalar(
                sm[:, 14:15], sm[:, 13:14], EPS, None, ALU.add))
            rv_dv = S.op("vector", lambda v: v.reciprocal(
                sm[:, 15:16], sm[:, 14:15]), inc="dv")
            S.wait("scalar", "dv", rv_dv)
            rstd_ac = S.op("scalar", lambda s: s.activation(
                sm[:, 16:17], sm[:, 15:16], AF.Sqrt), inc="ac")
            S.wait("vector", "ac", rstd_ac)
            S.wait("vector", "st", st_val)
            x1_dv = S.op("vector", lambda v: v.tensor_scalar(
                x1til[:], resid[:], sm[:, 9:10], sm[:, 16:17],
                ALU.add, ALU.mult), inc="dv")
            S.wait("sync", "dv", x1_dv)
            st_val = S.op("sync", lambda s, tt=tt: s.dma_start(
                out=x1_out[tt * 128:(tt + 1) * 128, :], in_=x1til[:]),
                inc="st", inc_n=16)
        if dbg:
            S.op("sync", lambda s: s.dma_start(out=d_at, in_=attnT[:]),
                 inc="st", inc_n=16)
        S.wait("sync", "st", S.cur("st"))

        block = ctx.enter_context(nc.Block())
        S.run(block)
    return nc


def build_ffn_v0(skip_f2=False, skip_down=False, skip_pool=False,
                 skip_gelu=False, n_groups=16):
    from contextlib import ExitStack
    nc = bass.Bass("TRN2", target_bir_lowering=False, debug=False)
    x1_in = nc.dram_tensor("x1_in", [TOWN, E], F32, kind="ExternalInput").ap()
    wg_all = nc.dram_tensor("wg_all", [E, MLP], BF16, kind="ExternalInput").ap()
    wu_all = nc.dram_tensor("wu_all", [E, MLP], BF16, kind="ExternalInput").ap()
    wd_all = nc.dram_tensor("wd_all", [MLP, E], BF16, kind="ExternalInput").ap()
    y_out = nc.dram_tensor("y_out", [TOWN, E], F32, kind="ExternalOutput").ap()
    acc_d = nc.dram_tensor("acc_d", [TOWN, E], F32).ap()

    C1 = 0.7978845608028654
    C2 = C1 * 0.044715

    with ExitStack() as ctx:
        ar = Arena(nc, 200 * KB)
        x1T = ar.at("x1T", 0, [128, 16 * TOWN], BF16)
        interT = ar.at("interT", 32 * KB, [128, 8 * TOWN], BF16)
        wg_sl = [ar.at(f"wg{p}", (48 + 16 * p) * KB, [128, 16 * 512], BF16)
                 for p in range(2)]
        wu_sl = [ar.at(f"wu{p}", (80 + 16 * p) * KB, [128, 16 * 512], BF16)
                 for p in range(2)]
        wd_sl = [ar.at(f"wd{p}", (112 + 32 * p) * KB, [128, 8 * E], BF16)
                 for p in range(2)]
        gt = [[ar.at(f"gt{p}{i}", (176 + 8 * p + i) * KB, [128, 512], BF16)
               for i in range(8)] for p in range(2)]
        accv = [ar.at(f"accv{p}", (192 + 2 * p) * KB, [128, 512], F32)
                for p in range(2)]
        xo1 = ar.at("xo1", 112 * KB, [128, E], F32)
        x1b = ar.at("x1b", 120 * KB, [128, E], BF16)
        resid = ar.at("resid3", 112 * KB, [128, E], F32)
        accr = ar.at("accr", 120 * KB, [128, E], F32)
        ytil = ar.at("ytil", 128 * KB, [128, E], F32)
        sqf = ar.at("sqf", 136 * KB, [128, E], F32)
        xo3 = ar.at("xo3", 144 * KB, [128, E], F32)
        sm = ar.at("sm3", 196 * KB, [128, 32], F32)
        ident = ar.at("ident3", 197 * KB, [128, 128], BF16)

        S = Sched(nc)
        for s in ("s_x1", "s_wd0", "s_wd1", "s_gu0", "s_gu1", "s_f3",
                  "ga0", "ga1", "pe", "dv", "ac", "gp", "st"):
            S.new_sem(ctx, s)
        ps_S0 = ctx.enter_context(nc.psum_tensor("S0", [128, 1024], F32))
        ps_S1 = ctx.enter_context(nc.psum_tensor("S1", [128, 1024], F32))
        ps_TP = [ctx.enter_context(nc.psum_tensor(f"TP{i}", [128, 1024], BF16))
                 for i in range(2)]
        ps_D = [ctx.enter_context(nc.psum_tensor(f"D{i}", [128, 512], F32))
                for i in range(2)]
        gsurf = [ps_S0[:, 0:512], ps_S0[:, 512:1024]]
        usurf = [ps_S1[:, 0:512], ps_S1[:, 512:1024]]

        m1 = S.op("gpsimd", lambda g: g.memset(ident[:], 0.0), inc="gp")
        S.wait("gpsimd", "gp", m1)
        id_gp = S.op("gpsimd", lambda g: g.affine_select(
            out=ident[:], in_=ident[:], compare_op=ALU.not_equal, fill=1.0,
            base=0, pattern=[[-1, 128]], channel_multiplier=1), inc="gp")

        S.wait("tensor", "gp", id_gp)
        f1_evac = None
        cast_dv = {}
        for tt in range(8):
            if tt >= 1:
                S.wait("sync", "dv", cast_dv[tt - 1])
            xo_ld = S.op("sync", lambda s, tt=tt: s.dma_start(
                out=xo1[:], in_=x1_in[tt * 128:(tt + 1) * 128, :]),
                inc="s_x1", inc_n=16)
            S.wait("vector", "s_x1", xo_ld)
            if tt >= 1:
                S.wait("vector", "pe", tp_pe)
            cast_dv[tt] = S.op("vector", lambda v: v.tensor_copy(x1b[:], xo1[:]),
                               inc="dv")
            S.wait("tensor", "dv", cast_dv[tt])
            if tt >= 1:
                S.wait("tensor", "dv", f1_evac)
            for hc in range(16):
                def tr1(t, hc=hc):
                    base = 256 * (hc // 2) + (hc % 2)
                    src = x1b[:, base: base + 255: 2]
                    return t.transpose(
                        ps_TP[hc // 8][:, (hc % 8) * 128:(hc % 8 + 1) * 128],
                        src, ident[:])
                S.op("tensor", tr1, inc="pe" if hc == 15 else None)
            tp_pe = S.cur("pe")
            S.wait("vector", "pe", tp_pe)
            for half in range(2):
                def ev1(v, tt=tt, half=half):
                    src = ps_TP[half][:].rearrange("p (m f) -> p m f", f=128)
                    dst = x1T[:].rearrange("p (c t) -> p c t", t=TOWN)[
                        :, half * 8:(half + 1) * 8, tt * 128:(tt + 1) * 128]
                    return v.tensor_copy(dst, src)
                f1_evac = S.op("vector", ev1, inc="dv")
        f1_done_dv = f1_evac

        wgsrc = wg_all.rearrange("(c p) n -> p c n", p=128)
        wusrc = wu_all.rearrange("(c p) n -> p c n", p=128)
        g_idx = 0
        gelu_dv = []
        gelu_free = []
        ac_th_l = []
        slab_sld = {}
        slab_pe_done = {}
        wd_sld = {}
        down_pe_G = {}
        ga_vals = []
        accv_dv = []
        inter_done = {}

        for G in range(n_groups):
            if skip_f2:
                break
            if G < 2:
                S.wait("sync", "dv", f1_done_dv)
            elif not skip_down:
                S.wait("sync", "pe", down_pe_G[G - 2])
            lds = []
            if not skip_down:
                for c_l in range(8):
                    row0 = (8 * G + c_l) * 128
                    def ldwd(s, row0=row0, G=G, c_l=c_l):
                        return s.dma_start(
                            out=wd_sl[G % 2][:, c_l * E:(c_l + 1) * E],
                            in_=wd_all[row0:row0 + 128, :])
                    lds.append(S.op("sync", ldwd, inc=f"s_wd{G % 2}", inc_n=16))
                wd_sld[G] = lds[-1]
            for sw in range(2):
                if G >= 1:
                    S.wait("sync", "pe", slab_pe_done[(G - 1, sw)])
                nsl = slice(1024 * G + 512 * sw, 1024 * G + 512 * (sw + 1))
                S.op("sync", lambda s, sw=sw, nsl=nsl: s.dma_start(
                    out=wg_sl[sw][:].rearrange("p (c n) -> p c n", n=512),
                    in_=wgsrc[:, :, nsl]), inc=f"s_gu{sw}", inc_n=16)
                slab_sld[(G, sw)] = S.op("sync", lambda s, sw=sw, nsl=nsl:
                                         s.dma_start(
                    out=wu_sl[sw][:].rearrange("p (c n) -> p c n", n=512),
                    in_=wusrc[:, :, nsl]), inc=f"s_gu{sw}", inc_n=16)
            for c_l in range(8):
                sw = c_l // 4
                woff = 256 * ((c_l % 4) // 2)
                sub = c_l % 2
                for tw in range(2):
                    gs = gsurf[g_idx % 2]
                    us = usurf[g_idx % 2]
                    S.wait("tensor", f"s_gu{sw}", slab_sld[(G, sw)])
                    if g_idx == 0:
                        S.wait("tensor", "dv", f1_done_dv)
                    if g_idx >= 2:
                        ac_g2, dv_ub = gelu_free[g_idx - 2]
                        S.wait("tensor", "ac", ac_g2)
                        S.wait("tensor", "dv", dv_ub)
                    for wgt, surf in ((wg_sl[sw], gs), (wu_sl[sw], us)):
                        for hc in range(16):
                            def mmg(t, wgt=wgt, surf=surf, hc=hc, woff=woff,
                                    sub=sub, tw=tw):
                                lhs = wgt[:, hc * 512 + woff + sub:
                                          hc * 512 + woff + sub + 255: 2]
                                rhs = x1T[:, hc * TOWN + tw * 512:
                                          hc * TOWN + tw * 512 + 512]
                                return t.matmul(surf, lhs, rhs,
                                                start=(hc == 0), stop=(hc == 15))
                            S.op("tensor", mmg,
                                 inc="pe" if hc == 15 else None)
                    pe_gu = S.cur("pe")
                    if skip_gelu:
                        S.wait("vector", "pe", pe_gu)
                        dst_i = interT[:, c_l * TOWN + tw * 512:
                                       c_l * TOWN + tw * 512 + 512]
                        dv_int = S.op("vector", lambda v, dst_i=dst_i, gs=gs:
                                      v.tensor_copy(dst_i, gs))
                        dv_ub = S.op("vector", lambda v, us=us:
                                     v.tensor_copy(gt[0][3][:], us))
                        gelu_dv.append(dv_int)
                        gelu_free.append((0, dv_ub))
                        gp_p_l.append(0); ac_th_l.append(0); gp_q_l.append(0)
                        dv_arg_l.append(dv_int)
                        slab_pe_done[(G, sw)] = pe_gu
                        g_idx += 1
                        continue
                    par = g_idx % 2
                    g2, pp, gb, ub, arg, th, gu, q8 = gt[par]
                    S.wait("vector", "pe", pe_gu)
                    dv_gb = S.op("vector", lambda v, gb=gb, gs=gs:
                                 v.tensor_copy(gb[:], gs))
                    dv_ub = S.op("vector", lambda v, ub=ub, us=us:
                                 v.tensor_copy(ub[:], us), inc="dv")
                    S.wait("scalar", "dv", dv_gb)
                    if g_idx >= 2:
                        S.wait("scalar", "gp", gp_p_l[g_idx - 2])
                        S.wait("scalar", "gp", gp_q_l[g_idx - 2])
                    ac_g2 = S.op("scalar", lambda s, g2=g2, gb=gb: s.activation(
                        g2[:], gb[:], AF.Square), inc="ac")
                    peng = "vector" if skip_pool else "gpsimd"
                    pchain = "dv" if skip_pool else "gp"
                    S.wait(peng, "ac", ac_g2)
                    if g_idx >= 2 and not skip_pool:
                        S.wait(peng, "dv", dv_arg_l[g_idx - 2])
                        S.wait(peng, "dv", gelu_dv[g_idx - 2])
                    gp_p = S.op(peng, lambda g, pp=pp, g2=g2: g.tensor_scalar(
                        pp[:], g2[:], C2, C1, ALU.mult, ALU.add), inc=pchain)
                    if not skip_pool:
                        S.wait("vector", "gp", gp_p)
                    if g_idx >= 2:
                        S.wait("vector", "ac", ac_th_l[g_idx - 2])
                    dv_arg = S.op("vector", lambda v, arg=arg, pp=pp, gb=gb:
                                  v.tensor_tensor(arg[:], pp[:], gb[:],
                                                  ALU.mult), inc="dv")
                    S.wait("scalar", "dv", dv_arg)
                    if g_idx >= 2:
                        S.wait("scalar", "gp", gp_q_l[g_idx - 2])
                    ac_th = S.op("scalar", lambda s, th=th, arg=arg: s.activation(
                        th[:], arg[:], AF.Tanh), inc="ac")
                    S.wait(peng, "ac", ac_th)
                    gp_q = S.op(peng, lambda g, q8=q8, th=th: g.tensor_scalar(
                        q8[:], th[:], 0.5, 0.5, ALU.mult, ALU.add), inc=pchain)
                    S.op("vector", lambda v, gu=gu, gb=gb, ub=ub:
                         v.tensor_tensor(gu[:], gb[:], ub[:], ALU.mult))
                    if not skip_pool:
                        S.wait("vector", "gp", gp_q)
                    dst_i = interT[:, c_l * TOWN + tw * 512:
                                   c_l * TOWN + tw * 512 + 512]
                    dv_int = S.op("vector", lambda v, dst_i=dst_i, q8=q8, gu=gu:
                                  v.tensor_tensor(dst_i, q8[:], gu[:],
                                                  ALU.mult), inc="dv")
                    gelu_dv.append(dv_int)
                    gelu_free.append((0, dv_ub))
                    gp_p_l.append(gp_p)
                    ac_th_l.append(ac_th)
                    gp_q_l.append(gp_q)
                    dv_arg_l.append(dv_arg)
                    slab_pe_done[(G, sw)] = pe_gu
                    g_idx += 1
            inter_done[G] = gelu_dv[-1]
            if skip_down:
                continue
            for tt in range(8):
                for hw in range(4):
                    di = (tt * 4 + hw) % 2
                    S.wait("tensor", "dv", inter_done[G])
                    S.wait("tensor", f"s_wd{G % 2}", wd_sld[G])
                    if (G, tt, hw) != (0, 0, 0):
                        idx = G * 32 + tt * 4 + hw
                        if idx >= 2:
                            S.wait("tensor", "dv", accv_dv[idx - 2])
                    for c_l in range(8):
                        def mmd(t, di=di, G=G, c_l=c_l, tt=tt, hw=hw):
                            lhs = interT[:, c_l * TOWN + tt * 128:
                                         c_l * TOWN + tt * 128 + 128]
                            rhs = wd_sl[G % 2][:, c_l * E + hw * 512:
                                               c_l * E + hw * 512 + 512]
                            return t.matmul(ps_D[di][:], lhs, rhs,
                                            start=(c_l == 0), stop=(c_l == 7))
                        S.op("tensor", mmd, inc="pe" if c_l == 7 else None)
                    pe_d = S.cur("pe")
                    S.wait("vector", "pe", pe_d)
                    idx = G * 32 + tt * 4 + hw
                    if idx >= 2:
                        S.wait("vector", f"ga{di}", ga_vals[idx - 2])
                    adv = S.op("vector", lambda v, di=di: v.tensor_copy(
                        accv[di][:], ps_D[di][:]), inc="dv")
                    accv_dv.append(adv)
                    S.wait("gpsimd", "dv", adv)
                    def ga_op(g, di=di, tt=tt, hw=hw, G=G):
                        dst = acc_d[tt * 128:(tt + 1) * 128,
                                    hw * 512:(hw + 1) * 512]
                        if G == 0:
                            return g.dma_start(out=dst, in_=accv[di][:])
                        return g.dma_start(out=dst, in_=accv[di][:],
                                           accum_op=ALU.add)
                    ga_vals.append(S.op("gpsimd", ga_op, inc=f"ga{di}",
                                        inc_n=16))
            down_pe_G[G] = S.cur("pe")

        S.wait("sync", "ga0", S.cur("ga0"))
        S.wait("sync", "ga1", S.cur("ga1"))
        st_val = 0
        sq_ac = {}
        for tt in range(8):
            if tt >= 1:
                S.wait("sync", "ac", sq_ac[tt - 1])
            ld_x = S.op("sync", lambda s, tt=tt: s.dma_start(
                out=xo3[:], in_=x1_in[tt * 128:(tt + 1) * 128, :]),
                inc="s_f3", inc_n=16)
            ld_a = S.op("sync", lambda s, tt=tt: s.dma_start(
                out=accr[:], in_=acc_d[tt * 128:(tt + 1) * 128, :]),
                inc="s_f3", inc_n=16)
            S.wait("vector", "s_f3", ld_a)
            S.op("vector", lambda v: v.tensor_tensor(resid[:], xo3[:], accr[:],
                                                     ALU.add))
            S.op("vector", lambda v: v.reduce_sum(sm[:, 0:1], resid[:], AX.X,
                                                  negate=True))
            nm_dv = S.op("vector", lambda v: v.tensor_scalar(
                sm[:, 1:2], sm[:, 0:1], 1.0 / E, None, ALU.mult), inc="dv")
            S.wait("scalar", "dv", nm_dv)
            sq_ac[tt] = S.op("scalar", lambda s: s.activation(
                sqf[:], resid[:], AF.Square, accum_out=sm[:, 2:3]), inc="ac")
            S.wait("vector", "ac", sq_ac[tt])
            S.op("vector", lambda v: v.tensor_scalar(
                sm[:, 3:4], sm[:, 2:3], 1.0 / E, None, ALU.mult))
            S.op("vector", lambda v: v.tensor_tensor(
                sm[:, 4:5], sm[:, 1:2], sm[:, 1:2], ALU.mult))
            S.op("vector", lambda v: v.tensor_tensor(
                sm[:, 5:6], sm[:, 3:4], sm[:, 4:5], ALU.subtract))
            S.op("vector", lambda v: v.tensor_scalar(
                sm[:, 6:7], sm[:, 5:6], EPS, None, ALU.add))
            rv_dv = S.op("vector", lambda v: v.reciprocal(
                sm[:, 7:8], sm[:, 6:7]), inc="dv")
            S.wait("scalar", "dv", rv_dv)
            rstd_ac = S.op("scalar", lambda s: s.activation(
                sm[:, 8:9], sm[:, 7:8], AF.Sqrt), inc="ac")
            S.wait("vector", "ac", rstd_ac)
            S.wait("vector", "st", st_val)
            y_dv = S.op("vector", lambda v: v.tensor_scalar(
                ytil[:], resid[:], sm[:, 1:2], sm[:, 8:9],
                ALU.add, ALU.mult), inc="dv")
            S.wait("sync", "dv", y_dv)
            st_val = S.op("sync", lambda s, tt=tt: s.dma_start(
                out=y_out[tt * 128:(tt + 1) * 128, :], in_=ytil[:]),
                inc="st", inc_n=16)
        S.wait("sync", "st", st_val)

        block = ctx.enter_context(nc.Block())
        S.run(block)
    return nc


def host_prep_inputs(L0_attn_x, L0_attn_w_qkv, L0_attn_w_out,
                     L0_ffn_gate_W_packed, L0_ffn_gate_scales,
                     L0_ffn_up_W_packed, L0_ffn_up_scales,
                     L0_ffn_down_W_packed, L0_ffn_down_scales):
    X = np.ascontiguousarray(np.asarray(L0_attn_x, np.float32).reshape(8192, E))
    XT = np.ascontiguousarray(X.T)
    XT_pad = np.zeros((E, 8192 + 512), np.float32)
    XT_pad[:, 512:] = XT
    wqkvT = np.ascontiguousarray(np.asarray(L0_attn_w_qkv, np.float32).T)
    woutT = np.ascontiguousarray(np.asarray(L0_attn_w_out, np.float32).T)

    def prep_pk_se(packed, scales, r, rows):
        pkT = np.ascontiguousarray(np.asarray(packed, np.int32).T)
        j0, j1 = r * rows, (r + 1) * rows
        sT = np.ascontiguousarray(np.asarray(scales, np.float32).T)
        se = np.repeat(sT[j0 // 16: j1 // 16], 16, axis=0).astype(np.float32)
        return np.ascontiguousarray(pkT[j0:j1]), np.ascontiguousarray(se)

    dec_maps, attn_maps = [], []
    for r in range(NCORES):
        pg, sg = prep_pk_se(L0_ffn_gate_W_packed, L0_ffn_gate_scales, r, 128)
        pu, su = prep_pk_se(L0_ffn_up_W_packed, L0_ffn_up_scales, r, 128)
        pd, sd = prep_pk_se(L0_ffn_down_W_packed, L0_ffn_down_scales, r, 1024)
        dec_maps.append(dict(pk_g=pg, se_g=sg, pk_u=pu, se_u=su, pk_d=pd, se_d=sd))

        xt_slice = np.ascontiguousarray(XT_pad[:, 1024 * r: 1024 * r + TLOC])
        x_own = np.ascontiguousarray(X[1024 * r: 1024 * (r + 1)])
        half = r % 2
        i = np.arange(128)
        j = np.arange(1024)
        mk0 = np.zeros((128, 4, 1024), np.float32)
        for k in range(4):
            pq = 1024 * half + 128 * k + i[:, None]
            pk_pos = 1024 * half - 512 + j[None, :]
            ok = (pk_pos >= pq - 512) & (pk_pos <= pq) & (pk_pos >= 0)
            mk0[:, k, :] = ok.astype(np.float32)
        mk0 = mk0.reshape(128, 4096).astype(ml_dtypes.bfloat16)
        u = np.arange(1408)
        d = u[None, :] - 384 - i[:, None]
        mband = ((d >= 0) & (d <= 512)).astype(ml_dtypes.bfloat16)
        attn_maps.append(dict(xt_in=xt_slice, x_own=x_own, wqkvT=wqkvT,
                              woutT=woutT, mk0_in=mk0, mband_in=mband))
    return dec_maps, attn_maps, X


DBG_TAPS = False
WO_PREFETCH = True
WO_CI = 32


def build_mega():
    from contextlib import ExitStack
    nc = bass.Bass("TRN2", target_bir_lowering=False, debug=False)
    xt_in = nc.dram_tensor("xt_in", [E, TLOC], F32, kind="ExternalInput").ap()
    x_own = nc.dram_tensor("x_own", [TOWN, E], F32, kind="ExternalInput").ap()
    wqkvT = nc.dram_tensor("wqkvT", [E, 3 * E], F32, kind="ExternalInput").ap()
    woutT = nc.dram_tensor("woutT", [E, E], F32, kind="ExternalInput").ap()
    mk0_in = nc.dram_tensor("mk0_in", [128, 4 * 1024], BF16,
                            kind="ExternalInput").ap()
    mband_in = nc.dram_tensor("mband_in", [128, 1408], BF16,
                              kind="ExternalInput").ap()
    pk_g = nc.dram_tensor("pk_g", [128, MLP], I32, kind="ExternalInput").ap()
    se_g = nc.dram_tensor("se_g", [128, MLP], F32, kind="ExternalInput").ap()
    pk_u = nc.dram_tensor("pk_u", [128, MLP], I32, kind="ExternalInput").ap()
    se_u = nc.dram_tensor("se_u", [128, MLP], F32, kind="ExternalInput").ap()
    pk_d = nc.dram_tensor("pk_d", [1024, E], I32, kind="ExternalInput").ap()
    se_d = nc.dram_tensor("se_d", [1024, E], F32, kind="ExternalInput").ap()
    y_out = nc.dram_tensor("y_out", [TOWN, E], F32, kind="ExternalOutput").ap()
    kT_d = nc.dram_tensor("kT_d", [128, NH * TLOC], BF16).ap()
    vv_d = nc.dram_tensor("vv_d", [128, 12 * E], BF16).ap()
    qT_d = nc.dram_tensor("qT_d", [128, NH * TOWN], BF16).ap()
    dbg_scr = nc.dram_tensor("dbg_scr", [128, 16, 2048], F32).ap()
    kindX = "ExternalOutput" if DBG_TAPS else "Internal"
    dbg_q = nc.dram_tensor("dbg_q", [128, NH * TOWN], BF16, kind=kindX).ap()
    kindQ = kindX
    qT_dbgd = nc.dram_tensor("qT_dbg", [128, NH * TOWN], BF16,
                             kind=kindQ).ap()
    x1_d = nc.dram_tensor("x1_d", [TOWN, E], F32, kind=kindX).ap()
    acc_d = nc.dram_tensor("acc_d", [TOWN, E], F32, kind=kindX).ap()
    cc_in_gh = [nc.dram_tensor(f"cc_in_g{h}", [256, MLP // 2], BF16)
                for h in range(2)]
    cc_out_gh = [nc.dram_tensor(f"cc_out_g{h}", [E, MLP // 2], BF16)
                 for h in range(2)]
    cc_in_uh = [nc.dram_tensor(f"cc_in_u{h}", [256, MLP // 2], BF16)
                for h in range(2)]
    cc_out_uh = [nc.dram_tensor(f"cc_out_u{h}", [E, MLP // 2], BF16)
                 for h in range(2)]
    cc_in_dh = [nc.dram_tensor(f"cc_in_d{h}", [1024, E], BF16)
                for h in range(2)]
    cc_out_dh = [nc.dram_tensor(f"cc_out_d{h}", [MLP // 2, E], BF16)
                 for h in range(2)]

    C1 = 0.7978845608028654
    C2 = C1 * 0.044715
    NW = 1024

    with ExitStack() as ctx:
        ar = Arena(nc, 204 * KB)
        xt = ar.at("xt", 0, [128, 16 * TLOC], BF16)
        wq_sl = [ar.at(f"wq{p}", (48 + 4 * p) * KB, [128, 16 * 128], BF16)
                 for p in range(2)]
        wv_sl = ar.at("wv", 56 * KB, [128, 16 * 512], BF16)
        stg = [ar.at(f"stg{p}", (72 + 2 * p) * KB, [128, 1024], BF16)
               for p in range(2)]
        dw_i = [ar.at(f"dw{p}", (80 + 4 * p) * KB, [128, NW], I32)
                for p in range(2)]
        dse = [ar.at(f"dse{p}", (88 + 4 * p) * KB, [128, NW], F32)
               for p in range(2)]
        dhs = [ar.at(f"dhs{p}", (96 + 4 * p) * KB, [128, NW], F32)
               for p in range(2)]
        dci = [[ar.at(f"dci{p}{pl}", (104 + 4 * (2 * p + pl)) * KB,
                      [128, NW], I32) for pl in range(2)] for p in range(2)]
        dcb = [ar.at(f"dcb{p}", (120 + 2 * p) * KB, [128, NW], BF16)
               for p in range(2)]
        dgb = [ar.at(f"dgb{p}", (124 + 2 * p) * KB, [128, NW], BF16)
               for p in range(2)]
        dg8 = [ar.at(f"dg8{p}", (128 + 2 * p) * KB, [128, NW], BF16)
               for p in range(2)]
        drr = [[ar.at(f"drr{p}{pl}", (132 + 2 * (2 * p + pl)) * KB,
                      [128, NW], BF16) for pl in range(2)] for p in range(2)]
        dss = [[ar.at(f"dss{p}{pl}", (140 + 2 * (2 * p + pl)) * KB,
                      [128, NW], BF16) for pl in range(2)] for p in range(2)]
        du1 = [[ar.at(f"du1{p}{pl}", (148 + 2 * (2 * p + pl)) * KB,
                      [128, NW], BF16) for pl in range(2)] for p in range(2)]
        du2 = [[ar.at(f"du2{p}{pl}", (156 + 2 * (2 * p + pl)) * KB,
                      [128, NW], BF16) for pl in range(2)] for p in range(2)]
        dt1 = [[ar.at(f"dt1{p}{pl}", (164 + 2 * (2 * p + pl)) * KB,
                      [128, NW], BF16) for pl in range(2)] for p in range(2)]
        dob = [[ar.at(f"dob{p}{pl}", (172 + 2 * (2 * p + pl)) * KB,
                      [128, NW], BF16) for pl in range(2)] for p in range(2)]
        dbias4 = ar.at("dbias4", 180 * KB, [128, 1], F32)
        dbias12 = ar.at("dbias12", int(180.5 * KB), [128, 1], F32)
        kT = ar.at("kT", 0, [128, NH * TLOC], BF16)
        vv = ar.at("vv", 48 * KB, [128, 12 * E], BF16)
        qT = ar.at("qT", 96 * KB, [128, NH * TOWN], BF16)
        mk0 = ar.at("mk0", 128 * KB, [128, 4 * 1024], BF16)
        mband = ar.at("mband", 136 * KB, [128, 1408], BF16)
        probs4 = [ar.at(f"pb{p}", (140 + 2 * p) * KB, [128, 1024], BF16)
                  for p in range(8)]
        probsn2 = [ar.at(f"pn{p}", (156 + 2 * p) * KB, [128, 1024], BF16)
                   for p in range(2)]
        probsT = ar.at("probsT", 160 * KB, [128, 8 * 512], BF16)
        attnT = ar.at("attnT", 168 * KB, [128, 2 * NH * 512], BF16)
        wo = ar.at("wo", 0, [128, 16 * E], BF16)
        xoB = [ar.at(f"xoB{p}", (64 + 8 * p) * KB, [128, E], F32)
               for p in range(2)]
        residB = [ar.at(f"residB{p}", (80 + 8 * p) * KB, [128, E], F32)
                  for p in range(2)]
        x1til = ar.at("x1til", 96 * KB, [128, E], F32)
        sqt = ar.at("sqt", 104 * KB, [128, E], F32)
        wof32 = [ar.at(f"wof{p}", (112 + 16 * p) * KB, [128, 8 * 512], F32)
                 for p in range(2)]
        x1T = ar.at("x1T", 0, [128, 16 * TOWN], BF16)
        interT = ar.at("interT", 32 * KB, [128, 8 * TOWN], BF16)
        wg_f = [ar.at(f"wgf{p}", (48 + 16 * p) * KB, [128, 16 * 512], BF16)
                for p in range(2)]
        wu_f = [ar.at(f"wuf{p}", (80 + 16 * p) * KB, [128, 16 * 512], BF16)
                for p in range(2)]
        wd_f = [ar.at(f"wdf{p}", (112 + 32 * p) * KB, [128, 8 * E], BF16)
                for p in range(2)]
        gt = [[ar.at(f"gt{p}{i}", (176 + 8 * p + i) * KB, [128, 512], BF16)
               for i in range(8)] for p in range(2)]
        accv = [ar.at(f"accv{p}", (192 + 4 * p) * KB, [128, 1024], F32)
                for p in range(2)]
        xo1 = ar.at("xo1", 112 * KB, [128, E], F32)
        x1b = ar.at("x1b", 120 * KB, [128, E], BF16)
        resid3B = [ar.at(f"resid3B{p}", (112 + 8 * p) * KB, [128, E], F32)
                   for p in range(2)]
        accrB = [ar.at(f"accrB{p}", (128 + 8 * p) * KB, [128, E], F32)
                 for p in range(2)]
        xo3B = [ar.at(f"xo3B{p}", (144 + 8 * p) * KB, [128, E], F32)
                for p in range(2)]
        ytil = ar.at("ytil", 160 * KB, [128, E], F32)
        sqf = ar.at("sqf", 168 * KB, [128, E], F32)
        sm = ar.at("sm", 202 * KB, [128, 32], F32)
        ident = ar.at("ident", int(202.5 * KB), [128, 128], BF16)
        shiftB = ar.at("shiftB", int(203 * KB), [128, 1], F32)

        S = Sched(nc)
        for s in ("s_xt", "s_wq0", "s_wq1", "s_wv", "s_wo",
                  "s_dl0", "s_dl1", "s_cs0", "s_cs1", "s_kv0", "s_kv1",
                  "s_cc0", "s_cc1", "s_cc2", "s_cc3", "s_cc4", "s_cc5",
                  "s_rl", "s_xo0", "s_xo1", "s_x1s",
                  "s_wd0", "s_wd1", "s_gu0", "s_gu1", "s_f1", "s_f30", "s_f31",
                  "ga0", "ga1", "s_y", "s_msk", "s_wo1",
                  "pe", "dv", "ac", "gp"):
            S.new_sem(ctx, s)
        ps_S0 = ctx.enter_context(nc.psum_tensor("S0", [128, 1024], F32))
        ps_S1 = ctx.enter_context(nc.psum_tensor("S1", [128, 1024], F32))
        ps_TP = [ctx.enter_context(nc.psum_tensor(f"TP{i}", [128, 1024], BF16))
                 for i in range(2)]
        ps_AV = ctx.enter_context(nc.psum_tensor("AVP", [128, 512], F32))
        ps_OP = ctx.enter_context(nc.psum_tensor("OPP", [128, 512], F32))
        surfs = [ps_S0[:, 0:512], ps_S0[:, 512:1024],
                 ps_S1[:, 0:512], ps_S1[:, 512:1024]]
        Sfull = [ps_S0, ps_S1]

        m1 = S.op("gpsimd", lambda g: g.memset(ident[:], 0.0))
        id_done = S.op("gpsimd", lambda g: g.affine_select(
            out=ident[:], in_=ident[:], compare_op=ALU.not_equal, fill=1.0,
            base=0, pattern=[[-1, 128]], channel_multiplier=1))
        S.op("gpsimd", lambda g: g.memset(dbias4[:], -4.0))
        c_init = S.op("gpsimd", lambda g: g.memset(dbias12[:], -12.0))
        c_shift = S.op("gpsimd", lambda g: g.memset(shiftB[:], EXP_SHIFT))

        xt_src = xt_in.rearrange("(c p) t -> p c t", p=128)
        xt_ld = S.op("gpsimd", lambda g: g.dma_start(
            out=xt[:].rearrange("p (c t) -> p c t", t=TLOC), in_=xt_src),
            inc="s_xt", inc_n=16)

        djobs = []
        for hf in range(2):
            for wi in range(hf, 16, 2):
                cs = slice(wi * NW, (wi + 1) * NW)
                dc = slice((wi // 2) * NW, (wi // 2 + 1) * NW)
                djobs.append((pk_g[:, cs], se_g[:, cs],
                              cc_in_gh[hf].ap()[0:128, dc],
                              cc_in_gh[hf].ap()[128:256, dc], None))
            djobs[-1] = djobs[-1][:4] + (3 * hf + 0,)
            for wi in range(hf, 16, 2):
                cs = slice(wi * NW, (wi + 1) * NW)
                dc = slice((wi // 2) * NW, (wi // 2 + 1) * NW)
                djobs.append((pk_u[:, cs], se_u[:, cs],
                              cc_in_uh[hf].ap()[0:128, dc],
                              cc_in_uh[hf].ap()[128:256, dc], None))
            djobs[-1] = djobs[-1][:4] + (3 * hf + 1,)
            for q in range(4 * hf, 4 * hf + 4):
                for wi in range(2):
                    cs = slice(wi * NW, (wi + 1) * NW)
                    ql = q - 4 * hf
                    djobs.append((pk_d[q * 128:(q + 1) * 128, cs],
                                  se_d[q * 128:(q + 1) * 128, cs],
                                  cc_in_dh[hf].ap()[ql * 256: ql * 256 + 128, cs],
                                  cc_in_dh[hf].ap()[ql * 256 + 128: ql * 256 + 256, cs],
                                  None))
            djobs[-1] = djobs[-1][:4] + (3 * hf + 2,)
        ndw = len(djobs)
        cc_ins = [cc_in_gh[0], cc_in_uh[0], cc_in_dh[0],
                  cc_in_gh[1], cc_in_uh[1], cc_in_dh[1]]
        cc_outs = [cc_out_gh[0], cc_out_uh[0], cc_out_dh[0],
                   cc_out_gh[1], cc_out_uh[1], cc_out_dh[1]]
        cc_done = {}
        cc_pending = []

        def flush_cc(lim=6):
            while cc_pending and cc_pending[0][0] < lim:
                cci, v0, v1 = cc_pending.pop(0)
                S.wait("gpsimd", "s_cs0", v0)
                S.wait("gpsimd", "s_cs1", v1)
                cc_done[cci] = S.op("gpsimd", lambda g, cci=cci:
                                    g.collective_compute(
                    "AllGather", ALU.bypass, replica_groups=[list(range(8))],
                    ins=[cc_ins[cci].ap().opt()],
                    outs=[cc_outs[cci].ap().opt()]),
                    inc=f"s_cc{cci}", inc_n=1)

        wsrc = wqkvT.rearrange("(c p) f -> p c f", p=128)
        dfin, drrv, duuv, dobv, dstv, dldv = {}, {}, {}, {}, {}, {}

        def emit_decode_loads(i):
            p = i % 2
            if i >= 2:
                S.wait("sync", "dv", dfin[i - 2])
            S.op("sync", lambda s, i=i, p=p: s.dma_start(
                out=dw_i[p][:], in_=djobs[i][0]), inc=f"s_dl{p}", inc_n=16)
            dldv[i] = S.op("sync", lambda s, i=i, p=p: s.dma_start(
                out=dse[p][:], in_=djobs[i][1]), inc=f"s_dl{p}", inc_n=16)

        def emit_decode_window(i):
            p = i % 2
            S.wait("vector", f"s_dl{p}", dldv[i])
            S.op("vector", lambda v, p=p: v.tensor_scalar(
                dci[p][0][:], dw_i[p][:], 15, None, ALU.bitwise_and))
            S.op("vector", lambda v, p=p: v.tensor_scalar(
                dci[p][1][:], dw_i[p][:], 4, 15,
                ALU.logical_shift_right, ALU.bitwise_and))
            dfin[i] = S.op("vector", lambda v, p=p: v.tensor_scalar(
                dhs[p][:], dse[p][:], 0.5, None, ALU.mult))
            for pl in range(2):
                if i >= 2:
                    S.wait("vector", "ac", duuv[(i - 2, pl)])
                S.op("vector", lambda v, p=p, pl=pl: v.tensor_copy(
                    dcb[p][:], dci[p][pl][:]))
                S.op("vector", lambda v, p=p: v.tensor_scalar(
                    dgb[p][:], dcb[p][:], 7.5, None, ALU.is_ge))
                S.op("vector", lambda v, p=p: v.tensor_scalar(
                    dg8[p][:], dgb[p][:], 8.0, None, ALU.mult))
                S.op("vector", lambda v, p=p, pl=pl: v.tensor_scalar(
                    dss[p][pl][:], dgb[p][:], -2.0, 1.0, ALU.mult, ALU.add))
                drrv[(i, pl)] = S.op("vector", lambda v, p=p, pl=pl:
                                     v.tensor_tensor(drr[p][pl][:], dcb[p][:],
                                                     dg8[p][:], ALU.subtract))
            for pl in range(2):
                S.wait("scalar", "dv", drrv[(i, pl)])
                if pl == 0 and i == 0:
                    S.wait("scalar", "gp", c_init)
                if i >= 2:
                    S.wait("scalar", "dv", dobv[(i - 2, pl)])
                S.op("scalar", lambda s, p=p, pl=pl: s.activation(
                    du1[p][pl][:], drr[p][pl][:], AF.Relu, bias=dbias4[:]))
                duuv[(i, pl)] = S.op("scalar", lambda s, p=p, pl=pl: s.activation(
                    du2[p][pl][:], drr[p][pl][:], AF.Relu, bias=dbias12[:],
                    scale=2.0))
            for pl in range(2):
                S.wait("vector", "ac", duuv[(i, pl)])
                if i >= 2:
                    S.wait("vector", f"s_cs{p}", dstv[i - 2])
                S.op("vector", lambda v, p=p, pl=pl: v.tensor_tensor(
                    dt1[p][pl][:], drr[p][pl][:], du1[p][pl][:], ALU.add))
                S.op("vector", lambda v, p=p, pl=pl: v.tensor_tensor(
                    dcb[p][:], dt1[p][pl][:], du2[p][pl][:], ALU.add))
                S.op("vector", lambda v, p=p, pl=pl: v.tensor_tensor(
                    dt1[p][pl][:], dcb[p][:], dss[p][pl][:], ALU.mult))
                dobv[(i, pl)] = S.op("vector", lambda v, p=p, pl=pl:
                                     v.tensor_tensor(dob[p][pl][:], dt1[p][pl][:],
                                                     dhs[p][:], ALU.mult))
            S.wait("sync", "dv", dobv[(i, 0)])
            S.op("sync", lambda s, i=i, p=p: s.dma_start(
                out=djobs[i][2], in_=dob[p][0][:]), inc=f"s_cs{p}", inc_n=16)
            S.wait("sync", "dv", dobv[(i, 1)])
            dstv[i] = S.op("sync", lambda s, i=i, p=p: s.dma_start(
                out=djobs[i][3], in_=dob[p][1][:]), inc=f"s_cs{p}", inc_n=16)
            cci = djobs[i][4]
            if cci is not None:
                cc_pending.append((cci, S.cur("s_cs0"), S.cur("s_cs1")))

        qkv_groups = []
        qk_jobs = []
        for ft in range(32):
            wins = ((512,), (1024,)) if ft < 16 else ((0,), (512,), (1024,))
            for (off,) in wins:
                qk_jobs.append((ft, off))
        v_jobs = [(fw, tt) for fw in range(4) for tt in range(12)]

        slab_emitted = {}
        slab_val = {}
        slab_pe_last = {}
        evac_ac = {}
        dstv_kv = {}

        def pair_dst_info(j):
            if j < 16:
                return qT_d[:, j * 1024:(j + 1) * 1024]
            if j < 40:
                jj = j - 16
                return kT_d[:, jj * 1024:(jj + 1) * 1024]
            jj = j - 40
            fw, tt0 = jj // 6, 2 * (jj % 6)
            i0 = tt0 * 4 + fw
            return vv_d.rearrange("p (t f) -> p t f", f=512)[
                :, i0:i0 + 5:4, :]

        n_groups = 128
        dec_emitted = 0
        for g in range(n_groups):
            pair = g // 2
            surf = surfs[g % 4]
            qpart = g < 32
            kpart = 32 <= g < 80
            if qpart or kpart:
                ft, off = qk_jobs[g]
                p = ft % 2
                if ft not in slab_emitted:
                    if ft >= 2:
                        S.wait("gpsimd", "pe", slab_pe_last[ft - 2])
                    fsl = slice(ft * 128, (ft + 1) * 128)
                    slab_val[ft] = S.op("gpsimd", lambda gg, p=p, fsl=fsl:
                                        gg.dma_start(
                        out=wq_sl[p][:].rearrange("p (c f) -> p c f", f=128),
                        in_=wsrc[:, :, fsl]), inc=f"s_wq{p}", inc_n=16)
                    slab_emitted[ft] = True
                if g == 0:
                    S.wait("tensor", "s_xt", xt_ld)
                S.wait("tensor", f"s_wq{p}", slab_val[ft])
                if pair >= 2:
                    S.wait("tensor", "dv", evac_ac[pair - 2])
                for ec in range(16):
                    def mm(t, surf=surf, p=p, ec=ec, off=off):
                        return t.matmul(
                            surf, wq_sl[p][:, ec * 128:(ec + 1) * 128],
                            xt[:, ec * TLOC + off: ec * TLOC + off + 512],
                            start=(ec == 0), stop=(ec == 15))
                    S.op("tensor", mm, inc="pe" if ec == 15 else None)
                slab_pe_last[ft] = S.cur("pe")
            else:
                fw, tt = v_jobs[g - 80]
                if tt == 0:
                    if fw >= 1:
                        S.wait("gpsimd", "pe", S.cur("pe"))
                    fsl = slice(2 * E + fw * 512, 2 * E + (fw + 1) * 512)
                    slab_val[("v", fw)] = S.op("gpsimd", lambda gg, fsl=fsl:
                                               gg.dma_start(
                        out=wv_sl[:].rearrange("p (c f) -> p c f", f=512),
                        in_=wsrc[:, :, fsl]), inc="s_wv", inc_n=16)
                S.wait("tensor", "s_wv", slab_val[("v", fw)])
                if pair >= 2:
                    S.wait("tensor", "dv", evac_ac[pair - 2])
                for ec in range(16):
                    def mmv(t, surf=surf, ec=ec, tt=tt):
                        return t.matmul(
                            surf,
                            xt[:, ec * TLOC + tt * 128: ec * TLOC + (tt + 1) * 128],
                            wv_sl[:, ec * 512:(ec + 1) * 512],
                            start=(ec == 0), stop=(ec == 15))
                    S.op("tensor", mmv, inc="pe" if ec == 15 else None)
            if g % 2 == 1:
                pe_pair = S.cur("pe")
                sp = pair % 2
                S.wait("vector", "pe", pe_pair)
                if pair >= 2:
                    S.wait("vector", f"s_kv{sp}", dstv_kv.get(pair - 2, 0))
                scl = QK_SCALE if pair < 16 else 1.0
                evac_ac[pair] = S.op("vector", lambda v, sp=sp, scl=scl:
                                     v.tensor_scalar(stg[sp][:], Sfull[sp][:],
                                                     scl, None, ALU.mult))
                S.wait("sync", "dv", evac_ac[pair])
                dst = pair_dst_info(pair)
                if pair >= 40:
                    src = stg[sp][:].rearrange("p (t f) -> p t f", f=512)
                else:
                    src = stg[sp][:]
                dstv_kv[pair] = S.op("sync", lambda s, dst=dst, src=src:
                                     s.dma_start(out=dst, in_=src),
                                     inc=f"s_kv{sp}", inc_n=16)
            want_dec = min(ndw, (g + 1) * ndw // n_groups + 2)
            while dec_emitted < want_dec:
                i = dec_emitted
                if i < 2:
                    emit_decode_loads(i)
                emit_decode_window(i)
                if i + 2 < ndw:
                    emit_decode_loads(i + 2)
                dec_emitted += 1
            if g in (63, 120):
                flush_cc(3)
        while dec_emitted < ndw:
            i = dec_emitted
            if i < 2:
                emit_decode_loads(i)
            emit_decode_window(i)
            if i + 2 < ndw:
                emit_decode_loads(i + 2)
            dec_emitted += 1

        flush_cc(3)
        qkv_pe_last = S.cur("pe")
        dec_dv_last = S.cur("dv")
        last_evac_pair = 63

        S.wait("sync", "pe", qkv_pe_last)
        S.wait("sync", "dv", dec_dv_last)
        S.wait("sync", "s_cs0", S.cur("s_cs0"))
        S.wait("sync", "s_cs1", S.cur("s_cs1"))
        S.wait("sync", "s_kv0", S.cur("s_kv0"))
        S.wait("sync", "s_kv1", S.cur("s_kv1"))
        rl_k = S.op("sync", lambda s: s.dma_start(out=kT[:], in_=kT_d),
                    inc="s_rl", inc_n=16)
        S.wait("scalar", "pe", qkv_pe_last)
        S.wait("scalar", "dv", dec_dv_last)
        S.wait("scalar", "s_kv0", S.cur("s_kv0"))
        S.wait("scalar", "s_kv1", S.cur("s_kv1"))
        rl_v = S.op("scalar", lambda s: s.dma_start(out=vv[:], in_=vv_d),
                    inc="s_rl", inc_n=16)
        S.op("scalar", lambda s: s.dma_start(out=qT[:, 0:8 * TOWN],
                                             in_=qT_d[:, 0:8 * TOWN]),
             inc="s_rl", inc_n=16)
        rl_q = S.op("sync", lambda s: s.dma_start(out=qT[:, 8 * TOWN:],
                                                  in_=qT_d[:, 8 * TOWN:]),
                    inc="s_rl", inc_n=16)
        S.op("sync", lambda s: s.dma_start(out=mk0[:], in_=mk0_in),
             inc="s_msk", inc_n=16)
        rl_m = S.op("sync", lambda s: s.dma_start(out=mband[:], in_=mband_in),
                    inc="s_msk", inc_n=16)

        S.wait("gpsimd", "s_rl", rl_q)
        S.wait("gpsimd", "s_msk", rl_m)
        flush_cc()

        S.wait("scalar", "gp", c_shift)
        S.wait("tensor", "gp", id_done)
        S.wait("vector", "s_msk", rl_m)

        exp_ac, tr_pe, ptk_dv, av_pe, att_ac, nrm_dv = ({}, {}, {}, {},
                                                         {}, {})
        sum_tok = {}
        recip_dv = {}
        last_scores_pe = None
        chs = [(c, h) for c in range(2) for h in range(16)]

        def emit_softmax(ci_):
            c, h = chs[ci_]
            for k in range(4):
                Sk = ps_S0 if k % 2 == 0 else ps_S1
                slot = (ci_ % 2) * 4 + k
                pn = probsn2[k % 2]
                if ci_ == 0 and k == 0:
                    S.wait("tensor", "s_rl", rl_q)
                prev = (c, h, k - 2) if k >= 2 else \
                    (chs[ci_ - 1] + (k + 2,) if ci_ > 0 else None)
                if prev in exp_ac:
                    S.wait("tensor", "ac", exp_ac[prev])
                elif ci_ == 0:
                    S.wait("tensor", "dv", evac_ac[63 - (1 - k % 2)])
                q0 = h * TOWN + c * 512 + k * 128
                k0 = h * TLOC + c * 512
                S.op("tensor", lambda t, Sk=Sk, q0=q0, k0=k0: t.matmul(
                    Sk[:, 0:512], qT[:, q0:q0 + 128], kT[:, k0:k0 + 512],
                    start=True, stop=True))
                pe_sc = S.op("tensor", lambda t, Sk=Sk, q0=q0, k0=k0: t.matmul(
                    Sk[:, 512:1024], qT[:, q0:q0 + 128],
                    kT[:, k0 + 512:k0 + 1024], start=True, stop=True), inc="pe")
                S.wait("scalar", "pe", pe_sc)
                pslot_prev = (chs[ci_ - 2] + (k,)) if ci_ >= 2 else None
                if pslot_prev in tr_pe:
                    S.wait("scalar", "pe", tr_pe[pslot_prev])
                exp_ac[(c, h, k)] = S.op("scalar", lambda s, Sk=Sk, slot=slot:
                                         s.activation(probs4[slot][:], Sk[:],
                                                      AF.Exp, bias=shiftB[:]))
                S.wait("vector", "ac", exp_ac[(c, h, k)])
                pk_prev = (c, h, k - 2) if k >= 2 else \
                    (chs[ci_ - 1] + (k + 2,) if ci_ > 0 else None)
                if k % 2 == 0 and pk_prev in sum_tok:
                    S.wait("vector", "ac", sum_tok[pk_prev][1])
                pm = S.op("vector", lambda v, pn=pn, slot=slot,
                          mview=(mk0[:, k * 1024:(k + 1) * 1024] if c == 0 else
                                 mband[:, 384 - 128 * k: 1408 - 128 * k]):
                          v.tensor_tensor(pn[:], probs4[slot][:], mview,
                                          ALU.mult))
                rprev = (chs[ci_ - 1] + (k,)) if ci_ > 0 else None
                if k % 2 == 0:
                    S.wait("scalar", "dv", pm)
                    if rprev in recip_dv:
                        S.wait("scalar", "dv", recip_dv[rprev])
                    sum_tok[(c, h, k)] = ("scalar", S.op(
                        "scalar", lambda s, pn=pn, k=k, slot=slot: s.activation(
                            probs4[slot][:], pn[:], AF.Identity,
                            accum_out=sm[:, k:k + 1])))
                    S.wait("vector", "ac", sum_tok[(c, h, k)][1])
                else:
                    sum_tok[(c, h, k)] = ("vector", S.op(
                        "vector", lambda v, pn=pn, k=k: v.reduce_sum(
                            sm[:, k:k + 1], pn[:], AX.X)))
                recip_dv[(c, h, k)] = S.op(
                    "vector", lambda v, k=k: v.reciprocal(sm[:, 4 + k:5 + k],
                                                          sm[:, k:k + 1]))
                nrm_dv[(c, h, k)] = S.op("vector", lambda v, pn=pn, k=k,
                                         slot=slot: v.tensor_scalar(
                    probs4[slot][:], pn[:], sm[:, 4 + k:5 + k], None,
                    ALU.mult))

        def emit_trs_av(ci_):
            c, h = chs[ci_]
            for k in range(4):
                slot = (ci_ % 2) * 4 + k
                S.wait("tensor", "dv", nrm_dv[(c, h, k)])
                prev = (c, h, k - 2) if k >= 2 else \
                    (chs[ci_ - 1] + (k + 2,) if ci_ > 0 else None)
                if prev in ptk_dv:
                    S.wait("tensor", "dv", ptk_dv[prev])
                for m in range(8):
                    def trp(t, k=k, m=m, slot=slot):
                        return t.transpose(
                            ps_TP[k % 2][:, m * 128:(m + 1) * 128],
                            probs4[slot][:, m * 128:(m + 1) * 128], ident[:])
                    if m == 7:
                        tr_pe[(c, h, k)] = S.op("tensor", trp, inc="pe")
                    else:
                        S.op("tensor", trp)
                S.wait("vector", "pe", tr_pe[(c, h, k)])
                if ci_ > 0 and k == 0:
                    S.wait("vector", "pe", av_pe[chs[ci_ - 1]])
                def evt(v, k=k):
                    src = ps_TP[k % 2][:].rearrange("p (m f) -> p m f", f=128)
                    dst = probsT[:].rearrange("p (m f) -> p m f", f=512)[
                        :, :, k * 128:(k + 1) * 128]
                    return v.tensor_copy(dst, src)
                ptk_dv[(c, h, k)] = S.op("vector", evt)
            S.wait("tensor", "dv", ptk_dv[(c, h, 3)])
            if ci_ > 0:
                S.wait("tensor", "ac", att_ac[chs[ci_ - 1]])
            if ci_ == 0:
                S.wait("tensor", "s_rl", rl_q)
            for m in range(8):
                def av(t, c=c, h=h, m=m):
                    return t.matmul(
                        ps_AV[:],
                        vv[:, (4 * c + m) * E + h * 128:
                           (4 * c + m) * E + (h + 1) * 128],
                        probsT[:, m * 512:(m + 1) * 512],
                        start=(m == 0), stop=(m == 7))
                if m == 7:
                    av_pe[(c, h)] = S.op("tensor", av, inc="pe")
                else:
                    S.op("tensor", av)
            S.wait("scalar", "pe", av_pe[(c, h)])
            dstA = attnT[:, (c * 16 + h) * 512:(c * 16 + h + 1) * 512]
            att_ac[(c, h)] = S.op("scalar", lambda s, dstA=dstA: s.activation(
                dstA, ps_AV[:], AF.Identity))

        wsrc_f32 = woutT.rearrange("(c p) f -> p c f", p=128)
        ld_wo_r = {}
        for ci_ in range(33):
            if ci_ == WO_CI and WO_PREFETCH:
                pe16 = S.cur("pe")
                S.wait("sync", "pe", pe16)
                if WO_PREFETCH == 2:
                    ld_wo_r[0] = S.op("sync", lambda s: s.dma_start(
                        out=dbg_scr[:, 0:8, 0:512],
                        in_=wsrc_f32[:, 0:8, 0:512]), inc="s_wo", inc_n=16)
                elif WO_PREFETCH == 3:
                    S.op("sync", lambda s: s.dma_start(
                        out=wof32[0][:].rearrange("p (c f) -> p c f", f=512),
                        in_=wsrc_f32[:, 0:8, 0:512]), inc="s_wo", inc_n=16)
                else:
                    ld_wo_r[0] = S.op("sync", lambda s: s.dma_start(
                        out=wof32[0][:].rearrange("p (c f) -> p c f", f=512),
                        in_=wsrc_f32[:, 0:8, 0:512]), inc="s_wo", inc_n=16)
            if ci_ < 32:
                emit_softmax(ci_)
            if ci_ >= 1:
                emit_trs_av(ci_ - 1)
        last_scores_pe = S.cur("pe")

        pe_b1_end = av_pe[chs[-1]]
        dv_b1_end = S.cur("dv")
        ac_b1_end = S.cur("ac")
        cast_wo = {}

        def emit_ld_wo(r):
            ring = "scalar" if r % 2 else "sync"
            fw, hh = r // 2, r % 2
            if r <= 2:
                S.wait(ring, "pe", pe_b1_end)
                S.wait(ring, "dv", dv_b1_end)
                S.wait(ring, "ac", ac_b1_end)
                if r - 2 in cast_wo:
                    S.wait(ring, "dv", cast_wo[r - 2])
            else:
                S.wait(ring, "dv", cast_wo[r - 2])
            ld_wo_r[r] = S.op(ring, lambda e, fw=fw, hh=hh, r=r: e.dma_start(
                out=wof32[r % 2][:].rearrange("p (c f) -> p c f", f=512),
                in_=wsrc_f32[:, 8 * hh:8 * hh + 8, fw * 512:(fw + 1) * 512]),
                inc="s_wo" if r % 2 == 0 else "s_wo1", inc_n=16)

        def emit_cast_wo(r):
            fw, hh = r // 2, r % 2
            if r == 0:
                S.wait("vector", "pe", pe_b1_end)
            S.wait("vector", "s_wo" if r % 2 == 0 else "s_wo1",
                   ld_wo_r[r])
            cast_wo[r] = S.op("vector", lambda v, fw=fw, hh=hh, r=r:
                              v.tensor_copy(
                wo[:, fw * 8192 + hh * 4096: fw * 8192 + (hh + 1) * 4096],
                wof32[r % 2][:]))

        if WO_PREFETCH != 1:
            rr0_ = 0
            S.wait("sync", "pe", pe_b1_end)
            S.wait("sync", "dv", dv_b1_end)
            S.wait("sync", "ac", ac_b1_end)
            ld_wo_r[0] = S.op("sync", lambda s: s.dma_start(
                out=wof32[0][:].rearrange("p (c f) -> p c f", f=512),
                in_=wsrc_f32[:, 0:8, 0:512]), inc="s_wo", inc_n=16)
        if DBG_TAPS:
            S.wait("sync", "pe", pe_b1_end)
            S.op("sync", lambda s: s.dma_start(out=dbg_q, in_=qT[:]),
                 inc="s_y", inc_n=16)
            S.op("sync", lambda s: s.dma_start(out=qT_dbgd, in_=qT_d),
                 inc="s_y", inc_n=16)
        emit_ld_wo(1)
        emit_cast_wo(0)
        emit_ld_wo(2)
        emit_cast_wo(1)
        emit_ld_wo(3)
        S.wait("sync", "ac", att_ac[chs[-1]])
        op_dv = None
        xo_ld = {}
        resid_dv = {}
        x1s_val = {}
        st_val = 0
        sq_of = {}

        for step in range(9):
            tt = step
            if tt < 8:
                c = tt // 4
                bp = tt % 2
                if tt >= 2:
                    S.wait("sync", "dv", resid_dv[tt - 2])
                xo_ld[tt] = S.op("sync", lambda s, tt=tt, bp=bp: s.dma_start(
                    out=xoB[bp][:], in_=x_own[tt * 128:(tt + 1) * 128, :]),
                    inc=f"s_xo{bp}", inc_n=16)
                for fw in range(4):
                    if (tt, fw) == (0, 0):
                        S.wait("tensor", "ac", att_ac[chs[-1]])
                    else:
                        S.wait("tensor", "dv", op_dv)
                    if tt == 0:
                        S.wait("tensor", "dv", cast_wo[2 * fw + 1])
                    for ec in range(16):
                        def mo(t, c=c, tt=tt, fw=fw, ec=ec):
                            lhs = attnT[:, (c * 16 + ec) * 512 + (tt % 4) * 128:
                                        (c * 16 + ec) * 512 + (tt % 4) * 128 + 128]
                            rhs = wo[:, (fw * 16 + ec) * 512:
                                      (fw * 16 + ec) * 512 + 512]
                            return t.matmul(ps_OP[:], lhs, rhs,
                                            start=(ec == 0), stop=(ec == 15))
                        S.op("tensor", mo, inc="pe" if ec == 15 else None)
                    pe_op = S.cur("pe")
                    S.wait("vector", "pe", pe_op)
                    S.wait("vector", f"s_xo{bp}", xo_ld[tt])
                    def rp(v, fw=fw, bp=bp):
                        return v.tensor_tensor(
                            residB[bp][:, fw * 512:(fw + 1) * 512], ps_OP[:],
                            xoB[bp][:, fw * 512:(fw + 1) * 512], ALU.add)
                    op_dv = S.op("vector", rp)
                    if tt == 0 and fw <= 2:
                        emit_cast_wo(2 * fw + 2)
                        emit_cast_wo(2 * fw + 3)
                        for rr_ in (2 * fw + 4, 2 * fw + 5):
                            if rr_ <= 7:
                                emit_ld_wo(rr_)
                resid_dv[tt] = op_dv
            if step >= 1:
                lt = step - 1
                bp = lt % 2
                S.op("vector", lambda v, bp=bp: v.reduce_sum(
                    sm[:, 8:9], residB[bp][:], AX.X, negate=True))
                nm_dv = S.op("vector", lambda v: v.tensor_scalar(
                    sm[:, 9:10], sm[:, 8:9], 1.0 / E, None, ALU.mult))
                S.wait("scalar", "dv", nm_dv)
                sq_of[lt] = S.op("scalar", lambda s, bp=bp: s.activation(
                    sqt[:], residB[bp][:], AF.Square, accum_out=sm[:, 10:11]))
                S.wait("vector", "ac", sq_of[lt])
                S.op("vector", lambda v: v.tensor_scalar(
                    sm[:, 11:12], sm[:, 10:11], 1.0 / E, None, ALU.mult))
                S.op("vector", lambda v: v.tensor_tensor(
                    sm[:, 12:13], sm[:, 9:10], sm[:, 9:10], ALU.mult))
                S.op("vector", lambda v: v.tensor_tensor(
                    sm[:, 13:14], sm[:, 11:12], sm[:, 12:13], ALU.subtract))
                S.op("vector", lambda v: v.tensor_scalar(
                    sm[:, 14:15], sm[:, 13:14], EPS, None, ALU.add))
                rv_dv = S.op("vector", lambda v: v.reciprocal(
                    sm[:, 15:16], sm[:, 14:15]))
                S.wait("scalar", "dv", rv_dv)
                rstd_ac = S.op("scalar", lambda s: s.activation(
                    sm[:, 16:17], sm[:, 15:16], AF.Sqrt))
                S.wait("vector", "ac", rstd_ac)
                S.wait("vector", "s_x1s", st_val)
                x1_dv = S.op("vector", lambda v, bp=bp: v.tensor_scalar(
                    x1til[:], residB[bp][:], sm[:, 9:10], sm[:, 16:17],
                    ALU.add, ALU.mult))
                S.wait("sync", "dv", x1_dv)
                st_val = S.op("sync", lambda s, lt=lt: s.dma_start(
                    out=x1_d[lt * 128:(lt + 1) * 128, :], in_=x1til[:]),
                    inc="s_x1s", inc_n=16)
                x1s_val[lt] = st_val
        x1_dv_last = S.cur("dv")

        wgsrc_h = [cc_out_gh[h].ap().rearrange("(c p) n -> p c n", p=128)
                   for h in range(2)]
        wusrc_h = [cc_out_uh[h].ap().rearrange("(c p) n -> p c n", p=128)
                   for h in range(2)]
        wd_ap_h = [cc_out_dh[h].ap() for h in range(2)]

        f1_evac = None
        cast_dv = {}
        tp_pe = 0
        for tt in range(8):
            if tt >= 1:
                S.wait("sync", "dv", cast_dv[tt - 1])
            S.wait("sync", "s_x1s", x1s_val[tt])
            xo_ld = S.op("sync", lambda s, tt=tt: s.dma_start(
                out=xo1[:], in_=x1_d[tt * 128:(tt + 1) * 128, :]),
                inc="s_f1", inc_n=16)
            S.wait("vector", "s_f1", xo_ld)
            if tt >= 1:
                S.wait("vector", "pe", tp_pe)
            cast_dv[tt] = S.op("vector", lambda v: v.tensor_copy(x1b[:], xo1[:]))
            S.wait("tensor", "dv", cast_dv[tt])
            if tt >= 1:
                S.wait("tensor", "dv", f1_evac)
            for hc in range(16):
                def tr1(t, hc=hc):
                    base = 256 * (hc // 2) + (hc % 2)
                    src = x1b[:, base: base + 255: 2]
                    return t.transpose(
                        ps_TP[hc // 8][:, (hc % 8) * 128:(hc % 8 + 1) * 128],
                        src, ident[:])
                S.op("tensor", tr1, inc="pe" if hc == 15 else None)
            tp_pe = S.cur("pe")
            S.wait("vector", "pe", tp_pe)
            for half in range(2):
                def ev1(v, tt=tt, half=half):
                    src = ps_TP[half][:].rearrange("p (m f) -> p m f", f=128)
                    dst = x1T[:].rearrange("p (c t) -> p c t", t=TOWN)[
                        :, half * 8:(half + 1) * 8, tt * 128:(tt + 1) * 128]
                    return v.tensor_copy(dst, src)
                f1_evac = S.op("vector", ev1)
        f1_done_dv = f1_evac

        g_idx = 0
        gelu_dv, gelu_free = [], []
        gp_p_l, ac_th_l, gp_q_l, dv_arg_l = [], [], [], []
        slab_sld, slab_pe_done = {}, {}
        wd_sld, down_pe_G = {}, {}
        ga_vals, accv_dv = [], []
        inter_done = {}
        first_wg = True
        Gseq = [0, 2, 4, 6, 8, 10, 12, 14, 1, 3, 5, 7, 9, 11, 13, 15]
        for gi, G in enumerate(Gseq):
            hf = G % 2
            wdbuf = gi % 2
            if gi < 2:
                S.wait("sync", "dv", f1_done_dv)
            else:
                S.wait("sync", "pe", down_pe_G[Gseq[gi - 2]])
            S.wait("sync", f"s_cc{3 * hf + 2}", cc_done[3 * hf + 2])
            for c_l in range(8):
                row0 = 1024 * (G // 2) + 256 * (c_l // 2) + 128 * (c_l % 2)
                def ldwd(s, row0=row0, hf=hf, wdbuf=wdbuf, c_l=c_l):
                    return s.dma_start(
                        out=wd_f[wdbuf][:, c_l * E:(c_l + 1) * E],
                        in_=wd_ap_h[hf][row0:row0 + 128, :])
                wd_sld[G] = S.op("sync", ldwd, inc=f"s_wd{wdbuf}", inc_n=16)
            for sw in range(2):
                if gi >= 1:
                    S.wait("sync", "pe", slab_pe_done[(Gseq[gi - 1], sw)])
                if first_wg:
                    S.wait("sync", "dv", x1_dv_last)
                    S.wait("sync", "s_x1s", S.cur("s_x1s"))
                    first_wg = False
                S.wait("sync", f"s_cc{3 * hf}", cc_done[3 * hf])
                S.wait("sync", f"s_cc{3 * hf + 1}", cc_done[3 * hf + 1])
                nsl = slice(1024 * (G // 2) + 512 * sw,
                            1024 * (G // 2) + 512 * (sw + 1))
                S.op("sync", lambda s, sw=sw, nsl=nsl, hf=hf: s.dma_start(
                    out=wg_f[sw][:].rearrange("p (c n) -> p c n", n=512),
                    in_=wgsrc_h[hf][:, :, nsl]), inc=f"s_gu{sw}", inc_n=16)
                slab_sld[(G, sw)] = S.op("sync", lambda s, sw=sw, nsl=nsl,
                                         hf=hf: s.dma_start(
                    out=wu_f[sw][:].rearrange("p (c n) -> p c n", n=512),
                    in_=wusrc_h[hf][:, :, nsl]), inc=f"s_gu{sw}", inc_n=16)
            for c_l in range(8):
                sw = c_l // 4
                woff = 256 * ((c_l % 4) // 2)
                sub = c_l % 2
                for tw in range(2):
                    gs = surfs[g_idx % 2]
                    us = surfs[2 + g_idx % 2]
                    S.wait("tensor", f"s_gu{sw}", slab_sld[(G, sw)])
                    if g_idx == 0:
                        S.wait("tensor", "dv", f1_done_dv)
                    if g_idx >= 2:
                        _, dv_ub2 = gelu_free[g_idx - 2]
                        S.wait("tensor", "dv", dv_ub2)
                    for wgt, surf in ((wg_f[sw], gs), (wu_f[sw], us)):
                        for hc in range(16):
                            def mmg(t, wgt=wgt, surf=surf, hc=hc, woff=woff,
                                    sub=sub, tw=tw):
                                lhs = wgt[:, hc * 512 + woff + sub:
                                          hc * 512 + woff + sub + 255: 2]
                                rhs = x1T[:, hc * TOWN + tw * 512:
                                          hc * TOWN + tw * 512 + 512]
                                return t.matmul(surf, lhs, rhs,
                                                start=(hc == 0), stop=(hc == 15))
                            S.op("tensor", mmg,
                                 inc="pe" if hc == 15 else None)
                    pe_gu = S.cur("pe")
                    par = g_idx % 2
                    g2, pp, gb, ub, arg, th, gu, q8 = gt[par]
                    S.wait("vector", "pe", pe_gu)
                    dv_gb = S.op("vector", lambda v, gb=gb, gs=gs:
                                 v.tensor_copy(gb[:], gs))
                    dv_ub = S.op("vector", lambda v, ub=ub, us=us:
                                 v.tensor_copy(ub[:], us))
                    S.wait("scalar", "dv", dv_gb)
                    ac_g2 = S.op("scalar", lambda s, g2=g2, gb=gb: s.activation(
                        g2[:], gb[:], AF.Square))
                    S.wait("vector", "ac", ac_g2)
                    S.op("vector", lambda v, pp=pp, g2=g2: v.tensor_scalar(
                        pp[:], g2[:], C2, C1, ALU.mult, ALU.add))
                    if g_idx >= 2:
                        S.wait("vector", "ac", ac_th_l[g_idx - 2])
                    dv_arg = S.op("vector", lambda v, arg=arg, pp=pp, gb=gb:
                                  v.tensor_tensor(arg[:], pp[:], gb[:],
                                                  ALU.mult))
                    gu_dv = S.op("vector", lambda v, gu=gu, gb=gb, ub=ub:
                                 v.tensor_tensor(gu[:], gb[:], ub[:], ALU.mult))
                    S.wait("scalar", "dv", dv_arg)
                    ac_th = S.op("scalar", lambda s, th=th, arg=arg: s.activation(
                        th[:], arg[:], AF.Tanh))
                    S.wait("vector", "ac", ac_th)
                    S.op("vector", lambda v, q8=q8, th=th: v.tensor_scalar(
                        q8[:], th[:], 0.5, 0.5, ALU.mult, ALU.add))
                    dst_i = interT[:, c_l * TOWN + tw * 512:
                                   c_l * TOWN + tw * 512 + 512]
                    dv_int = S.op("vector", lambda v, dst_i=dst_i, q8=q8, gu=gu:
                                  v.tensor_tensor(dst_i, q8[:], gu[:],
                                                  ALU.mult))
                    gelu_dv.append(dv_int)
                    gelu_free.append((0, dv_ub))
                    ac_th_l.append(ac_th)
                    slab_pe_done[(G, sw)] = pe_gu
                    g_idx += 1
            inter_done[G] = gelu_dv[-1]
            for tt in range(8):
                for hw in range(4):
                    psi = (tt * 4 + hw) % 2
                    ps_D = ps_AV if psi == 0 else ps_OP
                    S.wait("tensor", "dv", inter_done[G])
                    S.wait("tensor", f"s_wd{wdbuf}", wd_sld[G])
                    idx = gi * 32 + tt * 4 + hw
                    if idx >= 2:
                        S.wait("tensor", "dv", accv_dv[idx - 2])
                    for c_l in range(8):
                        def mmd(t, ps_D=ps_D, wdbuf=wdbuf, c_l=c_l, tt=tt, hw=hw):
                            lhs = interT[:, c_l * TOWN + tt * 128:
                                         c_l * TOWN + tt * 128 + 128]
                            rhs = wd_f[wdbuf][:, c_l * E + hw * 512:
                                              c_l * E + hw * 512 + 512]
                            return t.matmul(ps_D[:], lhs, rhs,
                                            start=(c_l == 0), stop=(c_l == 7))
                        S.op("tensor", mmd, inc="pe" if c_l == 7 else None)
                    pe_d = S.cur("pe")
                    S.wait("vector", "pe", pe_d)
                    pidx = gi * 16 + tt * 2 + hw // 2
                    di = pidx % 2
                    if hw % 2 == 0 and pidx >= 2:
                        S.wait("vector", f"ga{di}", ga_vals[pidx - 2])
                    adv = S.op("vector", lambda v, di=di, hw=hw, ps_D=ps_D:
                               v.tensor_copy(
                        accv[di][:, (hw % 2) * 512:(hw % 2 + 1) * 512],
                        ps_D[:]))
                    accv_dv.append(adv)
                    if hw % 2 == 1:
                        S.wait("gpsimd", "dv", adv)
                        def ga_op(g, di=di, tt=tt, hw=hw, gi=gi):
                            dst = acc_d[tt * 128:(tt + 1) * 128,
                                        (hw // 2) * 1024:(hw // 2 + 1) * 1024]
                            if gi == 0:
                                return g.dma_start(out=dst, in_=accv[di][:])
                            return g.dma_start(out=dst, in_=accv[di][:],
                                               accum_op=ALU.add)
                        ga_vals.append(S.op("gpsimd", ga_op, inc=f"ga{di}",
                                            inc_n=16))
            down_pe_G[G] = S.cur("pe")

        S.wait("sync", "ga0", S.cur("ga0"))
        S.wait("sync", "ga1", S.cur("ga1"))
        y_val = 0
        r3_dv = {}
        for step in range(9):
            tt = step
            if tt < 8:
                bp = tt % 2
                if tt >= 2:
                    S.wait("sync", "dv", r3_dv[tt - 2])
                S.op("sync", lambda s, tt=tt, bp=bp: s.dma_start(
                    out=xo3B[bp][:], in_=x1_d[tt * 128:(tt + 1) * 128, :]),
                    inc=f"s_f3{bp}", inc_n=16)
                ld_a = S.op("sync", lambda s, tt=tt, bp=bp: s.dma_start(
                    out=accrB[bp][:], in_=acc_d[tt * 128:(tt + 1) * 128, :]),
                    inc=f"s_f3{bp}", inc_n=16)
                S.wait("vector", f"s_f3{bp}", ld_a)
                r3_dv[tt] = S.op("vector", lambda v, bp=bp: v.tensor_tensor(
                    resid3B[bp][:], xo3B[bp][:], accrB[bp][:], ALU.add))
            if step >= 1:
                lt = step - 1
                bp = lt % 2
                S.op("vector", lambda v, bp=bp: v.reduce_sum(
                    sm[:, 0:1], resid3B[bp][:], AX.X, negate=True))
                nm3 = S.op("vector", lambda v: v.tensor_scalar(
                    sm[:, 1:2], sm[:, 0:1], 1.0 / E, None, ALU.mult))
                S.wait("scalar", "dv", nm3)
                sq3 = S.op("scalar", lambda s, bp=bp: s.activation(
                    sqf[:], resid3B[bp][:], AF.Square, accum_out=sm[:, 2:3]))
                S.wait("vector", "ac", sq3)
                S.op("vector", lambda v: v.tensor_scalar(
                    sm[:, 3:4], sm[:, 2:3], 1.0 / E, None, ALU.mult))
                S.op("vector", lambda v: v.tensor_tensor(
                    sm[:, 4:5], sm[:, 1:2], sm[:, 1:2], ALU.mult))
                S.op("vector", lambda v: v.tensor_tensor(
                    sm[:, 5:6], sm[:, 3:4], sm[:, 4:5], ALU.subtract))
                S.op("vector", lambda v: v.tensor_scalar(
                    sm[:, 6:7], sm[:, 5:6], EPS, None, ALU.add))
                rv3 = S.op("vector", lambda v: v.reciprocal(sm[:, 7:8],
                                                            sm[:, 6:7]))
                S.wait("scalar", "dv", rv3)
                rstd3 = S.op("scalar", lambda s: s.activation(
                    sm[:, 8:9], sm[:, 7:8], AF.Sqrt))
                S.wait("vector", "ac", rstd3)
                S.wait("vector", "s_y", y_val)
                y_dv = S.op("vector", lambda v, bp=bp: v.tensor_scalar(
                    ytil[:], resid3B[bp][:], sm[:, 1:2], sm[:, 8:9],
                    ALU.add, ALU.mult))
                S.wait("sync", "dv", y_dv)
                y_val = S.op("sync", lambda s, lt=lt: s.dma_start(
                    out=y_out[lt * 128:(lt + 1) * 128, :], in_=ytil[:]),
                    inc="s_y", inc_n=16)
        S.wait("sync", "s_y", y_val)

        block = ctx.enter_context(nc.Block())
        S.run(block)
    return nc


_BUILD_CACHE = {}


def _cached(name, builder):
    if name not in _BUILD_CACHE:
        _BUILD_CACHE[name] = builder()
    return _BUILD_CACHE[name]


USE_MEGA = True


def kernel(**inputs):
    dec_maps, attn_maps, X = host_prep_inputs(**inputs)

    if USE_MEGA:
        nc = _cached("mega", build_mega)
        maps = [dict(**dec_maps[r], **attn_maps[r]) for r in range(NCORES)]
        res = run_bass_kernel_spmd(nc, maps, core_ids=list(range(NCORES)))
        out = np.concatenate([np.asarray(res.results[r]["y_out"])
                              for r in range(NCORES)], axis=0)
        return out.reshape(4, 2048, E).astype(np.float32)

    nc_dec = _cached("dec", build_decode_nc)
    res_dec = run_bass_kernel_spmd(nc_dec, dec_maps, core_ids=list(range(NCORES)))
    wg_all = np.concatenate([np.asarray(res_dec.results[r]["wg_dec"])
                             for r in range(NCORES)], axis=0)
    wu_all = np.concatenate([np.asarray(res_dec.results[r]["wu_dec"])
                             for r in range(NCORES)], axis=0)
    wd_all = np.concatenate([np.asarray(res_dec.results[r]["wd_dec"])
                             for r in range(NCORES)], axis=0)

    nc_attn = _cached("attn", build_attn_v0)
    res_attn = run_bass_kernel_spmd(nc_attn, attn_maps, core_ids=list(range(NCORES)))
    x1_parts = [np.asarray(res_attn.results[r]["x1_out"]) for r in range(NCORES)]

    nc_ffn = _cached("ffn", build_ffn_v0)
    ffn_maps = [dict(x1_in=x1_parts[r], wg_all=wg_all, wu_all=wu_all,
                     wd_all=wd_all) for r in range(NCORES)]
    res_ffn = run_bass_kernel_spmd(nc_ffn, ffn_maps, core_ids=list(range(NCORES)))
    out = np.concatenate([np.asarray(res_ffn.results[r]["y_out"])
                          for r in range(NCORES)], axis=0)
    return out.reshape(4, 2048, E).astype(np.float32)
